# version 17
# baseline (speedup 1.0000x reference)
import contextlib
import numpy as np
import concourse.bass as bass
import concourse.mybir as mybir
from concourse.bass_utils import run_bass_kernel_spmd

F32 = mybir.dt.float32
BF16 = mybir.dt.bfloat16
AF = mybir.ActivationFunctionType
ALU = mybir.AluOpType
AX = mybir.AxisListType

P = 128
T = 2048
D = 1024
NCH = D // P
DIN = 2048
NIN = DIN // P
DST = 16
DTR = 64
DFF = 4096
NFF = DFF // P
NEXP = 16
INTER = 512
NH = 8
HD = 128
EPS = 1e-5
NEG = -30000.0
CONSTS = [1.0 / D, -0.5, -1.0, EPS, 1.0, 1.0 / P, 0.5, float(np.sqrt(HD))]

DEBUG_TAGS = set()
N_PHASES = 99



class _FakeInstr:
    __slots__ = ("incs",)

    def __init__(self, incs):
        self.incs = incs


class _FakeEng:
    __slots__ = ("rec",)

    def __init__(self, rec):
        self.rec = rec

    def __getattr__(self, name):
        rec = self.rec

        def call(*a, **k):
            incs = []
            rec.append((name, a, k, incs))
            return _FakeInstr(incs)
        return call


class Sec:

    def __init__(self, B, engname):
        self.B, self.engname = B, engname
        self.e = _FakeEng(B.streams[engname])

    def wait(self, *evs):
        for ev in evs:
            if ev is None:
                continue
            s, v = ev
            if v <= 0:
                continue
            key = (self.engname, s)
            if self.B.waited.get(key, 0) < v:
                self.e.wait_ge(self.B.sem(s), v)
                self.B.waited[key] = v

    def inc(self, instr, semname=None, amt=1):
        s = semname or self.B.engsem[self.engname]
        self.B.cnt[s] = self.B.cnt.get(s, 0) + amt
        instr.incs.append((self.B.sem(s), amt))
        return (s, self.B.cnt[s])

    def dma(self, out, in_, stream):
        ins = self.e.dma_start(out=out, in_=in_)
        return self.inc(ins, semname=stream, amt=16)

    def chain(self, instr):
        ev = self.inc(instr)
        self.wait(ev)
        return ev


class Bld:
    def __init__(self, nc, es):
        self.nc, self.es = nc, es
        self.block = es.enter_context(nc.Block())
        self.sems = {}
        self.cnt = {}
        self.waited = {}
        self.engsem = {"tensor": "T", "vector": "V", "scalar": "S",
                       "gpsimd": "G", "sync": "Y"}
        self.streams = {k: [] for k in self.engsem}
        self.lastev = {}

    def sem(self, name):
        if name not in self.sems:
            self.sems[name] = self.es.enter_context(self.nc.semaphore(name))
            self.cnt.setdefault(name, 0)
        return self.sems[name]

    def ev(self, name):
        return (name, self.cnt.get(name, 0))

    def sec(self, engname):
        def deco(fn):
            fn(Sec(self, engname))
        return deco

    def barrier(self):
        evs = [(s, c) for s, c in self.cnt.items() if c > 0]
        for engname in self.engsem:
            @self.sec(engname)
            def _(c):
                c.wait(*evs)

    def flush(self):
        for engname, rec in self.streams.items():
            if not rec:
                continue

            def run(e, rec=list(rec)):
                for name, a, k, incs in rec:
                    instr = getattr(e, name)(*a, **k)
                    for sem, amt in incs:
                        instr.then_inc(sem, amt)
            getattr(self.block, engname)(run)
            rec.clear()


def ceil_div(a, b):
    return -(-a) // b



def build_program():
    nc = bass.Bass("TRN2", target_bir_lowering=False)
    dram = {}

    def din(name, shape, dtype=F32):
        dram[name] = nc.dram_tensor(name, shape, dtype, kind="ExternalInput").ap()
        return dram[name]

    def dscr(name, shape, dtype=BF16):
        kind = "ExternalOutput" if name in DEBUG_TAGS else "Internal"
        dram[name] = nc.dram_tensor(name, shape, dtype, kind=kind).ap()
        return dram[name]

    x0 = din("x0", [D, T])
    w_inproj = din("w_inproj", [D, 2 * DIN])
    conv_w = din("conv_w", [P, NIN * 4])
    conv_b = din("conv_b", [P, NIN])
    w_xproj = din("w_xproj", [DIN, 112])
    w_dtproj = din("w_dtproj", [DTR, DIN])
    dt_bias = din("dt_bias", [P, NIN])
    a_log = din("a_log", [P, NIN * DST])
    d_vec = din("d_vec", [P, NIN])
    w_oproj = din("w_oproj", [DIN, D])
    ln1_g = din("ln1_g", [P, NCH]); ln1_b = din("ln1_b", [P, NCH])
    ln2_g = din("ln2_g", [P, NCH]); ln2_b = din("ln2_b", [P, NCH])
    w_ffn1 = din("w_ffn1", [D, DFF]); ffn_b1 = din("ffn_b1", [P, NFF])
    w_ffn2 = din("w_ffn2", [DFF, D]); ffn_b2 = din("ffn_b2", [P, NCH])
    gate_wT = din("gate_wT", [D, NEXP]); gate_b = din("gate_b", [NEXP, 1])
    w13T = din("w13T", [NEXP, D, 2 * INTER])
    w2T = din("w2T", [NEXP, INTER, D])
    w13sT = din("w13sT", [D, 2 * INTER])
    w2sT = din("w2sT", [INTER, D])
    wqT = din("wqT", [D, D]); wkT = din("wkT", [D, D])
    wvT = din("wvT", [D, D]); woT = din("woT", [D, D])
    l1_g = din("l1_g", [P, NCH]); l1_b = din("l1_b", [P, NCH])
    hscale = din("hscale", [1, NH])
    dirs = din("dirs", [P, NH])
    wrT = din("wrT", [D, D]); wkT2 = din("wkT2", [D, D]); wvT2 = din("wvT2", [D, D])
    mu_r = din("mu_r", [P, NCH]); mu_k = din("mu_k", [P, NCH])
    identf = din("identf", [P, P])
    ones_col = din("ones_col", [P, 1])
    ones_row = din("ones_row", [1, P])
    trimask = din("trimask", [P, P])
    rot_sT = din("rot_sT", [P, P])
    cos_q = din("cos_q", [P, T]); sin_q = din("sin_q", [P, T])
    cos_k = din("cos_k", [P, T]); sin_k = din("sin_k", [P, T])
    cpool = din("cpool", [P, len(CONSTS)])
    sel16 = din("sel16", [NEXP, NEXP * P])

    out = nc.dram_tensor("out", [D, T], F32, kind="ExternalOutput").ap()
    dram["out"] = out

    n0_sp = dscr("n0_sp", [P, NCH, T])
    xc_sp = dscr("xc_sp", [P, NIN, T])
    dt_sp = dscr("dt_sp", [P, NIN, T])
    sz_sp = dscr("sz_sp", [P, NIN, T])
    u1_sp = dscr("u1_sp", [P, NCH, T])
    h1_sp = dscr("h1_sp", [P, NFF, T])
    n1_sp = dscr("n1_sp", [P, NCH, T])
    xn_sp = dscr("xn_sp", [P, NCH, T])
    q_sp = dscr("q_sp", [P, NH, T])
    k_sp = dscr("k_sp", [P, NH, T])
    vt_sp = dscr("vt_sp", [P, NH, T // P, P])

    es = contextlib.ExitStack()
    with es:
        B = Bld(nc, es)

        def sb(name, shape, dtype, stack=es):
            return stack.enter_context(nc.sbuf_tensor(name, shape, dtype))

        def ps(name, shape, dtype=F32, stack=es):
            return stack.enter_context(nc.psum_tensor(name, shape, dtype))

        R = sb("R", [P, NCH * T], F32)
        onescol_b = sb("onescol_b", [P, 1], BF16)
        onesrow_f = sb("onesrow_f", [1, P], F32)
        onesrow_b = sb("onesrow_b", [1, P], BF16)
        ident_b = sb("ident_b", [P, P], BF16)
        ident_f = sb("ident_f", [P, P], F32)
        tri_f = sb("tri_f", [P, P], F32)
        cp = sb("cp", [P, len(CONSTS)], F32)
        sel16_b = sb("sel16_b", [NEXP, NEXP * P], BF16)
        for ci, cv in enumerate(CONSTS):
            nc.const_aps.register(F32, cv, cp[:, ci:ci + 1])

        @B.sec("gpsimd")
        def _(c):
            c.dma(R[:].rearrange("p (k t) -> p k t", k=NCH),
                  x0.rearrange("(k p) t -> p k t", p=P), "gC")
            c.dma(onescol_b[:], ones_col[:, :], "gC")
            c.dma(onesrow_b[:], ones_row[:, :], "gC")
            c.dma(ident_b[:], identf[:, :], "gC")
            c.dma(sel16_b[:], sel16[:, :], "gC")

        @B.sec("sync")
        def _(c):
            c.dma(onesrow_f[:], ones_row[:, :], "yM")
            c.dma(ident_f[:], identf[:, :], "yM")
            c.dma(tri_f[:], trimask[:, :], "yM")
            c.dma(cp[:], cpool[:, :], "yM")

        ph0_loads = [B.ev("gC"), B.ev("yM")]
        B.barrier()

        l0 = contextlib.ExitStack()
        with l0:
            B_fm = sb("B_fm", [DST, T], BF16, l0)
            C_fm = sb("C_fm", [DST, T], BF16, l0)
            comb_fm = sb("comb_fm", [NEXP, T], BF16, l0)

            phase1(B, nc, l0, dram, locals())
            if "dbg_l0" in DEBUG_TAGS:
                for nm, tens, dt in (("dbg_comb", comb_fm, BF16),
                                     ("dbg_bfm", B_fm, BF16),
                                     ("dbg_cfm", C_fm, BF16)):
                    dram[nm] = nc.dram_tensor(nm, list(tens[:].shape), dt,
                                              kind="ExternalOutput").ap()
                B.barrier()

                @B.sec("sync")
                def _(c):
                    c.dma(dram["dbg_comb"][:, :], comb_fm[:], "yD")
                    c.dma(dram["dbg_bfm"][:, :], B_fm[:], "yD")
                    c.dma(dram["dbg_cfm"][:, :], C_fm[:], "yD")
                    c.wait(B.ev("yD"))
            if N_PHASES >= 2:
                phase2_scan(B, nc, l0, dram, locals())
            if N_PHASES >= 3:
                phase3_mlp(B, nc, l0, dram, locals())
            if N_PHASES >= 4:
                phase4_moe(B, nc, l0, dram, locals())

        if N_PHASES >= 5:
            l1 = contextlib.ExitStack()
            with l1:
                phase5_l1(B, nc, l1, dram, locals())

        B.barrier()

        @B.sec("sync")
        def _(c):
            c.dma(out.rearrange("(k p) t -> p k t", p=P),
                  R[:].rearrange("p (k t) -> p k t", k=NCH), "yOUT")
            c.wait(B.ev("yOUT"))

        B.flush()
    return nc



def phase1(B, nc, l0, dram, env):
    R = env["R"]
    B_fm = env["B_fm"]; C_fm = env["C_fm"]; comb_fm = env["comb_fm"]
    onescol_b = env["onescol_b"]; onesrow_f = env["onesrow_f"]
    ident_f = env["ident_f"]
    ph0_loads = env["ph0_loads"]

    TC = 256
    NT = T // TC

    st = contextlib.ExitStack()
    with st:
        def sb(name, shape, dtype, stack=None):
            return (stack or st).enter_context(nc.sbuf_tensor(name, shape, dtype))

        def ps(name, shape, dtype=F32):
            return st.enter_context(nc.psum_tensor(name, shape, dtype))

        mu_row = sb("mu_row", [1, T], F32)
        r1_row = sb("r1_row", [1, T], F32)
        s_row = sb("s_row", [1, T], F32)
        w_xp = sb("w_xp_s", [P, NIN * 112], BF16)
        w_dt = sb("w_dt_s", [DTR, DIN], BF16)
        cw = sb("cw_s", [P, NIN * 4], F32)
        cb = sb("cb_s", [P, NIN], F32)
        dtb = sb("dtb_s", [P, NIN], F32)
        l1g = sb("l1g_s", [P, NCH], F32)
        l1bb = sb("l1bb_s", [P, NCH], F32)
        gw = sb("gw_s", [P, NCH * NEXP], F32)
        gb = sb("gb_s", [NEXP, 1], F32)
        nf = sb("nf", [P, NCH * TC], F32)
        ln1c = sb("ln1c", [P, NCH * TC], BF16)
        bc3 = sb("bc3", [P, 3 * TC], F32)
        sp1 = sb("sp1", [P, TC], F32)
        sp2 = sb("sp2", [P, TC], F32)
        tok16 = sb("tok16", [P, 5 * NEXP], F32)
        gbi_t = sb("gbi_t", [NEXP, P], F32)
        negbig = sb("negbig", [P, NEXP], F32)
        thr = sb("thr", [P, 1], F32)
        eqm_i = sb("eqm_i", [P, NEXP], mybir.dt.int32)
        rho = sb("rho", [P, 1], F32)
        gate_sc = sb("gate_sc", [NEXP, T], F32)
        r16_sb = sb("r16_sb", [NEXP, TC], F32)

        ps_bc = ps("ps_bc", [P, 512])
        ps_bc2 = ps("ps_bc2", [P, 512])
        ps_g = ps("ps_g", [NEXP, 512])
        ps_m0 = ps("ps_m0", [P, 512])
        ps_m1 = ps("ps_m1", [P, 512])
        ps_m2 = ps("ps_m2", [P, 512])
        ps_xd = ps("ps_xd", [112, 512])

        @B.sec("sync")
        def _(c):
            c.dma(cw[:], dram["conv_w"][:, :], "yM")
            c.dma(cb[:], dram["conv_b"][:, :], "yM")
            c.dma(dtb[:], dram["dt_bias"][:, :], "yM")
            c.dma(l1g[:], dram["ln1_g"][:, :], "yM")
            c.dma(l1bb[:], dram["ln1_b"][:, :], "yM")
            c.dma(gw[:].rearrange("p (k m) -> p k m", k=NCH),
                  dram["gate_wT"].rearrange("(k p) m -> p k m", p=P), "yM")
            c.dma(gb[:], dram["gate_b"][:, :], "yM")
        sload = B.ev("yM")

        @B.sec("gpsimd")
        def _(c):
            c.dma(w_xp[:].rearrange("p (k m) -> p k m", k=NIN),
                  dram["w_xproj"].rearrange("(k p) m -> p k m", p=P), "gB")
            c.dma(w_dt[:], dram["w_dtproj"][:, :], "gB")
        wsmall = B.ev("gB")

        sta = contextlib.ExitStack()
        with sta:
            Q_r = sb("Q_r", [1, T], F32, sta)
            rw = sb("rw", [1, 2 * T], F32, sta)
            xb_s = sb("xb_s", [P, NCH * TC], BF16, sta)
            xq_s = sb("xq_s", [P, NCH * TC], BF16, sta)

            for tc in range(NT):
                lo = tc * TC

                @B.sec("scalar")
                def _(c):
                    c.wait(*ph0_loads)
                    c.wait(B.ev("T"))
                    for k in range(NCH):
                        c.e.activation(xb_s[:, k * TC:(k + 1) * TC],
                                       R[:, k * T + lo:k * T + lo + TC], AF.Identity)
                        ins = c.e.activation(xq_s[:, k * TC:(k + 1) * TC],
                                             R[:, k * T + lo:k * T + lo + TC],
                                             AF.Square)
                    c.inc(ins)
                e_cast = B.ev("S")

                @B.sec("tensor")
                def _(c):
                    c.wait(*ph0_loads)
                    c.wait(e_cast, B.ev("S"))
                    for k in range(NCH):
                        c.e.matmul(ps_g[0:1, 0:TC], onescol_b[:],
                                   xb_s[:, k * TC:(k + 1) * TC],
                                   start=(k == 0), stop=(k == NCH - 1))
                    for k in range(NCH):
                        ins = c.e.matmul(ps_xd[0:1, 0:TC], onescol_b[:],
                                         xq_s[:, k * TC:(k + 1) * TC],
                                         start=(k == 0), stop=(k == NCH - 1))
                    c.inc(ins)
                e_mm = B.ev("T")

                @B.sec("scalar")
                def _(c):
                    c.wait(e_mm)
                    c.e.activation(mu_row[:, lo:lo + TC], ps_g[0:1, 0:TC], AF.Identity)
                    ins = c.e.activation(Q_r[:, lo:lo + TC], ps_xd[0:1, 0:TC],
                                         AF.Identity)
                    c.inc(ins)
            e_stats = B.ev("S")

            q1 = rw[:, 0:T]; v1 = rw[:, T:2 * T]

            @B.sec("scalar")
            def _(c):
                c.wait(e_stats)
                ins = c.e.activation(q1, mu_row[:, :], AF.Square, scale=1.0 / D)
                c.inc(ins)
            e1 = B.ev("S")

            @B.sec("vector")
            def _(c):
                c.wait(e1)
                ins = c.e.scalar_tensor_tensor(v1, Q_r[:, :], 1.0 / D, q1,
                                               ALU.mult, ALU.subtract)
                c.inc(ins)
            e2 = B.ev("V")

            @B.sec("scalar")
            def _(c):
                c.wait(e2)
                ev = c.inc(c.e.activation(q1, v1, AF.Ln, bias=EPS))
                c.wait(ev)
                ev = c.inc(c.e.activation(r1_row[:, :], q1, AF.Exp, scale=-0.5))
                c.wait(ev)
                c.inc(c.e.activation(q1, q1, AF.Exp, scale=-1.0))
            e3 = B.ev("S")

            @B.sec("vector")
            def _(c):
                c.wait(e3)
                ins = c.e.tensor_tensor(v1, v1, q1, ALU.mult)
                c.inc(ins)
            e4 = B.ev("V")

            @B.sec("scalar")
            def _(c):
                c.wait(e4)
                ev = c.inc(c.e.activation(v1, v1, AF.Ln, bias=EPS))
                c.wait(ev)
                c.inc(c.e.activation(v1, v1, AF.Exp, scale=-0.5))
            e5 = B.ev("S")

            @B.sec("vector")
            def _(c):
                c.wait(e5)
                ins = c.e.tensor_tensor(s_row[:, :], v1, r1_row[:, :], ALU.mult)
                c.inc(ins)
            B.barrier()
        e_rows = B.ev("V")

        def chunk_norm(c_lo, want_n0, n0c):
            lo = c_lo

            @B.sec("tensor")
            def _(c):
                c.wait(e_rows, B.ev("V"), B.ev("S"))
                ins = c.e.matmul(ps_bc[:, 0:TC], onesrow_f[:],
                                 mu_row[:, lo:lo + TC], start=True, stop=True)
                c.inc(ins)
            eb1 = B.ev("T")

            @B.sec("scalar")
            def _(c):
                c.wait(eb1)
                ins = c.e.activation(bc3[:, 0:TC], ps_bc[:, 0:TC], AF.Identity,
                                     scale=1.0 / D)
                c.inc(ins)
            eb1e = B.ev("S")

            @B.sec("tensor")
            def _(c):
                c.wait(eb1e)
                if want_n0:
                    c.e.matmul(ps_bc[:, 0:TC], onesrow_f[:], r1_row[:, lo:lo + TC],
                               start=True, stop=True)
                ins = c.e.matmul(ps_bc2[:, 0:TC], onesrow_f[:], s_row[:, lo:lo + TC],
                                 start=True, stop=True)
                c.inc(ins)
            eb2 = B.ev("T")

            @B.sec("scalar")
            def _(c):
                c.wait(eb2)
                if want_n0:
                    c.e.activation(bc3[:, TC:2 * TC], ps_bc[:, 0:TC], AF.Identity)
                ins = c.e.activation(bc3[:, 2 * TC:3 * TC], ps_bc2[:, 0:TC], AF.Identity)
                c.inc(ins)
            eb = B.ev("S")

            @B.sec("vector")
            def _(c):
                c.wait(eb, B.ev("T"), B.ev("yS"))
                for k in range(NCH):
                    c.chain(c.e.tensor_tensor(nf[:, k * TC:(k + 1) * TC],
                                              R[:, k * T + lo:k * T + lo + TC],
                                              bc3[:, 0:TC], ALU.subtract))
                    if want_n0:
                        c.e.tensor_tensor(n0c[:, k * TC:(k + 1) * TC],
                                          nf[:, k * TC:(k + 1) * TC],
                                          bc3[:, TC:2 * TC], ALU.mult)
                    ins = c.e.tensor_tensor(ln1c[:, k * TC:(k + 1) * TC],
                                            nf[:, k * TC:(k + 1) * TC],
                                            bc3[:, 2 * TC:3 * TC], ALU.mult)
                c.inc(ins)
            e_nf = B.ev("V")

            @B.sec("scalar")
            def _(c):
                c.wait(e_nf, sload)
                for k in range(NCH):
                    ins = c.e.activation(ln1c[:, k * TC:(k + 1) * TC],
                                         ln1c[:, k * TC:(k + 1) * TC], AF.Identity,
                                         scale=l1g[:, k:k + 1], bias=l1bb[:, k:k + 1])
                c.inc(ins)
            return e_nf, B.ev("S")

        stz = contextlib.ExitStack()
        with stz:
            w_z = sb("w_z_s", [P, NCH * DIN], BF16, stz)
            n0c = sb("n0c", [P, NCH * TC], BF16, stz)
            sz_st = sb("sz_st", [P, NIN * TC], BF16, stz)

            @B.sec("gpsimd")
            def _(c):
                wz = dram["w_inproj"][:, DIN:2 * DIN]
                c.dma(w_z[:].rearrange("p (k m) -> p k m", k=NCH),
                      wz.rearrange("(k p) m -> p k m", p=P), "gA")
            wload = B.ev("gA")

            for tc in range(NT):
                lo = tc * TC
                e_nf, e_ln1 = chunk_norm(lo, True, n0c)

                @B.sec("tensor")
                def _(c):
                    c.wait(e_nf, sload, B.ev("V"))
                    for k in range(NCH):
                        ins = c.e.matmul(ps_g[:, 0:TC],
                                         gw[:, k * NEXP:(k + 1) * NEXP],
                                         nf[:, k * TC:(k + 1) * TC],
                                         start=(k == 0), stop=(k == NCH - 1))
                    c.inc(ins)
                e_gmm = B.ev("T")

                @B.sec("tensor")
                def _(c):
                    c.wait(B.ev("S"))
                    ins = c.e.matmul(ps_bc[0:NEXP, 0:TC], onesrow_f[:, 0:NEXP],
                                     r1_row[:, lo:lo + TC], start=True, stop=True)
                    c.inc(ins)
                e_r116 = B.ev("T")

                @B.sec("scalar")
                def _(c):
                    c.wait(e_r116)
                    c.inc(c.e.activation(r16_sb[:], ps_bc[0:NEXP, 0:TC],
                                         AF.Identity))
                e_r116e = B.ev("S")

                @B.sec("vector")
                def _(c):
                    c.wait(e_gmm, e_r116e)
                    ins = c.e.tensor_tensor(gate_sc[:, lo:lo + TC], ps_g[:, 0:TC],
                                            r16_sb[:], ALU.mult)
                    c.inc(ins)
                e_gm = B.ev("V")

                @B.sec("scalar")
                def _(c):
                    c.wait(e_gm)
                    ins = c.e.activation(gate_sc[:, lo:lo + TC],
                                         gate_sc[:, lo:lo + TC], AF.Sigmoid)
                    c.inc(ins)

                for o in range(NIN):
                    pmm = (ps_m0, ps_m1, ps_m2)[o % 3]
                    pk = "m%d" % (o % 3)

                    @B.sec("tensor")
                    def _(c):
                        c.wait(e_ln1, wload, B.lastev.get(pk))
                        for k in range(NCH):
                            ins = c.e.matmul(
                                pmm[:, 0:TC],
                                w_z[:, k * DIN + o * P:k * DIN + o * P + P],
                                ln1c[:, k * TC:(k + 1) * TC],
                                start=(k == 0), stop=(k == NCH - 1))
                        c.inc(ins)
                    e_z = B.ev("T")

                    @B.sec("scalar")
                    def _(c):
                        c.wait(e_z, B.ev("yS"))
                        ins = c.e.activation(sz_st[:, o * TC:(o + 1) * TC],
                                             pmm[:, 0:TC], AF.Silu)
                        B.lastev[pk] = c.inc(ins)
                e_sz = B.ev("S")

                @B.sec("sync")
                def _(c):
                    c.wait(e_sz, e_nf)
                    c.dma(dram["sz_sp"][:, :, lo:lo + TC],
                          sz_st[:].rearrange("p (k t) -> p k t", k=NIN), "yS")
                    c.dma(dram["n0_sp"][:, :, lo:lo + TC],
                          n0c[:].rearrange("p (k t) -> p k t", k=NIN // 2), "yS")

        @B.sec("vector")
        def _(c):
            c.e.memset(negbig[:], NEG)

        for j in range(T // P):
            @B.sec("scalar")
            def _(c):
                c.wait(B.ev("V"), B.ev("T"))
                ins = c.e.activation(gbi_t[:, :], gate_sc[:, j * P:(j + 1) * P],
                                     AF.Identity, bias=gb[:, 0:1])
                c.inc(ins)
            e_bi = B.ev("S")

            @B.sec("tensor")
            def _(c):
                c.wait(e_bi)
                c.e.transpose(ps_m0[:, 0:NEXP], gate_sc[:, j * P:(j + 1) * P],
                              ident_f[0:NEXP, 0:NEXP])
                ins = c.e.transpose(ps_m1[:, 0:NEXP], gbi_t[:, :],
                                    ident_f[0:NEXP, 0:NEXP])
                c.inc(ins)
            e_t = B.ev("T")

            @B.sec("scalar")
            def _(c):
                c.wait(e_t, B.ev("V"))
                c.e.activation(tok16[:, 0:NEXP], ps_m0[:, 0:NEXP], AF.Identity)
                c.e.activation(tok16[:, NEXP:2 * NEXP], ps_m1[:, 0:NEXP], AF.Identity)
                ins = c.e.activation(tok16[:, 2 * NEXP:3 * NEXP],
                                     ps_m1[:, 0:NEXP], AF.Identity)
                c.inc(ins)
            e_tt = B.ev("S")

            @B.sec("vector")
            def _(c):
                c.wait(e_tt)
                sc_t = tok16[:, 0:NEXP]
                bi_t = tok16[:, NEXP:2 * NEXP]
                wrk = tok16[:, 2 * NEXP:3 * NEXP]
                cmb = tok16[:, 4 * NEXP:5 * NEXP]
                for r in range(4):
                    c.chain(c.e.reduce_max(thr[:], wrk, axis=AX.X))
                    if r < 3:
                        c.chain(c.e.tensor_single_scalar(eqm_i[:], wrk, thr[:, 0:1],
                                                         ALU.is_ge))
                        c.chain(c.e.copy_predicated(wrk, eqm_i[:], negbig[:]))
                ins = c.e.scalar_tensor_tensor(cmb, bi_t, thr[:, 0:1], sc_t,
                                               ALU.is_ge, ALU.mult,
                                               accum_out=rho[:])
                c.chain(ins)
                c.chain(c.e.reciprocal(rho[:], rho[:]))
                ins = c.e.tensor_scalar(cmb, cmb, rho[:, 0:1], 2.5,
                                        ALU.mult, ALU.mult)
                c.inc(ins)
            e_c = B.ev("V")

            @B.sec("tensor")
            def _(c):
                c.wait(e_c, B.ev("S"))
                ins = c.e.transpose(ps_m0[0:NEXP, 0:P],
                                    tok16[:, 4 * NEXP:5 * NEXP], ident_f[:])
                c.inc(ins)
            e_cb = B.ev("T")

            @B.sec("scalar")
            def _(c):
                c.wait(e_cb)
                ins = c.e.activation(comb_fm[:, j * P:(j + 1) * P],
                                     ps_m0[0:NEXP, 0:P], AF.Identity)
                c.inc(ins)

        stx = contextlib.ExitStack()
        with stx:
            w_x = sb("w_x_s", [P, NCH * DIN], BF16, stx)
            xi_buf = sb("xi_buf", [P, NIN * (TC + 3)], BF16, stx)
            xc_st = sb("xc_st", [P, NIN * TC], BF16, stx)
            dt_st = sb("dt_st", [P, NIN * TC], BF16, stx)
            dtr_s = sb("dtr_s", [DTR, TC], BF16, stx)

            @B.sec("gpsimd")
            def _(c):
                c.wait(B.ev("T"), B.ev("V"), B.ev("S"))
                wx = dram["w_inproj"][:, 0:DIN]
                c.dma(w_x[:].rearrange("p (k m) -> p k m", k=NCH),
                      wx.rearrange("(k p) m -> p k m", p=P), "gA")
            wload2 = B.ev("gA")

            @B.sec("vector")
            def _(c):
                c.wait(B.ev("yS"), B.ev("T"), B.ev("S"))
                for o in range(NIN):
                    ins = c.e.memset(xi_buf[:, o * (TC + 3):o * (TC + 3) + 3], 0.0)
                c.inc(ins)

            for tc in range(NT):
                lo = tc * TC
                e_nf, e_ln1 = chunk_norm(lo, False, None)

                for o in range(NIN):
                    pmm = (ps_m0, ps_m1, ps_m2)[o % 3]
                    pk = "m%d" % (o % 3)

                    @B.sec("tensor")
                    def _(c):
                        c.wait(e_ln1, wload2, B.lastev.get(pk), B.ev("V"))
                        for k in range(NCH):
                            ins = c.e.matmul(
                                pmm[:, 0:TC],
                                w_x[:, k * DIN + o * P:k * DIN + o * P + P],
                                ln1c[:, k * TC:(k + 1) * TC],
                                start=(k == 0), stop=(k == NCH - 1))
                        c.inc(ins)
                    e_x = B.ev("T")

                    @B.sec("scalar")
                    def _(c):
                        c.wait(e_x)
                        ins = c.e.activation(
                            xi_buf[:, o * (TC + 3) + 3:(o + 1) * (TC + 3)],
                            pmm[:, 0:TC], AF.Identity)
                        B.lastev[pk] = c.inc(ins)
                e_xz = B.ev("S")

                @B.sec("vector")
                def _(c):
                    c.wait(e_xz, B.ev("yS"), B.ev("V"))
                    for o in range(NIN):
                        base = o * (TC + 3)
                        c.chain(c.e.tensor_scalar(sp1[:, :],
                                                  xi_buf[:, base:base + TC],
                                                  cw[:, o * 4:o * 4 + 1], None,
                                                  ALU.mult))
                        for k in (1, 2):
                            c.chain(c.e.scalar_tensor_tensor(
                                sp1[:, :], xi_buf[:, base + k:base + k + TC],
                                cw[:, o * 4 + k:o * 4 + k + 1],
                                sp1[:, :], ALU.mult, ALU.add))
                        c.chain(c.e.scalar_tensor_tensor(
                            xc_st[:, o * TC:(o + 1) * TC],
                            xi_buf[:, base + 3:base + 3 + TC],
                            cw[:, o * 4 + 3:o * 4 + 4],
                            sp1[:, :], ALU.mult, ALU.add))
                e_conv = B.ev("V")

                @B.sec("vector")
                def _(c):
                    for o in range(NIN):
                        base = o * (TC + 3)
                        ins = c.e.tensor_copy(xi_buf[:, base:base + 3],
                                              xi_buf[:, base + TC:base + TC + 3])
                    c.inc(ins)

                @B.sec("scalar")
                def _(c):
                    c.wait(e_conv)
                    for o in range(NIN):
                        ins = c.e.activation(xc_st[:, o * TC:(o + 1) * TC],
                                             xc_st[:, o * TC:(o + 1) * TC], AF.Silu,
                                             bias=cb[:, o:o + 1])
                    c.inc(ins)
                e_xc = B.ev("S")

                @B.sec("tensor")
                def _(c):
                    c.wait(e_xc, wsmall)
                    for k in range(NIN):
                        ins = c.e.matmul(ps_xd[:, 0:TC],
                                         w_xp[:, k * 112:(k + 1) * 112],
                                         xc_st[:, k * TC:(k + 1) * TC],
                                         start=(k == 0), stop=(k == NIN - 1))
                    c.inc(ins)
                e_xd = B.ev("T")

                @B.sec("scalar")
                def _(c):
                    c.wait(e_xd)
                    c.e.activation(dtr_s[:, :], ps_xd[0:DTR, 0:TC], AF.Identity)
                    c.e.activation(B_fm[:, lo:lo + TC], ps_xd[64:80, 0:TC],
                                   AF.Identity)
                    ins = c.e.activation(C_fm[:, lo:lo + TC],
                                         ps_xd[96:112, 0:TC], AF.Identity)
                    c.inc(ins)
                e_dtr = B.ev("S")

                for o in range(NIN):
                    pmm = (ps_m0, ps_m1, ps_m2)[o % 3]
                    pk = "m%d" % (o % 3)

                    @B.sec("tensor")
                    def _(c):
                        c.wait(e_dtr, B.lastev.get(pk))
                        ins = c.e.matmul(pmm[:, 0:TC], w_dt[:, o * P:(o + 1) * P],
                                         dtr_s[:, :], start=True, stop=True)
                        c.inc(ins)
                    e_d2 = B.ev("T")

                    @B.sec("scalar")
                    def _(c):
                        c.wait(e_d2, B.ev("V"))
                        c.e.activation(sp2[:, :], pmm[:, 0:TC], AF.Relu,
                                       bias=dtb[:, o:o + 1])
                        ev = c.inc(c.e.activation(sp1[:, :], pmm[:, 0:TC], AF.Abs,
                                                  bias=dtb[:, o:o + 1]))
                        B.lastev[pk] = ev
                        c.wait(ev)
                        ev = c.inc(c.e.activation(sp1[:, :], sp1[:, :], AF.Exp,
                                                  scale=-1.0))
                        c.wait(ev)
                        c.inc(c.e.activation(sp1[:, :], sp1[:, :], AF.Ln, bias=1.0))
                    e_sp = B.ev("S")

                    @B.sec("vector")
                    def _(c):
                        c.wait(e_sp, B.ev("yS"))
                        ins = c.e.tensor_tensor(dt_st[:, o * TC:(o + 1) * TC],
                                                sp1[:, :], sp2[:, :], ALU.add)
                        c.inc(ins)
                e_dt = B.ev("V")

                @B.sec("sync")
                def _(c):
                    c.wait(e_xc, e_dt)
                    c.dma(dram["xc_sp"][:, :, lo:lo + TC],
                          xc_st[:].rearrange("p (k t) -> p k t", k=NIN), "yS")
                    c.dma(dram["dt_sp"][:, :, lo:lo + TC],
                          dt_st[:].rearrange("p (k t) -> p k t", k=NIN), "yS")

        B.barrier()


def phase2_scan(B, nc, l0, dram, env):
    R = env["R"]
    B_fm = env["B_fm"]; C_fm = env["C_fm"]
    onesrow_b = env["onesrow_b"]
    TC = 256
    NT = T // TC

    st = contextlib.ExitStack()
    with st:
        def sb(name, shape, dtype, stack=None):
            return (stack or st).enter_context(nc.sbuf_tensor(name, shape, dtype))

        def ps(name, shape, dtype=F32):
            return st.enter_context(nc.psum_tensor(name, shape, dtype))

        w_op = sb("w_op_s", [P, NIN * D], BF16)
        a_fm = sb("a_fm", [P, NIN * DST], F32)
        d_fm = sb("d_fm", [P, NIN], F32)
        dt_all = sb("dt_all", [P, NIN * TC], BF16)
        xc_all = sb("xc_all", [P, NIN * TC], BF16)
        sz_all = sb("sz_all", [P, NIN * TC], BF16)
        n0c2 = sb("n0c2", [P, NCH * TC], BF16)
        dtx_all = sb("dtx_all", [P, NIN * TC], BF16)
        dA = sb("dA_all", [P, NIN * TC], F32)
        W_all = sb("W_all", [P, NIN * TC], BF16)
        hs2 = [sb("hs_a", [P, NIN * TC], BF16), sb("hs_b", [P, NIN * TC], BF16)]
        y_all = sb("y_all", [P, NIN * TC], BF16)
        Bb2 = [sb("Bb_a", [P, TC], BF16), sb("Bb_b", [P, TC], BF16)]
        Cb2 = [sb("Cb_a", [P, TC], BF16), sb("Cb_b", [P, TC], BF16)]
        h_state = sb("h_state", [P, NIN * DST], F32)
        u1c = sb("u1c", [P, NCH * TC], BF16)
        gt = sb("gt", [P, TC], BF16)
        uf = sb("uf", [P, TC], F32)

        ps_b1 = ps("ps_b1", [P, 512])
        ps_b2 = ps("ps_b2", [P, 512])
        ps_o0 = ps("ps_o0", [P, 512])
        ps_o1 = ps("ps_o1", [P, 512])
        ps_o2 = ps("ps_o2", [P, 512])

        @B.sec("gpsimd")
        def _(c):
            c.wait(B.ev("T"), B.ev("V"), B.ev("S"))
            c.dma(w_op[:].rearrange("p (k m) -> p k m", k=NIN),
                  dram["w_oproj"].rearrange("(k p) m -> p k m", p=P), "gA")
            c.dma(a_fm[:], dram["a_log"][:, :], "gA")
            c.dma(d_fm[:], dram["d_vec"][:, :], "gA")
        wload = B.ev("gA")

        @B.sec("scalar")
        def _(c):
            c.wait(wload)
            c.chain(c.e.activation(a_fm[:], a_fm[:], AF.Exp))
        e_a1 = B.ev("S")

        @B.sec("vector")
        def _(c):
            c.wait(e_a1)
            c.chain(c.e.tensor_scalar_mul(a_fm[:], a_fm[:], -1.0))
        e_a = B.ev("V")

        a3 = a_fm[:].rearrange("p (kc n) -> p kc n", n=DST)
        for tc in range(NT):
            lo = tc * TC

            @B.sec("sync")
            def _(c):
                c.wait(B.ev("V"), B.ev("T"), B.ev("S"))
                c.dma(dt_all[:].rearrange("p (k t) -> p k t", k=NIN),
                      dram["dt_sp"][:, :, lo:lo + TC], "yA")
                c.dma(xc_all[:].rearrange("p (k t) -> p k t", k=NIN),
                      dram["xc_sp"][:, :, lo:lo + TC], "yA")
                c.dma(sz_all[:].rearrange("p (k t) -> p k t", k=NIN),
                      dram["sz_sp"][:, :, lo:lo + TC], "yA")
                c.dma(n0c2[:].rearrange("p (k t) -> p k t", k=NCH),
                      dram["n0_sp"][:, :, lo:lo + TC], "yA")
            e_ld = B.ev("yA")

            @B.sec("vector")
            def _(c):
                c.wait(e_ld)
                c.chain(c.e.tensor_tensor(dtx_all[:], dt_all[:], xc_all[:],
                                          ALU.mult))
            e_dtx = B.ev("V")

            def emit_exp(n):
                @B.sec("scalar")
                def _(c):
                    c.wait(e_ld, e_a)
                    for g in range(4):
                        c.wait(B.lastev.get("p2sc%d" % g))
                        for kc in range(4 * g, 4 * g + 4):
                            ev = c.inc(c.e.activation(
                                dA[:, kc * TC:(kc + 1) * TC],
                                dt_all[:, kc * TC:(kc + 1) * TC], AF.Exp,
                                scale=a_fm[:, kc * DST + n:kc * DST + n + 1]))
                        B.lastev["p2dA%d" % g] = ev

            for n in range(DST):
                par = n % 2

                @B.sec("tensor")
                def _(c):
                    c.wait(e_dtx, B.lastev.get("p2bc"))
                    sel = env["sel16_b"][:, n * P:(n + 1) * P]
                    c.e.matmul(ps_b1[:, 0:TC], sel, B_fm[:, lo:lo + TC],
                               start=True, stop=True)
                    ins = c.e.matmul(ps_b2[:, 0:TC], sel, C_fm[:, lo:lo + TC],
                                     start=True, stop=True)
                    c.inc(ins)
                e_bc = B.ev("T")

                @B.sec("scalar")
                def _(c):
                    c.wait(e_bc, B.lastev.get("p2W%d" % par),
                           B.lastev.get("p2G%d" % par))
                    c.e.activation(Bb2[par][:], ps_b1[:, 0:TC], AF.Identity)
                    ins = c.e.activation(Cb2[par][:], ps_b2[:, 0:TC], AF.Identity)
                    B.lastev["p2bc"] = c.inc(ins)
                e_bce = B.ev("S")

                emit_exp(n)

                @B.sec("vector")
                def _(c):
                    c.wait(e_bce, e_dtx, B.lastev.get("p2G%d" % par))
                    bb = Bb2[par][:].unsqueeze(1).broadcast_to([P, NIN, TC])
                    ev = c.chain(c.e.tensor_tensor(
                        W_all[:].rearrange("p (k t) -> p k t", k=NIN),
                        dtx_all[:].rearrange("p (k t) -> p k t", k=NIN),
                        bb, ALU.mult))
                    B.lastev["p2W%d" % par] = ev
                    for g in range(4):
                        c.wait(B.lastev.get("p2dA%d" % g))
                        for kc in range(4 * g, 4 * g + 4):
                            init = (0.0 if tc == 0
                                    else h_state[:, kc * DST + n:
                                                 kc * DST + n + 1])
                            ev = c.inc(c.e.tensor_tensor_scan(
                                hs2[par][:, kc * TC:(kc + 1) * TC],
                                dA[:, kc * TC:(kc + 1) * TC],
                                W_all[:, kc * TC:(kc + 1) * TC],
                                init, ALU.mult, ALU.add))
                        B.lastev["p2sc%d" % g] = ev
                    c.wait(B.ev("V"))
                    c.chain(c.e.tensor_copy(
                        h_state[:].rearrange("p (kc n) -> p kc n", n=DST)[:, :, n],
                        hs2[par][:].rearrange("p (kc t) -> p kc t", kc=NIN)
                        [:, :, TC - 1]))
                e_vn = B.ev("V")

                @B.sec("gpsimd")
                def _(c):
                    c.wait(e_vn, e_bce)
                    cb = Cb2[par][:].unsqueeze(1).broadcast_to([P, NIN, TC])
                    c.chain(c.e.tensor_tensor(
                        hs2[par][:].rearrange("p (k t) -> p k t", k=NIN),
                        hs2[par][:].rearrange("p (k t) -> p k t", k=NIN),
                        cb, ALU.mult))
                    if n == 0:
                        c.wait(B.ev("T"))
                        ev = c.chain(c.e.tensor_copy(y_all[:], hs2[par][:]))
                    else:
                        ev = c.chain(c.e.tensor_tensor(y_all[:], y_all[:],
                                                       hs2[par][:], ALU.add))
                    B.lastev["p2G%d" % par] = ev
                    B.lastev["p2yG"] = ev

            @B.sec("vector")
            def _(c):
                c.wait(B.lastev.get("p2yG"))
                for kc in range(NIN):
                    c.chain(c.e.scalar_tensor_tensor(
                        gt[:], xc_all[:, kc * TC:(kc + 1) * TC],
                        d_fm[:, kc:kc + 1],
                        y_all[:, kc * TC:(kc + 1) * TC], ALU.mult, ALU.add))
                    c.chain(c.e.tensor_tensor(
                        y_all[:, kc * TC:(kc + 1) * TC], gt[:],
                        sz_all[:, kc * TC:(kc + 1) * TC], ALU.mult))
            e_y = B.ev("V")

            for oc in range(NCH):
                pmm = (ps_o0, ps_o1, ps_o2)[oc % 3]
                pk = "o%d" % (oc % 3)

                @B.sec("tensor")
                def _(c):
                    c.wait(e_y, wload, B.lastev.get(pk))
                    for kc in range(NIN):
                        ins = c.e.matmul(
                            pmm[:, 0:TC],
                            w_op[:, kc * D + oc * P:kc * D + oc * P + P],
                            y_all[:, kc * TC:(kc + 1) * TC],
                            start=(kc == 0), stop=(kc == NIN - 1))
                    c.inc(ins)
                e_mm = B.ev("T")

                @B.sec("vector")
                def _(c):
                    c.wait(e_mm, B.lastev.get("p2uf"))
                    ev = c.chain(c.e.tensor_tensor(uf[:], pmm[:, 0:TC],
                                                   n0c2[:, oc * TC:(oc + 1) * TC],
                                                   ALU.add))
                    B.lastev[pk] = ev
                    c.chain(c.e.tensor_tensor(R[:, oc * T + lo:oc * T + lo + TC],
                                              R[:, oc * T + lo:oc * T + lo + TC],
                                              uf[:], ALU.add))
                e_u = B.ev("V")

                @B.sec("scalar")
                def _(c):
                    c.wait(e_u, B.ev("yS"))
                    B.lastev["p2uf"] = c.inc(
                        c.e.activation(u1c[:, oc * TC:(oc + 1) * TC], uf[:],
                                       AF.Identity))
            e_uc = B.ev("S")

            @B.sec("sync")
            def _(c):
                c.wait(e_uc)
                c.dma(dram["u1_sp"][:, :, lo:lo + TC],
                      u1c[:].rearrange("p (k t) -> p k t", k=NCH), "yS")

    B.barrier()


def phase3_mlp(B, nc, l0, dram, env):
    R = env["R"]
    onescol_b = env["onescol_b"]; onesrow_f = env["onesrow_f"]
    TC = 512
    NT = T // TC

    st = contextlib.ExitStack()
    with st:
        def sb(name, shape, dtype, stack=None):
            return (stack or st).enter_context(nc.sbuf_tensor(name, shape, dtype))

        def ps(name, shape, dtype=F32):
            return st.enter_context(nc.psum_tensor(name, shape, dtype))

        b1 = sb("b1_s", [P, NFF], F32)
        b2 = sb("b2_s", [P, NCH], F32)
        l2g = sb("l2g_s", [P, NCH], F32)
        l2b = sb("l2b_s", [P, NCH], F32)
        u1b = sb("u1b", [P, NCH * TC], BF16)
        ln2c = sb("ln2c", [P, NCH * TC], BF16)
        usq = ln2c
        rows2 = sb("rows2", [1, 2 * TC], F32)
        bc2 = sb("bc2", [P, 2 * TC], F32)
        tf = sb("tf", [P, TC], F32)
        h1b = sb("h1b", [P, NFF * TC], BF16)

        ps_r0 = ps("ps_r0", [1, 512])
        ps_r1 = ps("ps_r1", [1, 512])
        ps_b = ps("ps_b", [P, 512])
        ps_b2_ = ps("ps_b2_", [P, 512])
        ps_m0 = ps("ps3_m0", [P, 512])
        ps_m1 = ps("ps3_m1", [P, 512])
        ps_m2 = ps("ps3_m2", [P, 512])

        @B.sec("sync")
        def _(c):
            c.dma(b1[:], dram["ffn_b1"][:, :], "yM")
            c.dma(b2[:], dram["ffn_b2"][:, :], "yM")
            c.dma(l2g[:], dram["ln2_g"][:, :], "yM")
            c.dma(l2b[:], dram["ln2_b"][:, :], "yM")
        sload = B.ev("yM")

        sta = contextlib.ExitStack()
        with sta:
            w1 = sb("w1_s", [P, NCH * DFF], BF16, sta)

            @B.sec("gpsimd")
            def _(c):
                c.wait(B.ev("T"), B.ev("V"), B.ev("S"))
                c.dma(w1[:].rearrange("p (k m) -> p k m", k=NCH),
                      dram["w_ffn1"].rearrange("(k p) m -> p k m", p=P), "gA")
            wload = B.ev("gA")

            for tc in range(NT):
                lo = tc * TC

                @B.sec("sync")
                def _(c):
                    c.wait(B.ev("V"), B.ev("T"), B.ev("S"))
                    c.dma(u1b[:].rearrange("p (k t) -> p k t", k=NCH),
                          dram["u1_sp"][:, :, lo:lo + TC], "yA")
                e_ld = B.ev("yA")

                @B.sec("scalar")
                def _(c):
                    c.wait(e_ld, B.ev("T"))
                    for k in range(NCH):
                        ins = c.e.activation(usq[:, k * TC:(k + 1) * TC],
                                             u1b[:, k * TC:(k + 1) * TC], AF.Square)
                    c.inc(ins)
                e_sq = B.ev("S")

                @B.sec("tensor")
                def _(c):
                    c.wait(e_sq, B.ev("S"))
                    for k in range(NCH):
                        c.e.matmul(ps_r0[0:1, 0:TC], onescol_b[:],
                                   u1b[:, k * TC:(k + 1) * TC],
                                   start=(k == 0), stop=(k == NCH - 1))
                    for k in range(NCH):
                        ins = c.e.matmul(ps_r1[0:1, 0:TC], onescol_b[:],
                                         usq[:, k * TC:(k + 1) * TC],
                                         start=(k == 0), stop=(k == NCH - 1))
                    c.inc(ins)
                e_st = B.ev("T")

                @B.sec("scalar")
                def _(c):
                    c.wait(e_st)
                    c.e.activation(rows2[:, 0:TC], ps_r0[0:1, 0:TC], AF.Identity)
                    c.chain(c.e.activation(rows2[:, TC:2 * TC], ps_r0[0:1, 0:TC],
                                           AF.Square, scale=1.0 / D))
                e_r1 = B.ev("S")

                @B.sec("vector")
                def _(c):
                    c.wait(e_r1)
                    c.chain(c.e.scalar_tensor_tensor(
                        rows2[:, TC:2 * TC], ps_r1[0:1, 0:TC], 1.0 / D,
                        rows2[:, TC:2 * TC], ALU.mult, ALU.subtract))
                e_v = B.ev("V")

                @B.sec("scalar")
                def _(c):
                    c.wait(e_v)
                    ev = c.chain(c.e.activation(rows2[:, TC:2 * TC],
                                                rows2[:, TC:2 * TC], AF.Ln,
                                                bias=EPS))
                    c.chain(c.e.activation(rows2[:, TC:2 * TC],
                                           rows2[:, TC:2 * TC], AF.Exp,
                                           scale=-0.5))
                e_rw = B.ev("S")

                @B.sec("tensor")
                def _(c):
                    c.wait(e_rw, B.ev("V"))
                    c.e.matmul(ps_b[:, 0:TC], onesrow_f[:], rows2[:, 0:TC],
                               start=True, stop=True)
                    ins = c.e.matmul(ps_b2_[:, 0:TC], onesrow_f[:],
                                     rows2[:, TC:2 * TC], start=True, stop=True)
                    c.inc(ins)
                e_bc = B.ev("T")

                @B.sec("scalar")
                def _(c):
                    c.wait(e_bc)
                    c.e.activation(bc2[:, 0:TC], ps_b[:, 0:TC], AF.Identity,
                                   scale=1.0 / D)
                    ins = c.e.activation(bc2[:, TC:2 * TC], ps_b2_[:, 0:TC],
                                         AF.Identity)
                    c.inc(ins)
                e_bce = B.ev("S")

                @B.sec("vector")
                def _(c):
                    c.wait(e_bce, B.ev("T"))
                    for k in range(NCH):
                        c.chain(c.e.tensor_tensor(tf[:], u1b[:, k * TC:(k + 1) * TC],
                                                  bc2[:, 0:TC], ALU.subtract))
                        c.chain(c.e.tensor_tensor(ln2c[:, k * TC:(k + 1) * TC],
                                                  tf[:], bc2[:, TC:2 * TC],
                                                  ALU.mult))
                e_n = B.ev("V")

                @B.sec("scalar")
                def _(c):
                    c.wait(e_n, sload)
                    for k in range(NCH):
                        ins = c.e.activation(ln2c[:, k * TC:(k + 1) * TC],
                                             ln2c[:, k * TC:(k + 1) * TC],
                                             AF.Identity, scale=l2g[:, k:k + 1],
                                             bias=l2b[:, k:k + 1])
                    c.inc(ins)
                e_l2 = B.ev("S")

                for of in range(NFF):
                    pmm = (ps_m0, ps_m1, ps_m2)[of % 3]
                    pk = "3m%d" % (of % 3)

                    @B.sec("tensor")
                    def _(c):
                        c.wait(e_l2, wload, B.lastev.get(pk))
                        for k in range(NCH):
                            ins = c.e.matmul(
                                pmm[:, 0:TC],
                                w1[:, k * DFF + of * P:k * DFF + of * P + P],
                                ln2c[:, k * TC:(k + 1) * TC],
                                start=(k == 0), stop=(k == NCH - 1))
                        c.inc(ins)
                    e_h = B.ev("T")

                    @B.sec("scalar")
                    def _(c):
                        c.wait(e_h, B.ev("yS"))
                        B.lastev[pk] = c.inc(
                            c.e.activation(h1b[:, of * TC:(of + 1) * TC],
                                           pmm[:, 0:TC], AF.Gelu_apprx_tanh,
                                           bias=b1[:, of:of + 1]))
                e_g = B.ev("S")

                @B.sec("sync")
                def _(c):
                    c.wait(e_g)
                    c.dma(dram["h1_sp"][:, :, lo:lo + TC],
                          h1b[:].rearrange("p (k t) -> p k t", k=NFF), "yS")

        for half in (0, 1):
            sth = contextlib.ExitStack()
            with sth:
                w2h = sb(f"w2h_s{half}", [P, (NFF // 2) * D], BF16, sth)
                h1h = sb(f"h1h{half}", [P, (NFF // 2) * TC], BF16, sth)

                @B.sec("gpsimd")
                def _(c):
                    c.wait(B.ev("T"), B.ev("V"), B.ev("S"))
                    w2part = dram["w_ffn2"][half * (DFF // 2):(half + 1) * (DFF // 2), :]
                    c.dma(w2h[:].rearrange("p (k m) -> p k m", k=NFF // 2),
                          w2part.rearrange("(k p) m -> p k m", p=P), "gA")
                wload2 = B.ev("gA")

                for tc in range(NT):
                    lo = tc * TC

                    @B.sec("sync")
                    def _(c):
                        c.wait(B.ev("T"), B.ev("yS"))
                        c.dma(h1h[:].rearrange("p (k t) -> p k t", k=NFF // 2),
                              dram["h1_sp"][:, half * (NFF // 2):(half + 1) * (NFF // 2),
                                            lo:lo + TC], "yA")
                    e_ld = B.ev("yA")

                    for oc in range(NCH):
                        pmm = (ps_m0, ps_m1, ps_m2)[oc % 3]
                        pk = "3m%d" % (oc % 3)

                        @B.sec("tensor")
                        def _(c):
                            c.wait(e_ld, wload2, B.lastev.get(pk))
                            for k in range(NFF // 2):
                                ins = c.e.matmul(
                                    pmm[:, 0:TC],
                                    w2h[:, k * D + oc * P:k * D + oc * P + P],
                                    h1h[:, k * TC:(k + 1) * TC],
                                    start=(k == 0), stop=(k == NFF // 2 - 1))
                            c.inc(ins)
                        e_mm = B.ev("T")

                        @B.sec("vector")
                        def _(c):
                            c.wait(e_mm)
                            dst = R[:, oc * T + lo:oc * T + lo + TC]
                            if half == 0:
                                B.lastev[pk] = c.chain(
                                    c.e.tensor_tensor(dst, dst, pmm[:, 0:TC],
                                                      ALU.add))
                            else:
                                B.lastev[pk] = c.chain(
                                    c.e.scalar_tensor_tensor(
                                        dst, pmm[:, 0:TC], b2[:, oc:oc + 1], dst,
                                        ALU.add, ALU.add))
    B.barrier()


def phase4_moe(B, nc, l0, dram, env):
    R = env["R"]
    comb_fm = env["comb_fm"]
    sel16_b = env["sel16_b"]
    TC = 512
    NT = T // TC

    st = contextlib.ExitStack()
    with st:
        def sb(name, shape, dtype, stack=None):
            return (stack or st).enter_context(nc.sbuf_tensor(name, shape, dtype))

        def ps(name, shape, dtype=F32):
            return st.enter_context(nc.psum_tensor(name, shape, dtype))

        n0r = sb("n0r", [P, NCH * T], BF16)
        w13b = sb("w13b", [P, NCH * 2 * INTER], BF16)
        w2b = sb("w2b", [P, (INTER // P) * D], BF16)
        hgd = [sb("hg0", [P, 4 * TC], BF16), sb("hg1", [P, 4 * TC], BF16)]
        hud = [sb("hu0", [P, 4 * TC], BF16), sb("hu1", [P, 4 * TC], BF16)]
        cbb = sb("cbb", [P, TC], BF16)
        tmp = sb("tmp4", [P, TC], BF16)
        ymoe = sb("ymoe", [P, NCH * T], BF16)

        ps_s0 = ps("ps4_s0", [P, 512])
        ps_s1 = ps("ps4_s1", [P, 512])
        ps_s2 = ps("ps4_s2", [P, 512])
        ps_t0 = ps("ps4_t0", [P, 512])
        ps_t1 = ps("ps4_t1", [P, 512])
        ps_t2 = ps("ps4_t2", [P, 512])
        ps_cb = ps("ps4_cb", [P, 512])

        @B.sec("sync")
        def _(c):
            c.wait(B.ev("V"), B.ev("T"), B.ev("S"))
            c.dma(n0r[:].rearrange("p (k t) -> p k t", k=NCH),
                  dram["n0_sp"][:, :, :], "yA")
        e_n0 = B.ev("yA")

        for e in range(NEXP + 1):
            shared = (e == NEXP)

            @B.sec("gpsimd")
            def _(c):
                c.wait(B.ev("T"))
                if shared:
                    c.dma(w13b[:].rearrange("p (k m) -> p k m", k=NCH),
                          dram["w13sT"].rearrange("(k p) m -> p k m", p=P), "gA")
                    c.dma(w2b[:].rearrange("p (k m) -> p k m", k=INTER // P),
                          dram["w2sT"].rearrange("(k p) m -> p k m", p=P), "gA")
                else:
                    c.dma(w13b[:].rearrange("p (k m) -> p k m", k=NCH),
                          dram["w13T"][e].rearrange("(k p) m -> p k m", p=P), "gA")
                    c.dma(w2b[:].rearrange("p (k m) -> p k m", k=INTER // P),
                          dram["w2T"][e].rearrange("(k p) m -> p k m", p=P), "gA")
            e_w = B.ev("gA")

            hh_ev = {}

            def emit_front(tc):
                lo = tc * TC
                par = tc % 2
                hg = hgd[par]
                hu = hud[par]

                e_cb = None
                if not shared:
                    @B.sec("tensor")
                    def _(c):
                        c.wait(B.lastev.get("4cbe"))
                        c.inc(c.e.matmul(ps_cb[:, 0:TC],
                                         sel16_b[:, e * P:(e + 1) * P],
                                         comb_fm[:, lo:lo + TC],
                                         start=True, stop=True))
                    e_cbm = B.ev("T")

                    @B.sec("scalar")
                    def _(c):
                        c.wait(e_cbm, B.lastev.get("4cbu"))
                        B.lastev["4cbe"] = c.inc(
                            c.e.activation(cbb[:], ps_cb[:, 0:TC], AF.Identity))
                    e_cb = B.ev("S")

                for og in range(8):
                    pmm = (ps_s0, ps_s1, ps_s2)[og % 3]
                    pk = "4s%d" % (og % 3)

                    @B.sec("tensor")
                    def _(c):
                        c.wait(e_w, e_n0, B.lastev.get(pk))
                        for k in range(NCH):
                            ins = c.e.matmul(
                                pmm[:, 0:TC],
                                w13b[:, k * (2 * INTER) + og * P:
                                     k * (2 * INTER) + og * P + P],
                                n0r[:, k * T + lo:k * T + lo + TC],
                                start=(k == 0), stop=(k == NCH - 1))
                        c.inc(ins)
                    e_s1 = B.ev("T")

                    @B.sec("scalar")
                    def _(c):
                        c.wait(e_s1)
                        if og < 4:
                            B.lastev[pk] = c.inc(
                                c.e.activation(hg[:, og * TC:(og + 1) * TC],
                                               pmm[:, 0:TC], AF.Silu))
                        else:
                            B.lastev[pk] = c.inc(
                                c.e.activation(hu[:, (og - 4) * TC:(og - 3) * TC],
                                               pmm[:, 0:TC], AF.Identity))
                e_guev = B.ev("S")

                @B.sec("vector")
                def _(c):
                    c.wait(e_guev)
                    if not shared:
                        c.wait(e_cb)
                    ev = c.chain(c.e.tensor_tensor(hg[:], hg[:], hu[:], ALU.mult))
                    if not shared:
                        cb3 = cbb[:].unsqueeze(1).broadcast_to([P, 4, TC])
                        ev = c.chain(c.e.tensor_tensor(
                            hg[:].rearrange("p (k t) -> p k t", k=4),
                            hg[:].rearrange("p (k t) -> p k t", k=4),
                            cb3, ALU.mult))
                    B.lastev["4cbu"] = ev
                hh_ev[tc] = B.ev("V")

            def emit_back(tc):
                lo = tc * TC
                hg = hgd[tc % 2]

                for oc in range(NCH):
                    pmm = (ps_t0, ps_t1, ps_t2)[oc % 3]
                    pk = "4t%d" % (oc % 3)

                    @B.sec("tensor")
                    def _(c):
                        c.wait(hh_ev[tc], B.lastev.get(pk))
                        for k in range(INTER // P):
                            ins = c.e.matmul(
                                pmm[:, 0:TC],
                                w2b[:, k * D + oc * P:k * D + oc * P + P],
                                hg[:, k * TC:(k + 1) * TC],
                                start=(k == 0), stop=(k == INTER // P - 1))
                        c.inc(ins)
                    e_s2 = B.ev("T")

                    if e == 0:
                        @B.sec("scalar")
                        def _(c):
                            c.wait(e_s2)
                            B.lastev[pk] = c.inc(c.e.activation(
                                ymoe[:, oc * T + lo:oc * T + lo + TC],
                                pmm[:, 0:TC], AF.Identity))
                    else:
                        @B.sec("scalar")
                        def _(c):
                            c.wait(e_s2, B.lastev.get("4tmp"))
                            B.lastev[pk] = c.inc(
                                c.e.activation(tmp[:], pmm[:, 0:TC], AF.Identity))
                        e_t = B.ev("S")

                        @B.sec("vector")
                        def _(c):
                            c.wait(e_t)
                            dst = ymoe[:, oc * T + lo:oc * T + lo + TC]
                            B.lastev["4tmp"] = c.chain(
                                c.e.tensor_tensor(dst, dst, tmp[:], ALU.add))

            for tc in range(NT + 1):
                if tc < NT:
                    emit_front(tc)
                if tc >= 1:
                    emit_back(tc - 1)

        for tc in range(NT):
            lo = tc * TC

            @B.sec("vector")
            def _(c):
                c.wait(B.ev("S"))
                for oc in range(NCH):
                    dst = R[:, oc * T + lo:oc * T + lo + TC]
                    c.chain(c.e.tensor_tensor(
                        dst, dst, ymoe[:, oc * T + lo:oc * T + lo + TC], ALU.add))
    B.barrier()


def phase5_l1(B, nc, l1, dram, env):
    R = env["R"]
    onescol_b = env["onescol_b"]; onesrow_f = env["onesrow_f"]
    ident_b = env["ident_b"]; tri_f = env["tri_f"]

    TC = 512
    NT = T // TC
    st = contextlib.ExitStack()
    with st:
        def sb(name, shape, dtype, stack=None):
            return (stack or st).enter_context(nc.sbuf_tensor(name, shape, dtype))

        def psn(name, shape, dtype=F32, stack=None):
            return (stack or st).enter_context(nc.psum_tensor(name, shape, dtype))

        sta = contextlib.ExitStack()
        with sta:
            lg = sb("lg5", [P, NCH], F32, sta)
            lb = sb("lb5", [P, NCH], F32, sta)
            xb5 = sb("xb5", [P, NCH * TC], BF16, sta)
            xq5 = sb("xq5", [P, NCH * TC], BF16, sta)
            n1c = sb("n1c", [P, NCH * TC], BF16, sta)
            xnc = sb("xnc", [P, NCH * TC], BF16, sta)
            rows5 = sb("rows5", [1, 3 * TC], F32, sta)
            bc5 = sb("bc5", [P, 3 * TC], F32, sta)
            tf5 = sb("tf5", [P, TC], F32, sta)
            ps_r0 = psn("ps5_r0", [1, 512], F32, sta)
            ps_r1 = psn("ps5_r1", [1, 512], F32, sta)
            ps_b0 = psn("ps5_b0", [P, 512], F32, sta)
            ps_b1 = psn("ps5_b1", [P, 512], F32, sta)

            @B.sec("sync")
            def _(c):
                c.dma(lg[:], dram["l1_g"][:, :], "yM")
                c.dma(lb[:], dram["l1_b"][:, :], "yM")
            sload = B.ev("yM")

            for tc in range(NT):
                lo = tc * TC

                @B.sec("scalar")
                def _(c):
                    c.wait(B.ev("T"))
                    for k in range(NCH):
                        c.e.activation(xb5[:, k * TC:(k + 1) * TC],
                                       R[:, k * T + lo:k * T + lo + TC],
                                       AF.Identity)
                        ins = c.e.activation(xq5[:, k * TC:(k + 1) * TC],
                                             R[:, k * T + lo:k * T + lo + TC],
                                             AF.Square)
                    c.inc(ins)
                e_c = B.ev("S")

                @B.sec("tensor")
                def _(c):
                    c.wait(e_c, B.ev("S"))
                    for k in range(NCH):
                        c.e.matmul(ps_r0[0:1, 0:TC], onescol_b[:],
                                   xb5[:, k * TC:(k + 1) * TC],
                                   start=(k == 0), stop=(k == NCH - 1))
                    for k in range(NCH):
                        ins = c.e.matmul(ps_r1[0:1, 0:TC], onescol_b[:],
                                         xq5[:, k * TC:(k + 1) * TC],
                                         start=(k == 0), stop=(k == NCH - 1))
                    c.inc(ins)
                e_mm = B.ev("T")

                @B.sec("scalar")
                def _(c):
                    c.wait(e_mm)
                    c.e.activation(rows5[:, 0:TC], ps_r0[0:1, 0:TC], AF.Identity)
                    c.chain(c.e.activation(rows5[:, TC:2 * TC], ps_r0[0:1, 0:TC],
                                           AF.Square, scale=1.0 / D))
                e_r = B.ev("S")

                @B.sec("vector")
                def _(c):
                    c.wait(e_r)
                    c.chain(c.e.scalar_tensor_tensor(
                        rows5[:, TC:2 * TC], ps_r1[0:1, 0:TC], 1.0 / D,
                        rows5[:, TC:2 * TC], ALU.mult, ALU.subtract))
                e_v1 = B.ev("V")

                @B.sec("scalar")
                def _(c):
                    c.wait(e_v1)
                    c.chain(c.e.activation(rows5[:, 2 * TC:3 * TC],
                                           rows5[:, TC:2 * TC], AF.Ln, bias=EPS))
                    c.chain(c.e.activation(rows5[:, TC:2 * TC],
                                           rows5[:, 2 * TC:3 * TC], AF.Exp,
                                           scale=-0.5))
                e_s1 = B.ev("S")

                @B.sec("vector")
                def _(c):
                    c.wait(e_s1)
                    c.chain(c.e.tensor_tensor(rows5[:, 2 * TC:3 * TC],
                                              rows5[:, TC:2 * TC],
                                              rows5[:, TC:2 * TC], ALU.mult))
                    c.chain(c.e.tensor_scalar(rows5[:, 2 * TC:3 * TC],
                                              rows5[:, 2 * TC:3 * TC], -EPS, 1.0,
                                              ALU.mult, ALU.add))
                e_v2 = B.ev("V")

                @B.sec("scalar")
                def _(c):
                    c.wait(e_v2)
                    c.chain(c.e.activation(rows5[:, 2 * TC:3 * TC],
                                           rows5[:, 2 * TC:3 * TC], AF.Ln,
                                           bias=EPS))
                    c.chain(c.e.activation(rows5[:, 2 * TC:3 * TC],
                                           rows5[:, 2 * TC:3 * TC], AF.Exp,
                                           scale=-0.5))
                e_s2 = B.ev("S")

                @B.sec("vector")
                def _(c):
                    c.wait(e_s2)
                    c.chain(c.e.tensor_tensor(rows5[:, 2 * TC:3 * TC],
                                              rows5[:, 2 * TC:3 * TC],
                                              rows5[:, TC:2 * TC], ALU.mult))
                e_rows = B.ev("V")

                @B.sec("tensor")
                def _(c):
                    c.wait(e_rows, B.ev("S"))
                    c.e.matmul(ps_b0[:, 0:TC], onesrow_f[:], rows5[:, 0:TC],
                               start=True, stop=True)
                    ins = c.e.matmul(ps_b1[:, 0:TC], onesrow_f[:],
                                     rows5[:, TC:2 * TC], start=True, stop=True)
                    c.inc(ins)
                e_b1 = B.ev("T")

                @B.sec("scalar")
                def _(c):
                    c.wait(e_b1)
                    c.e.activation(bc5[:, 0:TC], ps_b0[:, 0:TC], AF.Identity,
                                   scale=1.0 / D)
                    c.inc(c.e.activation(bc5[:, TC:2 * TC], ps_b1[:, 0:TC],
                                         AF.Identity))
                e_b1e = B.ev("S")

                @B.sec("tensor")
                def _(c):
                    c.wait(e_b1e)
                    c.inc(c.e.matmul(ps_b0[:, 0:TC], onesrow_f[:],
                                     rows5[:, 2 * TC:3 * TC], start=True,
                                     stop=True))
                e_b2 = B.ev("T")

                @B.sec("scalar")
                def _(c):
                    c.wait(e_b2)
                    c.inc(c.e.activation(bc5[:, 2 * TC:3 * TC], ps_b0[:, 0:TC],
                                         AF.Identity))
                e_bc = B.ev("S")

                @B.sec("vector")
                def _(c):
                    c.wait(e_bc, B.ev("T"), B.ev("yS"))
                    for k in range(NCH):
                        c.chain(c.e.tensor_tensor(tf5[:],
                                                  R[:, k * T + lo:k * T + lo + TC],
                                                  bc5[:, 0:TC], ALU.subtract))
                        c.e.tensor_tensor(n1c[:, k * TC:(k + 1) * TC], tf5[:],
                                          bc5[:, TC:2 * TC], ALU.mult)
                        c.chain(c.e.tensor_tensor(xnc[:, k * TC:(k + 1) * TC],
                                                  tf5[:], bc5[:, 2 * TC:3 * TC],
                                                  ALU.mult))
                e_n = B.ev("V")

                @B.sec("scalar")
                def _(c):
                    c.wait(e_n, sload)
                    for k in range(NCH):
                        ins = c.e.activation(xnc[:, k * TC:(k + 1) * TC],
                                             xnc[:, k * TC:(k + 1) * TC],
                                             AF.Identity, scale=lg[:, k:k + 1],
                                             bias=lb[:, k:k + 1])
                    c.inc(ins)
                e_x = B.ev("S")

                @B.sec("sync")
                def _(c):
                    c.wait(e_x, e_n)
                    c.dma(dram["n1_sp"][:, :, lo:lo + TC],
                          n1c[:].rearrange("p (k t) -> p k t", k=NCH), "yS")
                    c.dma(dram["xn_sp"][:, :, lo:lo + TC],
                          xnc[:].rearrange("p (k t) -> p k t", k=NCH), "yS")
            B.barrier()

        stb = contextlib.ExitStack()
        with stb:
            wq = sb("wq5", [P, NCH * D], BF16, stb)
            wk = sb("wk5", [P, NCH * D], BF16, stb)
            wv = sb("wv5", [P, NCH * D], BF16, stb)
            rot_b = sb("rot_b", [P, P], BF16, stb)
            cosq = sb("cosq", [P, T], BF16, stb)
            sinq = sb("sinq", [P, T], BF16, stb)
            cosk = sb("cosk", [P, T], BF16, stb)
            sink = sb("sink", [P, T], BF16, stb)
            xnb = sb("xnb", [P, NCH * TC], BF16, stb)
            qbuf = sb("qbuf", [P, NH * TC], BF16, stb)
            vtb = sb("vtb", [P, NH * TC], BF16, stb)
            qt5 = sb("qt5", [P, TC], BF16, stb)
            rt5 = sb("rt5", [P, TC], BF16, stb)
            m15 = sb("m15", [P, TC], BF16, stb)
            ps_p0 = psn("ps5b_0", [P, 512], F32, stb)
            ps_p1 = psn("ps5b_1", [P, 512], F32, stb)
            ps_p2 = psn("ps5b_2", [P, 512], F32, stb)
            ps_rt = psn("ps5b_rt", [P, 512], F32, stb)
            ps_vt = psn("ps5b_vt", [P, 1024], BF16, stb)

            @B.sec("gpsimd")
            def _(c):
                c.wait(B.ev("T"), B.ev("V"), B.ev("S"))
                c.dma(wq[:].rearrange("p (k m) -> p k m", k=NCH),
                      dram["wqT"].rearrange("(k p) m -> p k m", p=P), "gA")
                c.dma(wk[:].rearrange("p (k m) -> p k m", k=NCH),
                      dram["wkT"].rearrange("(k p) m -> p k m", p=P), "gA")
                c.dma(wv[:].rearrange("p (k m) -> p k m", k=NCH),
                      dram["wvT"].rearrange("(k p) m -> p k m", p=P), "gA")
                c.dma(rot_b[:], dram["rot_sT"][:, :], "gA")
                c.dma(cosq[:], dram["cos_q"][:, :], "gA")
                c.dma(sinq[:], dram["sin_q"][:, :], "gA")
                c.dma(cosk[:], dram["cos_k"][:, :], "gA")
                c.dma(sink[:], dram["sin_k"][:, :], "gA")
            wload = B.ev("gA")

            for tc in range(NT):
                lo = tc * TC

                @B.sec("sync")
                def _(c):
                    c.wait(B.ev("T"), B.ev("yS"))
                    c.dma(xnb[:].rearrange("p (k t) -> p k t", k=NCH),
                          dram["xn_sp"][:, :, lo:lo + TC], "yA")
                e_ld = B.ev("yA")

                for which, wmat, ctab, stab, spname in (
                        ("q", wq, cosq, sinq, "q_sp"),
                        ("k", wk, cosk, sink, "k_sp")):
                    for h in range(NH):
                        pmm = (ps_p0, ps_p1, ps_p2)[h % 3]
                        pk = "5p%d" % (h % 3)

                        @B.sec("tensor")
                        def _(c):
                            c.wait(e_ld, wload, B.lastev.get(pk))
                            for k in range(NCH):
                                ins = c.e.matmul(
                                    pmm[:, 0:TC],
                                    wmat[:, k * D + h * P:k * D + h * P + P],
                                    xnb[:, k * TC:(k + 1) * TC],
                                    start=(k == 0), stop=(k == NCH - 1))
                            c.inc(ins)
                        e_p = B.ev("T")

                        @B.sec("vector")
                        def _(c):
                            c.wait(e_p, B.lastev.get("5qt"))
                            B.lastev[pk] = c.inc(
                                c.e.tensor_copy(qt5[:], pmm[:, 0:TC]))
                        e_qt = B.ev("V")

                        @B.sec("tensor")
                        def _(c):
                            c.wait(e_qt, B.lastev.get("5rt"))
                            c.inc(c.e.matmul(ps_rt[:, 0:TC], rot_b[:], qt5[:],
                                             start=True, stop=True))
                        e_rt = B.ev("T")

                        @B.sec("vector")
                        def _(c):
                            c.wait(e_rt, B.lastev.get("5rte"))
                            B.lastev["5rt"] = c.inc(
                                c.e.tensor_copy(rt5[:], ps_rt[:, 0:TC]))
                        e_rte = B.ev("V")

                        @B.sec("vector")
                        def _(c):
                            c.wait(e_rte, B.ev("yS"))
                            ev = c.chain(c.e.tensor_tensor(m15[:], qt5[:],
                                                           ctab[:, lo:lo + TC],
                                                           ALU.mult))
                            B.lastev["5qt"] = ev
                            c.chain(c.e.tensor_tensor(rt5[:], rt5[:],
                                                      stab[:, lo:lo + TC],
                                                      ALU.mult))
                            B.lastev["5rte"] = c.chain(c.e.tensor_tensor(
                                qbuf[:, h * TC:(h + 1) * TC], m15[:], rt5[:],
                                ALU.add))
                    e_rope = B.ev("V")

                    @B.sec("sync")
                    def _(c):
                        c.wait(e_rope)
                        c.dma(dram[spname][:, :, lo:lo + TC],
                              qbuf[:].rearrange("p (h t) -> p h t", h=NH), "yS")

                for h in range(NH):
                    pmm = (ps_p0, ps_p1, ps_p2)[h % 3]
                    pk = "5p%d" % (h % 3)

                    @B.sec("tensor")
                    def _(c):
                        c.wait(e_ld, wload, B.lastev.get(pk))
                        for k in range(NCH):
                            ins = c.e.matmul(
                                pmm[:, 0:TC],
                                wv[:, k * D + h * P:k * D + h * P + P],
                                xnb[:, k * TC:(k + 1) * TC],
                                start=(k == 0), stop=(k == NCH - 1))
                        c.inc(ins)
                    e_p = B.ev("T")

                    @B.sec("vector")
                    def _(c):
                        c.wait(e_p, B.ev("T"), B.lastev.get("5qt"))
                        B.lastev[pk] = c.inc(
                            c.e.tensor_copy(qt5[:], pmm[:, 0:TC]))
                    e_v = B.ev("V")

                    for j in range(TC // P):
                        @B.sec("tensor")
                        def _(c):
                            c.wait(e_v, B.lastev.get("5vt"))
                            c.inc(c.e.transpose(ps_vt[:, 0:P],
                                                qt5[:, j * P:(j + 1) * P],
                                                ident_b[:]))
                        e_t = B.ev("T")

                        @B.sec("vector")
                        def _(c):
                            c.wait(e_t, B.ev("yS"))
                            B.lastev["5vt"] = c.inc(c.e.tensor_copy(
                                vtb[:, h * TC + j * P:h * TC + (j + 1) * P],
                                ps_vt[:, 0:P]))
                e_vt = B.ev("V")

                @B.sec("sync")
                def _(c):
                    c.wait(e_vt, B.ev("S"))
                    c.dma(dram["vt_sp"][:, :, tc * (TC // P):(tc + 1) * (TC // P), :],
                          vtb[:].rearrange("p (h j q) -> p h j q", h=NH, j=TC // P),
                          "yS")
            B.barrier()

        stc = contextlib.ExitStack()
        with stc:
            dirs_b = sb("dirs_b", [P, NH], BF16, stc)
            hs_all = sb("hs_all", [P, NH], F32, stc)
            om_all = sb("om_all", [P, NH], F32, stc)
            hsr = sb("hsr", [1, NH], F32, stc)
            aout = sb("aout", [P, NCH * T], BF16, stc)
            stc2 = stc.enter_context(contextlib.ExitStack())
            qh = sb("qh", [P, T], BF16, stc2)
            kh = sb("kh", [P, T], BF16, stc2)
            vth = sb("vth", [P, (T // P) * P], BF16, stc2)
            qp_b = sb("qp_b", [1, T], BF16, stc2)
            kp_b = sb("kp_b", [1, T], BF16, stc2)
            attn = sb("attn", [P, T], BF16, stc2)
            scd = [sb("sc0", [P, T], BF16, stc2), sb("sc1", [P, T], BF16, stc2)]
            gsd = [sb("gs0", [P, T], BF16, stc2), sb("gs1", [P, T], BF16, stc2)]
            at0 = sb("at0", [P, 4 * P], BF16, stc2)
            at1 = sb("at1", [P, 4 * P], BF16, stc2)
            nmax = sb("nmax", [P, 1], F32, stc2)
            sums = sb("sums", [P, 1], F32, stc2)
            rsum = sb("rsum", [P, 1], F32, stc2)
            cmb5 = sb("cmb5", [P, P], F32, stc2)
            ps_sc2 = [psn("ps5c_sca", [P, 512], F32, stc2),
                      psn("ps5c_scb", [P, 512], F32, stc2)]
            ps_t0 = psn("ps5c_t0", [P, 1024], BF16, stc2)
            ps_t1 = psn("ps5c_t1", [P, 1024], BF16, stc2)
            ps_av = psn("ps5c_av", [P, 512], F32, stc2)
            ps_ag = psn("ps5c_ag", [P, 512], F32, stc2)

            @B.sec("gpsimd")
            def _(c):
                c.wait(B.ev("T"), B.ev("V"), B.ev("S"))
                c.dma(dirs_b[:], dram["dirs"][:, :], "gA")
                c.dma(hsr[:], dram["hscale"][:, :], "gA")
            e_dl = B.ev("gA")

            @B.sec("tensor")
            def _(c):
                c.wait(e_dl, B.ev("S"), B.ev("V"))
                c.inc(c.e.matmul(ps_av[0:P, 0:NH], onesrow_f[:], hsr[:, :],
                                 start=True, stop=True))
            e_hsm = B.ev("T")

            @B.sec("scalar")
            def _(c):
                c.wait(e_hsm)
                c.inc(c.e.activation(hs_all[:], ps_av[0:P, 0:NH], AF.Identity))
            e_hs = B.ev("S")

            @B.sec("vector")
            def _(c):
                c.wait(e_hs)
                c.chain(c.e.tensor_scalar(om_all[:], hs_all[:], -1.0, 1.0,
                                          ALU.mult, ALU.add))
            e_om = B.ev("V")

            for h in range(NH):
                @B.sec("sync")
                def _(c):
                    c.wait(B.ev("T"), B.ev("V"), B.ev("S"), B.ev("yS"))
                    c.dma(qh[:], dram["q_sp"][:, h, :], "yA")
                    c.dma(kh[:], dram["k_sp"][:, h, :], "yA")
                    c.dma(vth[:].rearrange("p (j q) -> p j q", j=T // P),
                          dram["vt_sp"][:, h, :, :], "yA")
                e_hl = B.ev("yA")

                for j in range(T // 512):
                    @B.sec("tensor")
                    def _(c):
                        c.wait(e_hl, e_om, B.ev("S"))
                        c.e.matmul(ps_av[0:1, 0:512], dirs_b[:, h:h + 1],
                                   qh[:, j * 512:(j + 1) * 512],
                                   start=True, stop=True)
                        c.inc(c.e.matmul(ps_ag[0:1, 0:512], dirs_b[:, h:h + 1],
                                         kh[:, j * 512:(j + 1) * 512],
                                         start=True, stop=True))
                    e_qpm = B.ev("T")

                    @B.sec("scalar")
                    def _(c):
                        c.wait(e_qpm)
                        c.e.activation(qp_b[:, j * 512:(j + 1) * 512],
                                       ps_av[0:1, 0:512], AF.Identity,
                                       scale=float(np.sqrt(HD)))
                        c.inc(c.e.activation(kp_b[:, j * 512:(j + 1) * 512],
                                             ps_ag[0:1, 0:512], AF.Identity))
                e_qp = B.ev("S")

                def scores_stage(qt):
                    par = qt % 2
                    kmax = (qt + 1) * P
                    nch_k = (kmax + 511) // 512
                    for which, lhs, dstb in (("m", qh, scd[par]),
                                             ("g", qp_b, gsd[par])):
                        for j in range(nch_k):
                            ke = min(kmax, (j + 1) * 512)
                            pb = ps_sc2[j % 2]
                            pkey = "5sb%d" % (j % 2)

                            @B.sec("tensor")
                            def _(c):
                                c.wait(e_hl, e_qp, B.lastev.get(pkey),
                                       B.lastev.get("5msk%s%d" % (which, par)))
                                if which == "m":
                                    ins = c.e.matmul(pb[:, 0:ke - j * 512],
                                                     qh[:, qt * P:(qt + 1) * P],
                                                     kh[:, j * 512:ke],
                                                     start=True, stop=True)
                                else:
                                    ins = c.e.matmul(pb[:, 0:ke - j * 512],
                                                     qp_b[:, qt * P:(qt + 1) * P],
                                                     kp_b[:, j * 512:ke],
                                                     start=True, stop=True)
                                c.inc(ins)
                            e_mm = B.ev("T")

                            @B.sec("scalar")
                            def _(c):
                                c.wait(e_mm)
                                B.lastev[pkey] = c.inc(c.e.activation(
                                    dstb[:, j * 512:ke], pb[:, 0:ke - j * 512],
                                    AF.Identity))
                    return B.ev("S")

                def softmax_av_stage(qt, e_sc):
                    par = qt % 2
                    kmax = (qt + 1) * P
                    e_avm = None
                    e_agm = None
                    for which, srcb, ps_out in (("m", scd[par], ps_av),
                                                ("g", gsd[par], ps_ag)):
                        @B.sec("vector")
                        def _(c):
                            c.wait(e_sc, B.lastev.get("5at"), B.ev("S"))
                            c.chain(c.e.tensor_tensor(
                                srcb[:, qt * P:(qt + 1) * P],
                                srcb[:, qt * P:(qt + 1) * P], tri_f[:],
                                ALU.add))
                            ev = c.chain(c.e.reduce_max(nmax[:], srcb[:, 0:kmax],
                                                        axis=AX.X, negate=True))
                            B.lastev["5msk%s%d" % (which, par)] = ev
                        e_mx = B.ev("V")

                        @B.sec("scalar")
                        def _(c):
                            c.wait(e_mx, B.ev("T"))
                            c.chain(c.e.activation(attn[:, 0:kmax],
                                                   srcb[:, 0:kmax], AF.Exp,
                                                   bias=nmax[:, 0:1],
                                                   accum_out=sums[:]))
                        e_ex = B.ev("S")

                        @B.sec("vector")
                        def _(c):
                            c.wait(e_ex)
                            c.chain(c.e.reciprocal(rsum[:], sums[:]))
                            c.chain(c.e.tensor_scalar(attn[:, 0:kmax],
                                                      attn[:, 0:kmax],
                                                      rsum[:, 0:1], None,
                                                      ALU.mult))
                        e_at = B.ev("V")
                        B.lastev["5at"] = e_at

                        groups = [(g, min(4, qt + 1 - g))
                                  for g in range(0, qt + 1, 4)]

                        def emit_tr(gi):
                            g, cnt = groups[gi]
                            pst = ps_t0 if (gi % 2 == 0) else ps_t1
                            tk = "5t%d" % (gi % 2)

                            @B.sec("tensor")
                            def _(c):
                                c.wait(e_at, B.lastev.get(tk))
                                for i in range(cnt):
                                    c.inc(c.e.transpose(
                                        pst[:, i * P:(i + 1) * P],
                                        attn[:, (g + i) * P:(g + i + 1) * P],
                                        ident_b[:]))
                            e_tr = B.ev("T")

                            @B.sec("vector")
                            def _(c):
                                c.wait(e_tr)
                                B.lastev[tk] = c.inc(c.e.tensor_copy(
                                    (at0 if gi % 2 == 0 else at1)[:, 0:cnt * P],
                                    pst[:, 0:cnt * P]))
                            return B.ev("V")

                        def emit_av(gi, e_te):
                            g, cnt = groups[gi]
                            atb = at0 if (gi % 2 == 0) else at1

                            @B.sec("tensor")
                            def _(c):
                                c.wait(e_te)
                                for i in range(cnt):
                                    c.inc(c.e.matmul(
                                        ps_out[:, 0:P],
                                        vth[:, (g + i) * P:(g + i + 1) * P],
                                        atb[:, i * P:(i + 1) * P],
                                        start=(g + i == 0),
                                        stop=(g + i == qt)))

                        pg = None
                        for gi in range(len(groups)):
                            e_te = emit_tr(gi)
                            if pg is not None:
                                emit_av(pg[0], pg[1])
                            pg = (gi, e_te)
                        emit_av(pg[0], pg[1])
                        if which == "m":
                            e_avm = B.ev("T")
                        else:
                            e_agm = B.ev("T")

                    @B.sec("scalar")
                    def _(c):
                        c.wait(e_agm, B.lastev.get("5cmb"))
                        c.inc(c.e.activation(cmb5[:], ps_ag[:, 0:P], AF.Identity,
                                             scale=hs_all[:, h:h + 1]))
                    e_cg = B.ev("S")

                    @B.sec("vector")
                    def _(c):
                        c.wait(e_cg, e_avm)
                        B.lastev["5cmb"] = c.chain(c.e.scalar_tensor_tensor(
                            aout[:, h * T + qt * P:h * T + (qt + 1) * P],
                            ps_av[:, 0:P], om_all[:, h:h + 1], cmb5[:],
                            ALU.mult, ALU.add))

                prev = None
                for qt in range(T // P):
                    e_sc = scores_stage(qt)
                    if prev is not None:
                        softmax_av_stage(prev[0], prev[1])
                    prev = (qt, e_sc)
                softmax_av_stage(prev[0], prev[1])
            B.barrier()
            stc2.close()

            std = contextlib.ExitStack()
            with std:
                wo = sb("wo5", [P, NCH * D], BF16, std)
                n1b5 = sb("n1b5", [P, NCH * TC], BF16, std)
                tfo = sb("tfo", [P, TC], F32, std)
                ps_d0 = psn("ps5d_0", [P, 512], F32, std)
                ps_d1 = psn("ps5d_1", [P, 512], F32, std)

                @B.sec("gpsimd")
                def _(c):
                    c.dma(wo[:].rearrange("p (k m) -> p k m", k=NCH),
                          dram["woT"].rearrange("(k p) m -> p k m", p=P), "gA")
                wlo = B.ev("gA")

                for tc in range(NT):
                    lo = tc * TC

                    @B.sec("sync")
                    def _(c):
                        c.wait(B.ev("V"))
                        c.dma(n1b5[:].rearrange("p (k t) -> p k t", k=NCH),
                              dram["n1_sp"][:, :, lo:lo + TC], "yA")
                    e_ld = B.ev("yA")

                    for oc in range(NCH):
                        pmm = ps_d0 if (oc % 2 == 0) else ps_d1

                        @B.sec("tensor")
                        def _(c):
                            c.wait(wlo, B.ev("V"))
                            for k in range(NCH):
                                ins = c.e.matmul(
                                    pmm[:, 0:TC],
                                    wo[:, k * D + oc * P:k * D + oc * P + P],
                                    aout[:, k * T + lo:k * T + lo + TC],
                                    start=(k == 0), stop=(k == NCH - 1))
                            c.inc(ins)
                        e_mm = B.ev("T")

                        @B.sec("vector")
                        def _(c):
                            c.wait(e_mm, e_ld)
                            c.chain(c.e.tensor_tensor(
                                tfo[:], pmm[:, 0:TC],
                                n1b5[:, oc * TC:(oc + 1) * TC], ALU.add))
                            c.chain(c.e.tensor_tensor(
                                R[:, oc * T + lo:oc * T + lo + TC],
                                R[:, oc * T + lo:oc * T + lo + TC],
                                tfo[:], ALU.add))
            B.barrier()


        ste = contextlib.ExitStack()
        with ste:
            wr = sb("wr5", [P, NCH * D], BF16, ste)
            wk2 = sb("wk25", [P, NCH * D], BF16, ste)
            wv2 = sb("wv25", [P, NCH * D], BF16, ste)
            mur = sb("mur5", [P, NCH], F32, ste)
            muk = sb("muk5", [P, NCH], F32, ste)
            n1x = sb("n1x", [P, NCH * (TC + 1)], BF16, ste)
            xr = sb("xr5", [P, NCH * TC], BF16, ste)
            xk = sb("xk5", [P, NCH * TC], BF16, ste)
            rb5 = sb("rb5", [P, NCH * TC], BF16, ste)
            kb5 = sb("kb5", [P, NCH * TC], BF16, ste)
            td5 = sb("td5", [P, TC], BF16, ste)
            tf6 = sb("tf6", [P, TC], F32, ste)
            ps_e0 = psn("ps5e_0", [P, 512], F32, ste)
            ps_e1 = psn("ps5e_1", [P, 512], F32, ste)
            ps_e2 = psn("ps5e_2", [P, 512], F32, ste)

            @B.sec("gpsimd")
            def _(c):
                c.wait(B.ev("T"), B.ev("V"), B.ev("S"))
                c.dma(wr[:].rearrange("p (k m) -> p k m", k=NCH),
                      dram["wrT"].rearrange("(k p) m -> p k m", p=P), "gA")
                c.dma(wk2[:].rearrange("p (k m) -> p k m", k=NCH),
                      dram["wkT2"].rearrange("(k p) m -> p k m", p=P), "gA")
                c.dma(wv2[:].rearrange("p (k m) -> p k m", k=NCH),
                      dram["wvT2"].rearrange("(k p) m -> p k m", p=P), "gA")
            wle = B.ev("gA")

            @B.sec("sync")
            def _(c):
                c.dma(mur[:], dram["mu_r"][:, :], "yM")
                c.dma(muk[:], dram["mu_k"][:, :], "yM")
            sle = B.ev("yM")

            for tc in range(NT):
                lo = tc * TC

                @B.sec("vector")
                def _(c):
                    c.wait(B.ev("S"), B.ev("T"))
                    if tc == 0:
                        for k in range(NCH):
                            c.inc(c.e.memset(n1x[:, k * (TC + 1):k * (TC + 1) + 1],
                                             0.0))
                e_z = B.ev("V")

                @B.sec("sync")
                def _(c):
                    c.wait(e_z, B.ev("T"))
                    if tc == 0:
                        c.dma(n1x[:].rearrange("p (k t) -> p k t", k=NCH)
                              [:, :, 1:TC + 1],
                              dram["n1_sp"][:, :, 0:TC], "yA")
                    else:
                        c.dma(n1x[:].rearrange("p (k t) -> p k t", k=NCH),
                              dram["n1_sp"][:, :, lo - 1:lo + TC], "yA")
                e_ld = B.ev("yA")

                @B.sec("vector")
                def _(c):
                    c.wait(e_ld, sle, B.ev("T"))
                    for k in range(NCH):
                        cur = n1x[:, k * (TC + 1) + 1:(k + 1) * (TC + 1)]
                        prv = n1x[:, k * (TC + 1):(k + 1) * (TC + 1) - 1]
                        c.chain(c.e.tensor_tensor(td5[:], cur, prv, ALU.subtract))
                        c.chain(c.e.scalar_tensor_tensor(
                            xr[:, k * TC:(k + 1) * TC], td5[:], mur[:, k:k + 1],
                            prv, ALU.mult, ALU.add))
                        c.chain(c.e.scalar_tensor_tensor(
                            xk[:, k * TC:(k + 1) * TC], td5[:], muk[:, k:k + 1],
                            prv, ALU.mult, ALU.add))
                e_mix = B.ev("V")

                for which, wmat, dst, act in (("r", wr, rb5, AF.Sigmoid),
                                              ("k", wk2, kb5, AF.Relu)):
                    for oc in range(NCH):
                        pmm = (ps_e0, ps_e1, ps_e2)[oc % 3]
                        pk = "5e%d" % (oc % 3)
                        src = xr if which == "r" else xk

                        @B.sec("tensor")
                        def _(c):
                            c.wait(e_mix, wle, B.lastev.get(pk))
                            for k in range(NCH):
                                ins = c.e.matmul(
                                    pmm[:, 0:TC],
                                    wmat[:, k * D + oc * P:k * D + oc * P + P],
                                    src[:, k * TC:(k + 1) * TC],
                                    start=(k == 0), stop=(k == NCH - 1))
                            c.inc(ins)
                        e_mm = B.ev("T")

                        @B.sec("scalar")
                        def _(c):
                            c.wait(e_mm)
                            ev = c.chain(c.e.activation(
                                dst[:, oc * TC:(oc + 1) * TC], pmm[:, 0:TC], act))
                            if which == "k":
                                ev = c.chain(c.e.activation(
                                    dst[:, oc * TC:(oc + 1) * TC],
                                    dst[:, oc * TC:(oc + 1) * TC], AF.Square))
                            B.lastev[pk] = ev
                e_rk = B.ev("S")

                for oc in range(NCH):
                    pmm = (ps_e0, ps_e1, ps_e2)[oc % 3]
                    pk = "5e%d" % (oc % 3)

                    @B.sec("tensor")
                    def _(c):
                        c.wait(e_rk, B.lastev.get(pk))
                        for k in range(NCH):
                            ins = c.e.matmul(
                                pmm[:, 0:TC],
                                wv2[:, k * D + oc * P:k * D + oc * P + P],
                                kb5[:, k * TC:(k + 1) * TC],
                                start=(k == 0), stop=(k == NCH - 1))
                        c.inc(ins)
                    e_mm = B.ev("T")

                    @B.sec("vector")
                    def _(c):
                        c.wait(e_mm)
                        B.lastev[pk] = c.chain(c.e.tensor_tensor(
                            tf6[:], pmm[:, 0:TC], rb5[:, oc * TC:(oc + 1) * TC],
                            ALU.mult))
                        c.chain(c.e.tensor_tensor(
                            R[:, oc * T + lo:oc * T + lo + TC],
                            R[:, oc * T + lo:oc * T + lo + TC], tf6[:], ALU.add))
    B.barrier()



def _fm(v, nch):
    return np.ascontiguousarray(v.reshape(nch, P).T.astype(np.float32))


def stage_inputs(inputs):
    i = {k: np.asarray(v) for k, v in inputs.items()}
    sh = {}
    sh["w_inproj"] = np.ascontiguousarray(i["L0_attn_in_proj_w"], np.float32)
    cw = i["L0_attn_conv_w"].reshape(DIN, 4)
    sh["conv_w"] = np.ascontiguousarray(
        cw.reshape(NIN, P, 4).transpose(1, 0, 2).reshape(P, NIN * 4), np.float32)
    sh["conv_b"] = _fm(i["L0_attn_conv_b"], NIN)
    xp = np.zeros((DIN, 112), np.float32)
    xp[:, 0:80] = i["L0_attn_x_proj_w"][:, 0:80]
    xp[:, 96:112] = i["L0_attn_x_proj_w"][:, 80:96]
    sh["w_xproj"] = xp
    sh["w_dtproj"] = np.ascontiguousarray(i["L0_attn_dt_proj_w"], np.float32)
    sh["dt_bias"] = _fm(i["L0_attn_dt_proj_b"], NIN)
    al = i["L0_attn_A_log"]
    sh["a_log"] = np.ascontiguousarray(
        al.reshape(NIN, P, DST).transpose(1, 0, 2).reshape(P, NIN * DST), np.float32)
    sh["d_vec"] = _fm(i["L0_attn_D"], NIN)
    sh["w_oproj"] = np.ascontiguousarray(i["L0_attn_out_proj_w"], np.float32)
    sh["ln1_g"] = _fm(i["L0_attn_ln1_g"], NCH)
    sh["ln1_b"] = _fm(i["L0_attn_ln1_b"], NCH)
    sh["ln2_g"] = _fm(i["L0_attn_ln2_g"], NCH)
    sh["ln2_b"] = _fm(i["L0_attn_ln2_b"], NCH)
    sh["w_ffn1"] = np.ascontiguousarray(i["L0_attn_ffn_w1"], np.float32)
    sh["ffn_b1"] = _fm(i["L0_attn_ffn_b1"], NFF)
    sh["w_ffn2"] = np.ascontiguousarray(i["L0_attn_ffn_w2"], np.float32)
    sh["ffn_b2"] = _fm(i["L0_attn_ffn_b2"], NCH)
    sh["gate_wT"] = np.ascontiguousarray(i["L0_ffn_gate_w"].T, np.float32)
    sh["gate_b"] = np.ascontiguousarray(i["L0_ffn_gate_b"].reshape(NEXP, 1),
                                        np.float32)
    sh["w13T"] = np.ascontiguousarray(i["L0_ffn_w13"].transpose(0, 2, 1), np.float32)
    sh["w2T"] = np.ascontiguousarray(i["L0_ffn_w2"].transpose(0, 2, 1), np.float32)
    sh["w13sT"] = np.ascontiguousarray(i["L0_ffn_w13_s"].T, np.float32)
    sh["w2sT"] = np.ascontiguousarray(i["L0_ffn_w2_s"].T, np.float32)
    sh["wqT"] = np.ascontiguousarray(i["L1_attn_Wq"].T, np.float32)
    sh["wkT"] = np.ascontiguousarray(i["L1_attn_Wk"].T, np.float32)
    sh["wvT"] = np.ascontiguousarray(i["L1_attn_Wv"].T, np.float32)
    sh["woT"] = np.ascontiguousarray(i["L1_attn_Wo"].T, np.float32)
    sh["l1_g"] = _fm(i["L1_attn_ln_g"], NCH)
    sh["l1_b"] = _fm(i["L1_attn_ln_b"], NCH)
    sh["hscale"] = np.ascontiguousarray(i["L1_attn_head_scales"].reshape(1, NH),
                                        np.float32)
    dirs = np.zeros((P, NH), np.float32)
    dirs[0:3, :] = i["L1_attn_head_dirs"].T
    sh["dirs"] = dirs
    sh["wrT"] = np.ascontiguousarray(i["L1_ffn_W_r"].T, np.float32)
    sh["wkT2"] = np.ascontiguousarray(i["L1_ffn_W_k"].T, np.float32)
    sh["wvT2"] = np.ascontiguousarray(i["L1_ffn_W_v"].T, np.float32)
    sh["mu_r"] = _fm(i["L1_ffn_mu_r"], NCH)
    sh["mu_k"] = _fm(i["L1_ffn_mu_k"], NCH)
    sh["identf"] = np.eye(P, dtype=np.float32)
    sh["ones_col"] = np.ones((P, 1), np.float32)
    sh["ones_row"] = np.ones((1, P), np.float32)
    tri = np.zeros((P, P), np.float32)
    tri[np.triu_indices(P, 1)] = NEG
    sh["trimask"] = tri
    rot = np.zeros((P, P), np.float32)
    for ii in range(64):
        rot[ii + 64, ii] = -1.0
        rot[ii, ii + 64] = 1.0
    sh["rot_sT"] = rot
    inv = 1.0 / (10000.0 ** (np.arange(0, HD, 2, dtype=np.float64) / HD))
    freqs = np.outer(np.arange(T, dtype=np.float64), inv)
    cos = np.concatenate([np.cos(freqs)] * 2, 1).T
    sin = np.concatenate([np.sin(freqs)] * 2, 1).T
    sc = 1.0 / np.sqrt(HD)
    sh["cos_q"] = np.ascontiguousarray(cos * sc, np.float32)
    sh["sin_q"] = np.ascontiguousarray(sin * sc, np.float32)
    sh["cos_k"] = np.ascontiguousarray(cos, np.float32)
    sh["sin_k"] = np.ascontiguousarray(sin, np.float32)
    sh["cpool"] = np.ascontiguousarray(
        np.broadcast_to(np.asarray(CONSTS, np.float32), (P, len(CONSTS))))
    s16 = np.zeros((NEXP, NEXP * P), np.float32)
    for n in range(NEXP):
        s16[n, n * P:(n + 1) * P] = 1.0
    sh["sel16"] = s16

    x = np.asarray(inputs["L0_attn_x"])
    per_core = [np.ascontiguousarray(x[c].T, np.float32) for c in range(8)]
    return sh, per_core


def kernel(**inputs):
    sh, per_core = stage_inputs(inputs)
    nc = build_program()
    in_maps = [dict(sh, x0=per_core[c]) for c in range(8)]
    res = run_bass_kernel_spmd(nc, in_maps, core_ids=list(range(8)))
    out = np.stack([np.asarray(res.results[c]["out"]).T for c in range(8)])
    return np.ascontiguousarray(out, np.float32)


# version 18
# speedup vs baseline: 1.0121x; 1.0121x over previous
import contextlib
import numpy as np
import concourse.bass as bass
import concourse.mybir as mybir
from concourse.bass_utils import run_bass_kernel_spmd

F32 = mybir.dt.float32
BF16 = mybir.dt.bfloat16
AF = mybir.ActivationFunctionType
ALU = mybir.AluOpType
AX = mybir.AxisListType

P = 128
T = 2048
D = 1024
NCH = D // P
DIN = 2048
NIN = DIN // P
DST = 16
DTR = 64
DFF = 4096
NFF = DFF // P
NEXP = 16
INTER = 512
NH = 8
HD = 128
EPS = 1e-5
NEG = -30000.0
CONSTS = [1.0 / D, -0.5, -1.0, EPS, 1.0, 1.0 / P, 0.5, float(np.sqrt(HD))]

DEBUG_TAGS = set()
N_PHASES = 99



class _FakeInstr:
    __slots__ = ("incs",)

    def __init__(self, incs):
        self.incs = incs


class _FakeEng:
    __slots__ = ("rec",)

    def __init__(self, rec):
        self.rec = rec

    def __getattr__(self, name):
        rec = self.rec

        def call(*a, **k):
            incs = []
            rec.append((name, a, k, incs))
            return _FakeInstr(incs)
        return call


class Sec:

    def __init__(self, B, engname):
        self.B, self.engname = B, engname
        self.e = _FakeEng(B.streams[engname])

    def wait(self, *evs):
        for ev in evs:
            if ev is None:
                continue
            s, v = ev
            if v <= 0:
                continue
            key = (self.engname, s)
            if self.B.waited.get(key, 0) < v:
                self.e.wait_ge(self.B.sem(s), v)
                self.B.waited[key] = v

    def inc(self, instr, semname=None, amt=1):
        s = semname or self.B.engsem[self.engname]
        self.B.cnt[s] = self.B.cnt.get(s, 0) + amt
        instr.incs.append((self.B.sem(s), amt))
        return (s, self.B.cnt[s])

    def dma(self, out, in_, stream):
        ins = self.e.dma_start(out=out, in_=in_)
        return self.inc(ins, semname=stream, amt=16)

    def chain(self, instr):
        ev = self.inc(instr)
        self.wait(ev)
        return ev


class Bld:
    def __init__(self, nc, es):
        self.nc, self.es = nc, es
        self.block = es.enter_context(nc.Block())
        self.sems = {}
        self.cnt = {}
        self.waited = {}
        self.engsem = {"tensor": "T", "vector": "V", "scalar": "S",
                       "gpsimd": "G", "sync": "Y"}
        self.streams = {k: [] for k in self.engsem}
        self.lastev = {}

    def sem(self, name):
        if name not in self.sems:
            self.sems[name] = self.es.enter_context(self.nc.semaphore(name))
            self.cnt.setdefault(name, 0)
        return self.sems[name]

    def ev(self, name):
        return (name, self.cnt.get(name, 0))

    def sec(self, engname):
        def deco(fn):
            fn(Sec(self, engname))
        return deco

    def barrier(self):
        evs = [(s, c) for s, c in self.cnt.items() if c > 0]
        for engname in self.engsem:
            @self.sec(engname)
            def _(c):
                c.wait(*evs)

    def flush(self):
        for engname, rec in self.streams.items():
            if not rec:
                continue

            def run(e, rec=list(rec)):
                for name, a, k, incs in rec:
                    instr = getattr(e, name)(*a, **k)
                    for sem, amt in incs:
                        instr.then_inc(sem, amt)
            getattr(self.block, engname)(run)
            rec.clear()


def ceil_div(a, b):
    return -(-a) // b



def build_program():
    nc = bass.Bass("TRN2", target_bir_lowering=False)
    dram = {}

    def din(name, shape, dtype=F32):
        dram[name] = nc.dram_tensor(name, shape, dtype, kind="ExternalInput").ap()
        return dram[name]

    def dscr(name, shape, dtype=BF16):
        kind = "ExternalOutput" if name in DEBUG_TAGS else "Internal"
        dram[name] = nc.dram_tensor(name, shape, dtype, kind=kind).ap()
        return dram[name]

    x0 = din("x0", [D, T])
    w_inproj = din("w_inproj", [D, 2 * DIN])
    conv_w = din("conv_w", [P, NIN * 4])
    conv_b = din("conv_b", [P, NIN])
    w_xproj = din("w_xproj", [DIN, 112])
    w_dtproj = din("w_dtproj", [DTR, DIN])
    dt_bias = din("dt_bias", [P, NIN])
    a_log = din("a_log", [P, NIN * DST])
    d_vec = din("d_vec", [P, NIN])
    w_oproj = din("w_oproj", [DIN, D])
    ln1_g = din("ln1_g", [P, NCH]); ln1_b = din("ln1_b", [P, NCH])
    ln2_g = din("ln2_g", [P, NCH]); ln2_b = din("ln2_b", [P, NCH])
    w_ffn1 = din("w_ffn1", [D, DFF]); ffn_b1 = din("ffn_b1", [P, NFF])
    w_ffn2 = din("w_ffn2", [DFF, D]); ffn_b2 = din("ffn_b2", [P, NCH])
    gate_wT = din("gate_wT", [D, NEXP]); gate_b = din("gate_b", [NEXP, 1])
    w13T = din("w13T", [NEXP, D, 2 * INTER])
    w2T = din("w2T", [NEXP, INTER, D])
    w13sT = din("w13sT", [D, 2 * INTER])
    w2sT = din("w2sT", [INTER, D])
    wqT = din("wqT", [D, D]); wkT = din("wkT", [D, D])
    wvT = din("wvT", [D, D]); woT = din("woT", [D, D])
    l1_g = din("l1_g", [P, NCH]); l1_b = din("l1_b", [P, NCH])
    hscale = din("hscale", [1, NH])
    dirs = din("dirs", [P, NH])
    wrT = din("wrT", [D, D]); wkT2 = din("wkT2", [D, D]); wvT2 = din("wvT2", [D, D])
    mu_r = din("mu_r", [P, NCH]); mu_k = din("mu_k", [P, NCH])
    identf = din("identf", [P, P])
    ones_col = din("ones_col", [P, 1])
    ones_row = din("ones_row", [1, P])
    trimask = din("trimask", [P, P])
    rot_sT = din("rot_sT", [P, P])
    cos_q = din("cos_q", [P, T]); sin_q = din("sin_q", [P, T])
    cos_k = din("cos_k", [P, T]); sin_k = din("sin_k", [P, T])
    cpool = din("cpool", [P, len(CONSTS)])
    sel16 = din("sel16", [NEXP, NEXP * P])

    out = nc.dram_tensor("out", [D, T], F32, kind="ExternalOutput").ap()
    dram["out"] = out

    n0_sp = dscr("n0_sp", [P, NCH, T])
    xc_sp = dscr("xc_sp", [P, NIN, T])
    dt_sp = dscr("dt_sp", [P, NIN, T])
    sz_sp = dscr("sz_sp", [P, NIN, T])
    u1_sp = dscr("u1_sp", [P, NCH, T])
    h1_sp = dscr("h1_sp", [P, NFF, T])
    n1_sp = dscr("n1_sp", [P, NCH, T])
    xn_sp = dscr("xn_sp", [P, NCH, T])
    q_sp = dscr("q_sp", [P, NH, T])
    k_sp = dscr("k_sp", [P, NH, T])
    vt_sp = dscr("vt_sp", [P, NH, T // P, P])

    es = contextlib.ExitStack()
    with es:
        B = Bld(nc, es)

        def sb(name, shape, dtype, stack=es):
            return stack.enter_context(nc.sbuf_tensor(name, shape, dtype))

        def ps(name, shape, dtype=F32, stack=es):
            return stack.enter_context(nc.psum_tensor(name, shape, dtype))

        R = sb("R", [P, NCH * T], F32)
        onescol_b = sb("onescol_b", [P, 1], BF16)
        onesrow_f = sb("onesrow_f", [1, P], F32)
        onesrow_b = sb("onesrow_b", [1, P], BF16)
        ident_b = sb("ident_b", [P, P], BF16)
        ident_f = sb("ident_f", [P, P], F32)
        tri_f = sb("tri_f", [P, P], F32)
        cp = sb("cp", [P, len(CONSTS)], F32)
        sel16_b = sb("sel16_b", [NEXP, NEXP * P], BF16)
        for ci, cv in enumerate(CONSTS):
            nc.const_aps.register(F32, cv, cp[:, ci:ci + 1])

        @B.sec("gpsimd")
        def _(c):
            c.dma(R[:].rearrange("p (k t) -> p k t", k=NCH),
                  x0.rearrange("(k p) t -> p k t", p=P), "gC")
            c.dma(onescol_b[:], ones_col[:, :], "gC")
            c.dma(onesrow_b[:], ones_row[:, :], "gC")
            c.dma(ident_b[:], identf[:, :], "gC")
            c.dma(sel16_b[:], sel16[:, :], "gC")

        @B.sec("sync")
        def _(c):
            c.dma(onesrow_f[:], ones_row[:, :], "yM")
            c.dma(ident_f[:], identf[:, :], "yM")
            c.dma(tri_f[:], trimask[:, :], "yM")
            c.dma(cp[:], cpool[:, :], "yM")

        ph0_loads = [B.ev("gC"), B.ev("yM")]
        B.barrier()

        l0 = contextlib.ExitStack()
        with l0:
            B_fm = sb("B_fm", [DST, T], BF16, l0)
            C_fm = sb("C_fm", [DST, T], BF16, l0)
            comb_fm = sb("comb_fm", [NEXP, T], BF16, l0)

            phase1(B, nc, l0, dram, locals())
            if "dbg_l0" in DEBUG_TAGS:
                for nm, tens, dt in (("dbg_comb", comb_fm, BF16),
                                     ("dbg_bfm", B_fm, BF16),
                                     ("dbg_cfm", C_fm, BF16)):
                    dram[nm] = nc.dram_tensor(nm, list(tens[:].shape), dt,
                                              kind="ExternalOutput").ap()
                B.barrier()

                @B.sec("sync")
                def _(c):
                    c.dma(dram["dbg_comb"][:, :], comb_fm[:], "yD")
                    c.dma(dram["dbg_bfm"][:, :], B_fm[:], "yD")
                    c.dma(dram["dbg_cfm"][:, :], C_fm[:], "yD")
                    c.wait(B.ev("yD"))
            if N_PHASES >= 2:
                phase2_scan(B, nc, l0, dram, locals())
            if N_PHASES >= 3:
                phase3_mlp(B, nc, l0, dram, locals())
            if N_PHASES >= 4:
                phase4_moe(B, nc, l0, dram, locals())

        if N_PHASES >= 5:
            l1 = contextlib.ExitStack()
            with l1:
                phase5_l1(B, nc, l1, dram, locals())

        B.barrier()

        @B.sec("sync")
        def _(c):
            c.dma(out.rearrange("(k p) t -> p k t", p=P),
                  R[:].rearrange("p (k t) -> p k t", k=NCH), "yOUT")
            c.wait(B.ev("yOUT"))

        B.flush()
    return nc



def phase1(B, nc, l0, dram, env):
    R = env["R"]
    B_fm = env["B_fm"]; C_fm = env["C_fm"]; comb_fm = env["comb_fm"]
    onescol_b = env["onescol_b"]; onesrow_f = env["onesrow_f"]
    ident_f = env["ident_f"]
    ph0_loads = env["ph0_loads"]

    TC = 256
    NT = T // TC

    st = contextlib.ExitStack()
    with st:
        def sb(name, shape, dtype, stack=None):
            return (stack or st).enter_context(nc.sbuf_tensor(name, shape, dtype))

        def ps(name, shape, dtype=F32):
            return st.enter_context(nc.psum_tensor(name, shape, dtype))

        mu_row = sb("mu_row", [1, T], F32)
        r1_row = sb("r1_row", [1, T], F32)
        s_row = sb("s_row", [1, T], F32)
        w_xp = sb("w_xp_s", [P, NIN * 112], BF16)
        w_dt = sb("w_dt_s", [DTR, DIN], BF16)
        cw = sb("cw_s", [P, NIN * 4], F32)
        cb = sb("cb_s", [P, NIN], F32)
        dtb = sb("dtb_s", [P, NIN], F32)
        l1g = sb("l1g_s", [P, NCH], F32)
        l1bb = sb("l1bb_s", [P, NCH], F32)
        gw = sb("gw_s", [P, NCH * NEXP], F32)
        gb = sb("gb_s", [NEXP, 1], F32)
        nf = sb("nf", [P, NCH * TC], F32)
        ln1c = sb("ln1c", [P, NCH * TC], BF16)
        bc3 = sb("bc3", [P, 3 * TC], F32)
        sp1 = sb("sp1", [P, TC], F32)
        sp2 = sb("sp2", [P, TC], F32)
        tok16 = sb("tok16", [P, 5 * NEXP], F32)
        gbi_t = sb("gbi_t", [NEXP, P], F32)
        negbig = sb("negbig", [P, NEXP], F32)
        thr = sb("thr", [P, 1], F32)
        eqm_i = sb("eqm_i", [P, NEXP], mybir.dt.int32)
        rho = sb("rho", [P, 1], F32)
        gate_sc = sb("gate_sc", [NEXP, T], F32)
        r16_sb = sb("r16_sb", [NEXP, TC], F32)

        ps_bc = ps("ps_bc", [P, 512])
        ps_bc2 = ps("ps_bc2", [P, 512])
        ps_g = ps("ps_g", [NEXP, 512])
        ps_m0 = ps("ps_m0", [P, 512])
        ps_m1 = ps("ps_m1", [P, 512])
        ps_m2 = ps("ps_m2", [P, 512])
        ps_xd = ps("ps_xd", [112, 512])

        @B.sec("sync")
        def _(c):
            c.dma(cw[:], dram["conv_w"][:, :], "yM")
            c.dma(cb[:], dram["conv_b"][:, :], "yM")
            c.dma(dtb[:], dram["dt_bias"][:, :], "yM")
            c.dma(l1g[:], dram["ln1_g"][:, :], "yM")
            c.dma(l1bb[:], dram["ln1_b"][:, :], "yM")
            c.dma(gw[:].rearrange("p (k m) -> p k m", k=NCH),
                  dram["gate_wT"].rearrange("(k p) m -> p k m", p=P), "yM")
            c.dma(gb[:], dram["gate_b"][:, :], "yM")
        sload = B.ev("yM")

        @B.sec("gpsimd")
        def _(c):
            c.dma(w_xp[:].rearrange("p (k m) -> p k m", k=NIN),
                  dram["w_xproj"].rearrange("(k p) m -> p k m", p=P), "gB")
            c.dma(w_dt[:], dram["w_dtproj"][:, :], "gB")
        wsmall = B.ev("gB")

        sta = contextlib.ExitStack()
        with sta:
            Q_r = sb("Q_r", [1, T], F32, sta)
            rw = sb("rw", [1, 2 * T], F32, sta)
            xb_s = sb("xb_s", [P, NCH * TC], BF16, sta)
            xq_s = sb("xq_s", [P, NCH * TC], BF16, sta)

            for tc in range(NT):
                lo = tc * TC

                @B.sec("scalar")
                def _(c):
                    c.wait(*ph0_loads)
                    c.wait(B.ev("T"))
                    for k in range(NCH):
                        c.e.activation(xb_s[:, k * TC:(k + 1) * TC],
                                       R[:, k * T + lo:k * T + lo + TC], AF.Identity)
                        ins = c.e.activation(xq_s[:, k * TC:(k + 1) * TC],
                                             R[:, k * T + lo:k * T + lo + TC],
                                             AF.Square)
                    c.inc(ins)
                e_cast = B.ev("S")

                @B.sec("tensor")
                def _(c):
                    c.wait(*ph0_loads)
                    c.wait(e_cast, B.ev("S"))
                    for k in range(NCH):
                        c.e.matmul(ps_g[0:1, 0:TC], onescol_b[:],
                                   xb_s[:, k * TC:(k + 1) * TC],
                                   start=(k == 0), stop=(k == NCH - 1))
                    for k in range(NCH):
                        ins = c.e.matmul(ps_xd[0:1, 0:TC], onescol_b[:],
                                         xq_s[:, k * TC:(k + 1) * TC],
                                         start=(k == 0), stop=(k == NCH - 1))
                    c.inc(ins)
                e_mm = B.ev("T")

                @B.sec("scalar")
                def _(c):
                    c.wait(e_mm)
                    c.e.activation(mu_row[:, lo:lo + TC], ps_g[0:1, 0:TC], AF.Identity)
                    ins = c.e.activation(Q_r[:, lo:lo + TC], ps_xd[0:1, 0:TC],
                                         AF.Identity)
                    c.inc(ins)
            e_stats = B.ev("S")

            q1 = rw[:, 0:T]; v1 = rw[:, T:2 * T]

            @B.sec("scalar")
            def _(c):
                c.wait(e_stats)
                ins = c.e.activation(q1, mu_row[:, :], AF.Square, scale=1.0 / D)
                c.inc(ins)
            e1 = B.ev("S")

            @B.sec("vector")
            def _(c):
                c.wait(e1)
                ins = c.e.scalar_tensor_tensor(v1, Q_r[:, :], 1.0 / D, q1,
                                               ALU.mult, ALU.subtract)
                c.inc(ins)
            e2 = B.ev("V")

            @B.sec("scalar")
            def _(c):
                c.wait(e2)
                ev = c.inc(c.e.activation(q1, v1, AF.Ln, bias=EPS))
                c.wait(ev)
                ev = c.inc(c.e.activation(r1_row[:, :], q1, AF.Exp, scale=-0.5))
                c.wait(ev)
                c.inc(c.e.activation(q1, q1, AF.Exp, scale=-1.0))
            e3 = B.ev("S")

            @B.sec("vector")
            def _(c):
                c.wait(e3)
                ins = c.e.tensor_tensor(v1, v1, q1, ALU.mult)
                c.inc(ins)
            e4 = B.ev("V")

            @B.sec("scalar")
            def _(c):
                c.wait(e4)
                ev = c.inc(c.e.activation(v1, v1, AF.Ln, bias=EPS))
                c.wait(ev)
                c.inc(c.e.activation(v1, v1, AF.Exp, scale=-0.5))
            e5 = B.ev("S")

            @B.sec("vector")
            def _(c):
                c.wait(e5)
                ins = c.e.tensor_tensor(s_row[:, :], v1, r1_row[:, :], ALU.mult)
                c.inc(ins)
            B.barrier()
        e_rows = B.ev("V")

        def chunk_norm(c_lo, want_n0, n0c):
            lo = c_lo

            @B.sec("tensor")
            def _(c):
                c.wait(e_rows, B.ev("V"), B.ev("S"))
                ins = c.e.matmul(ps_bc[:, 0:TC], onesrow_f[:],
                                 mu_row[:, lo:lo + TC], start=True, stop=True)
                c.inc(ins)
            eb1 = B.ev("T")

            @B.sec("scalar")
            def _(c):
                c.wait(eb1)
                ins = c.e.activation(bc3[:, 0:TC], ps_bc[:, 0:TC], AF.Identity,
                                     scale=1.0 / D)
                c.inc(ins)
            eb1e = B.ev("S")

            @B.sec("tensor")
            def _(c):
                c.wait(eb1e)
                if want_n0:
                    c.e.matmul(ps_bc[:, 0:TC], onesrow_f[:], r1_row[:, lo:lo + TC],
                               start=True, stop=True)
                ins = c.e.matmul(ps_bc2[:, 0:TC], onesrow_f[:], s_row[:, lo:lo + TC],
                                 start=True, stop=True)
                c.inc(ins)
            eb2 = B.ev("T")

            @B.sec("scalar")
            def _(c):
                c.wait(eb2)
                if want_n0:
                    c.e.activation(bc3[:, TC:2 * TC], ps_bc[:, 0:TC], AF.Identity)
                ins = c.e.activation(bc3[:, 2 * TC:3 * TC], ps_bc2[:, 0:TC], AF.Identity)
                c.inc(ins)
            eb = B.ev("S")

            @B.sec("vector")
            def _(c):
                c.wait(eb, B.ev("T"), B.ev("yS"))
                for k in range(NCH):
                    c.chain(c.e.tensor_tensor(nf[:, k * TC:(k + 1) * TC],
                                              R[:, k * T + lo:k * T + lo + TC],
                                              bc3[:, 0:TC], ALU.subtract))
                    if want_n0:
                        c.e.tensor_tensor(n0c[:, k * TC:(k + 1) * TC],
                                          nf[:, k * TC:(k + 1) * TC],
                                          bc3[:, TC:2 * TC], ALU.mult)
                    ins = c.e.tensor_tensor(ln1c[:, k * TC:(k + 1) * TC],
                                            nf[:, k * TC:(k + 1) * TC],
                                            bc3[:, 2 * TC:3 * TC], ALU.mult)
                c.inc(ins)
            e_nf = B.ev("V")

            @B.sec("scalar")
            def _(c):
                c.wait(e_nf, sload)
                for k in range(NCH):
                    ins = c.e.activation(ln1c[:, k * TC:(k + 1) * TC],
                                         ln1c[:, k * TC:(k + 1) * TC], AF.Identity,
                                         scale=l1g[:, k:k + 1], bias=l1bb[:, k:k + 1])
                c.inc(ins)
            return e_nf, B.ev("S")

        stz = contextlib.ExitStack()
        with stz:
            w_z = sb("w_z_s", [P, NCH * DIN], BF16, stz)
            n0c = sb("n0c", [P, NCH * TC], BF16, stz)
            sz_st = sb("sz_st", [P, NIN * TC], BF16, stz)

            @B.sec("gpsimd")
            def _(c):
                wz = dram["w_inproj"][:, DIN:2 * DIN]
                c.dma(w_z[:].rearrange("p (k m) -> p k m", k=NCH),
                      wz.rearrange("(k p) m -> p k m", p=P), "gA")
            wload = B.ev("gA")

            for tc in range(NT):
                lo = tc * TC
                e_nf, e_ln1 = chunk_norm(lo, True, n0c)

                @B.sec("tensor")
                def _(c):
                    c.wait(e_nf, sload, B.ev("V"))
                    for k in range(NCH):
                        ins = c.e.matmul(ps_g[:, 0:TC],
                                         gw[:, k * NEXP:(k + 1) * NEXP],
                                         nf[:, k * TC:(k + 1) * TC],
                                         start=(k == 0), stop=(k == NCH - 1))
                    c.inc(ins)
                e_gmm = B.ev("T")

                @B.sec("tensor")
                def _(c):
                    c.wait(B.ev("S"))
                    ins = c.e.matmul(ps_bc[0:NEXP, 0:TC], onesrow_f[:, 0:NEXP],
                                     r1_row[:, lo:lo + TC], start=True, stop=True)
                    c.inc(ins)
                e_r116 = B.ev("T")

                @B.sec("scalar")
                def _(c):
                    c.wait(e_r116)
                    c.inc(c.e.activation(r16_sb[:], ps_bc[0:NEXP, 0:TC],
                                         AF.Identity))
                e_r116e = B.ev("S")

                @B.sec("vector")
                def _(c):
                    c.wait(e_gmm, e_r116e)
                    ins = c.e.tensor_tensor(gate_sc[:, lo:lo + TC], ps_g[:, 0:TC],
                                            r16_sb[:], ALU.mult)
                    c.inc(ins)
                e_gm = B.ev("V")

                @B.sec("scalar")
                def _(c):
                    c.wait(e_gm)
                    ins = c.e.activation(gate_sc[:, lo:lo + TC],
                                         gate_sc[:, lo:lo + TC], AF.Sigmoid)
                    c.inc(ins)

                for o in range(NIN):
                    pmm = (ps_m0, ps_m1, ps_m2)[o % 3]
                    pk = "m%d" % (o % 3)

                    @B.sec("tensor")
                    def _(c):
                        c.wait(e_ln1, wload, B.lastev.get(pk))
                        for k in range(NCH):
                            ins = c.e.matmul(
                                pmm[:, 0:TC],
                                w_z[:, k * DIN + o * P:k * DIN + o * P + P],
                                ln1c[:, k * TC:(k + 1) * TC],
                                start=(k == 0), stop=(k == NCH - 1))
                        c.inc(ins)
                    e_z = B.ev("T")

                    @B.sec("scalar")
                    def _(c):
                        c.wait(e_z, B.ev("yS"))
                        ins = c.e.activation(sz_st[:, o * TC:(o + 1) * TC],
                                             pmm[:, 0:TC], AF.Silu)
                        B.lastev[pk] = c.inc(ins)
                e_sz = B.ev("S")

                @B.sec("sync")
                def _(c):
                    c.wait(e_sz, e_nf)
                    c.dma(dram["sz_sp"][:, :, lo:lo + TC],
                          sz_st[:].rearrange("p (k t) -> p k t", k=NIN), "yS")
                    c.dma(dram["n0_sp"][:, :, lo:lo + TC],
                          n0c[:].rearrange("p (k t) -> p k t", k=NIN // 2), "yS")

        @B.sec("vector")
        def _(c):
            c.e.memset(negbig[:], NEG)

        for j in range(T // P):
            @B.sec("scalar")
            def _(c):
                c.wait(B.ev("V"), B.ev("T"))
                ins = c.e.activation(gbi_t[:, :], gate_sc[:, j * P:(j + 1) * P],
                                     AF.Identity, bias=gb[:, 0:1])
                c.inc(ins)
            e_bi = B.ev("S")

            @B.sec("tensor")
            def _(c):
                c.wait(e_bi)
                c.e.transpose(ps_m0[:, 0:NEXP], gate_sc[:, j * P:(j + 1) * P],
                              ident_f[0:NEXP, 0:NEXP])
                ins = c.e.transpose(ps_m1[:, 0:NEXP], gbi_t[:, :],
                                    ident_f[0:NEXP, 0:NEXP])
                c.inc(ins)
            e_t = B.ev("T")

            @B.sec("scalar")
            def _(c):
                c.wait(e_t, B.ev("V"))
                c.e.activation(tok16[:, 0:NEXP], ps_m0[:, 0:NEXP], AF.Identity)
                c.e.activation(tok16[:, NEXP:2 * NEXP], ps_m1[:, 0:NEXP], AF.Identity)
                ins = c.e.activation(tok16[:, 2 * NEXP:3 * NEXP],
                                     ps_m1[:, 0:NEXP], AF.Identity)
                c.inc(ins)
            e_tt = B.ev("S")

            @B.sec("vector")
            def _(c):
                c.wait(e_tt)
                sc_t = tok16[:, 0:NEXP]
                bi_t = tok16[:, NEXP:2 * NEXP]
                wrk = tok16[:, 2 * NEXP:3 * NEXP]
                cmb = tok16[:, 4 * NEXP:5 * NEXP]
                for r in range(4):
                    c.chain(c.e.reduce_max(thr[:], wrk, axis=AX.X))
                    if r < 3:
                        c.chain(c.e.tensor_single_scalar(eqm_i[:], wrk, thr[:, 0:1],
                                                         ALU.is_ge))
                        c.chain(c.e.copy_predicated(wrk, eqm_i[:], negbig[:]))
                ins = c.e.scalar_tensor_tensor(cmb, bi_t, thr[:, 0:1], sc_t,
                                               ALU.is_ge, ALU.mult,
                                               accum_out=rho[:])
                c.chain(ins)
                c.chain(c.e.reciprocal(rho[:], rho[:]))
                ins = c.e.tensor_scalar(cmb, cmb, rho[:, 0:1], 2.5,
                                        ALU.mult, ALU.mult)
                c.inc(ins)
            e_c = B.ev("V")

            @B.sec("tensor")
            def _(c):
                c.wait(e_c, B.ev("S"))
                ins = c.e.transpose(ps_m0[0:NEXP, 0:P],
                                    tok16[:, 4 * NEXP:5 * NEXP], ident_f[:])
                c.inc(ins)
            e_cb = B.ev("T")

            @B.sec("scalar")
            def _(c):
                c.wait(e_cb)
                ins = c.e.activation(comb_fm[:, j * P:(j + 1) * P],
                                     ps_m0[0:NEXP, 0:P], AF.Identity)
                c.inc(ins)

        stx = contextlib.ExitStack()
        with stx:
            w_x = sb("w_x_s", [P, NCH * DIN], BF16, stx)
            xi_buf = sb("xi_buf", [P, NIN * (TC + 3)], BF16, stx)
            xc_st = sb("xc_st", [P, NIN * TC], BF16, stx)
            dt_st = sb("dt_st", [P, NIN * TC], BF16, stx)
            dtr_s = sb("dtr_s", [DTR, TC], BF16, stx)

            @B.sec("gpsimd")
            def _(c):
                c.wait(B.ev("T"), B.ev("V"), B.ev("S"))
                wx = dram["w_inproj"][:, 0:DIN]
                c.dma(w_x[:].rearrange("p (k m) -> p k m", k=NCH),
                      wx.rearrange("(k p) m -> p k m", p=P), "gA")
            wload2 = B.ev("gA")

            @B.sec("vector")
            def _(c):
                c.wait(B.ev("yS"), B.ev("T"), B.ev("S"))
                for o in range(NIN):
                    ins = c.e.memset(xi_buf[:, o * (TC + 3):o * (TC + 3) + 3], 0.0)
                c.inc(ins)

            for tc in range(NT):
                lo = tc * TC
                e_nf, e_ln1 = chunk_norm(lo, False, None)

                for o in range(NIN):
                    pmm = (ps_m0, ps_m1, ps_m2)[o % 3]
                    pk = "m%d" % (o % 3)

                    @B.sec("tensor")
                    def _(c):
                        c.wait(e_ln1, wload2, B.lastev.get(pk), B.ev("V"))
                        for k in range(NCH):
                            ins = c.e.matmul(
                                pmm[:, 0:TC],
                                w_x[:, k * DIN + o * P:k * DIN + o * P + P],
                                ln1c[:, k * TC:(k + 1) * TC],
                                start=(k == 0), stop=(k == NCH - 1))
                        c.inc(ins)
                    e_x = B.ev("T")

                    @B.sec("scalar")
                    def _(c):
                        c.wait(e_x)
                        ins = c.e.activation(
                            xi_buf[:, o * (TC + 3) + 3:(o + 1) * (TC + 3)],
                            pmm[:, 0:TC], AF.Identity)
                        B.lastev[pk] = c.inc(ins)
                e_xz = B.ev("S")

                @B.sec("vector")
                def _(c):
                    c.wait(e_xz, B.ev("yS"), B.ev("V"))
                    for o in range(NIN):
                        base = o * (TC + 3)
                        c.chain(c.e.tensor_scalar(sp1[:, :],
                                                  xi_buf[:, base:base + TC],
                                                  cw[:, o * 4:o * 4 + 1], None,
                                                  ALU.mult))
                        for k in (1, 2):
                            c.chain(c.e.scalar_tensor_tensor(
                                sp1[:, :], xi_buf[:, base + k:base + k + TC],
                                cw[:, o * 4 + k:o * 4 + k + 1],
                                sp1[:, :], ALU.mult, ALU.add))
                        c.chain(c.e.scalar_tensor_tensor(
                            xc_st[:, o * TC:(o + 1) * TC],
                            xi_buf[:, base + 3:base + 3 + TC],
                            cw[:, o * 4 + 3:o * 4 + 4],
                            sp1[:, :], ALU.mult, ALU.add))
                e_conv = B.ev("V")

                @B.sec("vector")
                def _(c):
                    for o in range(NIN):
                        base = o * (TC + 3)
                        ins = c.e.tensor_copy(xi_buf[:, base:base + 3],
                                              xi_buf[:, base + TC:base + TC + 3])
                    c.inc(ins)

                @B.sec("scalar")
                def _(c):
                    c.wait(e_conv)
                    for o in range(NIN):
                        ins = c.e.activation(xc_st[:, o * TC:(o + 1) * TC],
                                             xc_st[:, o * TC:(o + 1) * TC], AF.Silu,
                                             bias=cb[:, o:o + 1])
                    c.inc(ins)
                e_xc = B.ev("S")

                @B.sec("tensor")
                def _(c):
                    c.wait(e_xc, wsmall)
                    for k in range(NIN):
                        ins = c.e.matmul(ps_xd[:, 0:TC],
                                         w_xp[:, k * 112:(k + 1) * 112],
                                         xc_st[:, k * TC:(k + 1) * TC],
                                         start=(k == 0), stop=(k == NIN - 1))
                    c.inc(ins)
                e_xd = B.ev("T")

                @B.sec("scalar")
                def _(c):
                    c.wait(e_xd)
                    c.e.activation(dtr_s[:, :], ps_xd[0:DTR, 0:TC], AF.Identity)
                    c.e.activation(B_fm[:, lo:lo + TC], ps_xd[64:80, 0:TC],
                                   AF.Identity)
                    ins = c.e.activation(C_fm[:, lo:lo + TC],
                                         ps_xd[96:112, 0:TC], AF.Identity)
                    c.inc(ins)
                e_dtr = B.ev("S")

                for o in range(NIN):
                    pmm = (ps_m0, ps_m1, ps_m2)[o % 3]
                    pk = "m%d" % (o % 3)

                    @B.sec("tensor")
                    def _(c):
                        c.wait(e_dtr, B.lastev.get(pk))
                        ins = c.e.matmul(pmm[:, 0:TC], w_dt[:, o * P:(o + 1) * P],
                                         dtr_s[:, :], start=True, stop=True)
                        c.inc(ins)
                    e_d2 = B.ev("T")

                    @B.sec("scalar")
                    def _(c):
                        c.wait(e_d2, B.ev("V"))
                        c.e.activation(sp2[:, :], pmm[:, 0:TC], AF.Relu,
                                       bias=dtb[:, o:o + 1])
                        ev = c.inc(c.e.activation(sp1[:, :], pmm[:, 0:TC], AF.Abs,
                                                  bias=dtb[:, o:o + 1]))
                        B.lastev[pk] = ev
                        c.wait(ev)
                        ev = c.inc(c.e.activation(sp1[:, :], sp1[:, :], AF.Exp,
                                                  scale=-1.0))
                        c.wait(ev)
                        c.inc(c.e.activation(sp1[:, :], sp1[:, :], AF.Ln, bias=1.0))
                    e_sp = B.ev("S")

                    @B.sec("vector")
                    def _(c):
                        c.wait(e_sp, B.ev("yS"))
                        ins = c.e.tensor_tensor(dt_st[:, o * TC:(o + 1) * TC],
                                                sp1[:, :], sp2[:, :], ALU.add)
                        c.inc(ins)
                e_dt = B.ev("V")

                @B.sec("sync")
                def _(c):
                    c.wait(e_xc, e_dt)
                    c.dma(dram["xc_sp"][:, :, lo:lo + TC],
                          xc_st[:].rearrange("p (k t) -> p k t", k=NIN), "yS")
                    c.dma(dram["dt_sp"][:, :, lo:lo + TC],
                          dt_st[:].rearrange("p (k t) -> p k t", k=NIN), "yS")

        B.barrier()


def phase2_scan(B, nc, l0, dram, env):
    R = env["R"]
    B_fm = env["B_fm"]; C_fm = env["C_fm"]
    onesrow_b = env["onesrow_b"]
    TC = 256
    NT = T // TC

    st = contextlib.ExitStack()
    with st:
        def sb(name, shape, dtype, stack=None):
            return (stack or st).enter_context(nc.sbuf_tensor(name, shape, dtype))

        def ps(name, shape, dtype=F32):
            return st.enter_context(nc.psum_tensor(name, shape, dtype))

        w_op = sb("w_op_s", [P, NIN * D], BF16)
        a_fm = sb("a_fm", [P, NIN * DST], F32)
        d_fm = sb("d_fm", [P, NIN], F32)
        dt_all = sb("dt_all", [P, NIN * TC], BF16)
        xc_all = sb("xc_all", [P, NIN * TC], BF16)
        sz_all = sb("sz_all", [P, NIN * TC], BF16)
        n0c2 = sb("n0c2", [P, NCH * TC], BF16)
        dtx_all = sb("dtx_all", [P, NIN * TC], BF16)
        dA = sb("dA_all", [P, NIN * TC], F32)
        W_all = sb("W_all", [P, NIN * TC], BF16)
        hs2 = [sb("hs_a", [P, NIN * TC], BF16), sb("hs_b", [P, NIN * TC], BF16)]
        y_all = sb("y_all", [P, NIN * TC], BF16)
        Bb2 = [sb("Bb_a", [P, TC], BF16), sb("Bb_b", [P, TC], BF16)]
        Cb2 = [sb("Cb_a", [P, TC], BF16), sb("Cb_b", [P, TC], BF16)]
        h_state = sb("h_state", [P, NIN * DST], F32)
        u1c = sb("u1c", [P, NCH * TC], BF16)
        gt = sb("gt", [P, TC], BF16)
        uf = sb("uf", [P, TC], F32)

        ps_b1 = ps("ps_b1", [P, 512])
        ps_b2 = ps("ps_b2", [P, 512])
        ps_o0 = ps("ps_o0", [P, 512])
        ps_o1 = ps("ps_o1", [P, 512])
        ps_o2 = ps("ps_o2", [P, 512])

        @B.sec("gpsimd")
        def _(c):
            c.wait(B.ev("T"), B.ev("V"), B.ev("S"))
            c.dma(w_op[:].rearrange("p (k m) -> p k m", k=NIN),
                  dram["w_oproj"].rearrange("(k p) m -> p k m", p=P), "gA")
            c.dma(a_fm[:], dram["a_log"][:, :], "gA")
            c.dma(d_fm[:], dram["d_vec"][:, :], "gA")
        wload = B.ev("gA")

        @B.sec("scalar")
        def _(c):
            c.wait(wload)
            c.chain(c.e.activation(a_fm[:], a_fm[:], AF.Exp))
        e_a1 = B.ev("S")

        @B.sec("vector")
        def _(c):
            c.wait(e_a1)
            c.chain(c.e.tensor_scalar_mul(a_fm[:], a_fm[:], -1.0))
        e_a = B.ev("V")

        a3 = a_fm[:].rearrange("p (kc n) -> p kc n", n=DST)
        for tc in range(NT):
            lo = tc * TC

            @B.sec("sync")
            def _(c):
                c.wait(B.ev("V"), B.ev("T"), B.ev("S"))
                c.dma(dt_all[:].rearrange("p (k t) -> p k t", k=NIN),
                      dram["dt_sp"][:, :, lo:lo + TC], "yA")
                c.dma(xc_all[:].rearrange("p (k t) -> p k t", k=NIN),
                      dram["xc_sp"][:, :, lo:lo + TC], "yA")
                c.dma(sz_all[:].rearrange("p (k t) -> p k t", k=NIN),
                      dram["sz_sp"][:, :, lo:lo + TC], "yA")
                c.dma(n0c2[:].rearrange("p (k t) -> p k t", k=NCH),
                      dram["n0_sp"][:, :, lo:lo + TC], "yA")
            e_ld = B.ev("yA")

            @B.sec("vector")
            def _(c):
                c.wait(e_ld)
                c.chain(c.e.tensor_tensor(dtx_all[:], dt_all[:], xc_all[:],
                                          ALU.mult))
            e_dtx = B.ev("V")

            def emit_exp(n):
                @B.sec("scalar")
                def _(c):
                    c.wait(e_ld, e_a)
                    for g in range(2):
                        c.wait(B.lastev.get("p2sc%d" % g))
                        for kc in range(8 * g, 8 * g + 8):
                            ev = c.inc(c.e.activation(
                                dA[:, kc * TC:(kc + 1) * TC],
                                dt_all[:, kc * TC:(kc + 1) * TC], AF.Exp,
                                scale=a_fm[:, kc * DST + n:kc * DST + n + 1]))
                        B.lastev["p2dA%d" % g] = ev

            for n in range(DST):
                par = n % 2

                @B.sec("tensor")
                def _(c):
                    c.wait(e_dtx, B.lastev.get("p2bc"))
                    sel = env["sel16_b"][:, n * P:(n + 1) * P]
                    c.e.matmul(ps_b1[:, 0:TC], sel, B_fm[:, lo:lo + TC],
                               start=True, stop=True)
                    ins = c.e.matmul(ps_b2[:, 0:TC], sel, C_fm[:, lo:lo + TC],
                                     start=True, stop=True)
                    c.inc(ins)
                e_bc = B.ev("T")

                @B.sec("scalar")
                def _(c):
                    c.wait(e_bc, B.lastev.get("p2W%d" % par),
                           B.lastev.get("p2G%d" % par))
                    c.e.activation(Bb2[par][:], ps_b1[:, 0:TC], AF.Identity)
                    ins = c.e.activation(Cb2[par][:], ps_b2[:, 0:TC], AF.Identity)
                    B.lastev["p2bc"] = c.inc(ins)
                e_bce = B.ev("S")

                emit_exp(n)

                @B.sec("vector")
                def _(c):
                    c.wait(e_bce, e_dtx, B.lastev.get("p2G%d" % par))
                    bb = Bb2[par][:].unsqueeze(1).broadcast_to([P, NIN, TC])
                    ev = c.chain(c.e.tensor_tensor(
                        W_all[:].rearrange("p (k t) -> p k t", k=NIN),
                        dtx_all[:].rearrange("p (k t) -> p k t", k=NIN),
                        bb, ALU.mult))
                    B.lastev["p2W%d" % par] = ev
                    for g in range(2):
                        c.wait(B.lastev.get("p2dA%d" % g))
                        for kc in range(8 * g, 8 * g + 8):
                            init = (0.0 if tc == 0
                                    else h_state[:, kc * DST + n:
                                                 kc * DST + n + 1])
                            ev = c.inc(c.e.tensor_tensor_scan(
                                hs2[par][:, kc * TC:(kc + 1) * TC],
                                dA[:, kc * TC:(kc + 1) * TC],
                                W_all[:, kc * TC:(kc + 1) * TC],
                                init, ALU.mult, ALU.add))
                        B.lastev["p2sc%d" % g] = ev
                    c.wait(B.ev("V"))
                    c.chain(c.e.tensor_copy(
                        h_state[:].rearrange("p (kc n) -> p kc n", n=DST)[:, :, n],
                        hs2[par][:].rearrange("p (kc t) -> p kc t", kc=NIN)
                        [:, :, TC - 1]))
                e_vn = B.ev("V")

                @B.sec("gpsimd")
                def _(c):
                    c.wait(e_vn, e_bce)
                    cb = Cb2[par][:].unsqueeze(1).broadcast_to([P, NIN, TC])
                    if n == 0:
                        c.wait(B.ev("T"))
                        ev = c.chain(c.e.tensor_tensor(
                            y_all[:].rearrange("p (k t) -> p k t", k=NIN),
                            hs2[par][:].rearrange("p (k t) -> p k t", k=NIN),
                            cb, ALU.mult))
                    else:
                        c.chain(c.e.tensor_tensor(
                            hs2[par][:].rearrange("p (k t) -> p k t", k=NIN),
                            hs2[par][:].rearrange("p (k t) -> p k t", k=NIN),
                            cb, ALU.mult))
                        ev = c.chain(c.e.tensor_tensor(y_all[:], y_all[:],
                                                       hs2[par][:], ALU.add))
                    B.lastev["p2G%d" % par] = ev
                    B.lastev["p2yG"] = ev

            @B.sec("vector")
            def _(c):
                c.wait(B.lastev.get("p2yG"))
                for kc in range(NIN):
                    c.chain(c.e.scalar_tensor_tensor(
                        gt[:], xc_all[:, kc * TC:(kc + 1) * TC],
                        d_fm[:, kc:kc + 1],
                        y_all[:, kc * TC:(kc + 1) * TC], ALU.mult, ALU.add))
                    c.chain(c.e.tensor_tensor(
                        y_all[:, kc * TC:(kc + 1) * TC], gt[:],
                        sz_all[:, kc * TC:(kc + 1) * TC], ALU.mult))
            e_y = B.ev("V")

            for oc in range(NCH):
                pmm = (ps_o0, ps_o1, ps_o2)[oc % 3]
                pk = "o%d" % (oc % 3)

                @B.sec("tensor")
                def _(c):
                    c.wait(e_y, wload, B.lastev.get(pk))
                    for kc in range(NIN):
                        ins = c.e.matmul(
                            pmm[:, 0:TC],
                            w_op[:, kc * D + oc * P:kc * D + oc * P + P],
                            y_all[:, kc * TC:(kc + 1) * TC],
                            start=(kc == 0), stop=(kc == NIN - 1))
                    c.inc(ins)
                e_mm = B.ev("T")

                @B.sec("vector")
                def _(c):
                    c.wait(e_mm, B.lastev.get("p2uf"))
                    ev = c.chain(c.e.tensor_tensor(uf[:], pmm[:, 0:TC],
                                                   n0c2[:, oc * TC:(oc + 1) * TC],
                                                   ALU.add))
                    B.lastev[pk] = ev
                    c.chain(c.e.tensor_tensor(R[:, oc * T + lo:oc * T + lo + TC],
                                              R[:, oc * T + lo:oc * T + lo + TC],
                                              uf[:], ALU.add))
                e_u = B.ev("V")

                @B.sec("scalar")
                def _(c):
                    c.wait(e_u, B.ev("yS"))
                    B.lastev["p2uf"] = c.inc(
                        c.e.activation(u1c[:, oc * TC:(oc + 1) * TC], uf[:],
                                       AF.Identity))
            e_uc = B.ev("S")

            @B.sec("sync")
            def _(c):
                c.wait(e_uc)
                c.dma(dram["u1_sp"][:, :, lo:lo + TC],
                      u1c[:].rearrange("p (k t) -> p k t", k=NCH), "yS")

    B.barrier()


def phase3_mlp(B, nc, l0, dram, env):
    R = env["R"]
    onescol_b = env["onescol_b"]; onesrow_f = env["onesrow_f"]
    TC = 512
    NT = T // TC

    st = contextlib.ExitStack()
    with st:
        def sb(name, shape, dtype, stack=None):
            return (stack or st).enter_context(nc.sbuf_tensor(name, shape, dtype))

        def ps(name, shape, dtype=F32):
            return st.enter_context(nc.psum_tensor(name, shape, dtype))

        b1 = sb("b1_s", [P, NFF], F32)
        b2 = sb("b2_s", [P, NCH], F32)
        l2g = sb("l2g_s", [P, NCH], F32)
        l2b = sb("l2b_s", [P, NCH], F32)
        u1b = sb("u1b", [P, NCH * TC], BF16)
        ln2c = sb("ln2c", [P, NCH * TC], BF16)
        usq = ln2c
        rows2 = sb("rows2", [1, 2 * TC], F32)
        bc2 = sb("bc2", [P, 2 * TC], F32)
        tf = sb("tf", [P, TC], F32)
        h1b = sb("h1b", [P, NFF * TC], BF16)

        ps_r0 = ps("ps_r0", [1, 512])
        ps_r1 = ps("ps_r1", [1, 512])
        ps_b = ps("ps_b", [P, 512])
        ps_b2_ = ps("ps_b2_", [P, 512])
        ps_m0 = ps("ps3_m0", [P, 512])
        ps_m1 = ps("ps3_m1", [P, 512])
        ps_m2 = ps("ps3_m2", [P, 512])

        @B.sec("sync")
        def _(c):
            c.dma(b1[:], dram["ffn_b1"][:, :], "yM")
            c.dma(b2[:], dram["ffn_b2"][:, :], "yM")
            c.dma(l2g[:], dram["ln2_g"][:, :], "yM")
            c.dma(l2b[:], dram["ln2_b"][:, :], "yM")
        sload = B.ev("yM")

        sta = contextlib.ExitStack()
        with sta:
            w1 = sb("w1_s", [P, NCH * DFF], BF16, sta)

            @B.sec("gpsimd")
            def _(c):
                c.wait(B.ev("T"), B.ev("V"), B.ev("S"))
                c.dma(w1[:].rearrange("p (k m) -> p k m", k=NCH),
                      dram["w_ffn1"].rearrange("(k p) m -> p k m", p=P), "gA")
            wload = B.ev("gA")

            for tc in range(NT):
                lo = tc * TC

                @B.sec("sync")
                def _(c):
                    c.wait(B.ev("V"), B.ev("T"), B.ev("S"))
                    c.dma(u1b[:].rearrange("p (k t) -> p k t", k=NCH),
                          dram["u1_sp"][:, :, lo:lo + TC], "yA")
                e_ld = B.ev("yA")

                @B.sec("scalar")
                def _(c):
                    c.wait(e_ld, B.ev("T"))
                    for k in range(NCH):
                        ins = c.e.activation(usq[:, k * TC:(k + 1) * TC],
                                             u1b[:, k * TC:(k + 1) * TC], AF.Square)
                    c.inc(ins)
                e_sq = B.ev("S")

                @B.sec("tensor")
                def _(c):
                    c.wait(e_sq, B.ev("S"))
                    for k in range(NCH):
                        c.e.matmul(ps_r0[0:1, 0:TC], onescol_b[:],
                                   u1b[:, k * TC:(k + 1) * TC],
                                   start=(k == 0), stop=(k == NCH - 1))
                    for k in range(NCH):
                        ins = c.e.matmul(ps_r1[0:1, 0:TC], onescol_b[:],
                                         usq[:, k * TC:(k + 1) * TC],
                                         start=(k == 0), stop=(k == NCH - 1))
                    c.inc(ins)
                e_st = B.ev("T")

                @B.sec("scalar")
                def _(c):
                    c.wait(e_st)
                    c.e.activation(rows2[:, 0:TC], ps_r0[0:1, 0:TC], AF.Identity)
                    c.chain(c.e.activation(rows2[:, TC:2 * TC], ps_r0[0:1, 0:TC],
                                           AF.Square, scale=1.0 / D))
                e_r1 = B.ev("S")

                @B.sec("vector")
                def _(c):
                    c.wait(e_r1)
                    c.chain(c.e.scalar_tensor_tensor(
                        rows2[:, TC:2 * TC], ps_r1[0:1, 0:TC], 1.0 / D,
                        rows2[:, TC:2 * TC], ALU.mult, ALU.subtract))
                e_v = B.ev("V")

                @B.sec("scalar")
                def _(c):
                    c.wait(e_v)
                    ev = c.chain(c.e.activation(rows2[:, TC:2 * TC],
                                                rows2[:, TC:2 * TC], AF.Ln,
                                                bias=EPS))
                    c.chain(c.e.activation(rows2[:, TC:2 * TC],
                                           rows2[:, TC:2 * TC], AF.Exp,
                                           scale=-0.5))
                e_rw = B.ev("S")

                @B.sec("tensor")
                def _(c):
                    c.wait(e_rw, B.ev("V"))
                    c.e.matmul(ps_b[:, 0:TC], onesrow_f[:], rows2[:, 0:TC],
                               start=True, stop=True)
                    ins = c.e.matmul(ps_b2_[:, 0:TC], onesrow_f[:],
                                     rows2[:, TC:2 * TC], start=True, stop=True)
                    c.inc(ins)
                e_bc = B.ev("T")

                @B.sec("scalar")
                def _(c):
                    c.wait(e_bc)
                    c.e.activation(bc2[:, 0:TC], ps_b[:, 0:TC], AF.Identity,
                                   scale=1.0 / D)
                    ins = c.e.activation(bc2[:, TC:2 * TC], ps_b2_[:, 0:TC],
                                         AF.Identity)
                    c.inc(ins)
                e_bce = B.ev("S")

                @B.sec("vector")
                def _(c):
                    c.wait(e_bce, B.ev("T"))
                    for k in range(NCH):
                        c.chain(c.e.tensor_tensor(tf[:], u1b[:, k * TC:(k + 1) * TC],
                                                  bc2[:, 0:TC], ALU.subtract))
                        c.chain(c.e.tensor_tensor(ln2c[:, k * TC:(k + 1) * TC],
                                                  tf[:], bc2[:, TC:2 * TC],
                                                  ALU.mult))
                e_n = B.ev("V")

                @B.sec("scalar")
                def _(c):
                    c.wait(e_n, sload)
                    for k in range(NCH):
                        ins = c.e.activation(ln2c[:, k * TC:(k + 1) * TC],
                                             ln2c[:, k * TC:(k + 1) * TC],
                                             AF.Identity, scale=l2g[:, k:k + 1],
                                             bias=l2b[:, k:k + 1])
                    c.inc(ins)
                e_l2 = B.ev("S")

                for of in range(NFF):
                    pmm = (ps_m0, ps_m1, ps_m2)[of % 3]
                    pk = "3m%d" % (of % 3)

                    @B.sec("tensor")
                    def _(c):
                        c.wait(e_l2, wload, B.lastev.get(pk))
                        for k in range(NCH):
                            ins = c.e.matmul(
                                pmm[:, 0:TC],
                                w1[:, k * DFF + of * P:k * DFF + of * P + P],
                                ln2c[:, k * TC:(k + 1) * TC],
                                start=(k == 0), stop=(k == NCH - 1))
                        c.inc(ins)
                    e_h = B.ev("T")

                    @B.sec("scalar")
                    def _(c):
                        c.wait(e_h, B.ev("yS"))
                        B.lastev[pk] = c.inc(
                            c.e.activation(h1b[:, of * TC:(of + 1) * TC],
                                           pmm[:, 0:TC], AF.Gelu_apprx_tanh,
                                           bias=b1[:, of:of + 1]))
                e_g = B.ev("S")

                @B.sec("sync")
                def _(c):
                    c.wait(e_g)
                    c.dma(dram["h1_sp"][:, :, lo:lo + TC],
                          h1b[:].rearrange("p (k t) -> p k t", k=NFF), "yS")

        for half in (0, 1):
            sth = contextlib.ExitStack()
            with sth:
                w2h = sb(f"w2h_s{half}", [P, (NFF // 2) * D], BF16, sth)
                h1h = sb(f"h1h{half}", [P, (NFF // 2) * TC], BF16, sth)

                @B.sec("gpsimd")
                def _(c):
                    c.wait(B.ev("T"), B.ev("V"), B.ev("S"))
                    w2part = dram["w_ffn2"][half * (DFF // 2):(half + 1) * (DFF // 2), :]
                    c.dma(w2h[:].rearrange("p (k m) -> p k m", k=NFF // 2),
                          w2part.rearrange("(k p) m -> p k m", p=P), "gA")
                wload2 = B.ev("gA")

                for tc in range(NT):
                    lo = tc * TC

                    @B.sec("sync")
                    def _(c):
                        c.wait(B.ev("T"), B.ev("yS"))
                        c.dma(h1h[:].rearrange("p (k t) -> p k t", k=NFF // 2),
                              dram["h1_sp"][:, half * (NFF // 2):(half + 1) * (NFF // 2),
                                            lo:lo + TC], "yA")
                    e_ld = B.ev("yA")

                    for oc in range(NCH):
                        pmm = (ps_m0, ps_m1, ps_m2)[oc % 3]
                        pk = "3m%d" % (oc % 3)

                        @B.sec("tensor")
                        def _(c):
                            c.wait(e_ld, wload2, B.lastev.get(pk))
                            for k in range(NFF // 2):
                                ins = c.e.matmul(
                                    pmm[:, 0:TC],
                                    w2h[:, k * D + oc * P:k * D + oc * P + P],
                                    h1h[:, k * TC:(k + 1) * TC],
                                    start=(k == 0), stop=(k == NFF // 2 - 1))
                            c.inc(ins)
                        e_mm = B.ev("T")

                        @B.sec("vector")
                        def _(c):
                            c.wait(e_mm)
                            dst = R[:, oc * T + lo:oc * T + lo + TC]
                            if half == 0:
                                B.lastev[pk] = c.chain(
                                    c.e.tensor_tensor(dst, dst, pmm[:, 0:TC],
                                                      ALU.add))
                            else:
                                B.lastev[pk] = c.chain(
                                    c.e.scalar_tensor_tensor(
                                        dst, pmm[:, 0:TC], b2[:, oc:oc + 1], dst,
                                        ALU.add, ALU.add))
    B.barrier()


def phase4_moe(B, nc, l0, dram, env):
    R = env["R"]
    comb_fm = env["comb_fm"]
    sel16_b = env["sel16_b"]
    TC = 512
    NT = T // TC

    st = contextlib.ExitStack()
    with st:
        def sb(name, shape, dtype, stack=None):
            return (stack or st).enter_context(nc.sbuf_tensor(name, shape, dtype))

        def ps(name, shape, dtype=F32):
            return st.enter_context(nc.psum_tensor(name, shape, dtype))

        n0r = sb("n0r", [P, NCH * T], BF16)
        w13b = sb("w13b", [P, NCH * 2 * INTER], BF16)
        w2b = sb("w2b", [P, (INTER // P) * D], BF16)
        hgd = [sb("hg0", [P, 4 * TC], BF16), sb("hg1", [P, 4 * TC], BF16)]
        hud = [sb("hu0", [P, 4 * TC], BF16), sb("hu1", [P, 4 * TC], BF16)]
        cbb = sb("cbb", [P, TC], BF16)
        tmp = sb("tmp4", [P, TC], BF16)
        ymoe = sb("ymoe", [P, NCH * T], BF16)

        ps_s0 = ps("ps4_s0", [P, 512])
        ps_s1 = ps("ps4_s1", [P, 512])
        ps_s2 = ps("ps4_s2", [P, 512])
        ps_t0 = ps("ps4_t0", [P, 512])
        ps_t1 = ps("ps4_t1", [P, 512])
        ps_t2 = ps("ps4_t2", [P, 512])
        ps_cb = ps("ps4_cb", [P, 512])

        @B.sec("sync")
        def _(c):
            c.wait(B.ev("V"), B.ev("T"), B.ev("S"))
            c.dma(n0r[:].rearrange("p (k t) -> p k t", k=NCH),
                  dram["n0_sp"][:, :, :], "yA")
        e_n0 = B.ev("yA")

        for e in range(NEXP + 1):
            shared = (e == NEXP)

            @B.sec("gpsimd")
            def _(c):
                c.wait(B.ev("T"))
                if shared:
                    c.dma(w13b[:].rearrange("p (k m) -> p k m", k=NCH),
                          dram["w13sT"].rearrange("(k p) m -> p k m", p=P), "gA")
                    c.dma(w2b[:].rearrange("p (k m) -> p k m", k=INTER // P),
                          dram["w2sT"].rearrange("(k p) m -> p k m", p=P), "gA")
                else:
                    c.dma(w13b[:].rearrange("p (k m) -> p k m", k=NCH),
                          dram["w13T"][e].rearrange("(k p) m -> p k m", p=P), "gA")
                    c.dma(w2b[:].rearrange("p (k m) -> p k m", k=INTER // P),
                          dram["w2T"][e].rearrange("(k p) m -> p k m", p=P), "gA")
            e_w = B.ev("gA")

            hh_ev = {}

            def emit_front(tc):
                lo = tc * TC
                par = tc % 2
                hg = hgd[par]
                hu = hud[par]

                e_cb = None
                if not shared:
                    @B.sec("tensor")
                    def _(c):
                        c.wait(B.lastev.get("4cbe"))
                        c.inc(c.e.matmul(ps_cb[:, 0:TC],
                                         sel16_b[:, e * P:(e + 1) * P],
                                         comb_fm[:, lo:lo + TC],
                                         start=True, stop=True))
                    e_cbm = B.ev("T")

                    @B.sec("scalar")
                    def _(c):
                        c.wait(e_cbm, B.lastev.get("4cbu"))
                        B.lastev["4cbe"] = c.inc(
                            c.e.activation(cbb[:], ps_cb[:, 0:TC], AF.Identity))
                    e_cb = B.ev("S")

                for og in range(8):
                    pmm = (ps_s0, ps_s1, ps_s2)[og % 3]
                    pk = "4s%d" % (og % 3)

                    @B.sec("tensor")
                    def _(c):
                        c.wait(e_w, e_n0, B.lastev.get(pk))
                        for k in range(NCH):
                            ins = c.e.matmul(
                                pmm[:, 0:TC],
                                w13b[:, k * (2 * INTER) + og * P:
                                     k * (2 * INTER) + og * P + P],
                                n0r[:, k * T + lo:k * T + lo + TC],
                                start=(k == 0), stop=(k == NCH - 1))
                        c.inc(ins)
                    e_s1 = B.ev("T")

                    @B.sec("scalar")
                    def _(c):
                        c.wait(e_s1)
                        if og < 4:
                            B.lastev[pk] = c.inc(
                                c.e.activation(hg[:, og * TC:(og + 1) * TC],
                                               pmm[:, 0:TC], AF.Silu))
                        else:
                            B.lastev[pk] = c.inc(
                                c.e.activation(hu[:, (og - 4) * TC:(og - 3) * TC],
                                               pmm[:, 0:TC], AF.Identity))
                e_guev = B.ev("S")

                @B.sec("vector")
                def _(c):
                    c.wait(e_guev)
                    if not shared:
                        c.wait(e_cb)
                    ev = c.chain(c.e.tensor_tensor(hg[:], hg[:], hu[:], ALU.mult))
                    if not shared:
                        cb3 = cbb[:].unsqueeze(1).broadcast_to([P, 4, TC])
                        ev = c.chain(c.e.tensor_tensor(
                            hg[:].rearrange("p (k t) -> p k t", k=4),
                            hg[:].rearrange("p (k t) -> p k t", k=4),
                            cb3, ALU.mult))
                    B.lastev["4cbu"] = ev
                hh_ev[tc] = B.ev("V")

            def emit_back(tc):
                lo = tc * TC
                hg = hgd[tc % 2]

                for oc in range(NCH):
                    pmm = (ps_t0, ps_t1, ps_t2)[oc % 3]
                    pk = "4t%d" % (oc % 3)

                    @B.sec("tensor")
                    def _(c):
                        c.wait(hh_ev[tc], B.lastev.get(pk))
                        for k in range(INTER // P):
                            ins = c.e.matmul(
                                pmm[:, 0:TC],
                                w2b[:, k * D + oc * P:k * D + oc * P + P],
                                hg[:, k * TC:(k + 1) * TC],
                                start=(k == 0), stop=(k == INTER // P - 1))
                        c.inc(ins)
                    e_s2 = B.ev("T")

                    if e == 0:
                        @B.sec("scalar")
                        def _(c):
                            c.wait(e_s2)
                            B.lastev[pk] = c.inc(c.e.activation(
                                ymoe[:, oc * T + lo:oc * T + lo + TC],
                                pmm[:, 0:TC], AF.Identity))
                    else:
                        @B.sec("scalar")
                        def _(c):
                            c.wait(e_s2, B.lastev.get("4tmp"))
                            B.lastev[pk] = c.inc(
                                c.e.activation(tmp[:], pmm[:, 0:TC], AF.Identity))
                        e_t = B.ev("S")

                        @B.sec("vector")
                        def _(c):
                            c.wait(e_t)
                            dst = ymoe[:, oc * T + lo:oc * T + lo + TC]
                            B.lastev["4tmp"] = c.chain(
                                c.e.tensor_tensor(dst, dst, tmp[:], ALU.add))

            for tc in range(NT + 1):
                if tc < NT:
                    emit_front(tc)
                if tc >= 1:
                    emit_back(tc - 1)

        for tc in range(NT):
            lo = tc * TC

            @B.sec("vector")
            def _(c):
                c.wait(B.ev("S"))
                for oc in range(NCH):
                    dst = R[:, oc * T + lo:oc * T + lo + TC]
                    c.chain(c.e.tensor_tensor(
                        dst, dst, ymoe[:, oc * T + lo:oc * T + lo + TC], ALU.add))
    B.barrier()


def phase5_l1(B, nc, l1, dram, env):
    R = env["R"]
    onescol_b = env["onescol_b"]; onesrow_f = env["onesrow_f"]
    ident_b = env["ident_b"]; tri_f = env["tri_f"]

    TC = 512
    NT = T // TC
    st = contextlib.ExitStack()
    with st:
        def sb(name, shape, dtype, stack=None):
            return (stack or st).enter_context(nc.sbuf_tensor(name, shape, dtype))

        def psn(name, shape, dtype=F32, stack=None):
            return (stack or st).enter_context(nc.psum_tensor(name, shape, dtype))

        sta = contextlib.ExitStack()
        with sta:
            lg = sb("lg5", [P, NCH], F32, sta)
            lb = sb("lb5", [P, NCH], F32, sta)
            xb5 = sb("xb5", [P, NCH * TC], BF16, sta)
            xq5 = sb("xq5", [P, NCH * TC], BF16, sta)
            n1c = sb("n1c", [P, NCH * TC], BF16, sta)
            xnc = sb("xnc", [P, NCH * TC], BF16, sta)
            rows5 = sb("rows5", [1, 3 * TC], F32, sta)
            bc5 = sb("bc5", [P, 3 * TC], F32, sta)
            tf5 = sb("tf5", [P, TC], F32, sta)
            ps_r0 = psn("ps5_r0", [1, 512], F32, sta)
            ps_r1 = psn("ps5_r1", [1, 512], F32, sta)
            ps_b0 = psn("ps5_b0", [P, 512], F32, sta)
            ps_b1 = psn("ps5_b1", [P, 512], F32, sta)

            @B.sec("sync")
            def _(c):
                c.dma(lg[:], dram["l1_g"][:, :], "yM")
                c.dma(lb[:], dram["l1_b"][:, :], "yM")
            sload = B.ev("yM")

            for tc in range(NT):
                lo = tc * TC

                @B.sec("scalar")
                def _(c):
                    c.wait(B.ev("T"))
                    for k in range(NCH):
                        c.e.activation(xb5[:, k * TC:(k + 1) * TC],
                                       R[:, k * T + lo:k * T + lo + TC],
                                       AF.Identity)
                        ins = c.e.activation(xq5[:, k * TC:(k + 1) * TC],
                                             R[:, k * T + lo:k * T + lo + TC],
                                             AF.Square)
                    c.inc(ins)
                e_c = B.ev("S")

                @B.sec("tensor")
                def _(c):
                    c.wait(e_c, B.ev("S"))
                    for k in range(NCH):
                        c.e.matmul(ps_r0[0:1, 0:TC], onescol_b[:],
                                   xb5[:, k * TC:(k + 1) * TC],
                                   start=(k == 0), stop=(k == NCH - 1))
                    for k in range(NCH):
                        ins = c.e.matmul(ps_r1[0:1, 0:TC], onescol_b[:],
                                         xq5[:, k * TC:(k + 1) * TC],
                                         start=(k == 0), stop=(k == NCH - 1))
                    c.inc(ins)
                e_mm = B.ev("T")

                @B.sec("scalar")
                def _(c):
                    c.wait(e_mm)
                    c.e.activation(rows5[:, 0:TC], ps_r0[0:1, 0:TC], AF.Identity)
                    c.chain(c.e.activation(rows5[:, TC:2 * TC], ps_r0[0:1, 0:TC],
                                           AF.Square, scale=1.0 / D))
                e_r = B.ev("S")

                @B.sec("vector")
                def _(c):
                    c.wait(e_r)
                    c.chain(c.e.scalar_tensor_tensor(
                        rows5[:, TC:2 * TC], ps_r1[0:1, 0:TC], 1.0 / D,
                        rows5[:, TC:2 * TC], ALU.mult, ALU.subtract))
                e_v1 = B.ev("V")

                @B.sec("scalar")
                def _(c):
                    c.wait(e_v1)
                    c.chain(c.e.activation(rows5[:, 2 * TC:3 * TC],
                                           rows5[:, TC:2 * TC], AF.Ln, bias=EPS))
                    c.chain(c.e.activation(rows5[:, TC:2 * TC],
                                           rows5[:, 2 * TC:3 * TC], AF.Exp,
                                           scale=-0.5))
                e_s1 = B.ev("S")

                @B.sec("vector")
                def _(c):
                    c.wait(e_s1)
                    c.chain(c.e.tensor_tensor(rows5[:, 2 * TC:3 * TC],
                                              rows5[:, TC:2 * TC],
                                              rows5[:, TC:2 * TC], ALU.mult))
                    c.chain(c.e.tensor_scalar(rows5[:, 2 * TC:3 * TC],
                                              rows5[:, 2 * TC:3 * TC], -EPS, 1.0,
                                              ALU.mult, ALU.add))
                e_v2 = B.ev("V")

                @B.sec("scalar")
                def _(c):
                    c.wait(e_v2)
                    c.chain(c.e.activation(rows5[:, 2 * TC:3 * TC],
                                           rows5[:, 2 * TC:3 * TC], AF.Ln,
                                           bias=EPS))
                    c.chain(c.e.activation(rows5[:, 2 * TC:3 * TC],
                                           rows5[:, 2 * TC:3 * TC], AF.Exp,
                                           scale=-0.5))
                e_s2 = B.ev("S")

                @B.sec("vector")
                def _(c):
                    c.wait(e_s2)
                    c.chain(c.e.tensor_tensor(rows5[:, 2 * TC:3 * TC],
                                              rows5[:, 2 * TC:3 * TC],
                                              rows5[:, TC:2 * TC], ALU.mult))
                e_rows = B.ev("V")

                @B.sec("tensor")
                def _(c):
                    c.wait(e_rows, B.ev("S"))
                    c.e.matmul(ps_b0[:, 0:TC], onesrow_f[:], rows5[:, 0:TC],
                               start=True, stop=True)
                    ins = c.e.matmul(ps_b1[:, 0:TC], onesrow_f[:],
                                     rows5[:, TC:2 * TC], start=True, stop=True)
                    c.inc(ins)
                e_b1 = B.ev("T")

                @B.sec("scalar")
                def _(c):
                    c.wait(e_b1)
                    c.e.activation(bc5[:, 0:TC], ps_b0[:, 0:TC], AF.Identity,
                                   scale=1.0 / D)
                    c.inc(c.e.activation(bc5[:, TC:2 * TC], ps_b1[:, 0:TC],
                                         AF.Identity))
                e_b1e = B.ev("S")

                @B.sec("tensor")
                def _(c):
                    c.wait(e_b1e)
                    c.inc(c.e.matmul(ps_b0[:, 0:TC], onesrow_f[:],
                                     rows5[:, 2 * TC:3 * TC], start=True,
                                     stop=True))
                e_b2 = B.ev("T")

                @B.sec("scalar")
                def _(c):
                    c.wait(e_b2)
                    c.inc(c.e.activation(bc5[:, 2 * TC:3 * TC], ps_b0[:, 0:TC],
                                         AF.Identity))
                e_bc = B.ev("S")

                @B.sec("vector")
                def _(c):
                    c.wait(e_bc, B.ev("T"), B.ev("yS"))
                    for k in range(NCH):
                        c.chain(c.e.tensor_tensor(tf5[:],
                                                  R[:, k * T + lo:k * T + lo + TC],
                                                  bc5[:, 0:TC], ALU.subtract))
                        c.e.tensor_tensor(n1c[:, k * TC:(k + 1) * TC], tf5[:],
                                          bc5[:, TC:2 * TC], ALU.mult)
                        c.chain(c.e.tensor_tensor(xnc[:, k * TC:(k + 1) * TC],
                                                  tf5[:], bc5[:, 2 * TC:3 * TC],
                                                  ALU.mult))
                e_n = B.ev("V")

                @B.sec("scalar")
                def _(c):
                    c.wait(e_n, sload)
                    for k in range(NCH):
                        ins = c.e.activation(xnc[:, k * TC:(k + 1) * TC],
                                             xnc[:, k * TC:(k + 1) * TC],
                                             AF.Identity, scale=lg[:, k:k + 1],
                                             bias=lb[:, k:k + 1])
                    c.inc(ins)
                e_x = B.ev("S")

                @B.sec("sync")
                def _(c):
                    c.wait(e_x, e_n)
                    c.dma(dram["n1_sp"][:, :, lo:lo + TC],
                          n1c[:].rearrange("p (k t) -> p k t", k=NCH), "yS")
                    c.dma(dram["xn_sp"][:, :, lo:lo + TC],
                          xnc[:].rearrange("p (k t) -> p k t", k=NCH), "yS")
            B.barrier()

        stb = contextlib.ExitStack()
        with stb:
            wq = sb("wq5", [P, NCH * D], BF16, stb)
            wk = sb("wk5", [P, NCH * D], BF16, stb)
            wv = sb("wv5", [P, NCH * D], BF16, stb)
            rot_b = sb("rot_b", [P, P], BF16, stb)
            cosq = sb("cosq", [P, T], BF16, stb)
            sinq = sb("sinq", [P, T], BF16, stb)
            cosk = sb("cosk", [P, T], BF16, stb)
            sink = sb("sink", [P, T], BF16, stb)
            xnb = sb("xnb", [P, NCH * TC], BF16, stb)
            qbuf = sb("qbuf", [P, NH * TC], BF16, stb)
            vtb = sb("vtb", [P, NH * TC], BF16, stb)
            qt5 = sb("qt5", [P, TC], BF16, stb)
            rt5 = sb("rt5", [P, TC], BF16, stb)
            m15 = sb("m15", [P, TC], BF16, stb)
            ps_p0 = psn("ps5b_0", [P, 512], F32, stb)
            ps_p1 = psn("ps5b_1", [P, 512], F32, stb)
            ps_p2 = psn("ps5b_2", [P, 512], F32, stb)
            ps_rt = psn("ps5b_rt", [P, 512], F32, stb)
            ps_vt = psn("ps5b_vt", [P, 1024], BF16, stb)

            @B.sec("gpsimd")
            def _(c):
                c.wait(B.ev("T"), B.ev("V"), B.ev("S"))
                c.dma(wq[:].rearrange("p (k m) -> p k m", k=NCH),
                      dram["wqT"].rearrange("(k p) m -> p k m", p=P), "gA")
                c.dma(wk[:].rearrange("p (k m) -> p k m", k=NCH),
                      dram["wkT"].rearrange("(k p) m -> p k m", p=P), "gA")
                c.dma(wv[:].rearrange("p (k m) -> p k m", k=NCH),
                      dram["wvT"].rearrange("(k p) m -> p k m", p=P), "gA")
                c.dma(rot_b[:], dram["rot_sT"][:, :], "gA")
                c.dma(cosq[:], dram["cos_q"][:, :], "gA")
                c.dma(sinq[:], dram["sin_q"][:, :], "gA")
                c.dma(cosk[:], dram["cos_k"][:, :], "gA")
                c.dma(sink[:], dram["sin_k"][:, :], "gA")
            wload = B.ev("gA")

            for tc in range(NT):
                lo = tc * TC

                @B.sec("sync")
                def _(c):
                    c.wait(B.ev("T"), B.ev("yS"))
                    c.dma(xnb[:].rearrange("p (k t) -> p k t", k=NCH),
                          dram["xn_sp"][:, :, lo:lo + TC], "yA")
                e_ld = B.ev("yA")

                for which, wmat, ctab, stab, spname in (
                        ("q", wq, cosq, sinq, "q_sp"),
                        ("k", wk, cosk, sink, "k_sp")):
                    for h in range(NH):
                        pmm = (ps_p0, ps_p1, ps_p2)[h % 3]
                        pk = "5p%d" % (h % 3)

                        @B.sec("tensor")
                        def _(c):
                            c.wait(e_ld, wload, B.lastev.get(pk))
                            for k in range(NCH):
                                ins = c.e.matmul(
                                    pmm[:, 0:TC],
                                    wmat[:, k * D + h * P:k * D + h * P + P],
                                    xnb[:, k * TC:(k + 1) * TC],
                                    start=(k == 0), stop=(k == NCH - 1))
                            c.inc(ins)
                        e_p = B.ev("T")

                        @B.sec("vector")
                        def _(c):
                            c.wait(e_p, B.lastev.get("5qt"))
                            B.lastev[pk] = c.inc(
                                c.e.tensor_copy(qt5[:], pmm[:, 0:TC]))
                        e_qt = B.ev("V")

                        @B.sec("tensor")
                        def _(c):
                            c.wait(e_qt, B.lastev.get("5rt"))
                            c.inc(c.e.matmul(ps_rt[:, 0:TC], rot_b[:], qt5[:],
                                             start=True, stop=True))
                        e_rt = B.ev("T")

                        @B.sec("vector")
                        def _(c):
                            c.wait(e_rt, B.lastev.get("5rte"))
                            B.lastev["5rt"] = c.inc(
                                c.e.tensor_copy(rt5[:], ps_rt[:, 0:TC]))
                        e_rte = B.ev("V")

                        @B.sec("vector")
                        def _(c):
                            c.wait(e_rte, B.ev("yS"))
                            ev = c.chain(c.e.tensor_tensor(m15[:], qt5[:],
                                                           ctab[:, lo:lo + TC],
                                                           ALU.mult))
                            B.lastev["5qt"] = ev
                            c.chain(c.e.tensor_tensor(rt5[:], rt5[:],
                                                      stab[:, lo:lo + TC],
                                                      ALU.mult))
                            B.lastev["5rte"] = c.chain(c.e.tensor_tensor(
                                qbuf[:, h * TC:(h + 1) * TC], m15[:], rt5[:],
                                ALU.add))
                    e_rope = B.ev("V")

                    @B.sec("sync")
                    def _(c):
                        c.wait(e_rope)
                        c.dma(dram[spname][:, :, lo:lo + TC],
                              qbuf[:].rearrange("p (h t) -> p h t", h=NH), "yS")

                for h in range(NH):
                    pmm = (ps_p0, ps_p1, ps_p2)[h % 3]
                    pk = "5p%d" % (h % 3)

                    @B.sec("tensor")
                    def _(c):
                        c.wait(e_ld, wload, B.lastev.get(pk))
                        for k in range(NCH):
                            ins = c.e.matmul(
                                pmm[:, 0:TC],
                                wv[:, k * D + h * P:k * D + h * P + P],
                                xnb[:, k * TC:(k + 1) * TC],
                                start=(k == 0), stop=(k == NCH - 1))
                        c.inc(ins)
                    e_p = B.ev("T")

                    @B.sec("vector")
                    def _(c):
                        c.wait(e_p, B.ev("T"), B.lastev.get("5qt"))
                        B.lastev[pk] = c.inc(
                            c.e.tensor_copy(qt5[:], pmm[:, 0:TC]))
                    e_v = B.ev("V")

                    for j in range(TC // P):
                        @B.sec("tensor")
                        def _(c):
                            c.wait(e_v, B.lastev.get("5vt"))
                            c.inc(c.e.transpose(ps_vt[:, 0:P],
                                                qt5[:, j * P:(j + 1) * P],
                                                ident_b[:]))
                        e_t = B.ev("T")

                        @B.sec("vector")
                        def _(c):
                            c.wait(e_t, B.ev("yS"))
                            B.lastev["5vt"] = c.inc(c.e.tensor_copy(
                                vtb[:, h * TC + j * P:h * TC + (j + 1) * P],
                                ps_vt[:, 0:P]))
                e_vt = B.ev("V")

                @B.sec("sync")
                def _(c):
                    c.wait(e_vt, B.ev("S"))
                    c.dma(dram["vt_sp"][:, :, tc * (TC // P):(tc + 1) * (TC // P), :],
                          vtb[:].rearrange("p (h j q) -> p h j q", h=NH, j=TC // P),
                          "yS")
            B.barrier()

        stc = contextlib.ExitStack()
        with stc:
            dirs_b = sb("dirs_b", [P, NH], BF16, stc)
            hs_all = sb("hs_all", [P, NH], F32, stc)
            om_all = sb("om_all", [P, NH], F32, stc)
            hsr = sb("hsr", [1, NH], F32, stc)
            aout = sb("aout", [P, NCH * T], BF16, stc)
            stc2 = stc.enter_context(contextlib.ExitStack())
            qh = sb("qh", [P, T], BF16, stc2)
            kh = sb("kh", [P, T], BF16, stc2)
            vth = sb("vth", [P, (T // P) * P], BF16, stc2)
            qp_b = sb("qp_b", [1, T], BF16, stc2)
            kp_b = sb("kp_b", [1, T], BF16, stc2)
            attn = sb("attn", [P, T], BF16, stc2)
            scd = [sb("sc0", [P, T], BF16, stc2), sb("sc1", [P, T], BF16, stc2)]
            gsd = [sb("gs0", [P, T], BF16, stc2), sb("gs1", [P, T], BF16, stc2)]
            at0 = sb("at0", [P, 4 * P], BF16, stc2)
            at1 = sb("at1", [P, 4 * P], BF16, stc2)
            nmax = sb("nmax", [P, 1], F32, stc2)
            sums = sb("sums", [P, 1], F32, stc2)
            rsum = sb("rsum", [P, 1], F32, stc2)
            cmb5 = sb("cmb5", [P, P], F32, stc2)
            ps_sc2 = [psn("ps5c_sca", [P, 512], F32, stc2),
                      psn("ps5c_scb", [P, 512], F32, stc2)]
            ps_t0 = psn("ps5c_t0", [P, 1024], BF16, stc2)
            ps_t1 = psn("ps5c_t1", [P, 1024], BF16, stc2)
            ps_av = psn("ps5c_av", [P, 512], F32, stc2)
            ps_ag = psn("ps5c_ag", [P, 512], F32, stc2)

            @B.sec("gpsimd")
            def _(c):
                c.wait(B.ev("T"), B.ev("V"), B.ev("S"))
                c.dma(dirs_b[:], dram["dirs"][:, :], "gA")
                c.dma(hsr[:], dram["hscale"][:, :], "gA")
            e_dl = B.ev("gA")

            @B.sec("tensor")
            def _(c):
                c.wait(e_dl, B.ev("S"), B.ev("V"))
                c.inc(c.e.matmul(ps_av[0:P, 0:NH], onesrow_f[:], hsr[:, :],
                                 start=True, stop=True))
            e_hsm = B.ev("T")

            @B.sec("scalar")
            def _(c):
                c.wait(e_hsm)
                c.inc(c.e.activation(hs_all[:], ps_av[0:P, 0:NH], AF.Identity))
            e_hs = B.ev("S")

            @B.sec("vector")
            def _(c):
                c.wait(e_hs)
                c.chain(c.e.tensor_scalar(om_all[:], hs_all[:], -1.0, 1.0,
                                          ALU.mult, ALU.add))
            e_om = B.ev("V")

            for h in range(NH):
                @B.sec("sync")
                def _(c):
                    c.wait(B.ev("T"), B.ev("V"), B.ev("S"), B.ev("yS"))
                    c.dma(qh[:], dram["q_sp"][:, h, :], "yA")
                    c.dma(kh[:], dram["k_sp"][:, h, :], "yA")
                    c.dma(vth[:].rearrange("p (j q) -> p j q", j=T // P),
                          dram["vt_sp"][:, h, :, :], "yA")
                e_hl = B.ev("yA")

                for j in range(T // 512):
                    @B.sec("tensor")
                    def _(c):
                        c.wait(e_hl, e_om, B.ev("S"))
                        c.e.matmul(ps_av[0:1, 0:512], dirs_b[:, h:h + 1],
                                   qh[:, j * 512:(j + 1) * 512],
                                   start=True, stop=True)
                        c.inc(c.e.matmul(ps_ag[0:1, 0:512], dirs_b[:, h:h + 1],
                                         kh[:, j * 512:(j + 1) * 512],
                                         start=True, stop=True))
                    e_qpm = B.ev("T")

                    @B.sec("scalar")
                    def _(c):
                        c.wait(e_qpm)
                        c.e.activation(qp_b[:, j * 512:(j + 1) * 512],
                                       ps_av[0:1, 0:512], AF.Identity,
                                       scale=float(np.sqrt(HD)))
                        c.inc(c.e.activation(kp_b[:, j * 512:(j + 1) * 512],
                                             ps_ag[0:1, 0:512], AF.Identity))
                e_qp = B.ev("S")

                def scores_stage(qt):
                    par = qt % 2
                    kmax = (qt + 1) * P
                    nch_k = (kmax + 511) // 512
                    for which, lhs, dstb in (("m", qh, scd[par]),
                                             ("g", qp_b, gsd[par])):
                        for j in range(nch_k):
                            ke = min(kmax, (j + 1) * 512)
                            pb = ps_sc2[j % 2]
                            pkey = "5sb%d" % (j % 2)

                            @B.sec("tensor")
                            def _(c):
                                c.wait(e_hl, e_qp, B.lastev.get(pkey),
                                       B.lastev.get("5msk%s%d" % (which, par)))
                                if which == "m":
                                    ins = c.e.matmul(pb[:, 0:ke - j * 512],
                                                     qh[:, qt * P:(qt + 1) * P],
                                                     kh[:, j * 512:ke],
                                                     start=True, stop=True)
                                else:
                                    ins = c.e.matmul(pb[:, 0:ke - j * 512],
                                                     qp_b[:, qt * P:(qt + 1) * P],
                                                     kp_b[:, j * 512:ke],
                                                     start=True, stop=True)
                                c.inc(ins)
                            e_mm = B.ev("T")

                            @B.sec("scalar")
                            def _(c):
                                c.wait(e_mm)
                                B.lastev[pkey] = c.inc(c.e.activation(
                                    dstb[:, j * 512:ke], pb[:, 0:ke - j * 512],
                                    AF.Identity))
                    return B.ev("S")

                def softmax_av_stage(qt, e_sc):
                    par = qt % 2
                    kmax = (qt + 1) * P
                    e_avm = None
                    e_agm = None
                    for which, srcb, ps_out in (("m", scd[par], ps_av),
                                                ("g", gsd[par], ps_ag)):
                        @B.sec("vector")
                        def _(c):
                            c.wait(e_sc, B.lastev.get("5at"), B.ev("S"))
                            c.chain(c.e.tensor_tensor(
                                srcb[:, qt * P:(qt + 1) * P],
                                srcb[:, qt * P:(qt + 1) * P], tri_f[:],
                                ALU.add))
                            ev = c.chain(c.e.reduce_max(nmax[:], srcb[:, 0:kmax],
                                                        axis=AX.X, negate=True))
                            B.lastev["5msk%s%d" % (which, par)] = ev
                        e_mx = B.ev("V")

                        @B.sec("scalar")
                        def _(c):
                            c.wait(e_mx, B.ev("T"))
                            c.chain(c.e.activation(attn[:, 0:kmax],
                                                   srcb[:, 0:kmax], AF.Exp,
                                                   bias=nmax[:, 0:1],
                                                   accum_out=sums[:]))
                        e_ex = B.ev("S")

                        @B.sec("vector")
                        def _(c):
                            c.wait(e_ex)
                            c.chain(c.e.reciprocal(rsum[:], sums[:]))
                            c.chain(c.e.tensor_scalar(attn[:, 0:kmax],
                                                      attn[:, 0:kmax],
                                                      rsum[:, 0:1], None,
                                                      ALU.mult))
                        e_at = B.ev("V")
                        B.lastev["5at"] = e_at

                        groups = [(g, min(4, qt + 1 - g))
                                  for g in range(0, qt + 1, 4)]

                        def emit_tr(gi):
                            g, cnt = groups[gi]
                            pst = ps_t0 if (gi % 2 == 0) else ps_t1
                            tk = "5t%d" % (gi % 2)

                            @B.sec("tensor")
                            def _(c):
                                c.wait(e_at, B.lastev.get(tk))
                                for i in range(cnt):
                                    c.inc(c.e.transpose(
                                        pst[:, i * P:(i + 1) * P],
                                        attn[:, (g + i) * P:(g + i + 1) * P],
                                        ident_b[:]))
                            e_tr = B.ev("T")

                            @B.sec("vector")
                            def _(c):
                                c.wait(e_tr)
                                B.lastev[tk] = c.inc(c.e.tensor_copy(
                                    (at0 if gi % 2 == 0 else at1)[:, 0:cnt * P],
                                    pst[:, 0:cnt * P]))
                            return B.ev("V")

                        def emit_av(gi, e_te):
                            g, cnt = groups[gi]
                            atb = at0 if (gi % 2 == 0) else at1

                            @B.sec("tensor")
                            def _(c):
                                c.wait(e_te)
                                for i in range(cnt):
                                    c.inc(c.e.matmul(
                                        ps_out[:, 0:P],
                                        vth[:, (g + i) * P:(g + i + 1) * P],
                                        atb[:, i * P:(i + 1) * P],
                                        start=(g + i == 0),
                                        stop=(g + i == qt)))

                        pg = None
                        for gi in range(len(groups)):
                            e_te = emit_tr(gi)
                            if pg is not None:
                                emit_av(pg[0], pg[1])
                            pg = (gi, e_te)
                        emit_av(pg[0], pg[1])
                        if which == "m":
                            e_avm = B.ev("T")
                        else:
                            e_agm = B.ev("T")

                    @B.sec("scalar")
                    def _(c):
                        c.wait(e_agm, B.lastev.get("5cmb"))
                        c.inc(c.e.activation(cmb5[:], ps_ag[:, 0:P], AF.Identity,
                                             scale=hs_all[:, h:h + 1]))
                    e_cg = B.ev("S")

                    @B.sec("vector")
                    def _(c):
                        c.wait(e_cg, e_avm)
                        B.lastev["5cmb"] = c.chain(c.e.scalar_tensor_tensor(
                            aout[:, h * T + qt * P:h * T + (qt + 1) * P],
                            ps_av[:, 0:P], om_all[:, h:h + 1], cmb5[:],
                            ALU.mult, ALU.add))

                prev = None
                for qt in range(T // P):
                    e_sc = scores_stage(qt)
                    if prev is not None:
                        softmax_av_stage(prev[0], prev[1])
                    prev = (qt, e_sc)
                softmax_av_stage(prev[0], prev[1])
            B.barrier()
            stc2.close()

            std = contextlib.ExitStack()
            with std:
                wo = sb("wo5", [P, NCH * D], BF16, std)
                n1b5 = sb("n1b5", [P, NCH * TC], BF16, std)
                tfo = sb("tfo", [P, TC], F32, std)
                ps_d0 = psn("ps5d_0", [P, 512], F32, std)
                ps_d1 = psn("ps5d_1", [P, 512], F32, std)

                @B.sec("gpsimd")
                def _(c):
                    c.dma(wo[:].rearrange("p (k m) -> p k m", k=NCH),
                          dram["woT"].rearrange("(k p) m -> p k m", p=P), "gA")
                wlo = B.ev("gA")

                for tc in range(NT):
                    lo = tc * TC

                    @B.sec("sync")
                    def _(c):
                        c.wait(B.ev("V"))
                        c.dma(n1b5[:].rearrange("p (k t) -> p k t", k=NCH),
                              dram["n1_sp"][:, :, lo:lo + TC], "yA")
                    e_ld = B.ev("yA")

                    for oc in range(NCH):
                        pmm = ps_d0 if (oc % 2 == 0) else ps_d1

                        @B.sec("tensor")
                        def _(c):
                            c.wait(wlo, B.ev("V"))
                            for k in range(NCH):
                                ins = c.e.matmul(
                                    pmm[:, 0:TC],
                                    wo[:, k * D + oc * P:k * D + oc * P + P],
                                    aout[:, k * T + lo:k * T + lo + TC],
                                    start=(k == 0), stop=(k == NCH - 1))
                            c.inc(ins)
                        e_mm = B.ev("T")

                        @B.sec("vector")
                        def _(c):
                            c.wait(e_mm, e_ld)
                            c.chain(c.e.tensor_tensor(
                                tfo[:], pmm[:, 0:TC],
                                n1b5[:, oc * TC:(oc + 1) * TC], ALU.add))
                            c.chain(c.e.tensor_tensor(
                                R[:, oc * T + lo:oc * T + lo + TC],
                                R[:, oc * T + lo:oc * T + lo + TC],
                                tfo[:], ALU.add))
            B.barrier()


        ste = contextlib.ExitStack()
        with ste:
            wr = sb("wr5", [P, NCH * D], BF16, ste)
            wk2 = sb("wk25", [P, NCH * D], BF16, ste)
            wv2 = sb("wv25", [P, NCH * D], BF16, ste)
            mur = sb("mur5", [P, NCH], F32, ste)
            muk = sb("muk5", [P, NCH], F32, ste)
            n1x = sb("n1x", [P, NCH * (TC + 1)], BF16, ste)
            xr = sb("xr5", [P, NCH * TC], BF16, ste)
            xk = sb("xk5", [P, NCH * TC], BF16, ste)
            rb5 = sb("rb5", [P, NCH * TC], BF16, ste)
            kb5 = sb("kb5", [P, NCH * TC], BF16, ste)
            td5 = sb("td5", [P, TC], BF16, ste)
            tf6 = sb("tf6", [P, TC], F32, ste)
            ps_e0 = psn("ps5e_0", [P, 512], F32, ste)
            ps_e1 = psn("ps5e_1", [P, 512], F32, ste)
            ps_e2 = psn("ps5e_2", [P, 512], F32, ste)

            @B.sec("gpsimd")
            def _(c):
                c.wait(B.ev("T"), B.ev("V"), B.ev("S"))
                c.dma(wr[:].rearrange("p (k m) -> p k m", k=NCH),
                      dram["wrT"].rearrange("(k p) m -> p k m", p=P), "gA")
                c.dma(wk2[:].rearrange("p (k m) -> p k m", k=NCH),
                      dram["wkT2"].rearrange("(k p) m -> p k m", p=P), "gA")
                c.dma(wv2[:].rearrange("p (k m) -> p k m", k=NCH),
                      dram["wvT2"].rearrange("(k p) m -> p k m", p=P), "gA")
            wle = B.ev("gA")

            @B.sec("sync")
            def _(c):
                c.dma(mur[:], dram["mu_r"][:, :], "yM")
                c.dma(muk[:], dram["mu_k"][:, :], "yM")
            sle = B.ev("yM")

            for tc in range(NT):
                lo = tc * TC

                @B.sec("vector")
                def _(c):
                    c.wait(B.ev("S"), B.ev("T"))
                    if tc == 0:
                        for k in range(NCH):
                            c.inc(c.e.memset(n1x[:, k * (TC + 1):k * (TC + 1) + 1],
                                             0.0))
                e_z = B.ev("V")

                @B.sec("sync")
                def _(c):
                    c.wait(e_z, B.ev("T"))
                    if tc == 0:
                        c.dma(n1x[:].rearrange("p (k t) -> p k t", k=NCH)
                              [:, :, 1:TC + 1],
                              dram["n1_sp"][:, :, 0:TC], "yA")
                    else:
                        c.dma(n1x[:].rearrange("p (k t) -> p k t", k=NCH),
                              dram["n1_sp"][:, :, lo - 1:lo + TC], "yA")
                e_ld = B.ev("yA")

                @B.sec("vector")
                def _(c):
                    c.wait(e_ld, sle, B.ev("T"))
                    for k in range(NCH):
                        cur = n1x[:, k * (TC + 1) + 1:(k + 1) * (TC + 1)]
                        prv = n1x[:, k * (TC + 1):(k + 1) * (TC + 1) - 1]
                        c.chain(c.e.tensor_tensor(td5[:], cur, prv, ALU.subtract))
                        c.chain(c.e.scalar_tensor_tensor(
                            xr[:, k * TC:(k + 1) * TC], td5[:], mur[:, k:k + 1],
                            prv, ALU.mult, ALU.add))
                        c.chain(c.e.scalar_tensor_tensor(
                            xk[:, k * TC:(k + 1) * TC], td5[:], muk[:, k:k + 1],
                            prv, ALU.mult, ALU.add))
                e_mix = B.ev("V")

                for which, wmat, dst, act in (("r", wr, rb5, AF.Sigmoid),
                                              ("k", wk2, kb5, AF.Relu)):
                    for oc in range(NCH):
                        pmm = (ps_e0, ps_e1, ps_e2)[oc % 3]
                        pk = "5e%d" % (oc % 3)
                        src = xr if which == "r" else xk

                        @B.sec("tensor")
                        def _(c):
                            c.wait(e_mix, wle, B.lastev.get(pk))
                            for k in range(NCH):
                                ins = c.e.matmul(
                                    pmm[:, 0:TC],
                                    wmat[:, k * D + oc * P:k * D + oc * P + P],
                                    src[:, k * TC:(k + 1) * TC],
                                    start=(k == 0), stop=(k == NCH - 1))
                            c.inc(ins)
                        e_mm = B.ev("T")

                        @B.sec("scalar")
                        def _(c):
                            c.wait(e_mm)
                            ev = c.chain(c.e.activation(
                                dst[:, oc * TC:(oc + 1) * TC], pmm[:, 0:TC], act))
                            if which == "k":
                                ev = c.chain(c.e.activation(
                                    dst[:, oc * TC:(oc + 1) * TC],
                                    dst[:, oc * TC:(oc + 1) * TC], AF.Square))
                            B.lastev[pk] = ev
                e_rk = B.ev("S")

                for oc in range(NCH):
                    pmm = (ps_e0, ps_e1, ps_e2)[oc % 3]
                    pk = "5e%d" % (oc % 3)

                    @B.sec("tensor")
                    def _(c):
                        c.wait(e_rk, B.lastev.get(pk))
                        for k in range(NCH):
                            ins = c.e.matmul(
                                pmm[:, 0:TC],
                                wv2[:, k * D + oc * P:k * D + oc * P + P],
                                kb5[:, k * TC:(k + 1) * TC],
                                start=(k == 0), stop=(k == NCH - 1))
                        c.inc(ins)
                    e_mm = B.ev("T")

                    @B.sec("vector")
                    def _(c):
                        c.wait(e_mm)
                        B.lastev[pk] = c.chain(c.e.tensor_tensor(
                            tf6[:], pmm[:, 0:TC], rb5[:, oc * TC:(oc + 1) * TC],
                            ALU.mult))
                        c.chain(c.e.tensor_tensor(
                            R[:, oc * T + lo:oc * T + lo + TC],
                            R[:, oc * T + lo:oc * T + lo + TC], tf6[:], ALU.add))
    B.barrier()



def _fm(v, nch):
    return np.ascontiguousarray(v.reshape(nch, P).T.astype(np.float32))


def stage_inputs(inputs):
    i = {k: np.asarray(v) for k, v in inputs.items()}
    sh = {}
    sh["w_inproj"] = np.ascontiguousarray(i["L0_attn_in_proj_w"], np.float32)
    cw = i["L0_attn_conv_w"].reshape(DIN, 4)
    sh["conv_w"] = np.ascontiguousarray(
        cw.reshape(NIN, P, 4).transpose(1, 0, 2).reshape(P, NIN * 4), np.float32)
    sh["conv_b"] = _fm(i["L0_attn_conv_b"], NIN)
    xp = np.zeros((DIN, 112), np.float32)
    xp[:, 0:80] = i["L0_attn_x_proj_w"][:, 0:80]
    xp[:, 96:112] = i["L0_attn_x_proj_w"][:, 80:96]
    sh["w_xproj"] = xp
    sh["w_dtproj"] = np.ascontiguousarray(i["L0_attn_dt_proj_w"], np.float32)
    sh["dt_bias"] = _fm(i["L0_attn_dt_proj_b"], NIN)
    al = i["L0_attn_A_log"]
    sh["a_log"] = np.ascontiguousarray(
        al.reshape(NIN, P, DST).transpose(1, 0, 2).reshape(P, NIN * DST), np.float32)
    sh["d_vec"] = _fm(i["L0_attn_D"], NIN)
    sh["w_oproj"] = np.ascontiguousarray(i["L0_attn_out_proj_w"], np.float32)
    sh["ln1_g"] = _fm(i["L0_attn_ln1_g"], NCH)
    sh["ln1_b"] = _fm(i["L0_attn_ln1_b"], NCH)
    sh["ln2_g"] = _fm(i["L0_attn_ln2_g"], NCH)
    sh["ln2_b"] = _fm(i["L0_attn_ln2_b"], NCH)
    sh["w_ffn1"] = np.ascontiguousarray(i["L0_attn_ffn_w1"], np.float32)
    sh["ffn_b1"] = _fm(i["L0_attn_ffn_b1"], NFF)
    sh["w_ffn2"] = np.ascontiguousarray(i["L0_attn_ffn_w2"], np.float32)
    sh["ffn_b2"] = _fm(i["L0_attn_ffn_b2"], NCH)
    sh["gate_wT"] = np.ascontiguousarray(i["L0_ffn_gate_w"].T, np.float32)
    sh["gate_b"] = np.ascontiguousarray(i["L0_ffn_gate_b"].reshape(NEXP, 1),
                                        np.float32)
    sh["w13T"] = np.ascontiguousarray(i["L0_ffn_w13"].transpose(0, 2, 1), np.float32)
    sh["w2T"] = np.ascontiguousarray(i["L0_ffn_w2"].transpose(0, 2, 1), np.float32)
    sh["w13sT"] = np.ascontiguousarray(i["L0_ffn_w13_s"].T, np.float32)
    sh["w2sT"] = np.ascontiguousarray(i["L0_ffn_w2_s"].T, np.float32)
    sh["wqT"] = np.ascontiguousarray(i["L1_attn_Wq"].T, np.float32)
    sh["wkT"] = np.ascontiguousarray(i["L1_attn_Wk"].T, np.float32)
    sh["wvT"] = np.ascontiguousarray(i["L1_attn_Wv"].T, np.float32)
    sh["woT"] = np.ascontiguousarray(i["L1_attn_Wo"].T, np.float32)
    sh["l1_g"] = _fm(i["L1_attn_ln_g"], NCH)
    sh["l1_b"] = _fm(i["L1_attn_ln_b"], NCH)
    sh["hscale"] = np.ascontiguousarray(i["L1_attn_head_scales"].reshape(1, NH),
                                        np.float32)
    dirs = np.zeros((P, NH), np.float32)
    dirs[0:3, :] = i["L1_attn_head_dirs"].T
    sh["dirs"] = dirs
    sh["wrT"] = np.ascontiguousarray(i["L1_ffn_W_r"].T, np.float32)
    sh["wkT2"] = np.ascontiguousarray(i["L1_ffn_W_k"].T, np.float32)
    sh["wvT2"] = np.ascontiguousarray(i["L1_ffn_W_v"].T, np.float32)
    sh["mu_r"] = _fm(i["L1_ffn_mu_r"], NCH)
    sh["mu_k"] = _fm(i["L1_ffn_mu_k"], NCH)
    sh["identf"] = np.eye(P, dtype=np.float32)
    sh["ones_col"] = np.ones((P, 1), np.float32)
    sh["ones_row"] = np.ones((1, P), np.float32)
    tri = np.zeros((P, P), np.float32)
    tri[np.triu_indices(P, 1)] = NEG
    sh["trimask"] = tri
    rot = np.zeros((P, P), np.float32)
    for ii in range(64):
        rot[ii + 64, ii] = -1.0
        rot[ii, ii + 64] = 1.0
    sh["rot_sT"] = rot
    inv = 1.0 / (10000.0 ** (np.arange(0, HD, 2, dtype=np.float64) / HD))
    freqs = np.outer(np.arange(T, dtype=np.float64), inv)
    cos = np.concatenate([np.cos(freqs)] * 2, 1).T
    sin = np.concatenate([np.sin(freqs)] * 2, 1).T
    sc = 1.0 / np.sqrt(HD)
    sh["cos_q"] = np.ascontiguousarray(cos * sc, np.float32)
    sh["sin_q"] = np.ascontiguousarray(sin * sc, np.float32)
    sh["cos_k"] = np.ascontiguousarray(cos, np.float32)
    sh["sin_k"] = np.ascontiguousarray(sin, np.float32)
    sh["cpool"] = np.ascontiguousarray(
        np.broadcast_to(np.asarray(CONSTS, np.float32), (P, len(CONSTS))))
    s16 = np.zeros((NEXP, NEXP * P), np.float32)
    for n in range(NEXP):
        s16[n, n * P:(n + 1) * P] = 1.0
    sh["sel16"] = s16

    x = np.asarray(inputs["L0_attn_x"])
    per_core = [np.ascontiguousarray(x[c].T, np.float32) for c in range(8)]
    return sh, per_core


def kernel(**inputs):
    sh, per_core = stage_inputs(inputs)
    nc = build_program()
    in_maps = [dict(sh, x0=per_core[c]) for c in range(8)]
    res = run_bass_kernel_spmd(nc, in_maps, core_ids=list(range(8)))
    out = np.stack([np.asarray(res.results[c]["out"]).T for c in range(8)])
    return np.ascontiguousarray(out, np.float32)


# version 19
# speedup vs baseline: 1.0464x; 1.0338x over previous
import contextlib
import numpy as np
import concourse.bass as bass
import concourse.mybir as mybir
from concourse.bass_utils import run_bass_kernel_spmd

F32 = mybir.dt.float32
BF16 = mybir.dt.bfloat16
AF = mybir.ActivationFunctionType
ALU = mybir.AluOpType
AX = mybir.AxisListType

P = 128
T = 2048
D = 1024
NCH = D // P
DIN = 2048
NIN = DIN // P
DST = 16
DTR = 64
DFF = 4096
NFF = DFF // P
NEXP = 16
INTER = 512
NH = 8
HD = 128
EPS = 1e-5
NEG = -30000.0
CONSTS = [1.0 / D, -0.5, -1.0, EPS, 1.0, 1.0 / P, 0.5, float(np.sqrt(HD))]

DEBUG_TAGS = set()
N_PHASES = 99



class _FakeInstr:
    __slots__ = ("incs",)

    def __init__(self, incs):
        self.incs = incs


class _FakeEng:
    __slots__ = ("rec",)

    def __init__(self, rec):
        self.rec = rec

    def __getattr__(self, name):
        rec = self.rec

        def call(*a, **k):
            incs = []
            rec.append((name, a, k, incs))
            return _FakeInstr(incs)
        return call


class Sec:

    def __init__(self, B, engname):
        self.B, self.engname = B, engname
        self.e = _FakeEng(B.streams[engname])

    def wait(self, *evs):
        for ev in evs:
            if ev is None:
                continue
            s, v = ev
            if v <= 0:
                continue
            key = (self.engname, s)
            if self.B.waited.get(key, 0) < v:
                self.e.wait_ge(self.B.sem(s), v)
                self.B.waited[key] = v

    def inc(self, instr, semname=None, amt=1):
        s = semname or self.B.engsem[self.engname]
        self.B.cnt[s] = self.B.cnt.get(s, 0) + amt
        instr.incs.append((self.B.sem(s), amt))
        return (s, self.B.cnt[s])

    def dma(self, out, in_, stream):
        ins = self.e.dma_start(out=out, in_=in_)
        return self.inc(ins, semname=stream, amt=16)

    def chain(self, instr):
        ev = self.inc(instr)
        self.wait(ev)
        return ev


class Bld:
    def __init__(self, nc, es):
        self.nc, self.es = nc, es
        self.block = es.enter_context(nc.Block())
        self.sems = {}
        self.cnt = {}
        self.waited = {}
        self.engsem = {"tensor": "T", "vector": "V", "scalar": "S",
                       "gpsimd": "G", "sync": "Y"}
        self.streams = {k: [] for k in self.engsem}
        self.lastev = {}

    def sem(self, name):
        if name not in self.sems:
            self.sems[name] = self.es.enter_context(self.nc.semaphore(name))
            self.cnt.setdefault(name, 0)
        return self.sems[name]

    def ev(self, name):
        return (name, self.cnt.get(name, 0))

    def sec(self, engname):
        def deco(fn):
            fn(Sec(self, engname))
        return deco

    def barrier(self):
        evs = [(s, c) for s, c in self.cnt.items() if c > 0]
        for engname in self.engsem:
            @self.sec(engname)
            def _(c):
                c.wait(*evs)

    def flush(self):
        for engname, rec in self.streams.items():
            if not rec:
                continue

            def run(e, rec=list(rec)):
                for name, a, k, incs in rec:
                    instr = getattr(e, name)(*a, **k)
                    for sem, amt in incs:
                        instr.then_inc(sem, amt)
            getattr(self.block, engname)(run)
            rec.clear()


def ceil_div(a, b):
    return -(-a) // b



def build_program():
    nc = bass.Bass("TRN2", target_bir_lowering=False)
    dram = {}

    def din(name, shape, dtype=F32):
        dram[name] = nc.dram_tensor(name, shape, dtype, kind="ExternalInput").ap()
        return dram[name]

    def dscr(name, shape, dtype=BF16):
        kind = "ExternalOutput" if name in DEBUG_TAGS else "Internal"
        dram[name] = nc.dram_tensor(name, shape, dtype, kind=kind).ap()
        return dram[name]

    x0 = din("x0", [D, T])
    w_inproj = din("w_inproj", [D, 2 * DIN])
    conv_w = din("conv_w", [P, NIN * 4])
    conv_b = din("conv_b", [P, NIN])
    w_xproj = din("w_xproj", [DIN, 112])
    w_dtproj = din("w_dtproj", [DTR, DIN])
    dt_bias = din("dt_bias", [P, NIN])
    a_log = din("a_log", [P, NIN * DST])
    d_vec = din("d_vec", [P, NIN])
    w_oproj = din("w_oproj", [DIN, D])
    ln1_g = din("ln1_g", [P, NCH]); ln1_b = din("ln1_b", [P, NCH])
    ln2_g = din("ln2_g", [P, NCH]); ln2_b = din("ln2_b", [P, NCH])
    w_ffn1 = din("w_ffn1", [D, DFF]); ffn_b1 = din("ffn_b1", [P, NFF])
    w_ffn2 = din("w_ffn2", [DFF, D]); ffn_b2 = din("ffn_b2", [P, NCH])
    gate_wT = din("gate_wT", [D, NEXP]); gate_b = din("gate_b", [NEXP, 1])
    w13T = din("w13T", [NEXP, D, 2 * INTER])
    w2T = din("w2T", [NEXP, INTER, D])
    w13sT = din("w13sT", [D, 2 * INTER])
    w2sT = din("w2sT", [INTER, D])
    wqT = din("wqT", [D, D]); wkT = din("wkT", [D, D])
    wvT = din("wvT", [D, D]); woT = din("woT", [D, D])
    l1_g = din("l1_g", [P, NCH]); l1_b = din("l1_b", [P, NCH])
    hscale = din("hscale", [1, NH])
    dirs = din("dirs", [P, NH])
    wrT = din("wrT", [D, D]); wkT2 = din("wkT2", [D, D]); wvT2 = din("wvT2", [D, D])
    mu_r = din("mu_r", [P, NCH]); mu_k = din("mu_k", [P, NCH])
    identf = din("identf", [P, P])
    ones_col = din("ones_col", [P, 1])
    ones_row = din("ones_row", [1, P])
    trimask = din("trimask", [P, P])
    rot_sT = din("rot_sT", [P, P])
    cos_q = din("cos_q", [P, T]); sin_q = din("sin_q", [P, T])
    cos_k = din("cos_k", [P, T]); sin_k = din("sin_k", [P, T])
    cpool = din("cpool", [P, len(CONSTS)])
    sel16 = din("sel16", [NEXP, NEXP * P])

    out = nc.dram_tensor("out", [D, T], F32, kind="ExternalOutput").ap()
    dram["out"] = out

    n0_sp = dscr("n0_sp", [P, NCH, T])
    xc_sp = dscr("xc_sp", [P, NIN, T])
    dt_sp = dscr("dt_sp", [P, NIN, T])
    sz_sp = dscr("sz_sp", [P, NIN, T])
    u1_sp = dscr("u1_sp", [P, NCH, T])
    h1_sp = dscr("h1_sp", [P, NFF, T])
    n1_sp = dscr("n1_sp", [P, NCH, T])
    xn_sp = dscr("xn_sp", [P, NCH, T])
    q_sp = dscr("q_sp", [P, NH, T])
    k_sp = dscr("k_sp", [P, NH, T])
    vt_sp = dscr("vt_sp", [P, NH, T // P, P])

    es = contextlib.ExitStack()
    with es:
        B = Bld(nc, es)

        def sb(name, shape, dtype, stack=es):
            return stack.enter_context(nc.sbuf_tensor(name, shape, dtype))

        def ps(name, shape, dtype=F32, stack=es):
            return stack.enter_context(nc.psum_tensor(name, shape, dtype))

        R = sb("R", [P, NCH * T], F32)
        onescol_b = sb("onescol_b", [P, 1], BF16)
        onesrow_f = sb("onesrow_f", [1, P], F32)
        onesrow_b = sb("onesrow_b", [1, P], BF16)
        ident_b = sb("ident_b", [P, P], BF16)
        ident_f = sb("ident_f", [P, P], F32)
        tri_f = sb("tri_f", [P, P], F32)
        cp = sb("cp", [P, len(CONSTS)], F32)
        sel16_b = sb("sel16_b", [NEXP, NEXP * P], BF16)
        for ci, cv in enumerate(CONSTS):
            nc.const_aps.register(F32, cv, cp[:, ci:ci + 1])

        @B.sec("gpsimd")
        def _(c):
            c.dma(R[:].rearrange("p (k t) -> p k t", k=NCH),
                  x0.rearrange("(k p) t -> p k t", p=P), "gC")
            c.dma(onescol_b[:], ones_col[:, :], "gC")
            c.dma(onesrow_b[:], ones_row[:, :], "gC")
            c.dma(ident_b[:], identf[:, :], "gC")
            c.dma(sel16_b[:], sel16[:, :], "gC")

        @B.sec("sync")
        def _(c):
            c.dma(onesrow_f[:], ones_row[:, :], "yM")
            c.dma(ident_f[:], identf[:, :], "yM")
            c.dma(tri_f[:], trimask[:, :], "yM")
            c.dma(cp[:], cpool[:, :], "yM")

        ph0_loads = [B.ev("gC"), B.ev("yM")]
        B.barrier()

        l0 = contextlib.ExitStack()
        with l0:
            B_fm = sb("B_fm", [DST, T], BF16, l0)
            C_fm = sb("C_fm", [DST, T], BF16, l0)
            comb_fm = sb("comb_fm", [NEXP, T], BF16, l0)

            phase1(B, nc, l0, dram, locals())
            if "dbg_l0" in DEBUG_TAGS:
                for nm, tens, dt in (("dbg_comb", comb_fm, BF16),
                                     ("dbg_bfm", B_fm, BF16),
                                     ("dbg_cfm", C_fm, BF16)):
                    dram[nm] = nc.dram_tensor(nm, list(tens[:].shape), dt,
                                              kind="ExternalOutput").ap()
                B.barrier()

                @B.sec("sync")
                def _(c):
                    c.dma(dram["dbg_comb"][:, :], comb_fm[:], "yD")
                    c.dma(dram["dbg_bfm"][:, :], B_fm[:], "yD")
                    c.dma(dram["dbg_cfm"][:, :], C_fm[:], "yD")
                    c.wait(B.ev("yD"))
            if N_PHASES >= 2:
                phase2_scan(B, nc, l0, dram, locals())
            if N_PHASES >= 3:
                phase3_mlp(B, nc, l0, dram, locals())
            if N_PHASES >= 4:
                phase4_moe(B, nc, l0, dram, locals())

        if N_PHASES >= 5:
            l1 = contextlib.ExitStack()
            with l1:
                phase5_l1(B, nc, l1, dram, locals())

        B.barrier()

        @B.sec("sync")
        def _(c):
            c.dma(out.rearrange("(k p) t -> p k t", p=P),
                  R[:].rearrange("p (k t) -> p k t", k=NCH), "yOUT")
            c.wait(B.ev("yOUT"))

        B.flush()
    return nc



def phase1(B, nc, l0, dram, env):
    R = env["R"]
    B_fm = env["B_fm"]; C_fm = env["C_fm"]; comb_fm = env["comb_fm"]
    onescol_b = env["onescol_b"]; onesrow_f = env["onesrow_f"]
    ident_f = env["ident_f"]
    ph0_loads = env["ph0_loads"]

    TC = 256
    NT = T // TC

    st = contextlib.ExitStack()
    with st:
        def sb(name, shape, dtype, stack=None):
            return (stack or st).enter_context(nc.sbuf_tensor(name, shape, dtype))

        def ps(name, shape, dtype=F32):
            return st.enter_context(nc.psum_tensor(name, shape, dtype))

        mu_row = sb("mu_row", [1, T], F32)
        r1_row = sb("r1_row", [1, T], F32)
        s_row = sb("s_row", [1, T], F32)
        w_xp = sb("w_xp_s", [P, NIN * 112], BF16)
        w_dt = sb("w_dt_s", [DTR, DIN], BF16)
        cw = sb("cw_s", [P, NIN * 4], F32)
        cb = sb("cb_s", [P, NIN], F32)
        dtb = sb("dtb_s", [P, NIN], F32)
        l1g = sb("l1g_s", [P, NCH], F32)
        l1bb = sb("l1bb_s", [P, NCH], F32)
        gw = sb("gw_s", [P, NCH * NEXP], F32)
        gb = sb("gb_s", [NEXP, 1], F32)
        nf = sb("nf", [P, NCH * TC], F32)
        ln1c = sb("ln1c", [P, NCH * TC], BF16)
        bc3 = sb("bc3", [P, 3 * TC], F32)
        sp1 = sb("sp1", [P, TC], F32)
        sp2 = sb("sp2", [P, TC], F32)
        tok16 = sb("tok16", [P, 5 * NEXP], F32)
        gbi_t = sb("gbi_t", [NEXP, P], F32)
        negbig = sb("negbig", [P, NEXP], F32)
        thr = sb("thr", [P, 1], F32)
        eqm_i = sb("eqm_i", [P, NEXP], mybir.dt.int32)
        rho = sb("rho", [P, 1], F32)
        gate_sc = sb("gate_sc", [NEXP, T], F32)
        r16_sb = sb("r16_sb", [NEXP, TC], F32)

        ps_bc = ps("ps_bc", [P, 512])
        ps_bc2 = ps("ps_bc2", [P, 512])
        ps_g = ps("ps_g", [NEXP, 512])
        ps_m0 = ps("ps_m0", [P, 512])
        ps_m1 = ps("ps_m1", [P, 512])
        ps_m2 = ps("ps_m2", [P, 512])
        ps_xd = ps("ps_xd", [112, 512])

        @B.sec("sync")
        def _(c):
            c.dma(cw[:], dram["conv_w"][:, :], "yM")
            c.dma(cb[:], dram["conv_b"][:, :], "yM")
            c.dma(dtb[:], dram["dt_bias"][:, :], "yM")
            c.dma(l1g[:], dram["ln1_g"][:, :], "yM")
            c.dma(l1bb[:], dram["ln1_b"][:, :], "yM")
            c.dma(gw[:].rearrange("p (k m) -> p k m", k=NCH),
                  dram["gate_wT"].rearrange("(k p) m -> p k m", p=P), "yM")
            c.dma(gb[:], dram["gate_b"][:, :], "yM")
        sload = B.ev("yM")

        @B.sec("gpsimd")
        def _(c):
            c.dma(w_xp[:].rearrange("p (k m) -> p k m", k=NIN),
                  dram["w_xproj"].rearrange("(k p) m -> p k m", p=P), "gB")
            c.dma(w_dt[:], dram["w_dtproj"][:, :], "gB")
        wsmall = B.ev("gB")

        sta = contextlib.ExitStack()
        with sta:
            Q_r = sb("Q_r", [1, T], F32, sta)
            rw = sb("rw", [1, 2 * T], F32, sta)
            xb_s = sb("xb_s", [P, NCH * TC], BF16, sta)
            xq_s = sb("xq_s", [P, NCH * TC], BF16, sta)

            for tc in range(NT):
                lo = tc * TC

                @B.sec("scalar")
                def _(c):
                    c.wait(*ph0_loads)
                    c.wait(B.ev("T"))
                    for k in range(NCH):
                        c.e.activation(xb_s[:, k * TC:(k + 1) * TC],
                                       R[:, k * T + lo:k * T + lo + TC], AF.Identity)
                        ins = c.e.activation(xq_s[:, k * TC:(k + 1) * TC],
                                             R[:, k * T + lo:k * T + lo + TC],
                                             AF.Square)
                    c.inc(ins)
                e_cast = B.ev("S")

                @B.sec("tensor")
                def _(c):
                    c.wait(*ph0_loads)
                    c.wait(e_cast, B.ev("S"))
                    for k in range(NCH):
                        c.e.matmul(ps_g[0:1, 0:TC], onescol_b[:],
                                   xb_s[:, k * TC:(k + 1) * TC],
                                   start=(k == 0), stop=(k == NCH - 1))
                    for k in range(NCH):
                        ins = c.e.matmul(ps_xd[0:1, 0:TC], onescol_b[:],
                                         xq_s[:, k * TC:(k + 1) * TC],
                                         start=(k == 0), stop=(k == NCH - 1))
                    c.inc(ins)
                e_mm = B.ev("T")

                @B.sec("scalar")
                def _(c):
                    c.wait(e_mm)
                    c.e.activation(mu_row[:, lo:lo + TC], ps_g[0:1, 0:TC], AF.Identity)
                    ins = c.e.activation(Q_r[:, lo:lo + TC], ps_xd[0:1, 0:TC],
                                         AF.Identity)
                    c.inc(ins)
            e_stats = B.ev("S")

            q1 = rw[:, 0:T]; v1 = rw[:, T:2 * T]

            @B.sec("scalar")
            def _(c):
                c.wait(e_stats)
                ins = c.e.activation(q1, mu_row[:, :], AF.Square, scale=1.0 / D)
                c.inc(ins)
            e1 = B.ev("S")

            @B.sec("vector")
            def _(c):
                c.wait(e1)
                ins = c.e.scalar_tensor_tensor(v1, Q_r[:, :], 1.0 / D, q1,
                                               ALU.mult, ALU.subtract)
                c.inc(ins)
            e2 = B.ev("V")

            @B.sec("scalar")
            def _(c):
                c.wait(e2)
                ev = c.inc(c.e.activation(q1, v1, AF.Ln, bias=EPS))
                c.wait(ev)
                ev = c.inc(c.e.activation(r1_row[:, :], q1, AF.Exp, scale=-0.5))
                c.wait(ev)
                c.inc(c.e.activation(q1, q1, AF.Exp, scale=-1.0))
            e3 = B.ev("S")

            @B.sec("vector")
            def _(c):
                c.wait(e3)
                ins = c.e.tensor_tensor(v1, v1, q1, ALU.mult)
                c.inc(ins)
            e4 = B.ev("V")

            @B.sec("scalar")
            def _(c):
                c.wait(e4)
                ev = c.inc(c.e.activation(v1, v1, AF.Ln, bias=EPS))
                c.wait(ev)
                c.inc(c.e.activation(v1, v1, AF.Exp, scale=-0.5))
            e5 = B.ev("S")

            @B.sec("vector")
            def _(c):
                c.wait(e5)
                ins = c.e.tensor_tensor(s_row[:, :], v1, r1_row[:, :], ALU.mult)
                c.inc(ins)
            B.barrier()
        e_rows = B.ev("V")

        def chunk_norm(c_lo, want_n0, n0c):
            lo = c_lo

            @B.sec("tensor")
            def _(c):
                c.wait(e_rows, B.ev("V"), B.ev("S"))
                ins = c.e.matmul(ps_bc[:, 0:TC], onesrow_f[:],
                                 mu_row[:, lo:lo + TC], start=True, stop=True)
                c.inc(ins)
            eb1 = B.ev("T")

            @B.sec("scalar")
            def _(c):
                c.wait(eb1)
                ins = c.e.activation(bc3[:, 0:TC], ps_bc[:, 0:TC], AF.Identity,
                                     scale=1.0 / D)
                c.inc(ins)
            eb1e = B.ev("S")

            @B.sec("tensor")
            def _(c):
                c.wait(eb1e)
                if want_n0:
                    c.e.matmul(ps_bc[:, 0:TC], onesrow_f[:], r1_row[:, lo:lo + TC],
                               start=True, stop=True)
                ins = c.e.matmul(ps_bc2[:, 0:TC], onesrow_f[:], s_row[:, lo:lo + TC],
                                 start=True, stop=True)
                c.inc(ins)
            eb2 = B.ev("T")

            @B.sec("scalar")
            def _(c):
                c.wait(eb2)
                if want_n0:
                    c.e.activation(bc3[:, TC:2 * TC], ps_bc[:, 0:TC], AF.Identity)
                ins = c.e.activation(bc3[:, 2 * TC:3 * TC], ps_bc2[:, 0:TC], AF.Identity)
                c.inc(ins)
            eb = B.ev("S")

            @B.sec("vector")
            def _(c):
                c.wait(eb, B.ev("T"), B.ev("yS"))
                for k in range(NCH):
                    c.chain(c.e.tensor_tensor(nf[:, k * TC:(k + 1) * TC],
                                              R[:, k * T + lo:k * T + lo + TC],
                                              bc3[:, 0:TC], ALU.subtract))
                    if want_n0:
                        c.e.tensor_tensor(n0c[:, k * TC:(k + 1) * TC],
                                          nf[:, k * TC:(k + 1) * TC],
                                          bc3[:, TC:2 * TC], ALU.mult)
                    ins = c.e.tensor_tensor(ln1c[:, k * TC:(k + 1) * TC],
                                            nf[:, k * TC:(k + 1) * TC],
                                            bc3[:, 2 * TC:3 * TC], ALU.mult)
                c.inc(ins)
            e_nf = B.ev("V")

            @B.sec("scalar")
            def _(c):
                c.wait(e_nf, sload)
                for k in range(NCH):
                    ins = c.e.activation(ln1c[:, k * TC:(k + 1) * TC],
                                         ln1c[:, k * TC:(k + 1) * TC], AF.Identity,
                                         scale=l1g[:, k:k + 1], bias=l1bb[:, k:k + 1])
                c.inc(ins)
            return e_nf, B.ev("S")

        stz = contextlib.ExitStack()
        with stz:
            w_z = sb("w_z_s", [P, NCH * DIN], BF16, stz)
            n0c = sb("n0c", [P, NCH * TC], BF16, stz)
            sz_st = sb("sz_st", [P, NIN * TC], BF16, stz)

            @B.sec("gpsimd")
            def _(c):
                wz = dram["w_inproj"][:, DIN:2 * DIN]
                c.dma(w_z[:].rearrange("p (k m) -> p k m", k=NCH),
                      wz.rearrange("(k p) m -> p k m", p=P), "gA")
            wload = B.ev("gA")

            for tc in range(NT):
                lo = tc * TC
                e_nf, e_ln1 = chunk_norm(lo, True, n0c)

                @B.sec("tensor")
                def _(c):
                    c.wait(e_nf, sload, B.ev("V"))
                    for k in range(NCH):
                        ins = c.e.matmul(ps_g[:, 0:TC],
                                         gw[:, k * NEXP:(k + 1) * NEXP],
                                         nf[:, k * TC:(k + 1) * TC],
                                         start=(k == 0), stop=(k == NCH - 1))
                    c.inc(ins)
                e_gmm = B.ev("T")

                @B.sec("tensor")
                def _(c):
                    c.wait(B.ev("S"))
                    ins = c.e.matmul(ps_bc[0:NEXP, 0:TC], onesrow_f[:, 0:NEXP],
                                     r1_row[:, lo:lo + TC], start=True, stop=True)
                    c.inc(ins)
                e_r116 = B.ev("T")

                @B.sec("scalar")
                def _(c):
                    c.wait(e_r116)
                    c.inc(c.e.activation(r16_sb[:], ps_bc[0:NEXP, 0:TC],
                                         AF.Identity))
                e_r116e = B.ev("S")

                @B.sec("vector")
                def _(c):
                    c.wait(e_gmm, e_r116e)
                    ins = c.e.tensor_tensor(gate_sc[:, lo:lo + TC], ps_g[:, 0:TC],
                                            r16_sb[:], ALU.mult)
                    c.inc(ins)
                e_gm = B.ev("V")

                @B.sec("scalar")
                def _(c):
                    c.wait(e_gm)
                    ins = c.e.activation(gate_sc[:, lo:lo + TC],
                                         gate_sc[:, lo:lo + TC], AF.Sigmoid)
                    c.inc(ins)

                for o in range(NIN):
                    pmm = (ps_m0, ps_m1, ps_m2)[o % 3]
                    pk = "m%d" % (o % 3)

                    @B.sec("tensor")
                    def _(c):
                        c.wait(e_ln1, wload, B.lastev.get(pk))
                        for k in range(NCH):
                            ins = c.e.matmul(
                                pmm[:, 0:TC],
                                w_z[:, k * DIN + o * P:k * DIN + o * P + P],
                                ln1c[:, k * TC:(k + 1) * TC],
                                start=(k == 0), stop=(k == NCH - 1))
                        c.inc(ins)
                    e_z = B.ev("T")

                    @B.sec("scalar")
                    def _(c):
                        c.wait(e_z, B.ev("yS"))
                        ins = c.e.activation(sz_st[:, o * TC:(o + 1) * TC],
                                             pmm[:, 0:TC], AF.Silu)
                        B.lastev[pk] = c.inc(ins)
                e_sz = B.ev("S")

                @B.sec("sync")
                def _(c):
                    c.wait(e_sz, e_nf)
                    c.dma(dram["sz_sp"][:, :, lo:lo + TC],
                          sz_st[:].rearrange("p (k t) -> p k t", k=NIN), "yS")
                    c.dma(dram["n0_sp"][:, :, lo:lo + TC],
                          n0c[:].rearrange("p (k t) -> p k t", k=NIN // 2), "yS")

        @B.sec("vector")
        def _(c):
            c.e.memset(negbig[:], NEG)

        for j in range(T // P):
            @B.sec("scalar")
            def _(c):
                c.wait(B.ev("V"), B.ev("T"))
                ins = c.e.activation(gbi_t[:, :], gate_sc[:, j * P:(j + 1) * P],
                                     AF.Identity, bias=gb[:, 0:1])
                c.inc(ins)
            e_bi = B.ev("S")

            @B.sec("tensor")
            def _(c):
                c.wait(e_bi)
                c.e.transpose(ps_m0[:, 0:NEXP], gate_sc[:, j * P:(j + 1) * P],
                              ident_f[0:NEXP, 0:NEXP])
                ins = c.e.transpose(ps_m1[:, 0:NEXP], gbi_t[:, :],
                                    ident_f[0:NEXP, 0:NEXP])
                c.inc(ins)
            e_t = B.ev("T")

            @B.sec("scalar")
            def _(c):
                c.wait(e_t, B.ev("V"))
                c.e.activation(tok16[:, 0:NEXP], ps_m0[:, 0:NEXP], AF.Identity)
                c.e.activation(tok16[:, NEXP:2 * NEXP], ps_m1[:, 0:NEXP], AF.Identity)
                ins = c.e.activation(tok16[:, 2 * NEXP:3 * NEXP],
                                     ps_m1[:, 0:NEXP], AF.Identity)
                c.inc(ins)
            e_tt = B.ev("S")

            @B.sec("vector")
            def _(c):
                c.wait(e_tt)
                sc_t = tok16[:, 0:NEXP]
                bi_t = tok16[:, NEXP:2 * NEXP]
                wrk = tok16[:, 2 * NEXP:3 * NEXP]
                cmb = tok16[:, 4 * NEXP:5 * NEXP]
                for r in range(4):
                    c.chain(c.e.reduce_max(thr[:], wrk, axis=AX.X))
                    if r < 3:
                        c.chain(c.e.tensor_single_scalar(eqm_i[:], wrk, thr[:, 0:1],
                                                         ALU.is_ge))
                        c.chain(c.e.copy_predicated(wrk, eqm_i[:], negbig[:]))
                ins = c.e.scalar_tensor_tensor(cmb, bi_t, thr[:, 0:1], sc_t,
                                               ALU.is_ge, ALU.mult,
                                               accum_out=rho[:])
                c.chain(ins)
                c.chain(c.e.reciprocal(rho[:], rho[:]))
                ins = c.e.tensor_scalar(cmb, cmb, rho[:, 0:1], 2.5,
                                        ALU.mult, ALU.mult)
                c.inc(ins)
            e_c = B.ev("V")

            @B.sec("tensor")
            def _(c):
                c.wait(e_c, B.ev("S"))
                ins = c.e.transpose(ps_m0[0:NEXP, 0:P],
                                    tok16[:, 4 * NEXP:5 * NEXP], ident_f[:])
                c.inc(ins)
            e_cb = B.ev("T")

            @B.sec("scalar")
            def _(c):
                c.wait(e_cb)
                ins = c.e.activation(comb_fm[:, j * P:(j + 1) * P],
                                     ps_m0[0:NEXP, 0:P], AF.Identity)
                c.inc(ins)

        stx = contextlib.ExitStack()
        with stx:
            w_x = sb("w_x_s", [P, NCH * DIN], BF16, stx)
            xi_buf = sb("xi_buf", [P, NIN * (TC + 3)], BF16, stx)
            xc_st = sb("xc_st", [P, NIN * TC], BF16, stx)
            dt_st = sb("dt_st", [P, NIN * TC], BF16, stx)
            dtr_s = sb("dtr_s", [DTR, TC], BF16, stx)

            @B.sec("gpsimd")
            def _(c):
                c.wait(B.ev("T"), B.ev("V"), B.ev("S"))
                wx = dram["w_inproj"][:, 0:DIN]
                c.dma(w_x[:].rearrange("p (k m) -> p k m", k=NCH),
                      wx.rearrange("(k p) m -> p k m", p=P), "gA")
            wload2 = B.ev("gA")

            @B.sec("vector")
            def _(c):
                c.wait(B.ev("yS"), B.ev("T"), B.ev("S"))
                for o in range(NIN):
                    ins = c.e.memset(xi_buf[:, o * (TC + 3):o * (TC + 3) + 3], 0.0)
                c.inc(ins)

            for tc in range(NT):
                lo = tc * TC
                e_nf, e_ln1 = chunk_norm(lo, False, None)

                for o in range(NIN):
                    pmm = (ps_m0, ps_m1, ps_m2)[o % 3]
                    pk = "m%d" % (o % 3)

                    @B.sec("tensor")
                    def _(c):
                        c.wait(e_ln1, wload2, B.lastev.get(pk), B.ev("V"))
                        for k in range(NCH):
                            ins = c.e.matmul(
                                pmm[:, 0:TC],
                                w_x[:, k * DIN + o * P:k * DIN + o * P + P],
                                ln1c[:, k * TC:(k + 1) * TC],
                                start=(k == 0), stop=(k == NCH - 1))
                        c.inc(ins)
                    e_x = B.ev("T")

                    @B.sec("scalar")
                    def _(c):
                        c.wait(e_x)
                        ins = c.e.activation(
                            xi_buf[:, o * (TC + 3) + 3:(o + 1) * (TC + 3)],
                            pmm[:, 0:TC], AF.Identity)
                        B.lastev[pk] = c.inc(ins)
                e_xz = B.ev("S")

                @B.sec("vector")
                def _(c):
                    c.wait(e_xz, B.ev("yS"), B.ev("V"))
                    for o in range(NIN):
                        base = o * (TC + 3)
                        c.chain(c.e.tensor_scalar(sp1[:, :],
                                                  xi_buf[:, base:base + TC],
                                                  cw[:, o * 4:o * 4 + 1], None,
                                                  ALU.mult))
                        for k in (1, 2):
                            c.chain(c.e.scalar_tensor_tensor(
                                sp1[:, :], xi_buf[:, base + k:base + k + TC],
                                cw[:, o * 4 + k:o * 4 + k + 1],
                                sp1[:, :], ALU.mult, ALU.add))
                        c.chain(c.e.scalar_tensor_tensor(
                            xc_st[:, o * TC:(o + 1) * TC],
                            xi_buf[:, base + 3:base + 3 + TC],
                            cw[:, o * 4 + 3:o * 4 + 4],
                            sp1[:, :], ALU.mult, ALU.add))
                e_conv = B.ev("V")

                @B.sec("vector")
                def _(c):
                    for o in range(NIN):
                        base = o * (TC + 3)
                        ins = c.e.tensor_copy(xi_buf[:, base:base + 3],
                                              xi_buf[:, base + TC:base + TC + 3])
                    c.inc(ins)

                @B.sec("scalar")
                def _(c):
                    c.wait(e_conv)
                    for o in range(NIN):
                        ins = c.e.activation(xc_st[:, o * TC:(o + 1) * TC],
                                             xc_st[:, o * TC:(o + 1) * TC], AF.Silu,
                                             bias=cb[:, o:o + 1])
                    c.inc(ins)
                e_xc = B.ev("S")

                @B.sec("tensor")
                def _(c):
                    c.wait(e_xc, wsmall)
                    for k in range(NIN):
                        ins = c.e.matmul(ps_xd[:, 0:TC],
                                         w_xp[:, k * 112:(k + 1) * 112],
                                         xc_st[:, k * TC:(k + 1) * TC],
                                         start=(k == 0), stop=(k == NIN - 1))
                    c.inc(ins)
                e_xd = B.ev("T")

                @B.sec("scalar")
                def _(c):
                    c.wait(e_xd)
                    c.e.activation(dtr_s[:, :], ps_xd[0:DTR, 0:TC], AF.Identity)
                    c.e.activation(B_fm[:, lo:lo + TC], ps_xd[64:80, 0:TC],
                                   AF.Identity)
                    ins = c.e.activation(C_fm[:, lo:lo + TC],
                                         ps_xd[96:112, 0:TC], AF.Identity)
                    c.inc(ins)
                e_dtr = B.ev("S")

                for o in range(NIN):
                    pmm = (ps_m0, ps_m1, ps_m2)[o % 3]
                    pk = "m%d" % (o % 3)

                    @B.sec("tensor")
                    def _(c):
                        c.wait(e_dtr, B.lastev.get(pk))
                        ins = c.e.matmul(pmm[:, 0:TC], w_dt[:, o * P:(o + 1) * P],
                                         dtr_s[:, :], start=True, stop=True)
                        c.inc(ins)
                    e_d2 = B.ev("T")

                    @B.sec("scalar")
                    def _(c):
                        c.wait(e_d2, B.ev("V"))
                        c.e.activation(sp2[:, :], pmm[:, 0:TC], AF.Relu,
                                       bias=dtb[:, o:o + 1])
                        ev = c.inc(c.e.activation(sp1[:, :], pmm[:, 0:TC], AF.Abs,
                                                  bias=dtb[:, o:o + 1]))
                        B.lastev[pk] = ev
                        c.wait(ev)
                        ev = c.inc(c.e.activation(sp1[:, :], sp1[:, :], AF.Exp,
                                                  scale=-1.0))
                        c.wait(ev)
                        c.inc(c.e.activation(sp1[:, :], sp1[:, :], AF.Ln, bias=1.0))
                    e_sp = B.ev("S")

                    @B.sec("vector")
                    def _(c):
                        c.wait(e_sp, B.ev("yS"))
                        ins = c.e.tensor_tensor(dt_st[:, o * TC:(o + 1) * TC],
                                                sp1[:, :], sp2[:, :], ALU.add)
                        c.inc(ins)
                e_dt = B.ev("V")

                @B.sec("sync")
                def _(c):
                    c.wait(e_xc, e_dt)
                    c.dma(dram["xc_sp"][:, :, lo:lo + TC],
                          xc_st[:].rearrange("p (k t) -> p k t", k=NIN), "yS")
                    c.dma(dram["dt_sp"][:, :, lo:lo + TC],
                          dt_st[:].rearrange("p (k t) -> p k t", k=NIN), "yS")

        B.barrier()


def phase2_scan(B, nc, l0, dram, env):
    R = env["R"]
    B_fm = env["B_fm"]; C_fm = env["C_fm"]
    onesrow_b = env["onesrow_b"]
    TC = 256
    NT = T // TC

    st = contextlib.ExitStack()
    with st:
        def sb(name, shape, dtype, stack=None):
            return (stack or st).enter_context(nc.sbuf_tensor(name, shape, dtype))

        def ps(name, shape, dtype=F32):
            return st.enter_context(nc.psum_tensor(name, shape, dtype))

        w_op = sb("w_op_s", [P, NIN * D], BF16)
        a_fm = sb("a_fm", [P, NIN * DST], F32)
        d_fm = sb("d_fm", [P, NIN], F32)
        dt_all = sb("dt_all", [P, NIN * TC], BF16)
        xc_all = sb("xc_all", [P, NIN * TC], BF16)
        sz_all = sb("sz_all", [P, NIN * TC], BF16)
        n0c2 = sb("n0c2", [P, NCH * TC], BF16)
        dtx_all = sb("dtx_all", [P, NIN * TC], BF16)
        dA = sb("dA_all", [P, NIN * TC], F32)
        W_all = sb("W_all", [P, NIN * TC], BF16)
        hs2 = [sb("hs_a", [P, NIN * TC], BF16), sb("hs_b", [P, NIN * TC], BF16)]
        y_all = sb("y_all", [P, NIN * TC], BF16)
        Bb2 = [sb("Bb_a", [P, TC], BF16), sb("Bb_b", [P, TC], BF16)]
        Cb2 = [sb("Cb_a", [P, TC], BF16), sb("Cb_b", [P, TC], BF16)]
        h_state = sb("h_state", [P, NIN * DST], F32)
        u1c = sb("u1c", [P, NCH * TC], BF16)
        gt = sb("gt", [P, TC], BF16)
        uf = sb("uf", [P, TC], F32)

        ps_b1 = ps("ps_b1", [P, 512])
        ps_b2 = ps("ps_b2", [P, 512])
        ps_o0 = ps("ps_o0", [P, 512])
        ps_o1 = ps("ps_o1", [P, 512])
        ps_o2 = ps("ps_o2", [P, 512])

        @B.sec("gpsimd")
        def _(c):
            c.wait(B.ev("T"), B.ev("V"), B.ev("S"))
            c.dma(w_op[:].rearrange("p (k m) -> p k m", k=NIN),
                  dram["w_oproj"].rearrange("(k p) m -> p k m", p=P), "gA")
            c.dma(a_fm[:], dram["a_log"][:, :], "gA")
            c.dma(d_fm[:], dram["d_vec"][:, :], "gA")
        wload = B.ev("gA")

        @B.sec("scalar")
        def _(c):
            c.wait(wload)
            c.chain(c.e.activation(a_fm[:], a_fm[:], AF.Exp))
        e_a1 = B.ev("S")

        @B.sec("vector")
        def _(c):
            c.wait(e_a1)
            c.chain(c.e.tensor_scalar_mul(a_fm[:], a_fm[:], -1.0))
        e_a = B.ev("V")

        a3 = a_fm[:].rearrange("p (kc n) -> p kc n", n=DST)
        for tc in range(NT):
            lo = tc * TC

            @B.sec("sync")
            def _(c):
                c.wait(B.ev("V"), B.ev("T"), B.ev("S"))
                c.dma(dt_all[:].rearrange("p (k t) -> p k t", k=NIN),
                      dram["dt_sp"][:, :, lo:lo + TC], "yA")
                c.dma(xc_all[:].rearrange("p (k t) -> p k t", k=NIN),
                      dram["xc_sp"][:, :, lo:lo + TC], "yA")
                c.dma(sz_all[:].rearrange("p (k t) -> p k t", k=NIN),
                      dram["sz_sp"][:, :, lo:lo + TC], "yA")
                c.dma(n0c2[:].rearrange("p (k t) -> p k t", k=NCH),
                      dram["n0_sp"][:, :, lo:lo + TC], "yA")
            e_ld = B.ev("yA")

            @B.sec("vector")
            def _(c):
                c.wait(e_ld)
                c.chain(c.e.tensor_tensor(dtx_all[:], dt_all[:], xc_all[:],
                                          ALU.mult))
            e_dtx = B.ev("V")

            def emit_exp(n):
                @B.sec("scalar")
                def _(c):
                    c.wait(e_ld, e_a)
                    for g in range(2):
                        c.wait(B.lastev.get("p2sc%d" % g))
                        for kc in range(8 * g, 8 * g + 8):
                            ev = c.inc(c.e.activation(
                                dA[:, kc * TC:(kc + 1) * TC],
                                dt_all[:, kc * TC:(kc + 1) * TC], AF.Exp,
                                scale=a_fm[:, kc * DST + n:kc * DST + n + 1]))
                        B.lastev["p2dA%d" % g] = ev

            for n in range(DST):
                par = n % 2

                @B.sec("tensor")
                def _(c):
                    c.wait(e_dtx, B.lastev.get("p2bc"))
                    sel = env["sel16_b"][:, n * P:(n + 1) * P]
                    c.e.matmul(ps_b1[:, 0:TC], sel, B_fm[:, lo:lo + TC],
                               start=True, stop=True)
                    ins = c.e.matmul(ps_b2[:, 0:TC], sel, C_fm[:, lo:lo + TC],
                                     start=True, stop=True)
                    c.inc(ins)
                e_bc = B.ev("T")

                @B.sec("scalar")
                def _(c):
                    c.wait(e_bc, B.lastev.get("p2W%d" % par),
                           B.lastev.get("p2G%d" % par))
                    c.e.activation(Bb2[par][:], ps_b1[:, 0:TC], AF.Identity)
                    ins = c.e.activation(Cb2[par][:], ps_b2[:, 0:TC], AF.Identity)
                    B.lastev["p2bc"] = c.inc(ins)
                e_bce = B.ev("S")

                emit_exp(n)

                @B.sec("vector")
                def _(c):
                    c.wait(e_bce, e_dtx, B.lastev.get("p2G%d" % par))
                    bb = Bb2[par][:].unsqueeze(1).broadcast_to([P, NIN, TC])
                    ev = c.chain(c.e.tensor_tensor(
                        W_all[:].rearrange("p (k t) -> p k t", k=NIN),
                        dtx_all[:].rearrange("p (k t) -> p k t", k=NIN),
                        bb, ALU.mult))
                    B.lastev["p2W%d" % par] = ev
                    for g in range(2):
                        c.wait(B.lastev.get("p2dA%d" % g))
                        for kc in range(8 * g, 8 * g + 8):
                            init = (0.0 if tc == 0
                                    else h_state[:, kc * DST + n:
                                                 kc * DST + n + 1])
                            ev = c.inc(c.e.tensor_tensor_scan(
                                hs2[par][:, kc * TC:(kc + 1) * TC],
                                dA[:, kc * TC:(kc + 1) * TC],
                                W_all[:, kc * TC:(kc + 1) * TC],
                                init, ALU.mult, ALU.add))
                        B.lastev["p2sc%d" % g] = ev
                    c.wait(B.ev("V"))
                    c.chain(c.e.tensor_copy(
                        h_state[:].rearrange("p (kc n) -> p kc n", n=DST)[:, :, n],
                        hs2[par][:].rearrange("p (kc t) -> p kc t", kc=NIN)
                        [:, :, TC - 1]))
                e_vn = B.ev("V")

                @B.sec("gpsimd")
                def _(c):
                    c.wait(e_vn, e_bce)
                    cb = Cb2[par][:].unsqueeze(1).broadcast_to([P, NIN, TC])
                    if n == 0:
                        c.wait(B.ev("T"))
                        ev = c.chain(c.e.tensor_tensor(
                            y_all[:].rearrange("p (k t) -> p k t", k=NIN),
                            hs2[par][:].rearrange("p (k t) -> p k t", k=NIN),
                            cb, ALU.mult))
                    else:
                        c.chain(c.e.tensor_tensor(
                            hs2[par][:].rearrange("p (k t) -> p k t", k=NIN),
                            hs2[par][:].rearrange("p (k t) -> p k t", k=NIN),
                            cb, ALU.mult))
                        ev = c.chain(c.e.tensor_tensor(y_all[:], y_all[:],
                                                       hs2[par][:], ALU.add))
                    B.lastev["p2G%d" % par] = ev
                    B.lastev["p2yG"] = ev

            @B.sec("vector")
            def _(c):
                c.wait(B.lastev.get("p2yG"))
                for kc in range(NIN):
                    c.chain(c.e.scalar_tensor_tensor(
                        gt[:], xc_all[:, kc * TC:(kc + 1) * TC],
                        d_fm[:, kc:kc + 1],
                        y_all[:, kc * TC:(kc + 1) * TC], ALU.mult, ALU.add))
                    c.chain(c.e.tensor_tensor(
                        y_all[:, kc * TC:(kc + 1) * TC], gt[:],
                        sz_all[:, kc * TC:(kc + 1) * TC], ALU.mult))
            e_y = B.ev("V")

            for oc in range(NCH):
                pmm = (ps_o0, ps_o1, ps_o2)[oc % 3]
                pk = "o%d" % (oc % 3)

                @B.sec("tensor")
                def _(c):
                    c.wait(e_y, wload, B.lastev.get(pk))
                    for kc in range(NIN):
                        ins = c.e.matmul(
                            pmm[:, 0:TC],
                            w_op[:, kc * D + oc * P:kc * D + oc * P + P],
                            y_all[:, kc * TC:(kc + 1) * TC],
                            start=(kc == 0), stop=(kc == NIN - 1))
                    c.inc(ins)
                e_mm = B.ev("T")

                @B.sec("vector")
                def _(c):
                    c.wait(e_mm, B.lastev.get("p2uf"))
                    ev = c.chain(c.e.tensor_tensor(uf[:], pmm[:, 0:TC],
                                                   n0c2[:, oc * TC:(oc + 1) * TC],
                                                   ALU.add))
                    B.lastev[pk] = ev
                    c.chain(c.e.tensor_tensor(R[:, oc * T + lo:oc * T + lo + TC],
                                              R[:, oc * T + lo:oc * T + lo + TC],
                                              uf[:], ALU.add))
                e_u = B.ev("V")

                @B.sec("scalar")
                def _(c):
                    c.wait(e_u, B.ev("yS"))
                    B.lastev["p2uf"] = c.inc(
                        c.e.activation(u1c[:, oc * TC:(oc + 1) * TC], uf[:],
                                       AF.Identity))
            e_uc = B.ev("S")

            @B.sec("sync")
            def _(c):
                c.wait(e_uc)
                c.dma(dram["u1_sp"][:, :, lo:lo + TC],
                      u1c[:].rearrange("p (k t) -> p k t", k=NCH), "yS")

    B.barrier()


def phase3_mlp(B, nc, l0, dram, env):
    R = env["R"]
    onescol_b = env["onescol_b"]; onesrow_f = env["onesrow_f"]
    TC = 512
    NT = T // TC

    st = contextlib.ExitStack()
    with st:
        def sb(name, shape, dtype, stack=None):
            return (stack or st).enter_context(nc.sbuf_tensor(name, shape, dtype))

        def ps(name, shape, dtype=F32):
            return st.enter_context(nc.psum_tensor(name, shape, dtype))

        b1 = sb("b1_s", [P, NFF], F32)
        b2 = sb("b2_s", [P, NCH], F32)
        l2g = sb("l2g_s", [P, NCH], F32)
        l2b = sb("l2b_s", [P, NCH], F32)
        u1b = sb("u1b", [P, NCH * TC], BF16)
        ln2c = sb("ln2c", [P, NCH * TC], BF16)
        usq = ln2c
        rows2 = sb("rows2", [1, 2 * TC], F32)
        bc2 = sb("bc2", [P, 2 * TC], F32)
        tf = sb("tf", [P, TC], F32)
        h1b = sb("h1b", [P, NFF * TC], BF16)

        ps_r0 = ps("ps_r0", [1, 512])
        ps_r1 = ps("ps_r1", [1, 512])
        ps_b = ps("ps_b", [P, 512])
        ps_b2_ = ps("ps_b2_", [P, 512])
        ps_m0 = ps("ps3_m0", [P, 512])
        ps_m1 = ps("ps3_m1", [P, 512])
        ps_m2 = ps("ps3_m2", [P, 512])

        @B.sec("sync")
        def _(c):
            c.dma(b1[:], dram["ffn_b1"][:, :], "yM")
            c.dma(b2[:], dram["ffn_b2"][:, :], "yM")
            c.dma(l2g[:], dram["ln2_g"][:, :], "yM")
            c.dma(l2b[:], dram["ln2_b"][:, :], "yM")
        sload = B.ev("yM")

        sta = contextlib.ExitStack()
        with sta:
            w1 = sb("w1_s", [P, NCH * DFF], BF16, sta)

            @B.sec("gpsimd")
            def _(c):
                c.wait(B.ev("T"), B.ev("V"), B.ev("S"))
                c.dma(w1[:].rearrange("p (k m) -> p k m", k=NCH),
                      dram["w_ffn1"].rearrange("(k p) m -> p k m", p=P), "gA")
            wload = B.ev("gA")

            for tc in range(NT):
                lo = tc * TC

                @B.sec("sync")
                def _(c):
                    c.wait(B.ev("V"), B.ev("T"), B.ev("S"))
                    c.dma(u1b[:].rearrange("p (k t) -> p k t", k=NCH),
                          dram["u1_sp"][:, :, lo:lo + TC], "yA")
                e_ld = B.ev("yA")

                @B.sec("scalar")
                def _(c):
                    c.wait(e_ld, B.ev("T"))
                    for k in range(NCH):
                        ins = c.e.activation(usq[:, k * TC:(k + 1) * TC],
                                             u1b[:, k * TC:(k + 1) * TC], AF.Square)
                    c.inc(ins)
                e_sq = B.ev("S")

                @B.sec("tensor")
                def _(c):
                    c.wait(e_sq, B.ev("S"))
                    for k in range(NCH):
                        c.e.matmul(ps_r0[0:1, 0:TC], onescol_b[:],
                                   u1b[:, k * TC:(k + 1) * TC],
                                   start=(k == 0), stop=(k == NCH - 1))
                    for k in range(NCH):
                        ins = c.e.matmul(ps_r1[0:1, 0:TC], onescol_b[:],
                                         usq[:, k * TC:(k + 1) * TC],
                                         start=(k == 0), stop=(k == NCH - 1))
                    c.inc(ins)
                e_st = B.ev("T")

                @B.sec("scalar")
                def _(c):
                    c.wait(e_st)
                    c.e.activation(rows2[:, 0:TC], ps_r0[0:1, 0:TC], AF.Identity)
                    c.chain(c.e.activation(rows2[:, TC:2 * TC], ps_r0[0:1, 0:TC],
                                           AF.Square, scale=1.0 / D))
                e_r1 = B.ev("S")

                @B.sec("vector")
                def _(c):
                    c.wait(e_r1)
                    c.chain(c.e.scalar_tensor_tensor(
                        rows2[:, TC:2 * TC], ps_r1[0:1, 0:TC], 1.0 / D,
                        rows2[:, TC:2 * TC], ALU.mult, ALU.subtract))
                e_v = B.ev("V")

                @B.sec("scalar")
                def _(c):
                    c.wait(e_v)
                    ev = c.chain(c.e.activation(rows2[:, TC:2 * TC],
                                                rows2[:, TC:2 * TC], AF.Ln,
                                                bias=EPS))
                    c.chain(c.e.activation(rows2[:, TC:2 * TC],
                                           rows2[:, TC:2 * TC], AF.Exp,
                                           scale=-0.5))
                e_rw = B.ev("S")

                @B.sec("tensor")
                def _(c):
                    c.wait(e_rw, B.ev("V"))
                    c.e.matmul(ps_b[:, 0:TC], onesrow_f[:], rows2[:, 0:TC],
                               start=True, stop=True)
                    ins = c.e.matmul(ps_b2_[:, 0:TC], onesrow_f[:],
                                     rows2[:, TC:2 * TC], start=True, stop=True)
                    c.inc(ins)
                e_bc = B.ev("T")

                @B.sec("scalar")
                def _(c):
                    c.wait(e_bc)
                    c.e.activation(bc2[:, 0:TC], ps_b[:, 0:TC], AF.Identity,
                                   scale=1.0 / D)
                    ins = c.e.activation(bc2[:, TC:2 * TC], ps_b2_[:, 0:TC],
                                         AF.Identity)
                    c.inc(ins)
                e_bce = B.ev("S")

                @B.sec("vector")
                def _(c):
                    c.wait(e_bce, B.ev("T"))
                    for k in range(NCH):
                        c.chain(c.e.tensor_tensor(tf[:], u1b[:, k * TC:(k + 1) * TC],
                                                  bc2[:, 0:TC], ALU.subtract))
                        c.chain(c.e.tensor_tensor(ln2c[:, k * TC:(k + 1) * TC],
                                                  tf[:], bc2[:, TC:2 * TC],
                                                  ALU.mult))
                e_n = B.ev("V")

                @B.sec("scalar")
                def _(c):
                    c.wait(e_n, sload)
                    for k in range(NCH):
                        ins = c.e.activation(ln2c[:, k * TC:(k + 1) * TC],
                                             ln2c[:, k * TC:(k + 1) * TC],
                                             AF.Identity, scale=l2g[:, k:k + 1],
                                             bias=l2b[:, k:k + 1])
                    c.inc(ins)
                e_l2 = B.ev("S")

                for of in range(NFF):
                    pmm = (ps_m0, ps_m1, ps_m2)[of % 3]
                    pk = "3m%d" % (of % 3)

                    @B.sec("tensor")
                    def _(c):
                        c.wait(e_l2, wload, B.lastev.get(pk))
                        for k in range(NCH):
                            ins = c.e.matmul(
                                pmm[:, 0:TC],
                                w1[:, k * DFF + of * P:k * DFF + of * P + P],
                                ln2c[:, k * TC:(k + 1) * TC],
                                start=(k == 0), stop=(k == NCH - 1))
                        c.inc(ins)
                    e_h = B.ev("T")

                    @B.sec("scalar")
                    def _(c):
                        c.wait(e_h, B.ev("yS"))
                        B.lastev[pk] = c.inc(
                            c.e.activation(h1b[:, of * TC:(of + 1) * TC],
                                           pmm[:, 0:TC], AF.Gelu_apprx_tanh,
                                           bias=b1[:, of:of + 1]))
                e_g = B.ev("S")

                @B.sec("sync")
                def _(c):
                    c.wait(e_g)
                    c.dma(dram["h1_sp"][:, :, lo:lo + TC],
                          h1b[:].rearrange("p (k t) -> p k t", k=NFF), "yS")

        for half in (0, 1):
            sth = contextlib.ExitStack()
            with sth:
                w2h = sb(f"w2h_s{half}", [P, (NFF // 2) * D], BF16, sth)
                h1h = sb(f"h1h{half}", [P, (NFF // 2) * TC], BF16, sth)

                @B.sec("gpsimd")
                def _(c):
                    c.wait(B.ev("T"), B.ev("V"), B.ev("S"))
                    w2part = dram["w_ffn2"][half * (DFF // 2):(half + 1) * (DFF // 2), :]
                    c.dma(w2h[:].rearrange("p (k m) -> p k m", k=NFF // 2),
                          w2part.rearrange("(k p) m -> p k m", p=P), "gA")
                wload2 = B.ev("gA")

                for tc in range(NT):
                    lo = tc * TC

                    @B.sec("sync")
                    def _(c):
                        c.wait(B.ev("T"), B.ev("yS"))
                        c.dma(h1h[:].rearrange("p (k t) -> p k t", k=NFF // 2),
                              dram["h1_sp"][:, half * (NFF // 2):(half + 1) * (NFF // 2),
                                            lo:lo + TC], "yA")
                    e_ld = B.ev("yA")

                    for oc in range(NCH):
                        pmm = (ps_m0, ps_m1, ps_m2)[oc % 3]
                        pk = "3m%d" % (oc % 3)

                        @B.sec("tensor")
                        def _(c):
                            c.wait(e_ld, wload2, B.lastev.get(pk))
                            for k in range(NFF // 2):
                                ins = c.e.matmul(
                                    pmm[:, 0:TC],
                                    w2h[:, k * D + oc * P:k * D + oc * P + P],
                                    h1h[:, k * TC:(k + 1) * TC],
                                    start=(k == 0), stop=(k == NFF // 2 - 1))
                            c.inc(ins)
                        e_mm = B.ev("T")

                        @B.sec("vector")
                        def _(c):
                            c.wait(e_mm)
                            dst = R[:, oc * T + lo:oc * T + lo + TC]
                            if half == 0:
                                B.lastev[pk] = c.chain(
                                    c.e.tensor_tensor(dst, dst, pmm[:, 0:TC],
                                                      ALU.add))
                            else:
                                B.lastev[pk] = c.chain(
                                    c.e.scalar_tensor_tensor(
                                        dst, pmm[:, 0:TC], b2[:, oc:oc + 1], dst,
                                        ALU.add, ALU.add))
    B.barrier()


def phase4_moe(B, nc, l0, dram, env):
    R = env["R"]
    comb_fm = env["comb_fm"]
    sel16_b = env["sel16_b"]
    TC = 512
    NT = T // TC

    st = contextlib.ExitStack()
    with st:
        def sb(name, shape, dtype, stack=None):
            return (stack or st).enter_context(nc.sbuf_tensor(name, shape, dtype))

        def ps(name, shape, dtype=F32):
            return st.enter_context(nc.psum_tensor(name, shape, dtype))

        n0r = sb("n0r", [P, NCH * T], BF16)
        w13d = [sb("w13b0", [P, NCH * 2 * INTER], BF16),
                sb("w13b1", [P, NCH * 2 * INTER], BF16)]
        w2b = sb("w2b", [P, (INTER // P) * D], BF16)
        hgd = [sb("hg0", [P, 4 * TC], BF16), sb("hg1", [P, 4 * TC], BF16)]
        hud = [sb("hu0", [P, 4 * TC], BF16), sb("hu1", [P, 4 * TC], BF16)]
        cbb = sb("cbb", [P, TC], BF16)
        tmp = sb("tmp4", [P, TC], BF16)
        ymoe = sb("ymoe", [P, NCH * T], BF16)

        ps_s0 = ps("ps4_s0", [P, 512])
        ps_s1 = ps("ps4_s1", [P, 512])
        ps_s2 = ps("ps4_s2", [P, 512])
        ps_t0 = ps("ps4_t0", [P, 512])
        ps_t1 = ps("ps4_t1", [P, 512])
        ps_t2 = ps("ps4_t2", [P, 512])
        ps_cb = ps("ps4_cb", [P, 512])

        @B.sec("sync")
        def _(c):
            c.wait(B.ev("V"), B.ev("T"), B.ev("S"))
            c.dma(n0r[:].rearrange("p (k t) -> p k t", k=NCH),
                  dram["n0_sp"][:, :, :], "yA")
        e_n0 = B.ev("yA")

        for e in range(NEXP + 1):
            shared = (e == NEXP)
            par_e = e % 2
            w13b = w13d[par_e]
            qn = "gA%d" % par_e

            @B.sec("gpsimd")
            def _(c):
                c.wait(B.lastev.get("4wT%d" % par_e))
                src13 = dram["w13sT"] if shared else dram["w13T"][e]
                c.dma(w13b[:].rearrange("p (k m) -> p k m", k=NCH),
                      src13.rearrange("(k p) m -> p k m", p=P), qn)
            e_w = B.ev(qn)

            @B.sec("gpsimd")
            def _(c):
                c.wait(B.ev("T"))
                src2 = dram["w2sT"] if shared else dram["w2T"][e]
                c.dma(w2b[:].rearrange("p (k m) -> p k m", k=INTER // P),
                      src2.rearrange("(k p) m -> p k m", p=P), qn)
            e_w2 = B.ev(qn)

            hh_ev = {}

            def emit_front(tc):
                lo = tc * TC
                par = tc % 2
                hg = hgd[par]
                hu = hud[par]

                e_cb = None
                if not shared:
                    @B.sec("tensor")
                    def _(c):
                        c.wait(B.lastev.get("4cbe"))
                        c.inc(c.e.matmul(ps_cb[:, 0:TC],
                                         sel16_b[:, e * P:(e + 1) * P],
                                         comb_fm[:, lo:lo + TC],
                                         start=True, stop=True))
                    e_cbm = B.ev("T")

                    @B.sec("scalar")
                    def _(c):
                        c.wait(e_cbm, B.lastev.get("4cbu"))
                        B.lastev["4cbe"] = c.inc(
                            c.e.activation(cbb[:], ps_cb[:, 0:TC], AF.Identity))
                    e_cb = B.ev("S")

                for og in range(8):
                    pmm = (ps_s0, ps_s1, ps_s2)[og % 3]
                    pk = "4s%d" % (og % 3)

                    @B.sec("tensor")
                    def _(c):
                        c.wait(e_w, e_n0, B.lastev.get(pk))
                        for k in range(NCH):
                            ins = c.e.matmul(
                                pmm[:, 0:TC],
                                w13b[:, k * (2 * INTER) + og * P:
                                     k * (2 * INTER) + og * P + P],
                                n0r[:, k * T + lo:k * T + lo + TC],
                                start=(k == 0), stop=(k == NCH - 1))
                        c.inc(ins)
                    e_s1 = B.ev("T")

                    @B.sec("scalar")
                    def _(c):
                        c.wait(e_s1)
                        if og < 4:
                            B.lastev[pk] = c.inc(
                                c.e.activation(hg[:, og * TC:(og + 1) * TC],
                                               pmm[:, 0:TC], AF.Silu))
                        else:
                            B.lastev[pk] = c.inc(
                                c.e.activation(hu[:, (og - 4) * TC:(og - 3) * TC],
                                               pmm[:, 0:TC], AF.Identity))
                e_guev = B.ev("S")

                @B.sec("vector")
                def _(c):
                    c.wait(e_guev)
                    if not shared:
                        c.wait(e_cb)
                    ev = c.chain(c.e.tensor_tensor(hg[:], hg[:], hu[:], ALU.mult))
                    if not shared:
                        cb3 = cbb[:].unsqueeze(1).broadcast_to([P, 4, TC])
                        ev = c.chain(c.e.tensor_tensor(
                            hg[:].rearrange("p (k t) -> p k t", k=4),
                            hg[:].rearrange("p (k t) -> p k t", k=4),
                            cb3, ALU.mult))
                    B.lastev["4cbu"] = ev
                hh_ev[tc] = B.ev("V")

            def emit_back(tc):
                lo = tc * TC
                hg = hgd[tc % 2]

                for oc in range(NCH):
                    pmm = (ps_t0, ps_t1, ps_t2)[oc % 3]
                    pk = "4t%d" % (oc % 3)

                    @B.sec("tensor")
                    def _(c):
                        c.wait(hh_ev[tc], B.lastev.get(pk), e_w2)
                        for k in range(INTER // P):
                            ins = c.e.matmul(
                                pmm[:, 0:TC],
                                w2b[:, k * D + oc * P:k * D + oc * P + P],
                                hg[:, k * TC:(k + 1) * TC],
                                start=(k == 0), stop=(k == INTER // P - 1))
                        c.inc(ins)
                    e_s2 = B.ev("T")

                    if e == 0:
                        @B.sec("scalar")
                        def _(c):
                            c.wait(e_s2)
                            B.lastev[pk] = c.inc(c.e.activation(
                                ymoe[:, oc * T + lo:oc * T + lo + TC],
                                pmm[:, 0:TC], AF.Identity))
                    else:
                        @B.sec("scalar")
                        def _(c):
                            c.wait(e_s2, B.lastev.get("4tmp"))
                            B.lastev[pk] = c.inc(
                                c.e.activation(tmp[:], pmm[:, 0:TC], AF.Identity))
                        e_t = B.ev("S")

                        @B.sec("vector")
                        def _(c):
                            c.wait(e_t)
                            dst = ymoe[:, oc * T + lo:oc * T + lo + TC]
                            B.lastev["4tmp"] = c.chain(
                                c.e.tensor_tensor(dst, dst, tmp[:], ALU.add))

            for tc in range(NT + 1):
                if tc < NT:
                    emit_front(tc)
                if tc >= 1:
                    emit_back(tc - 1)
            B.lastev["4wT%d" % par_e] = B.ev("T")

        for tc in range(NT):
            lo = tc * TC

            @B.sec("vector")
            def _(c):
                c.wait(B.ev("S"))
                for oc in range(NCH):
                    dst = R[:, oc * T + lo:oc * T + lo + TC]
                    c.chain(c.e.tensor_tensor(
                        dst, dst, ymoe[:, oc * T + lo:oc * T + lo + TC], ALU.add))
    B.barrier()


def phase5_l1(B, nc, l1, dram, env):
    R = env["R"]
    onescol_b = env["onescol_b"]; onesrow_f = env["onesrow_f"]
    ident_b = env["ident_b"]; tri_f = env["tri_f"]

    TC = 512
    NT = T // TC
    st = contextlib.ExitStack()
    with st:
        def sb(name, shape, dtype, stack=None):
            return (stack or st).enter_context(nc.sbuf_tensor(name, shape, dtype))

        def psn(name, shape, dtype=F32, stack=None):
            return (stack or st).enter_context(nc.psum_tensor(name, shape, dtype))

        sta = contextlib.ExitStack()
        with sta:
            lg = sb("lg5", [P, NCH], F32, sta)
            lb = sb("lb5", [P, NCH], F32, sta)
            xb5 = sb("xb5", [P, NCH * TC], BF16, sta)
            xq5 = sb("xq5", [P, NCH * TC], BF16, sta)
            n1c = sb("n1c", [P, NCH * TC], BF16, sta)
            xnc = sb("xnc", [P, NCH * TC], BF16, sta)
            rows5 = sb("rows5", [1, 3 * TC], F32, sta)
            bc5 = sb("bc5", [P, 3 * TC], F32, sta)
            tf5 = sb("tf5", [P, TC], F32, sta)
            ps_r0 = psn("ps5_r0", [1, 512], F32, sta)
            ps_r1 = psn("ps5_r1", [1, 512], F32, sta)
            ps_b0 = psn("ps5_b0", [P, 512], F32, sta)
            ps_b1 = psn("ps5_b1", [P, 512], F32, sta)

            @B.sec("sync")
            def _(c):
                c.dma(lg[:], dram["l1_g"][:, :], "yM")
                c.dma(lb[:], dram["l1_b"][:, :], "yM")
            sload = B.ev("yM")

            for tc in range(NT):
                lo = tc * TC

                @B.sec("scalar")
                def _(c):
                    c.wait(B.ev("T"))
                    for k in range(NCH):
                        c.e.activation(xb5[:, k * TC:(k + 1) * TC],
                                       R[:, k * T + lo:k * T + lo + TC],
                                       AF.Identity)
                        ins = c.e.activation(xq5[:, k * TC:(k + 1) * TC],
                                             R[:, k * T + lo:k * T + lo + TC],
                                             AF.Square)
                    c.inc(ins)
                e_c = B.ev("S")

                @B.sec("tensor")
                def _(c):
                    c.wait(e_c, B.ev("S"))
                    for k in range(NCH):
                        c.e.matmul(ps_r0[0:1, 0:TC], onescol_b[:],
                                   xb5[:, k * TC:(k + 1) * TC],
                                   start=(k == 0), stop=(k == NCH - 1))
                    for k in range(NCH):
                        ins = c.e.matmul(ps_r1[0:1, 0:TC], onescol_b[:],
                                         xq5[:, k * TC:(k + 1) * TC],
                                         start=(k == 0), stop=(k == NCH - 1))
                    c.inc(ins)
                e_mm = B.ev("T")

                @B.sec("scalar")
                def _(c):
                    c.wait(e_mm)
                    c.e.activation(rows5[:, 0:TC], ps_r0[0:1, 0:TC], AF.Identity)
                    c.chain(c.e.activation(rows5[:, TC:2 * TC], ps_r0[0:1, 0:TC],
                                           AF.Square, scale=1.0 / D))
                e_r = B.ev("S")

                @B.sec("vector")
                def _(c):
                    c.wait(e_r)
                    c.chain(c.e.scalar_tensor_tensor(
                        rows5[:, TC:2 * TC], ps_r1[0:1, 0:TC], 1.0 / D,
                        rows5[:, TC:2 * TC], ALU.mult, ALU.subtract))
                e_v1 = B.ev("V")

                @B.sec("scalar")
                def _(c):
                    c.wait(e_v1)
                    c.chain(c.e.activation(rows5[:, 2 * TC:3 * TC],
                                           rows5[:, TC:2 * TC], AF.Ln, bias=EPS))
                    c.chain(c.e.activation(rows5[:, TC:2 * TC],
                                           rows5[:, 2 * TC:3 * TC], AF.Exp,
                                           scale=-0.5))
                e_s1 = B.ev("S")

                @B.sec("vector")
                def _(c):
                    c.wait(e_s1)
                    c.chain(c.e.tensor_tensor(rows5[:, 2 * TC:3 * TC],
                                              rows5[:, TC:2 * TC],
                                              rows5[:, TC:2 * TC], ALU.mult))
                    c.chain(c.e.tensor_scalar(rows5[:, 2 * TC:3 * TC],
                                              rows5[:, 2 * TC:3 * TC], -EPS, 1.0,
                                              ALU.mult, ALU.add))
                e_v2 = B.ev("V")

                @B.sec("scalar")
                def _(c):
                    c.wait(e_v2)
                    c.chain(c.e.activation(rows5[:, 2 * TC:3 * TC],
                                           rows5[:, 2 * TC:3 * TC], AF.Ln,
                                           bias=EPS))
                    c.chain(c.e.activation(rows5[:, 2 * TC:3 * TC],
                                           rows5[:, 2 * TC:3 * TC], AF.Exp,
                                           scale=-0.5))
                e_s2 = B.ev("S")

                @B.sec("vector")
                def _(c):
                    c.wait(e_s2)
                    c.chain(c.e.tensor_tensor(rows5[:, 2 * TC:3 * TC],
                                              rows5[:, 2 * TC:3 * TC],
                                              rows5[:, TC:2 * TC], ALU.mult))
                e_rows = B.ev("V")

                @B.sec("tensor")
                def _(c):
                    c.wait(e_rows, B.ev("S"))
                    c.e.matmul(ps_b0[:, 0:TC], onesrow_f[:], rows5[:, 0:TC],
                               start=True, stop=True)
                    ins = c.e.matmul(ps_b1[:, 0:TC], onesrow_f[:],
                                     rows5[:, TC:2 * TC], start=True, stop=True)
                    c.inc(ins)
                e_b1 = B.ev("T")

                @B.sec("scalar")
                def _(c):
                    c.wait(e_b1)
                    c.e.activation(bc5[:, 0:TC], ps_b0[:, 0:TC], AF.Identity,
                                   scale=1.0 / D)
                    c.inc(c.e.activation(bc5[:, TC:2 * TC], ps_b1[:, 0:TC],
                                         AF.Identity))
                e_b1e = B.ev("S")

                @B.sec("tensor")
                def _(c):
                    c.wait(e_b1e)
                    c.inc(c.e.matmul(ps_b0[:, 0:TC], onesrow_f[:],
                                     rows5[:, 2 * TC:3 * TC], start=True,
                                     stop=True))
                e_b2 = B.ev("T")

                @B.sec("scalar")
                def _(c):
                    c.wait(e_b2)
                    c.inc(c.e.activation(bc5[:, 2 * TC:3 * TC], ps_b0[:, 0:TC],
                                         AF.Identity))
                e_bc = B.ev("S")

                @B.sec("vector")
                def _(c):
                    c.wait(e_bc, B.ev("T"), B.ev("yS"))
                    for k in range(NCH):
                        c.chain(c.e.tensor_tensor(tf5[:],
                                                  R[:, k * T + lo:k * T + lo + TC],
                                                  bc5[:, 0:TC], ALU.subtract))
                        c.e.tensor_tensor(n1c[:, k * TC:(k + 1) * TC], tf5[:],
                                          bc5[:, TC:2 * TC], ALU.mult)
                        c.chain(c.e.tensor_tensor(xnc[:, k * TC:(k + 1) * TC],
                                                  tf5[:], bc5[:, 2 * TC:3 * TC],
                                                  ALU.mult))
                e_n = B.ev("V")

                @B.sec("scalar")
                def _(c):
                    c.wait(e_n, sload)
                    for k in range(NCH):
                        ins = c.e.activation(xnc[:, k * TC:(k + 1) * TC],
                                             xnc[:, k * TC:(k + 1) * TC],
                                             AF.Identity, scale=lg[:, k:k + 1],
                                             bias=lb[:, k:k + 1])
                    c.inc(ins)
                e_x = B.ev("S")

                @B.sec("sync")
                def _(c):
                    c.wait(e_x, e_n)
                    c.dma(dram["n1_sp"][:, :, lo:lo + TC],
                          n1c[:].rearrange("p (k t) -> p k t", k=NCH), "yS")
                    c.dma(dram["xn_sp"][:, :, lo:lo + TC],
                          xnc[:].rearrange("p (k t) -> p k t", k=NCH), "yS")
            B.barrier()

        stb = contextlib.ExitStack()
        with stb:
            wq = sb("wq5", [P, NCH * D], BF16, stb)
            wk = sb("wk5", [P, NCH * D], BF16, stb)
            wv = sb("wv5", [P, NCH * D], BF16, stb)
            rot_b = sb("rot_b", [P, P], BF16, stb)
            cosq = sb("cosq", [P, T], BF16, stb)
            sinq = sb("sinq", [P, T], BF16, stb)
            cosk = sb("cosk", [P, T], BF16, stb)
            sink = sb("sink", [P, T], BF16, stb)
            xnb = sb("xnb", [P, NCH * TC], BF16, stb)
            qbuf = sb("qbuf", [P, NH * TC], BF16, stb)
            vtb = sb("vtb", [P, NH * TC], BF16, stb)
            qt5 = sb("qt5", [P, TC], BF16, stb)
            rt5 = sb("rt5", [P, TC], BF16, stb)
            m15 = sb("m15", [P, TC], BF16, stb)
            ps_p0 = psn("ps5b_0", [P, 512], F32, stb)
            ps_p1 = psn("ps5b_1", [P, 512], F32, stb)
            ps_p2 = psn("ps5b_2", [P, 512], F32, stb)
            ps_rt = psn("ps5b_rt", [P, 512], F32, stb)
            ps_vt = psn("ps5b_vt", [P, 1024], BF16, stb)

            @B.sec("gpsimd")
            def _(c):
                c.wait(B.ev("T"), B.ev("V"), B.ev("S"))
                c.dma(wq[:].rearrange("p (k m) -> p k m", k=NCH),
                      dram["wqT"].rearrange("(k p) m -> p k m", p=P), "gA")
                c.dma(wk[:].rearrange("p (k m) -> p k m", k=NCH),
                      dram["wkT"].rearrange("(k p) m -> p k m", p=P), "gA")
                c.dma(wv[:].rearrange("p (k m) -> p k m", k=NCH),
                      dram["wvT"].rearrange("(k p) m -> p k m", p=P), "gA")
                c.dma(rot_b[:], dram["rot_sT"][:, :], "gA")
                c.dma(cosq[:], dram["cos_q"][:, :], "gA")
                c.dma(sinq[:], dram["sin_q"][:, :], "gA")
                c.dma(cosk[:], dram["cos_k"][:, :], "gA")
                c.dma(sink[:], dram["sin_k"][:, :], "gA")
            wload = B.ev("gA")

            for tc in range(NT):
                lo = tc * TC

                @B.sec("sync")
                def _(c):
                    c.wait(B.ev("T"), B.ev("yS"))
                    c.dma(xnb[:].rearrange("p (k t) -> p k t", k=NCH),
                          dram["xn_sp"][:, :, lo:lo + TC], "yA")
                e_ld = B.ev("yA")

                for which, wmat, ctab, stab, spname in (
                        ("q", wq, cosq, sinq, "q_sp"),
                        ("k", wk, cosk, sink, "k_sp")):
                    for h in range(NH):
                        pmm = (ps_p0, ps_p1, ps_p2)[h % 3]
                        pk = "5p%d" % (h % 3)

                        @B.sec("tensor")
                        def _(c):
                            c.wait(e_ld, wload, B.lastev.get(pk))
                            for k in range(NCH):
                                ins = c.e.matmul(
                                    pmm[:, 0:TC],
                                    wmat[:, k * D + h * P:k * D + h * P + P],
                                    xnb[:, k * TC:(k + 1) * TC],
                                    start=(k == 0), stop=(k == NCH - 1))
                            c.inc(ins)
                        e_p = B.ev("T")

                        @B.sec("vector")
                        def _(c):
                            c.wait(e_p, B.lastev.get("5qt"))
                            B.lastev[pk] = c.inc(
                                c.e.tensor_copy(qt5[:], pmm[:, 0:TC]))
                        e_qt = B.ev("V")

                        @B.sec("tensor")
                        def _(c):
                            c.wait(e_qt, B.lastev.get("5rt"))
                            c.inc(c.e.matmul(ps_rt[:, 0:TC], rot_b[:], qt5[:],
                                             start=True, stop=True))
                        e_rt = B.ev("T")

                        @B.sec("vector")
                        def _(c):
                            c.wait(e_rt, B.lastev.get("5rte"))
                            B.lastev["5rt"] = c.inc(
                                c.e.tensor_copy(rt5[:], ps_rt[:, 0:TC]))
                        e_rte = B.ev("V")

                        @B.sec("vector")
                        def _(c):
                            c.wait(e_rte, B.ev("yS"))
                            ev = c.chain(c.e.tensor_tensor(m15[:], qt5[:],
                                                           ctab[:, lo:lo + TC],
                                                           ALU.mult))
                            B.lastev["5qt"] = ev
                            c.chain(c.e.tensor_tensor(rt5[:], rt5[:],
                                                      stab[:, lo:lo + TC],
                                                      ALU.mult))
                            B.lastev["5rte"] = c.chain(c.e.tensor_tensor(
                                qbuf[:, h * TC:(h + 1) * TC], m15[:], rt5[:],
                                ALU.add))
                    e_rope = B.ev("V")

                    @B.sec("sync")
                    def _(c):
                        c.wait(e_rope)
                        c.dma(dram[spname][:, :, lo:lo + TC],
                              qbuf[:].rearrange("p (h t) -> p h t", h=NH), "yS")

                for h in range(NH):
                    pmm = (ps_p0, ps_p1, ps_p2)[h % 3]
                    pk = "5p%d" % (h % 3)

                    @B.sec("tensor")
                    def _(c):
                        c.wait(e_ld, wload, B.lastev.get(pk))
                        for k in range(NCH):
                            ins = c.e.matmul(
                                pmm[:, 0:TC],
                                wv[:, k * D + h * P:k * D + h * P + P],
                                xnb[:, k * TC:(k + 1) * TC],
                                start=(k == 0), stop=(k == NCH - 1))
                        c.inc(ins)
                    e_p = B.ev("T")

                    @B.sec("vector")
                    def _(c):
                        c.wait(e_p, B.ev("T"), B.lastev.get("5qt"))
                        B.lastev[pk] = c.inc(
                            c.e.tensor_copy(qt5[:], pmm[:, 0:TC]))
                    e_v = B.ev("V")

                    for j in range(TC // P):
                        @B.sec("tensor")
                        def _(c):
                            c.wait(e_v, B.lastev.get("5vt"))
                            c.inc(c.e.transpose(ps_vt[:, 0:P],
                                                qt5[:, j * P:(j + 1) * P],
                                                ident_b[:]))
                        e_t = B.ev("T")

                        @B.sec("vector")
                        def _(c):
                            c.wait(e_t, B.ev("yS"))
                            B.lastev["5vt"] = c.inc(c.e.tensor_copy(
                                vtb[:, h * TC + j * P:h * TC + (j + 1) * P],
                                ps_vt[:, 0:P]))
                e_vt = B.ev("V")

                @B.sec("sync")
                def _(c):
                    c.wait(e_vt, B.ev("S"))
                    c.dma(dram["vt_sp"][:, :, tc * (TC // P):(tc + 1) * (TC // P), :],
                          vtb[:].rearrange("p (h j q) -> p h j q", h=NH, j=TC // P),
                          "yS")
            B.barrier()

        stc = contextlib.ExitStack()
        with stc:
            dirs_b = sb("dirs_b", [P, NH], BF16, stc)
            hs_all = sb("hs_all", [P, NH], F32, stc)
            om_all = sb("om_all", [P, NH], F32, stc)
            hsr = sb("hsr", [1, NH], F32, stc)
            aout = sb("aout", [P, NCH * T], BF16, stc)
            stc2 = stc.enter_context(contextlib.ExitStack())
            qh = sb("qh", [P, T], BF16, stc2)
            kh = sb("kh", [P, T], BF16, stc2)
            vth = sb("vth", [P, (T // P) * P], BF16, stc2)
            qp_b = sb("qp_b", [1, T], BF16, stc2)
            kp_b = sb("kp_b", [1, T], BF16, stc2)
            attn = sb("attn", [P, T], BF16, stc2)
            scd = [sb("sc0", [P, T], BF16, stc2), sb("sc1", [P, T], BF16, stc2)]
            gsd = [sb("gs0", [P, T], BF16, stc2), sb("gs1", [P, T], BF16, stc2)]
            at0 = sb("at0", [P, 4 * P], BF16, stc2)
            at1 = sb("at1", [P, 4 * P], BF16, stc2)
            nmax = sb("nmax", [P, 1], F32, stc2)
            sums = sb("sums", [P, 1], F32, stc2)
            rsum = sb("rsum", [P, 1], F32, stc2)
            cmb5 = sb("cmb5", [P, P], F32, stc2)
            ps_sc2 = [psn("ps5c_sca", [P, 512], F32, stc2),
                      psn("ps5c_scb", [P, 512], F32, stc2)]
            ps_t0 = psn("ps5c_t0", [P, 1024], BF16, stc2)
            ps_t1 = psn("ps5c_t1", [P, 1024], BF16, stc2)
            ps_av = psn("ps5c_av", [P, 512], F32, stc2)
            ps_ag = psn("ps5c_ag", [P, 512], F32, stc2)

            @B.sec("gpsimd")
            def _(c):
                c.wait(B.ev("T"), B.ev("V"), B.ev("S"))
                c.dma(dirs_b[:], dram["dirs"][:, :], "gA")
                c.dma(hsr[:], dram["hscale"][:, :], "gA")
            e_dl = B.ev("gA")

            @B.sec("tensor")
            def _(c):
                c.wait(e_dl, B.ev("S"), B.ev("V"))
                c.inc(c.e.matmul(ps_av[0:P, 0:NH], onesrow_f[:], hsr[:, :],
                                 start=True, stop=True))
            e_hsm = B.ev("T")

            @B.sec("scalar")
            def _(c):
                c.wait(e_hsm)
                c.inc(c.e.activation(hs_all[:], ps_av[0:P, 0:NH], AF.Identity))
            e_hs = B.ev("S")

            @B.sec("vector")
            def _(c):
                c.wait(e_hs)
                c.chain(c.e.tensor_scalar(om_all[:], hs_all[:], -1.0, 1.0,
                                          ALU.mult, ALU.add))
            e_om = B.ev("V")

            for h in range(NH):
                @B.sec("sync")
                def _(c):
                    c.wait(B.ev("T"), B.ev("V"), B.ev("S"), B.ev("yS"))
                    c.dma(qh[:], dram["q_sp"][:, h, :], "yA")
                    c.dma(kh[:], dram["k_sp"][:, h, :], "yA")
                    c.dma(vth[:].rearrange("p (j q) -> p j q", j=T // P),
                          dram["vt_sp"][:, h, :, :], "yA")
                e_hl = B.ev("yA")

                for j in range(T // 512):
                    @B.sec("tensor")
                    def _(c):
                        c.wait(e_hl, e_om, B.ev("S"))
                        c.e.matmul(ps_av[0:1, 0:512], dirs_b[:, h:h + 1],
                                   qh[:, j * 512:(j + 1) * 512],
                                   start=True, stop=True)
                        c.inc(c.e.matmul(ps_ag[0:1, 0:512], dirs_b[:, h:h + 1],
                                         kh[:, j * 512:(j + 1) * 512],
                                         start=True, stop=True))
                    e_qpm = B.ev("T")

                    @B.sec("scalar")
                    def _(c):
                        c.wait(e_qpm)
                        c.e.activation(qp_b[:, j * 512:(j + 1) * 512],
                                       ps_av[0:1, 0:512], AF.Identity,
                                       scale=float(np.sqrt(HD)))
                        c.inc(c.e.activation(kp_b[:, j * 512:(j + 1) * 512],
                                             ps_ag[0:1, 0:512], AF.Identity))
                e_qp = B.ev("S")

                def scores_stage(qt):
                    par = qt % 2
                    kmax = (qt + 1) * P
                    nch_k = (kmax + 511) // 512
                    for which, lhs, dstb in (("m", qh, scd[par]),
                                             ("g", qp_b, gsd[par])):
                        for j in range(nch_k):
                            ke = min(kmax, (j + 1) * 512)
                            pb = ps_sc2[j % 2]
                            pkey = "5sb%d" % (j % 2)

                            @B.sec("tensor")
                            def _(c):
                                c.wait(e_hl, e_qp, B.lastev.get(pkey),
                                       B.lastev.get("5msk%s%d" % (which, par)))
                                if which == "m":
                                    ins = c.e.matmul(pb[:, 0:ke - j * 512],
                                                     qh[:, qt * P:(qt + 1) * P],
                                                     kh[:, j * 512:ke],
                                                     start=True, stop=True)
                                else:
                                    ins = c.e.matmul(pb[:, 0:ke - j * 512],
                                                     qp_b[:, qt * P:(qt + 1) * P],
                                                     kp_b[:, j * 512:ke],
                                                     start=True, stop=True)
                                c.inc(ins)
                            e_mm = B.ev("T")

                            @B.sec("scalar")
                            def _(c):
                                c.wait(e_mm)
                                B.lastev[pkey] = c.inc(c.e.activation(
                                    dstb[:, j * 512:ke], pb[:, 0:ke - j * 512],
                                    AF.Identity))
                    return B.ev("S")

                def softmax_av_stage(qt, e_sc):
                    par = qt % 2
                    kmax = (qt + 1) * P
                    e_avm = None
                    e_agm = None
                    for which, srcb, ps_out in (("m", scd[par], ps_av),
                                                ("g", gsd[par], ps_ag)):
                        @B.sec("vector")
                        def _(c):
                            c.wait(e_sc, B.lastev.get("5at"), B.ev("S"))
                            c.chain(c.e.tensor_tensor(
                                srcb[:, qt * P:(qt + 1) * P],
                                srcb[:, qt * P:(qt + 1) * P], tri_f[:],
                                ALU.add))
                            ev = c.chain(c.e.reduce_max(nmax[:], srcb[:, 0:kmax],
                                                        axis=AX.X, negate=True))
                            B.lastev["5msk%s%d" % (which, par)] = ev
                        e_mx = B.ev("V")

                        @B.sec("scalar")
                        def _(c):
                            c.wait(e_mx, B.ev("T"))
                            c.chain(c.e.activation(attn[:, 0:kmax],
                                                   srcb[:, 0:kmax], AF.Exp,
                                                   bias=nmax[:, 0:1],
                                                   accum_out=sums[:]))
                        e_ex = B.ev("S")

                        @B.sec("vector")
                        def _(c):
                            c.wait(e_ex)
                            c.chain(c.e.reciprocal(rsum[:], sums[:]))
                            c.chain(c.e.tensor_scalar(attn[:, 0:kmax],
                                                      attn[:, 0:kmax],
                                                      rsum[:, 0:1], None,
                                                      ALU.mult))
                        e_at = B.ev("V")
                        B.lastev["5at"] = e_at

                        groups = [(g, min(4, qt + 1 - g))
                                  for g in range(0, qt + 1, 4)]

                        def emit_tr(gi):
                            g, cnt = groups[gi]
                            pst = ps_t0 if (gi % 2 == 0) else ps_t1
                            tk = "5t%d" % (gi % 2)

                            @B.sec("tensor")
                            def _(c):
                                c.wait(e_at, B.lastev.get(tk))
                                for i in range(cnt):
                                    c.inc(c.e.transpose(
                                        pst[:, i * P:(i + 1) * P],
                                        attn[:, (g + i) * P:(g + i + 1) * P],
                                        ident_b[:]))
                            e_tr = B.ev("T")

                            @B.sec("vector")
                            def _(c):
                                c.wait(e_tr)
                                B.lastev[tk] = c.inc(c.e.tensor_copy(
                                    (at0 if gi % 2 == 0 else at1)[:, 0:cnt * P],
                                    pst[:, 0:cnt * P]))
                            return B.ev("V")

                        def emit_av(gi, e_te):
                            g, cnt = groups[gi]
                            atb = at0 if (gi % 2 == 0) else at1

                            @B.sec("tensor")
                            def _(c):
                                c.wait(e_te)
                                for i in range(cnt):
                                    c.inc(c.e.matmul(
                                        ps_out[:, 0:P],
                                        vth[:, (g + i) * P:(g + i + 1) * P],
                                        atb[:, i * P:(i + 1) * P],
                                        start=(g + i == 0),
                                        stop=(g + i == qt)))

                        pg = None
                        for gi in range(len(groups)):
                            e_te = emit_tr(gi)
                            if pg is not None:
                                emit_av(pg[0], pg[1])
                            pg = (gi, e_te)
                        emit_av(pg[0], pg[1])
                        if which == "m":
                            e_avm = B.ev("T")
                        else:
                            e_agm = B.ev("T")

                    @B.sec("scalar")
                    def _(c):
                        c.wait(e_agm, B.lastev.get("5cmb"))
                        c.inc(c.e.activation(cmb5[:], ps_ag[:, 0:P], AF.Identity,
                                             scale=hs_all[:, h:h + 1]))
                    e_cg = B.ev("S")

                    @B.sec("vector")
                    def _(c):
                        c.wait(e_cg, e_avm)
                        B.lastev["5cmb"] = c.chain(c.e.scalar_tensor_tensor(
                            aout[:, h * T + qt * P:h * T + (qt + 1) * P],
                            ps_av[:, 0:P], om_all[:, h:h + 1], cmb5[:],
                            ALU.mult, ALU.add))

                prev = None
                for qt in range(T // P):
                    e_sc = scores_stage(qt)
                    if prev is not None:
                        softmax_av_stage(prev[0], prev[1])
                    prev = (qt, e_sc)
                softmax_av_stage(prev[0], prev[1])
            B.barrier()
            stc2.close()

            std = contextlib.ExitStack()
            with std:
                wo = sb("wo5", [P, NCH * D], BF16, std)
                n1b5 = sb("n1b5", [P, NCH * TC], BF16, std)
                tfo = sb("tfo", [P, TC], F32, std)
                ps_d0 = psn("ps5d_0", [P, 512], F32, std)
                ps_d1 = psn("ps5d_1", [P, 512], F32, std)

                @B.sec("gpsimd")
                def _(c):
                    c.dma(wo[:].rearrange("p (k m) -> p k m", k=NCH),
                          dram["woT"].rearrange("(k p) m -> p k m", p=P), "gA")
                wlo = B.ev("gA")

                for tc in range(NT):
                    lo = tc * TC

                    @B.sec("sync")
                    def _(c):
                        c.wait(B.ev("V"))
                        c.dma(n1b5[:].rearrange("p (k t) -> p k t", k=NCH),
                              dram["n1_sp"][:, :, lo:lo + TC], "yA")
                    e_ld = B.ev("yA")

                    for oc in range(NCH):
                        pmm = ps_d0 if (oc % 2 == 0) else ps_d1

                        @B.sec("tensor")
                        def _(c):
                            c.wait(wlo, B.ev("V"))
                            for k in range(NCH):
                                ins = c.e.matmul(
                                    pmm[:, 0:TC],
                                    wo[:, k * D + oc * P:k * D + oc * P + P],
                                    aout[:, k * T + lo:k * T + lo + TC],
                                    start=(k == 0), stop=(k == NCH - 1))
                            c.inc(ins)
                        e_mm = B.ev("T")

                        @B.sec("vector")
                        def _(c):
                            c.wait(e_mm, e_ld)
                            c.chain(c.e.tensor_tensor(
                                tfo[:], pmm[:, 0:TC],
                                n1b5[:, oc * TC:(oc + 1) * TC], ALU.add))
                            c.chain(c.e.tensor_tensor(
                                R[:, oc * T + lo:oc * T + lo + TC],
                                R[:, oc * T + lo:oc * T + lo + TC],
                                tfo[:], ALU.add))
            B.barrier()


        ste = contextlib.ExitStack()
        with ste:
            wr = sb("wr5", [P, NCH * D], BF16, ste)
            wk2 = sb("wk25", [P, NCH * D], BF16, ste)
            wv2 = sb("wv25", [P, NCH * D], BF16, ste)
            mur = sb("mur5", [P, NCH], F32, ste)
            muk = sb("muk5", [P, NCH], F32, ste)
            n1x = sb("n1x", [P, NCH * (TC + 1)], BF16, ste)
            xr = sb("xr5", [P, NCH * TC], BF16, ste)
            xk = sb("xk5", [P, NCH * TC], BF16, ste)
            rb5 = sb("rb5", [P, NCH * TC], BF16, ste)
            kb5 = sb("kb5", [P, NCH * TC], BF16, ste)
            td5 = sb("td5", [P, TC], BF16, ste)
            tf6 = sb("tf6", [P, TC], F32, ste)
            ps_e0 = psn("ps5e_0", [P, 512], F32, ste)
            ps_e1 = psn("ps5e_1", [P, 512], F32, ste)
            ps_e2 = psn("ps5e_2", [P, 512], F32, ste)

            @B.sec("gpsimd")
            def _(c):
                c.wait(B.ev("T"), B.ev("V"), B.ev("S"))
                c.dma(wr[:].rearrange("p (k m) -> p k m", k=NCH),
                      dram["wrT"].rearrange("(k p) m -> p k m", p=P), "gA")
                c.dma(wk2[:].rearrange("p (k m) -> p k m", k=NCH),
                      dram["wkT2"].rearrange("(k p) m -> p k m", p=P), "gA")
                c.dma(wv2[:].rearrange("p (k m) -> p k m", k=NCH),
                      dram["wvT2"].rearrange("(k p) m -> p k m", p=P), "gA")
            wle = B.ev("gA")

            @B.sec("sync")
            def _(c):
                c.dma(mur[:], dram["mu_r"][:, :], "yM")
                c.dma(muk[:], dram["mu_k"][:, :], "yM")
            sle = B.ev("yM")

            for tc in range(NT):
                lo = tc * TC

                @B.sec("vector")
                def _(c):
                    c.wait(B.ev("S"), B.ev("T"))
                    if tc == 0:
                        for k in range(NCH):
                            c.inc(c.e.memset(n1x[:, k * (TC + 1):k * (TC + 1) + 1],
                                             0.0))
                e_z = B.ev("V")

                @B.sec("sync")
                def _(c):
                    c.wait(e_z, B.ev("T"))
                    if tc == 0:
                        c.dma(n1x[:].rearrange("p (k t) -> p k t", k=NCH)
                              [:, :, 1:TC + 1],
                              dram["n1_sp"][:, :, 0:TC], "yA")
                    else:
                        c.dma(n1x[:].rearrange("p (k t) -> p k t", k=NCH),
                              dram["n1_sp"][:, :, lo - 1:lo + TC], "yA")
                e_ld = B.ev("yA")

                @B.sec("vector")
                def _(c):
                    c.wait(e_ld, sle, B.ev("T"))
                    for k in range(NCH):
                        cur = n1x[:, k * (TC + 1) + 1:(k + 1) * (TC + 1)]
                        prv = n1x[:, k * (TC + 1):(k + 1) * (TC + 1) - 1]
                        c.chain(c.e.tensor_tensor(td5[:], cur, prv, ALU.subtract))
                        c.chain(c.e.scalar_tensor_tensor(
                            xr[:, k * TC:(k + 1) * TC], td5[:], mur[:, k:k + 1],
                            prv, ALU.mult, ALU.add))
                        c.chain(c.e.scalar_tensor_tensor(
                            xk[:, k * TC:(k + 1) * TC], td5[:], muk[:, k:k + 1],
                            prv, ALU.mult, ALU.add))
                e_mix = B.ev("V")

                for which, wmat, dst, act in (("r", wr, rb5, AF.Sigmoid),
                                              ("k", wk2, kb5, AF.Relu)):
                    for oc in range(NCH):
                        pmm = (ps_e0, ps_e1, ps_e2)[oc % 3]
                        pk = "5e%d" % (oc % 3)
                        src = xr if which == "r" else xk

                        @B.sec("tensor")
                        def _(c):
                            c.wait(e_mix, wle, B.lastev.get(pk))
                            for k in range(NCH):
                                ins = c.e.matmul(
                                    pmm[:, 0:TC],
                                    wmat[:, k * D + oc * P:k * D + oc * P + P],
                                    src[:, k * TC:(k + 1) * TC],
                                    start=(k == 0), stop=(k == NCH - 1))
                            c.inc(ins)
                        e_mm = B.ev("T")

                        @B.sec("scalar")
                        def _(c):
                            c.wait(e_mm)
                            ev = c.chain(c.e.activation(
                                dst[:, oc * TC:(oc + 1) * TC], pmm[:, 0:TC], act))
                            if which == "k":
                                ev = c.chain(c.e.activation(
                                    dst[:, oc * TC:(oc + 1) * TC],
                                    dst[:, oc * TC:(oc + 1) * TC], AF.Square))
                            B.lastev[pk] = ev
                e_rk = B.ev("S")

                for oc in range(NCH):
                    pmm = (ps_e0, ps_e1, ps_e2)[oc % 3]
                    pk = "5e%d" % (oc % 3)

                    @B.sec("tensor")
                    def _(c):
                        c.wait(e_rk, B.lastev.get(pk))
                        for k in range(NCH):
                            ins = c.e.matmul(
                                pmm[:, 0:TC],
                                wv2[:, k * D + oc * P:k * D + oc * P + P],
                                kb5[:, k * TC:(k + 1) * TC],
                                start=(k == 0), stop=(k == NCH - 1))
                        c.inc(ins)
                    e_mm = B.ev("T")

                    @B.sec("vector")
                    def _(c):
                        c.wait(e_mm)
                        B.lastev[pk] = c.chain(c.e.tensor_tensor(
                            tf6[:], pmm[:, 0:TC], rb5[:, oc * TC:(oc + 1) * TC],
                            ALU.mult))
                        c.chain(c.e.tensor_tensor(
                            R[:, oc * T + lo:oc * T + lo + TC],
                            R[:, oc * T + lo:oc * T + lo + TC], tf6[:], ALU.add))
    B.barrier()



def _fm(v, nch):
    return np.ascontiguousarray(v.reshape(nch, P).T.astype(np.float32))


def stage_inputs(inputs):
    i = {k: np.asarray(v) for k, v in inputs.items()}
    sh = {}
    sh["w_inproj"] = np.ascontiguousarray(i["L0_attn_in_proj_w"], np.float32)
    cw = i["L0_attn_conv_w"].reshape(DIN, 4)
    sh["conv_w"] = np.ascontiguousarray(
        cw.reshape(NIN, P, 4).transpose(1, 0, 2).reshape(P, NIN * 4), np.float32)
    sh["conv_b"] = _fm(i["L0_attn_conv_b"], NIN)
    xp = np.zeros((DIN, 112), np.float32)
    xp[:, 0:80] = i["L0_attn_x_proj_w"][:, 0:80]
    xp[:, 96:112] = i["L0_attn_x_proj_w"][:, 80:96]
    sh["w_xproj"] = xp
    sh["w_dtproj"] = np.ascontiguousarray(i["L0_attn_dt_proj_w"], np.float32)
    sh["dt_bias"] = _fm(i["L0_attn_dt_proj_b"], NIN)
    al = i["L0_attn_A_log"]
    sh["a_log"] = np.ascontiguousarray(
        al.reshape(NIN, P, DST).transpose(1, 0, 2).reshape(P, NIN * DST), np.float32)
    sh["d_vec"] = _fm(i["L0_attn_D"], NIN)
    sh["w_oproj"] = np.ascontiguousarray(i["L0_attn_out_proj_w"], np.float32)
    sh["ln1_g"] = _fm(i["L0_attn_ln1_g"], NCH)
    sh["ln1_b"] = _fm(i["L0_attn_ln1_b"], NCH)
    sh["ln2_g"] = _fm(i["L0_attn_ln2_g"], NCH)
    sh["ln2_b"] = _fm(i["L0_attn_ln2_b"], NCH)
    sh["w_ffn1"] = np.ascontiguousarray(i["L0_attn_ffn_w1"], np.float32)
    sh["ffn_b1"] = _fm(i["L0_attn_ffn_b1"], NFF)
    sh["w_ffn2"] = np.ascontiguousarray(i["L0_attn_ffn_w2"], np.float32)
    sh["ffn_b2"] = _fm(i["L0_attn_ffn_b2"], NCH)
    sh["gate_wT"] = np.ascontiguousarray(i["L0_ffn_gate_w"].T, np.float32)
    sh["gate_b"] = np.ascontiguousarray(i["L0_ffn_gate_b"].reshape(NEXP, 1),
                                        np.float32)
    sh["w13T"] = np.ascontiguousarray(i["L0_ffn_w13"].transpose(0, 2, 1), np.float32)
    sh["w2T"] = np.ascontiguousarray(i["L0_ffn_w2"].transpose(0, 2, 1), np.float32)
    sh["w13sT"] = np.ascontiguousarray(i["L0_ffn_w13_s"].T, np.float32)
    sh["w2sT"] = np.ascontiguousarray(i["L0_ffn_w2_s"].T, np.float32)
    sh["wqT"] = np.ascontiguousarray(i["L1_attn_Wq"].T, np.float32)
    sh["wkT"] = np.ascontiguousarray(i["L1_attn_Wk"].T, np.float32)
    sh["wvT"] = np.ascontiguousarray(i["L1_attn_Wv"].T, np.float32)
    sh["woT"] = np.ascontiguousarray(i["L1_attn_Wo"].T, np.float32)
    sh["l1_g"] = _fm(i["L1_attn_ln_g"], NCH)
    sh["l1_b"] = _fm(i["L1_attn_ln_b"], NCH)
    sh["hscale"] = np.ascontiguousarray(i["L1_attn_head_scales"].reshape(1, NH),
                                        np.float32)
    dirs = np.zeros((P, NH), np.float32)
    dirs[0:3, :] = i["L1_attn_head_dirs"].T
    sh["dirs"] = dirs
    sh["wrT"] = np.ascontiguousarray(i["L1_ffn_W_r"].T, np.float32)
    sh["wkT2"] = np.ascontiguousarray(i["L1_ffn_W_k"].T, np.float32)
    sh["wvT2"] = np.ascontiguousarray(i["L1_ffn_W_v"].T, np.float32)
    sh["mu_r"] = _fm(i["L1_ffn_mu_r"], NCH)
    sh["mu_k"] = _fm(i["L1_ffn_mu_k"], NCH)
    sh["identf"] = np.eye(P, dtype=np.float32)
    sh["ones_col"] = np.ones((P, 1), np.float32)
    sh["ones_row"] = np.ones((1, P), np.float32)
    tri = np.zeros((P, P), np.float32)
    tri[np.triu_indices(P, 1)] = NEG
    sh["trimask"] = tri
    rot = np.zeros((P, P), np.float32)
    for ii in range(64):
        rot[ii + 64, ii] = -1.0
        rot[ii, ii + 64] = 1.0
    sh["rot_sT"] = rot
    inv = 1.0 / (10000.0 ** (np.arange(0, HD, 2, dtype=np.float64) / HD))
    freqs = np.outer(np.arange(T, dtype=np.float64), inv)
    cos = np.concatenate([np.cos(freqs)] * 2, 1).T
    sin = np.concatenate([np.sin(freqs)] * 2, 1).T
    sc = 1.0 / np.sqrt(HD)
    sh["cos_q"] = np.ascontiguousarray(cos * sc, np.float32)
    sh["sin_q"] = np.ascontiguousarray(sin * sc, np.float32)
    sh["cos_k"] = np.ascontiguousarray(cos, np.float32)
    sh["sin_k"] = np.ascontiguousarray(sin, np.float32)
    sh["cpool"] = np.ascontiguousarray(
        np.broadcast_to(np.asarray(CONSTS, np.float32), (P, len(CONSTS))))
    s16 = np.zeros((NEXP, NEXP * P), np.float32)
    for n in range(NEXP):
        s16[n, n * P:(n + 1) * P] = 1.0
    sh["sel16"] = s16

    x = np.asarray(inputs["L0_attn_x"])
    per_core = [np.ascontiguousarray(x[c].T, np.float32) for c in range(8)]
    return sh, per_core


def kernel(**inputs):
    sh, per_core = stage_inputs(inputs)
    nc = build_program()
    in_maps = [dict(sh, x0=per_core[c]) for c in range(8)]
    res = run_bass_kernel_spmd(nc, in_maps, core_ids=list(range(8)))
    out = np.stack([np.asarray(res.results[c]["out"]).T for c in range(8)])
    return np.ascontiguousarray(out, np.float32)


# version 22 (final)
# speedup vs baseline: 1.0486x; 1.0021x over previous
import contextlib
import numpy as np
import concourse.bass as bass
import concourse.mybir as mybir
from concourse.bass_utils import run_bass_kernel_spmd

F32 = mybir.dt.float32
BF16 = mybir.dt.bfloat16
AF = mybir.ActivationFunctionType
ALU = mybir.AluOpType
AX = mybir.AxisListType

P = 128
T = 2048
D = 1024
NCH = D // P
DIN = 2048
NIN = DIN // P
DST = 16
DTR = 64
DFF = 4096
NFF = DFF // P
NEXP = 16
INTER = 512
NH = 8
HD = 128
EPS = 1e-5
NEG = -30000.0
CONSTS = [1.0 / D, -0.5, -1.0, EPS, 1.0, 1.0 / P, 0.5, float(np.sqrt(HD))]

DEBUG_TAGS = set()
N_PHASES = 99



class _FakeInstr:
    __slots__ = ("incs",)

    def __init__(self, incs):
        self.incs = incs


class _FakeEng:
    __slots__ = ("rec",)

    def __init__(self, rec):
        self.rec = rec

    def __getattr__(self, name):
        rec = self.rec

        def call(*a, **k):
            incs = []
            rec.append((name, a, k, incs))
            return _FakeInstr(incs)
        return call


class Sec:

    def __init__(self, B, engname):
        self.B, self.engname = B, engname
        self.e = _FakeEng(B.streams[engname])

    def wait(self, *evs):
        for ev in evs:
            if ev is None:
                continue
            s, v = ev
            if v <= 0:
                continue
            key = (self.engname, s)
            if self.B.waited.get(key, 0) < v:
                self.e.wait_ge(self.B.sem(s), v)
                self.B.waited[key] = v

    def inc(self, instr, semname=None, amt=1):
        s = semname or self.B.engsem[self.engname]
        self.B.cnt[s] = self.B.cnt.get(s, 0) + amt
        instr.incs.append((self.B.sem(s), amt))
        return (s, self.B.cnt[s])

    def dma(self, out, in_, stream):
        ins = self.e.dma_start(out=out, in_=in_)
        return self.inc(ins, semname=stream, amt=16)

    def chain(self, instr):
        ev = self.inc(instr)
        self.wait(ev)
        return ev


class Bld:
    def __init__(self, nc, es):
        self.nc, self.es = nc, es
        self.block = es.enter_context(nc.Block())
        self.sems = {}
        self.cnt = {}
        self.waited = {}
        self.engsem = {"tensor": "T", "vector": "V", "scalar": "S",
                       "gpsimd": "G", "sync": "Y"}
        self.streams = {k: [] for k in self.engsem}
        self.lastev = {}

    def sem(self, name):
        if name not in self.sems:
            self.sems[name] = self.es.enter_context(self.nc.semaphore(name))
            self.cnt.setdefault(name, 0)
        return self.sems[name]

    def ev(self, name):
        return (name, self.cnt.get(name, 0))

    def sec(self, engname):
        def deco(fn):
            fn(Sec(self, engname))
        return deco

    def barrier(self):
        evs = [(s, c) for s, c in self.cnt.items() if c > 0]
        for engname in self.engsem:
            @self.sec(engname)
            def _(c):
                c.wait(*evs)

    def flush(self):
        for engname, rec in self.streams.items():
            if not rec:
                continue

            def run(e, rec=list(rec)):
                for name, a, k, incs in rec:
                    instr = getattr(e, name)(*a, **k)
                    for sem, amt in incs:
                        instr.then_inc(sem, amt)
            getattr(self.block, engname)(run)
            rec.clear()


def ceil_div(a, b):
    return -(-a) // b



def build_program():
    nc = bass.Bass("TRN2", target_bir_lowering=False)
    dram = {}

    def din(name, shape, dtype=F32):
        dram[name] = nc.dram_tensor(name, shape, dtype, kind="ExternalInput").ap()
        return dram[name]

    def dscr(name, shape, dtype=BF16):
        kind = "ExternalOutput" if name in DEBUG_TAGS else "Internal"
        dram[name] = nc.dram_tensor(name, shape, dtype, kind=kind).ap()
        return dram[name]

    x0 = din("x0", [D, T])
    w_inproj = din("w_inproj", [D, 2 * DIN])
    conv_w = din("conv_w", [P, NIN * 4])
    conv_b = din("conv_b", [P, NIN])
    w_xproj = din("w_xproj", [DIN, 112])
    w_dtproj = din("w_dtproj", [DTR, DIN])
    dt_bias = din("dt_bias", [P, NIN])
    a_log = din("a_log", [P, NIN * DST])
    d_vec = din("d_vec", [P, NIN])
    w_oproj = din("w_oproj", [DIN, D])
    ln1_g = din("ln1_g", [P, NCH]); ln1_b = din("ln1_b", [P, NCH])
    ln2_g = din("ln2_g", [P, NCH]); ln2_b = din("ln2_b", [P, NCH])
    w_ffn1 = din("w_ffn1", [D, DFF]); ffn_b1 = din("ffn_b1", [P, NFF])
    w_ffn2 = din("w_ffn2", [DFF, D]); ffn_b2 = din("ffn_b2", [P, NCH])
    gate_wT = din("gate_wT", [D, NEXP]); gate_b = din("gate_b", [NEXP, 1])
    w13T = din("w13T", [NEXP, D, 2 * INTER])
    w2T = din("w2T", [NEXP, INTER, D])
    w13sT = din("w13sT", [D, 2 * INTER])
    w2sT = din("w2sT", [INTER, D])
    wqT = din("wqT", [D, D]); wkT = din("wkT", [D, D])
    wvT = din("wvT", [D, D]); woT = din("woT", [D, D])
    l1_g = din("l1_g", [P, NCH]); l1_b = din("l1_b", [P, NCH])
    hscale = din("hscale", [1, NH])
    dirs = din("dirs", [P, NH])
    wrT = din("wrT", [D, D]); wkT2 = din("wkT2", [D, D]); wvT2 = din("wvT2", [D, D])
    mu_r = din("mu_r", [P, NCH]); mu_k = din("mu_k", [P, NCH])
    identf = din("identf", [P, P])
    ones_col = din("ones_col", [P, 1])
    ones_row = din("ones_row", [1, P])
    trimask = din("trimask", [P, P])
    rot_sT = din("rot_sT", [P, P])
    cos_q = din("cos_q", [P, T]); sin_q = din("sin_q", [P, T])
    cos_k = din("cos_k", [P, T]); sin_k = din("sin_k", [P, T])
    cpool = din("cpool", [P, len(CONSTS)])
    sel16 = din("sel16", [NEXP, NEXP * P])

    out = nc.dram_tensor("out", [D, T], F32, kind="ExternalOutput").ap()
    dram["out"] = out

    n0_sp = dscr("n0_sp", [P, NCH, T])
    xc_sp = dscr("xc_sp", [P, NIN, T])
    dt_sp = dscr("dt_sp", [P, NIN, T])
    sz_sp = dscr("sz_sp", [P, NIN, T])
    u1_sp = dscr("u1_sp", [P, NCH, T])
    h1_sp = dscr("h1_sp", [P, NFF, T])
    n1_sp = dscr("n1_sp", [P, NCH, T])
    xn_sp = dscr("xn_sp", [P, NCH, T])
    q_sp = dscr("q_sp", [P, NH, T])
    k_sp = dscr("k_sp", [P, NH, T])
    vt_sp = dscr("vt_sp", [P, NH, T // P, P])

    es = contextlib.ExitStack()
    with es:
        B = Bld(nc, es)

        def sb(name, shape, dtype, stack=es):
            return stack.enter_context(nc.sbuf_tensor(name, shape, dtype))

        def ps(name, shape, dtype=F32, stack=es):
            return stack.enter_context(nc.psum_tensor(name, shape, dtype))

        R = sb("R", [P, NCH * T], F32)
        onescol_b = sb("onescol_b", [P, 1], BF16)
        onesrow_f = sb("onesrow_f", [1, P], F32)
        onesrow_b = sb("onesrow_b", [1, P], BF16)
        ident_b = sb("ident_b", [P, P], BF16)
        ident_f = sb("ident_f", [P, P], F32)
        tri_f = sb("tri_f", [P, P], F32)
        cp = sb("cp", [P, len(CONSTS)], F32)
        sel16_b = sb("sel16_b", [NEXP, NEXP * P], BF16)
        for ci, cv in enumerate(CONSTS):
            nc.const_aps.register(F32, cv, cp[:, ci:ci + 1])

        @B.sec("gpsimd")
        def _(c):
            c.dma(R[:].rearrange("p (k t) -> p k t", k=NCH),
                  x0.rearrange("(k p) t -> p k t", p=P), "gC")
            c.dma(onescol_b[:], ones_col[:, :], "gC")
            c.dma(onesrow_b[:], ones_row[:, :], "gC")
            c.dma(ident_b[:], identf[:, :], "gC")
            c.dma(sel16_b[:], sel16[:, :], "gC")

        @B.sec("sync")
        def _(c):
            c.dma(onesrow_f[:], ones_row[:, :], "yM")
            c.dma(ident_f[:], identf[:, :], "yM")
            c.dma(tri_f[:], trimask[:, :], "yM")
            c.dma(cp[:], cpool[:, :], "yM")

        ph0_loads = [B.ev("gC"), B.ev("yM")]
        B.barrier()

        l0 = contextlib.ExitStack()
        with l0:
            B_fm = sb("B_fm", [DST, T], BF16, l0)
            C_fm = sb("C_fm", [DST, T], BF16, l0)
            comb_fm = sb("comb_fm", [NEXP, T], BF16, l0)

            phase1(B, nc, l0, dram, locals())
            if "dbg_l0" in DEBUG_TAGS:
                for nm, tens, dt in (("dbg_comb", comb_fm, BF16),
                                     ("dbg_bfm", B_fm, BF16),
                                     ("dbg_cfm", C_fm, BF16)):
                    dram[nm] = nc.dram_tensor(nm, list(tens[:].shape), dt,
                                              kind="ExternalOutput").ap()
                B.barrier()

                @B.sec("sync")
                def _(c):
                    c.dma(dram["dbg_comb"][:, :], comb_fm[:], "yD")
                    c.dma(dram["dbg_bfm"][:, :], B_fm[:], "yD")
                    c.dma(dram["dbg_cfm"][:, :], C_fm[:], "yD")
                    c.wait(B.ev("yD"))
            if N_PHASES >= 2:
                phase2_scan(B, nc, l0, dram, locals())
            if N_PHASES >= 3:
                phase3_mlp(B, nc, l0, dram, locals())
            if N_PHASES >= 4:
                phase4_moe(B, nc, l0, dram, locals())

        if N_PHASES >= 5:
            l1 = contextlib.ExitStack()
            with l1:
                phase5_l1(B, nc, l1, dram, locals())

        B.barrier()

        @B.sec("sync")
        def _(c):
            c.dma(out.rearrange("(k p) t -> p k t", p=P),
                  R[:].rearrange("p (k t) -> p k t", k=NCH), "yOUT")
            c.wait(B.ev("yOUT"))

        B.flush()
    return nc



def phase1(B, nc, l0, dram, env):
    R = env["R"]
    B_fm = env["B_fm"]; C_fm = env["C_fm"]; comb_fm = env["comb_fm"]
    onescol_b = env["onescol_b"]; onesrow_f = env["onesrow_f"]
    ident_f = env["ident_f"]
    ph0_loads = env["ph0_loads"]

    TC = 256
    NT = T // TC

    st = contextlib.ExitStack()
    with st:
        def sb(name, shape, dtype, stack=None):
            return (stack or st).enter_context(nc.sbuf_tensor(name, shape, dtype))

        def ps(name, shape, dtype=F32):
            return st.enter_context(nc.psum_tensor(name, shape, dtype))

        mu_row = sb("mu_row", [1, T], F32)
        r1_row = sb("r1_row", [1, T], F32)
        s_row = sb("s_row", [1, T], F32)
        w_xp = sb("w_xp_s", [P, NIN * 112], BF16)
        w_dt = sb("w_dt_s", [DTR, DIN], BF16)
        cw = sb("cw_s", [P, NIN * 4], F32)
        cb = sb("cb_s", [P, NIN], F32)
        dtb = sb("dtb_s", [P, NIN], F32)
        l1g = sb("l1g_s", [P, NCH], F32)
        l1bb = sb("l1bb_s", [P, NCH], F32)
        gw = sb("gw_s", [P, NCH * NEXP], F32)
        gb = sb("gb_s", [NEXP, 1], F32)
        nf = sb("nf", [P, NCH * TC], F32)
        ln1c = sb("ln1c", [P, NCH * TC], BF16)
        bc3 = sb("bc3", [P, 3 * TC], F32)
        sp1 = sb("sp1", [P, TC], F32)
        sp2 = sb("sp2", [P, TC], F32)
        tok16 = sb("tok16", [P, 5 * NEXP], F32)
        gbi_t = sb("gbi_t", [NEXP, P], F32)
        negbig = sb("negbig", [P, NEXP], F32)
        thr = sb("thr", [P, 1], F32)
        eqm_i = sb("eqm_i", [P, NEXP], mybir.dt.int32)
        rho = sb("rho", [P, 1], F32)
        gate_sc = sb("gate_sc", [NEXP, T], F32)
        r16_sb = sb("r16_sb", [NEXP, TC], F32)

        ps_bc = ps("ps_bc", [P, 512])
        ps_bc2 = ps("ps_bc2", [P, 512])
        ps_g = ps("ps_g", [NEXP, 512])
        ps_m0 = ps("ps_m0", [P, 512])
        ps_m1 = ps("ps_m1", [P, 512])
        ps_m2 = ps("ps_m2", [P, 512])
        ps_xd = ps("ps_xd", [112, 512])

        @B.sec("sync")
        def _(c):
            c.dma(cw[:], dram["conv_w"][:, :], "yM")
            c.dma(cb[:], dram["conv_b"][:, :], "yM")
            c.dma(dtb[:], dram["dt_bias"][:, :], "yM")
            c.dma(l1g[:], dram["ln1_g"][:, :], "yM")
            c.dma(l1bb[:], dram["ln1_b"][:, :], "yM")
            c.dma(gw[:].rearrange("p (k m) -> p k m", k=NCH),
                  dram["gate_wT"].rearrange("(k p) m -> p k m", p=P), "yM")
            c.dma(gb[:], dram["gate_b"][:, :], "yM")
        sload = B.ev("yM")

        @B.sec("gpsimd")
        def _(c):
            c.dma(w_xp[:].rearrange("p (k m) -> p k m", k=NIN),
                  dram["w_xproj"].rearrange("(k p) m -> p k m", p=P), "gB")
            c.dma(w_dt[:], dram["w_dtproj"][:, :], "gB")
        wsmall = B.ev("gB")

        sta = contextlib.ExitStack()
        with sta:
            Q_r = sb("Q_r", [1, T], F32, sta)
            rw = sb("rw", [1, 2 * T], F32, sta)
            xb_s = sb("xb_s", [P, NCH * TC], BF16, sta)
            xq_s = sb("xq_s", [P, NCH * TC], BF16, sta)

            for tc in range(NT):
                lo = tc * TC

                @B.sec("scalar")
                def _(c):
                    c.wait(*ph0_loads)
                    c.wait(B.ev("T"))
                    for k in range(NCH):
                        c.e.activation(xb_s[:, k * TC:(k + 1) * TC],
                                       R[:, k * T + lo:k * T + lo + TC], AF.Identity)
                        ins = c.e.activation(xq_s[:, k * TC:(k + 1) * TC],
                                             R[:, k * T + lo:k * T + lo + TC],
                                             AF.Square)
                    c.inc(ins)
                e_cast = B.ev("S")

                @B.sec("tensor")
                def _(c):
                    c.wait(*ph0_loads)
                    c.wait(e_cast, B.ev("S"))
                    for k in range(NCH):
                        c.e.matmul(ps_g[0:1, 0:TC], onescol_b[:],
                                   xb_s[:, k * TC:(k + 1) * TC],
                                   start=(k == 0), stop=(k == NCH - 1))
                    for k in range(NCH):
                        ins = c.e.matmul(ps_xd[0:1, 0:TC], onescol_b[:],
                                         xq_s[:, k * TC:(k + 1) * TC],
                                         start=(k == 0), stop=(k == NCH - 1))
                    c.inc(ins)
                e_mm = B.ev("T")

                @B.sec("scalar")
                def _(c):
                    c.wait(e_mm)
                    c.e.activation(mu_row[:, lo:lo + TC], ps_g[0:1, 0:TC], AF.Identity)
                    ins = c.e.activation(Q_r[:, lo:lo + TC], ps_xd[0:1, 0:TC],
                                         AF.Identity)
                    c.inc(ins)
            e_stats = B.ev("S")

            q1 = rw[:, 0:T]; v1 = rw[:, T:2 * T]

            @B.sec("scalar")
            def _(c):
                c.wait(e_stats)
                ins = c.e.activation(q1, mu_row[:, :], AF.Square, scale=1.0 / D)
                c.inc(ins)
            e1 = B.ev("S")

            @B.sec("vector")
            def _(c):
                c.wait(e1)
                ins = c.e.scalar_tensor_tensor(v1, Q_r[:, :], 1.0 / D, q1,
                                               ALU.mult, ALU.subtract)
                c.inc(ins)
            e2 = B.ev("V")

            @B.sec("scalar")
            def _(c):
                c.wait(e2)
                ev = c.inc(c.e.activation(q1, v1, AF.Ln, bias=EPS))
                c.wait(ev)
                ev = c.inc(c.e.activation(r1_row[:, :], q1, AF.Exp, scale=-0.5))
                c.wait(ev)
                c.inc(c.e.activation(q1, q1, AF.Exp, scale=-1.0))
            e3 = B.ev("S")

            @B.sec("vector")
            def _(c):
                c.wait(e3)
                ins = c.e.tensor_tensor(v1, v1, q1, ALU.mult)
                c.inc(ins)
            e4 = B.ev("V")

            @B.sec("scalar")
            def _(c):
                c.wait(e4)
                ev = c.inc(c.e.activation(v1, v1, AF.Ln, bias=EPS))
                c.wait(ev)
                c.inc(c.e.activation(v1, v1, AF.Exp, scale=-0.5))
            e5 = B.ev("S")

            @B.sec("vector")
            def _(c):
                c.wait(e5)
                ins = c.e.tensor_tensor(s_row[:, :], v1, r1_row[:, :], ALU.mult)
                c.inc(ins)
            B.barrier()
        e_rows = B.ev("V")

        def chunk_norm(c_lo, want_n0, n0c):
            lo = c_lo

            @B.sec("tensor")
            def _(c):
                c.wait(e_rows, B.ev("V"), B.ev("S"))
                ins = c.e.matmul(ps_bc[:, 0:TC], onesrow_f[:],
                                 mu_row[:, lo:lo + TC], start=True, stop=True)
                c.inc(ins)
            eb1 = B.ev("T")

            @B.sec("scalar")
            def _(c):
                c.wait(eb1)
                ins = c.e.activation(bc3[:, 0:TC], ps_bc[:, 0:TC], AF.Identity,
                                     scale=1.0 / D)
                c.inc(ins)
            eb1e = B.ev("S")

            @B.sec("tensor")
            def _(c):
                c.wait(eb1e)
                if want_n0:
                    c.e.matmul(ps_bc[:, 0:TC], onesrow_f[:], r1_row[:, lo:lo + TC],
                               start=True, stop=True)
                ins = c.e.matmul(ps_bc2[:, 0:TC], onesrow_f[:], s_row[:, lo:lo + TC],
                                 start=True, stop=True)
                c.inc(ins)
            eb2 = B.ev("T")

            @B.sec("scalar")
            def _(c):
                c.wait(eb2)
                if want_n0:
                    c.e.activation(bc3[:, TC:2 * TC], ps_bc[:, 0:TC], AF.Identity)
                ins = c.e.activation(bc3[:, 2 * TC:3 * TC], ps_bc2[:, 0:TC], AF.Identity)
                c.inc(ins)
            eb = B.ev("S")

            @B.sec("vector")
            def _(c):
                c.wait(eb, B.ev("T"), B.ev("yS"))
                for k in range(NCH):
                    c.chain(c.e.tensor_tensor(nf[:, k * TC:(k + 1) * TC],
                                              R[:, k * T + lo:k * T + lo + TC],
                                              bc3[:, 0:TC], ALU.subtract))
                    if want_n0:
                        c.e.tensor_tensor(n0c[:, k * TC:(k + 1) * TC],
                                          nf[:, k * TC:(k + 1) * TC],
                                          bc3[:, TC:2 * TC], ALU.mult)
                    ins = c.e.tensor_tensor(ln1c[:, k * TC:(k + 1) * TC],
                                            nf[:, k * TC:(k + 1) * TC],
                                            bc3[:, 2 * TC:3 * TC], ALU.mult)
                c.inc(ins)
            e_nf = B.ev("V")

            @B.sec("scalar")
            def _(c):
                c.wait(e_nf, sload)
                for k in range(NCH):
                    ins = c.e.activation(ln1c[:, k * TC:(k + 1) * TC],
                                         ln1c[:, k * TC:(k + 1) * TC], AF.Identity,
                                         scale=l1g[:, k:k + 1], bias=l1bb[:, k:k + 1])
                c.inc(ins)
            return e_nf, B.ev("S")

        stz = contextlib.ExitStack()
        with stz:
            w_z = sb("w_z_s", [P, NCH * DIN], BF16, stz)
            n0c = sb("n0c", [P, NCH * TC], BF16, stz)
            sz_st = sb("sz_st", [P, NIN * TC], BF16, stz)

            @B.sec("gpsimd")
            def _(c):
                wz = dram["w_inproj"][:, DIN:2 * DIN]
                c.dma(w_z[:].rearrange("p (k m) -> p k m", k=NCH),
                      wz.rearrange("(k p) m -> p k m", p=P), "gA")
            wload = B.ev("gA")

            for tc in range(NT):
                lo = tc * TC
                e_nf, e_ln1 = chunk_norm(lo, True, n0c)

                @B.sec("tensor")
                def _(c):
                    c.wait(e_nf, sload, B.ev("V"))
                    for k in range(NCH):
                        ins = c.e.matmul(ps_g[:, 0:TC],
                                         gw[:, k * NEXP:(k + 1) * NEXP],
                                         nf[:, k * TC:(k + 1) * TC],
                                         start=(k == 0), stop=(k == NCH - 1))
                    c.inc(ins)
                e_gmm = B.ev("T")

                @B.sec("tensor")
                def _(c):
                    c.wait(B.ev("S"))
                    ins = c.e.matmul(ps_bc[0:NEXP, 0:TC], onesrow_f[:, 0:NEXP],
                                     r1_row[:, lo:lo + TC], start=True, stop=True)
                    c.inc(ins)
                e_r116 = B.ev("T")

                @B.sec("scalar")
                def _(c):
                    c.wait(e_r116)
                    c.inc(c.e.activation(r16_sb[:], ps_bc[0:NEXP, 0:TC],
                                         AF.Identity))
                e_r116e = B.ev("S")

                @B.sec("vector")
                def _(c):
                    c.wait(e_gmm, e_r116e)
                    ins = c.e.tensor_tensor(gate_sc[:, lo:lo + TC], ps_g[:, 0:TC],
                                            r16_sb[:], ALU.mult)
                    c.inc(ins)
                e_gm = B.ev("V")

                @B.sec("scalar")
                def _(c):
                    c.wait(e_gm)
                    ins = c.e.activation(gate_sc[:, lo:lo + TC],
                                         gate_sc[:, lo:lo + TC], AF.Sigmoid)
                    c.inc(ins)

                for o in range(NIN):
                    pmm = (ps_m0, ps_m1, ps_m2)[o % 3]
                    pk = "m%d" % (o % 3)

                    @B.sec("tensor")
                    def _(c):
                        c.wait(e_ln1, wload, B.lastev.get(pk))
                        for k in range(NCH):
                            ins = c.e.matmul(
                                pmm[:, 0:TC],
                                w_z[:, k * DIN + o * P:k * DIN + o * P + P],
                                ln1c[:, k * TC:(k + 1) * TC],
                                start=(k == 0), stop=(k == NCH - 1))
                        c.inc(ins)
                    e_z = B.ev("T")

                    @B.sec("scalar")
                    def _(c):
                        c.wait(e_z, B.ev("yS"))
                        ins = c.e.activation(sz_st[:, o * TC:(o + 1) * TC],
                                             pmm[:, 0:TC], AF.Silu)
                        B.lastev[pk] = c.inc(ins)
                e_sz = B.ev("S")

                @B.sec("sync")
                def _(c):
                    c.wait(e_sz, e_nf)
                    c.dma(dram["sz_sp"][:, :, lo:lo + TC],
                          sz_st[:].rearrange("p (k t) -> p k t", k=NIN), "yS")
                    c.dma(dram["n0_sp"][:, :, lo:lo + TC],
                          n0c[:].rearrange("p (k t) -> p k t", k=NIN // 2), "yS")

        @B.sec("vector")
        def _(c):
            c.e.memset(negbig[:], NEG)

        for j in range(T // P):
            @B.sec("scalar")
            def _(c):
                c.wait(B.ev("V"), B.ev("T"))
                ins = c.e.activation(gbi_t[:, :], gate_sc[:, j * P:(j + 1) * P],
                                     AF.Identity, bias=gb[:, 0:1])
                c.inc(ins)
            e_bi = B.ev("S")

            @B.sec("tensor")
            def _(c):
                c.wait(e_bi)
                c.e.transpose(ps_m0[:, 0:NEXP], gate_sc[:, j * P:(j + 1) * P],
                              ident_f[0:NEXP, 0:NEXP])
                ins = c.e.transpose(ps_m1[:, 0:NEXP], gbi_t[:, :],
                                    ident_f[0:NEXP, 0:NEXP])
                c.inc(ins)
            e_t = B.ev("T")

            @B.sec("scalar")
            def _(c):
                c.wait(e_t, B.ev("V"))
                c.e.activation(tok16[:, 0:NEXP], ps_m0[:, 0:NEXP], AF.Identity)
                c.e.activation(tok16[:, NEXP:2 * NEXP], ps_m1[:, 0:NEXP], AF.Identity)
                ins = c.e.activation(tok16[:, 2 * NEXP:3 * NEXP],
                                     ps_m1[:, 0:NEXP], AF.Identity)
                c.inc(ins)
            e_tt = B.ev("S")

            @B.sec("vector")
            def _(c):
                c.wait(e_tt)
                sc_t = tok16[:, 0:NEXP]
                bi_t = tok16[:, NEXP:2 * NEXP]
                wrk = tok16[:, 2 * NEXP:3 * NEXP]
                cmb = tok16[:, 4 * NEXP:5 * NEXP]
                for r in range(4):
                    c.chain(c.e.reduce_max(thr[:], wrk, axis=AX.X))
                    if r < 3:
                        c.chain(c.e.tensor_single_scalar(eqm_i[:], wrk, thr[:, 0:1],
                                                         ALU.is_ge))
                        c.chain(c.e.copy_predicated(wrk, eqm_i[:], negbig[:]))
                ins = c.e.scalar_tensor_tensor(cmb, bi_t, thr[:, 0:1], sc_t,
                                               ALU.is_ge, ALU.mult,
                                               accum_out=rho[:])
                c.chain(ins)
                c.chain(c.e.reciprocal(rho[:], rho[:]))
                ins = c.e.tensor_scalar(cmb, cmb, rho[:, 0:1], 2.5,
                                        ALU.mult, ALU.mult)
                c.inc(ins)
            e_c = B.ev("V")

            @B.sec("tensor")
            def _(c):
                c.wait(e_c, B.ev("S"))
                ins = c.e.transpose(ps_m0[0:NEXP, 0:P],
                                    tok16[:, 4 * NEXP:5 * NEXP], ident_f[:])
                c.inc(ins)
            e_cb = B.ev("T")

            @B.sec("scalar")
            def _(c):
                c.wait(e_cb)
                ins = c.e.activation(comb_fm[:, j * P:(j + 1) * P],
                                     ps_m0[0:NEXP, 0:P], AF.Identity)
                c.inc(ins)

        stx = contextlib.ExitStack()
        with stx:
            w_x = sb("w_x_s", [P, NCH * DIN], BF16, stx)
            xi_buf = sb("xi_buf", [P, NIN * (TC + 3)], BF16, stx)
            xc_st = sb("xc_st", [P, NIN * TC], BF16, stx)
            dt_st = sb("dt_st", [P, NIN * TC], BF16, stx)
            dtr_s = sb("dtr_s", [DTR, TC], BF16, stx)

            @B.sec("gpsimd")
            def _(c):
                c.wait(B.ev("T"), B.ev("V"), B.ev("S"))
                wx = dram["w_inproj"][:, 0:DIN]
                c.dma(w_x[:].rearrange("p (k m) -> p k m", k=NCH),
                      wx.rearrange("(k p) m -> p k m", p=P), "gA")
            wload2 = B.ev("gA")

            @B.sec("vector")
            def _(c):
                c.wait(B.ev("yS"), B.ev("T"), B.ev("S"))
                for o in range(NIN):
                    ins = c.e.memset(xi_buf[:, o * (TC + 3):o * (TC + 3) + 3], 0.0)
                c.inc(ins)

            for tc in range(NT):
                lo = tc * TC
                e_nf, e_ln1 = chunk_norm(lo, False, None)

                for o in range(NIN):
                    pmm = (ps_m0, ps_m1, ps_m2)[o % 3]
                    pk = "m%d" % (o % 3)

                    @B.sec("tensor")
                    def _(c):
                        c.wait(e_ln1, wload2, B.lastev.get(pk), B.ev("V"))
                        for k in range(NCH):
                            ins = c.e.matmul(
                                pmm[:, 0:TC],
                                w_x[:, k * DIN + o * P:k * DIN + o * P + P],
                                ln1c[:, k * TC:(k + 1) * TC],
                                start=(k == 0), stop=(k == NCH - 1))
                        c.inc(ins)
                    e_x = B.ev("T")

                    @B.sec("scalar")
                    def _(c):
                        c.wait(e_x)
                        ins = c.e.activation(
                            xi_buf[:, o * (TC + 3) + 3:(o + 1) * (TC + 3)],
                            pmm[:, 0:TC], AF.Identity)
                        B.lastev[pk] = c.inc(ins)
                e_xz = B.ev("S")

                @B.sec("vector")
                def _(c):
                    c.wait(e_xz, B.ev("yS"), B.ev("V"))
                    for o in range(NIN):
                        base = o * (TC + 3)
                        c.chain(c.e.tensor_scalar(sp1[:, :],
                                                  xi_buf[:, base:base + TC],
                                                  cw[:, o * 4:o * 4 + 1], None,
                                                  ALU.mult))
                        for k in (1, 2):
                            c.chain(c.e.scalar_tensor_tensor(
                                sp1[:, :], xi_buf[:, base + k:base + k + TC],
                                cw[:, o * 4 + k:o * 4 + k + 1],
                                sp1[:, :], ALU.mult, ALU.add))
                        c.chain(c.e.scalar_tensor_tensor(
                            xc_st[:, o * TC:(o + 1) * TC],
                            xi_buf[:, base + 3:base + 3 + TC],
                            cw[:, o * 4 + 3:o * 4 + 4],
                            sp1[:, :], ALU.mult, ALU.add))
                e_conv = B.ev("V")

                @B.sec("vector")
                def _(c):
                    for o in range(NIN):
                        base = o * (TC + 3)
                        ins = c.e.tensor_copy(xi_buf[:, base:base + 3],
                                              xi_buf[:, base + TC:base + TC + 3])
                    c.inc(ins)

                @B.sec("scalar")
                def _(c):
                    c.wait(e_conv)
                    for o in range(NIN):
                        ins = c.e.activation(xc_st[:, o * TC:(o + 1) * TC],
                                             xc_st[:, o * TC:(o + 1) * TC], AF.Silu,
                                             bias=cb[:, o:o + 1])
                    c.inc(ins)
                e_xc = B.ev("S")

                @B.sec("tensor")
                def _(c):
                    c.wait(e_xc, wsmall)
                    for k in range(NIN):
                        ins = c.e.matmul(ps_xd[:, 0:TC],
                                         w_xp[:, k * 112:(k + 1) * 112],
                                         xc_st[:, k * TC:(k + 1) * TC],
                                         start=(k == 0), stop=(k == NIN - 1))
                    c.inc(ins)
                e_xd = B.ev("T")

                @B.sec("scalar")
                def _(c):
                    c.wait(e_xd)
                    c.e.activation(dtr_s[:, :], ps_xd[0:DTR, 0:TC], AF.Identity)
                    c.e.activation(B_fm[:, lo:lo + TC], ps_xd[64:80, 0:TC],
                                   AF.Identity)
                    ins = c.e.activation(C_fm[:, lo:lo + TC],
                                         ps_xd[96:112, 0:TC], AF.Identity)
                    c.inc(ins)
                e_dtr = B.ev("S")

                for o in range(NIN):
                    pmm = (ps_m0, ps_m1, ps_m2)[o % 3]
                    pk = "m%d" % (o % 3)

                    @B.sec("tensor")
                    def _(c):
                        c.wait(e_dtr, B.lastev.get(pk))
                        ins = c.e.matmul(pmm[:, 0:TC], w_dt[:, o * P:(o + 1) * P],
                                         dtr_s[:, :], start=True, stop=True)
                        c.inc(ins)
                    e_d2 = B.ev("T")

                    @B.sec("scalar")
                    def _(c):
                        c.wait(e_d2, B.ev("V"))
                        c.e.activation(sp2[:, :], pmm[:, 0:TC], AF.Relu,
                                       bias=dtb[:, o:o + 1])
                        ev = c.inc(c.e.activation(sp1[:, :], pmm[:, 0:TC], AF.Abs,
                                                  bias=dtb[:, o:o + 1]))
                        B.lastev[pk] = ev
                        c.wait(ev)
                        ev = c.inc(c.e.activation(sp1[:, :], sp1[:, :], AF.Exp,
                                                  scale=-1.0))
                        c.wait(ev)
                        c.inc(c.e.activation(sp1[:, :], sp1[:, :], AF.Ln, bias=1.0))
                    e_sp = B.ev("S")

                    @B.sec("vector")
                    def _(c):
                        c.wait(e_sp, B.ev("yS"))
                        ins = c.e.tensor_tensor(dt_st[:, o * TC:(o + 1) * TC],
                                                sp1[:, :], sp2[:, :], ALU.add)
                        c.inc(ins)
                e_dt = B.ev("V")

                @B.sec("sync")
                def _(c):
                    c.wait(e_xc, e_dt)
                    c.dma(dram["xc_sp"][:, :, lo:lo + TC],
                          xc_st[:].rearrange("p (k t) -> p k t", k=NIN), "yS")
                    c.dma(dram["dt_sp"][:, :, lo:lo + TC],
                          dt_st[:].rearrange("p (k t) -> p k t", k=NIN), "yS")

        B.barrier()


def phase2_scan(B, nc, l0, dram, env):
    R = env["R"]
    B_fm = env["B_fm"]; C_fm = env["C_fm"]
    onesrow_b = env["onesrow_b"]
    TC = 256
    NT = T // TC

    st = contextlib.ExitStack()
    with st:
        def sb(name, shape, dtype, stack=None):
            return (stack or st).enter_context(nc.sbuf_tensor(name, shape, dtype))

        def ps(name, shape, dtype=F32):
            return st.enter_context(nc.psum_tensor(name, shape, dtype))

        w_op = sb("w_op_s", [P, NIN * D], BF16)
        a_fm = sb("a_fm", [P, NIN * DST], F32)
        d_fm = sb("d_fm", [P, NIN], F32)
        dt_all = sb("dt_all", [P, NIN * TC], BF16)
        xc_all = sb("xc_all", [P, NIN * TC], BF16)
        sz_all = sb("sz_all", [P, NIN * TC], BF16)
        n0c2 = sb("n0c2", [P, NCH * TC], BF16)
        dtx_all = sb("dtx_all", [P, NIN * TC], BF16)
        dA = sb("dA_all", [P, NIN * TC], F32)
        W_all = sb("W_all", [P, NIN * TC], BF16)
        hs2 = [sb("hs_a", [P, NIN * TC], BF16), sb("hs_b", [P, NIN * TC], BF16)]
        y_all = sb("y_all", [P, NIN * TC], BF16)
        Bb2 = [sb("Bb_a", [P, TC], BF16), sb("Bb_b", [P, TC], BF16)]
        Cb2 = [sb("Cb_a", [P, TC], BF16), sb("Cb_b", [P, TC], BF16)]
        h_state = sb("h_state", [P, NIN * DST], F32)
        u1c = sb("u1c", [P, NCH * TC], BF16)
        gt = sb("gt", [P, TC], BF16)
        uf = sb("uf", [P, TC], F32)

        ps_b1 = ps("ps_b1", [P, 512])
        ps_b2 = ps("ps_b2", [P, 512])
        ps_o0 = ps("ps_o0", [P, 512])
        ps_o1 = ps("ps_o1", [P, 512])
        ps_o2 = ps("ps_o2", [P, 512])
        ps_wm = ps("ps_wm", [P, 512])

        @B.sec("gpsimd")
        def _(c):
            c.wait(B.ev("T"), B.ev("V"), B.ev("S"))
            c.dma(w_op[:].rearrange("p (k m) -> p k m", k=NIN),
                  dram["w_oproj"].rearrange("(k p) m -> p k m", p=P), "gA")
            c.dma(a_fm[:], dram["a_log"][:, :], "gA")
            c.dma(d_fm[:], dram["d_vec"][:, :], "gA")
        wload = B.ev("gA")

        @B.sec("scalar")
        def _(c):
            c.wait(wload)
            c.chain(c.e.activation(a_fm[:], a_fm[:], AF.Exp))
        e_a1 = B.ev("S")

        @B.sec("vector")
        def _(c):
            c.wait(e_a1)
            c.chain(c.e.tensor_scalar_mul(a_fm[:], a_fm[:], -1.0))
        e_a = B.ev("V")

        a3 = a_fm[:].rearrange("p (kc n) -> p kc n", n=DST)
        for tc in range(NT):
            lo = tc * TC

            @B.sec("sync")
            def _(c):
                c.wait(B.ev("V"), B.ev("T"), B.ev("S"))
                c.dma(dt_all[:].rearrange("p (k t) -> p k t", k=NIN),
                      dram["dt_sp"][:, :, lo:lo + TC], "yA")
                c.dma(xc_all[:].rearrange("p (k t) -> p k t", k=NIN),
                      dram["xc_sp"][:, :, lo:lo + TC], "yA")
                c.dma(sz_all[:].rearrange("p (k t) -> p k t", k=NIN),
                      dram["sz_sp"][:, :, lo:lo + TC], "yA")
                c.dma(n0c2[:].rearrange("p (k t) -> p k t", k=NCH),
                      dram["n0_sp"][:, :, lo:lo + TC], "yA")
            e_ld = B.ev("yA")

            @B.sec("vector")
            def _(c):
                c.wait(e_ld)
                c.chain(c.e.tensor_tensor(dtx_all[:], dt_all[:], xc_all[:],
                                          ALU.mult))
            e_dtx = B.ev("V")

            def emit_exp(n):
                @B.sec("scalar")
                def _(c):
                    c.wait(e_ld, e_a)
                    for g in range(2):
                        c.wait(B.lastev.get("p2sc%d" % g))
                        for kc in range(8 * g, 8 * g + 8):
                            ev = c.inc(c.e.activation(
                                dA[:, kc * TC:(kc + 1) * TC],
                                dt_all[:, kc * TC:(kc + 1) * TC], AF.Exp,
                                scale=a_fm[:, kc * DST + n:kc * DST + n + 1]))
                        B.lastev["p2dA%d" % g] = ev

            for n in range(DST):
                par = n % 2

                @B.sec("tensor")
                def _(c):
                    c.wait(e_dtx, B.lastev.get("p2bc"))
                    sel = env["sel16_b"][:, n * P:(n + 1) * P]
                    c.e.matmul(ps_b1[:, 0:TC], sel, B_fm[:, lo:lo + TC],
                               start=True, stop=True)
                    ins = c.e.matmul(ps_b2[:, 0:TC], sel, C_fm[:, lo:lo + TC],
                                     start=True, stop=True)
                    c.inc(ins)
                e_bc = B.ev("T")

                @B.sec("scalar")
                def _(c):
                    c.wait(e_bc, B.lastev.get("p2W%d" % par),
                           B.lastev.get("p2G%d" % par))
                    c.e.activation(Bb2[par][:], ps_b1[:, 0:TC], AF.Identity)
                    ins = c.e.activation(Cb2[par][:], ps_b2[:, 0:TC], AF.Identity)
                    B.lastev["p2bc"] = c.inc(ins)
                e_bce = B.ev("S")

                emit_exp(n)

                @B.sec("vector")
                def _(c):
                    c.wait(e_bce, e_dtx, B.lastev.get("p2G%d" % par))
                    bb = Bb2[par][:].unsqueeze(1).broadcast_to([P, NIN, TC])
                    ev = c.chain(c.e.tensor_tensor(
                        W_all[:].rearrange("p (k t) -> p k t", k=NIN),
                        dtx_all[:].rearrange("p (k t) -> p k t", k=NIN),
                        bb, ALU.mult))
                    B.lastev["p2W%d" % par] = ev
                    for g in range(2):
                        c.wait(B.lastev.get("p2dA%d" % g))
                        for kc in range(8 * g, 8 * g + 8):
                            init = (0.0 if tc == 0
                                    else h_state[:, kc * DST + n:
                                                 kc * DST + n + 1])
                            ev = c.inc(c.e.tensor_tensor_scan(
                                hs2[par][:, kc * TC:(kc + 1) * TC],
                                dA[:, kc * TC:(kc + 1) * TC],
                                W_all[:, kc * TC:(kc + 1) * TC],
                                init, ALU.mult, ALU.add))
                        B.lastev["p2sc%d" % g] = ev
                    c.wait(B.ev("V"))
                    c.chain(c.e.tensor_copy(
                        h_state[:].rearrange("p (kc n) -> p kc n", n=DST)[:, :, n],
                        hs2[par][:].rearrange("p (kc t) -> p kc t", kc=NIN)
                        [:, :, TC - 1]))
                e_vn = B.ev("V")

                @B.sec("gpsimd")
                def _(c):
                    c.wait(e_vn, e_bce)
                    cb = Cb2[par][:].unsqueeze(1).broadcast_to([P, NIN, TC])
                    if n == 0:
                        c.wait(B.ev("T"))
                        ev = c.chain(c.e.tensor_tensor(
                            y_all[:].rearrange("p (k t) -> p k t", k=NIN),
                            hs2[par][:].rearrange("p (k t) -> p k t", k=NIN),
                            cb, ALU.mult))
                    else:
                        c.chain(c.e.tensor_tensor(
                            hs2[par][:].rearrange("p (k t) -> p k t", k=NIN),
                            hs2[par][:].rearrange("p (k t) -> p k t", k=NIN),
                            cb, ALU.mult))
                        ev = c.chain(c.e.tensor_tensor(y_all[:], y_all[:],
                                                       hs2[par][:], ALU.add))
                    B.lastev["p2G%d" % par] = ev
                    B.lastev["p2yG"] = ev

            @B.sec("vector")
            def _(c):
                c.wait(B.lastev.get("p2yG"))
                for kc in range(NIN):
                    c.chain(c.e.scalar_tensor_tensor(
                        gt[:], xc_all[:, kc * TC:(kc + 1) * TC],
                        d_fm[:, kc:kc + 1],
                        y_all[:, kc * TC:(kc + 1) * TC], ALU.mult, ALU.add))
                    c.chain(c.e.tensor_tensor(
                        y_all[:, kc * TC:(kc + 1) * TC], gt[:],
                        sz_all[:, kc * TC:(kc + 1) * TC], ALU.mult))
            e_y = B.ev("V")

            if tc >= NT - 2:
                @B.sec("tensor")
                def _(c):
                    for _i in range(150):
                        c.e.matmul(ps_wm[:, 0:512], w_op[:, 0:P],
                                   w_op[:, 0:512], start=True, stop=True)

            for oc in range(NCH):
                pmm = (ps_o0, ps_o1, ps_o2)[oc % 3]
                pk = "o%d" % (oc % 3)

                @B.sec("tensor")
                def _(c):
                    c.wait(e_y, wload, B.lastev.get(pk))
                    for kc in range(NIN):
                        ins = c.e.matmul(
                            pmm[:, 0:TC],
                            w_op[:, kc * D + oc * P:kc * D + oc * P + P],
                            y_all[:, kc * TC:(kc + 1) * TC],
                            start=(kc == 0), stop=(kc == NIN - 1))
                    c.inc(ins)
                e_mm = B.ev("T")

                @B.sec("vector")
                def _(c):
                    c.wait(e_mm, B.lastev.get("p2uf"))
                    ev = c.chain(c.e.tensor_tensor(uf[:], pmm[:, 0:TC],
                                                   n0c2[:, oc * TC:(oc + 1) * TC],
                                                   ALU.add))
                    B.lastev[pk] = ev
                    c.chain(c.e.tensor_tensor(R[:, oc * T + lo:oc * T + lo + TC],
                                              R[:, oc * T + lo:oc * T + lo + TC],
                                              uf[:], ALU.add))
                e_u = B.ev("V")

                @B.sec("scalar")
                def _(c):
                    c.wait(e_u, B.ev("yS"))
                    B.lastev["p2uf"] = c.inc(
                        c.e.activation(u1c[:, oc * TC:(oc + 1) * TC], uf[:],
                                       AF.Identity))
            e_uc = B.ev("S")

            @B.sec("sync")
            def _(c):
                c.wait(e_uc)
                c.dma(dram["u1_sp"][:, :, lo:lo + TC],
                      u1c[:].rearrange("p (k t) -> p k t", k=NCH), "yS")

    B.barrier()


def phase3_mlp(B, nc, l0, dram, env):
    R = env["R"]
    onescol_b = env["onescol_b"]; onesrow_f = env["onesrow_f"]
    TC = 512
    NT = T // TC

    st = contextlib.ExitStack()
    with st:
        def sb(name, shape, dtype, stack=None):
            return (stack or st).enter_context(nc.sbuf_tensor(name, shape, dtype))

        def ps(name, shape, dtype=F32):
            return st.enter_context(nc.psum_tensor(name, shape, dtype))

        b1 = sb("b1_s", [P, NFF], F32)
        b2 = sb("b2_s", [P, NCH], F32)
        l2g = sb("l2g_s", [P, NCH], F32)
        l2b = sb("l2b_s", [P, NCH], F32)
        u1b = sb("u1b", [P, NCH * TC], BF16)
        ln2c = sb("ln2c", [P, NCH * TC], BF16)
        usq = ln2c
        rows2 = sb("rows2", [1, 2 * TC], F32)
        bc2 = sb("bc2", [P, 2 * TC], F32)
        tf = sb("tf", [P, TC], F32)
        h1b = sb("h1b", [P, NFF * TC], BF16)

        ps_r0 = ps("ps_r0", [1, 512])
        ps_r1 = ps("ps_r1", [1, 512])
        ps_b = ps("ps_b", [P, 512])
        ps_b2_ = ps("ps_b2_", [P, 512])
        ps_m0 = ps("ps3_m0", [P, 512])
        ps_m1 = ps("ps3_m1", [P, 512])
        ps_m2 = ps("ps3_m2", [P, 512])

        @B.sec("sync")
        def _(c):
            c.dma(b1[:], dram["ffn_b1"][:, :], "yM")
            c.dma(b2[:], dram["ffn_b2"][:, :], "yM")
            c.dma(l2g[:], dram["ln2_g"][:, :], "yM")
            c.dma(l2b[:], dram["ln2_b"][:, :], "yM")
        sload = B.ev("yM")

        sta = contextlib.ExitStack()
        with sta:
            w1 = sb("w1_s", [P, NCH * DFF], BF16, sta)

            @B.sec("gpsimd")
            def _(c):
                c.wait(B.ev("T"), B.ev("V"), B.ev("S"))
                c.dma(w1[:].rearrange("p (k m) -> p k m", k=NCH),
                      dram["w_ffn1"].rearrange("(k p) m -> p k m", p=P), "gA")
            wload = B.ev("gA")

            for tc in range(NT):
                lo = tc * TC

                @B.sec("sync")
                def _(c):
                    c.wait(B.ev("V"), B.ev("T"), B.ev("S"))
                    c.dma(u1b[:].rearrange("p (k t) -> p k t", k=NCH),
                          dram["u1_sp"][:, :, lo:lo + TC], "yA")
                e_ld = B.ev("yA")

                @B.sec("scalar")
                def _(c):
                    c.wait(e_ld, B.ev("T"))
                    for k in range(NCH):
                        ins = c.e.activation(usq[:, k * TC:(k + 1) * TC],
                                             u1b[:, k * TC:(k + 1) * TC], AF.Square)
                    c.inc(ins)
                e_sq = B.ev("S")

                @B.sec("tensor")
                def _(c):
                    c.wait(e_sq, B.ev("S"))
                    for k in range(NCH):
                        c.e.matmul(ps_r0[0:1, 0:TC], onescol_b[:],
                                   u1b[:, k * TC:(k + 1) * TC],
                                   start=(k == 0), stop=(k == NCH - 1))
                    for k in range(NCH):
                        ins = c.e.matmul(ps_r1[0:1, 0:TC], onescol_b[:],
                                         usq[:, k * TC:(k + 1) * TC],
                                         start=(k == 0), stop=(k == NCH - 1))
                    c.inc(ins)
                e_st = B.ev("T")

                @B.sec("scalar")
                def _(c):
                    c.wait(e_st)
                    c.e.activation(rows2[:, 0:TC], ps_r0[0:1, 0:TC], AF.Identity)
                    c.chain(c.e.activation(rows2[:, TC:2 * TC], ps_r0[0:1, 0:TC],
                                           AF.Square, scale=1.0 / D))
                e_r1 = B.ev("S")

                @B.sec("vector")
                def _(c):
                    c.wait(e_r1)
                    c.chain(c.e.scalar_tensor_tensor(
                        rows2[:, TC:2 * TC], ps_r1[0:1, 0:TC], 1.0 / D,
                        rows2[:, TC:2 * TC], ALU.mult, ALU.subtract))
                e_v = B.ev("V")

                @B.sec("scalar")
                def _(c):
                    c.wait(e_v)
                    ev = c.chain(c.e.activation(rows2[:, TC:2 * TC],
                                                rows2[:, TC:2 * TC], AF.Ln,
                                                bias=EPS))
                    c.chain(c.e.activation(rows2[:, TC:2 * TC],
                                           rows2[:, TC:2 * TC], AF.Exp,
                                           scale=-0.5))
                e_rw = B.ev("S")

                @B.sec("tensor")
                def _(c):
                    c.wait(e_rw, B.ev("V"))
                    c.e.matmul(ps_b[:, 0:TC], onesrow_f[:], rows2[:, 0:TC],
                               start=True, stop=True)
                    ins = c.e.matmul(ps_b2_[:, 0:TC], onesrow_f[:],
                                     rows2[:, TC:2 * TC], start=True, stop=True)
                    c.inc(ins)
                e_bc = B.ev("T")

                @B.sec("scalar")
                def _(c):
                    c.wait(e_bc)
                    c.e.activation(bc2[:, 0:TC], ps_b[:, 0:TC], AF.Identity,
                                   scale=1.0 / D)
                    ins = c.e.activation(bc2[:, TC:2 * TC], ps_b2_[:, 0:TC],
                                         AF.Identity)
                    c.inc(ins)
                e_bce = B.ev("S")

                @B.sec("vector")
                def _(c):
                    c.wait(e_bce, B.ev("T"))
                    for k in range(NCH):
                        c.chain(c.e.tensor_tensor(tf[:], u1b[:, k * TC:(k + 1) * TC],
                                                  bc2[:, 0:TC], ALU.subtract))
                        c.chain(c.e.tensor_tensor(ln2c[:, k * TC:(k + 1) * TC],
                                                  tf[:], bc2[:, TC:2 * TC],
                                                  ALU.mult))
                e_n = B.ev("V")

                @B.sec("scalar")
                def _(c):
                    c.wait(e_n, sload)
                    for k in range(NCH):
                        ins = c.e.activation(ln2c[:, k * TC:(k + 1) * TC],
                                             ln2c[:, k * TC:(k + 1) * TC],
                                             AF.Identity, scale=l2g[:, k:k + 1],
                                             bias=l2b[:, k:k + 1])
                    c.inc(ins)
                e_l2 = B.ev("S")

                for of in range(NFF):
                    pmm = (ps_m0, ps_m1, ps_m2)[of % 3]
                    pk = "3m%d" % (of % 3)

                    @B.sec("tensor")
                    def _(c):
                        c.wait(e_l2, wload, B.lastev.get(pk))
                        for k in range(NCH):
                            ins = c.e.matmul(
                                pmm[:, 0:TC],
                                w1[:, k * DFF + of * P:k * DFF + of * P + P],
                                ln2c[:, k * TC:(k + 1) * TC],
                                start=(k == 0), stop=(k == NCH - 1))
                        c.inc(ins)
                    e_h = B.ev("T")

                    @B.sec("scalar")
                    def _(c):
                        c.wait(e_h, B.ev("yS"))
                        B.lastev[pk] = c.inc(
                            c.e.activation(h1b[:, of * TC:(of + 1) * TC],
                                           pmm[:, 0:TC], AF.Gelu_apprx_tanh,
                                           bias=b1[:, of:of + 1]))
                e_g = B.ev("S")

                @B.sec("sync")
                def _(c):
                    c.wait(e_g)
                    c.dma(dram["h1_sp"][:, :, lo:lo + TC],
                          h1b[:].rearrange("p (k t) -> p k t", k=NFF), "yS")

        for half in (0, 1):
            sth = contextlib.ExitStack()
            with sth:
                w2h = sb(f"w2h_s{half}", [P, (NFF // 2) * D], BF16, sth)
                h1h = sb(f"h1h{half}", [P, (NFF // 2) * TC], BF16, sth)

                @B.sec("gpsimd")
                def _(c):
                    c.wait(B.ev("T"), B.ev("V"), B.ev("S"))
                    w2part = dram["w_ffn2"][half * (DFF // 2):(half + 1) * (DFF // 2), :]
                    c.dma(w2h[:].rearrange("p (k m) -> p k m", k=NFF // 2),
                          w2part.rearrange("(k p) m -> p k m", p=P), "gA")
                wload2 = B.ev("gA")

                for tc in range(NT):
                    lo = tc * TC

                    @B.sec("sync")
                    def _(c):
                        c.wait(B.ev("T"), B.ev("yS"))
                        c.dma(h1h[:].rearrange("p (k t) -> p k t", k=NFF // 2),
                              dram["h1_sp"][:, half * (NFF // 2):(half + 1) * (NFF // 2),
                                            lo:lo + TC], "yA")
                    e_ld = B.ev("yA")

                    for oc in range(NCH):
                        pmm = (ps_m0, ps_m1, ps_m2)[oc % 3]
                        pk = "3m%d" % (oc % 3)

                        @B.sec("tensor")
                        def _(c):
                            c.wait(e_ld, wload2, B.lastev.get(pk))
                            for k in range(NFF // 2):
                                ins = c.e.matmul(
                                    pmm[:, 0:TC],
                                    w2h[:, k * D + oc * P:k * D + oc * P + P],
                                    h1h[:, k * TC:(k + 1) * TC],
                                    start=(k == 0), stop=(k == NFF // 2 - 1))
                            c.inc(ins)
                        e_mm = B.ev("T")

                        @B.sec("vector")
                        def _(c):
                            c.wait(e_mm)
                            dst = R[:, oc * T + lo:oc * T + lo + TC]
                            if half == 0:
                                B.lastev[pk] = c.chain(
                                    c.e.tensor_tensor(dst, dst, pmm[:, 0:TC],
                                                      ALU.add))
                            else:
                                B.lastev[pk] = c.chain(
                                    c.e.scalar_tensor_tensor(
                                        dst, pmm[:, 0:TC], b2[:, oc:oc + 1], dst,
                                        ALU.add, ALU.add))
    B.barrier()


def phase4_moe(B, nc, l0, dram, env):
    R = env["R"]
    comb_fm = env["comb_fm"]
    sel16_b = env["sel16_b"]
    TC = 512
    NT = T // TC

    st = contextlib.ExitStack()
    with st:
        def sb(name, shape, dtype, stack=None):
            return (stack or st).enter_context(nc.sbuf_tensor(name, shape, dtype))

        def ps(name, shape, dtype=F32):
            return st.enter_context(nc.psum_tensor(name, shape, dtype))

        n0r = sb("n0r", [P, NCH * T], BF16)
        w13d = [sb("w13b0", [P, NCH * 2 * INTER], BF16),
                sb("w13b1", [P, NCH * 2 * INTER], BF16)]
        w2b = sb("w2b", [P, (INTER // P) * D], BF16)
        hgd = [sb("hg0", [P, 4 * TC], BF16), sb("hg1", [P, 4 * TC], BF16)]
        hud = [sb("hu0", [P, 4 * TC], BF16), sb("hu1", [P, 4 * TC], BF16)]
        cbb = sb("cbb", [P, TC], BF16)
        tmp = sb("tmp4", [P, TC], BF16)
        ymoe = sb("ymoe", [P, NCH * T], BF16)

        ps_s0 = ps("ps4_s0", [P, 512])
        ps_s1 = ps("ps4_s1", [P, 512])
        ps_s2 = ps("ps4_s2", [P, 512])
        ps_t0 = ps("ps4_t0", [P, 512])
        ps_t1 = ps("ps4_t1", [P, 512])
        ps_t2 = ps("ps4_t2", [P, 512])
        ps_cb = ps("ps4_cb", [P, 512])

        @B.sec("sync")
        def _(c):
            c.wait(B.ev("V"), B.ev("T"), B.ev("S"))
            c.dma(n0r[:].rearrange("p (k t) -> p k t", k=NCH),
                  dram["n0_sp"][:, :, :], "yA")
        e_n0 = B.ev("yA")

        for e in range(NEXP + 1):
            shared = (e == NEXP)
            par_e = e % 2
            w13b = w13d[par_e]
            qn = "gA%d" % par_e

            @B.sec("gpsimd")
            def _(c):
                c.wait(B.lastev.get("4wT%d" % par_e))
                src13 = dram["w13sT"] if shared else dram["w13T"][e]
                c.dma(w13b[:].rearrange("p (k m) -> p k m", k=NCH),
                      src13.rearrange("(k p) m -> p k m", p=P), qn)
            e_w = B.ev(qn)

            @B.sec("gpsimd")
            def _(c):
                c.wait(B.ev("T"))
                src2 = dram["w2sT"] if shared else dram["w2T"][e]
                c.dma(w2b[:].rearrange("p (k m) -> p k m", k=INTER // P),
                      src2.rearrange("(k p) m -> p k m", p=P), qn)
            e_w2 = B.ev(qn)

            hh_ev = {}

            def emit_front(tc):
                lo = tc * TC
                par = tc % 2
                hg = hgd[par]
                hu = hud[par]

                e_cb = None
                if not shared:
                    @B.sec("tensor")
                    def _(c):
                        c.wait(B.lastev.get("4cbe"))
                        c.inc(c.e.matmul(ps_cb[:, 0:TC],
                                         sel16_b[:, e * P:(e + 1) * P],
                                         comb_fm[:, lo:lo + TC],
                                         start=True, stop=True))
                    e_cbm = B.ev("T")

                    @B.sec("scalar")
                    def _(c):
                        c.wait(e_cbm, B.lastev.get("4cbu"))
                        B.lastev["4cbe"] = c.inc(
                            c.e.activation(cbb[:], ps_cb[:, 0:TC], AF.Identity))
                    e_cb = B.ev("S")

                for og in range(8):
                    pmm = (ps_s0, ps_s1, ps_s2)[og % 3]
                    pk = "4s%d" % (og % 3)

                    @B.sec("tensor")
                    def _(c):
                        c.wait(e_w, e_n0, B.lastev.get(pk))
                        for k in range(NCH):
                            ins = c.e.matmul(
                                pmm[:, 0:TC],
                                w13b[:, k * (2 * INTER) + og * P:
                                     k * (2 * INTER) + og * P + P],
                                n0r[:, k * T + lo:k * T + lo + TC],
                                start=(k == 0), stop=(k == NCH - 1))
                        c.inc(ins)
                    e_s1 = B.ev("T")

                    @B.sec("scalar")
                    def _(c):
                        c.wait(e_s1)
                        if og < 4:
                            B.lastev[pk] = c.inc(
                                c.e.activation(hg[:, og * TC:(og + 1) * TC],
                                               pmm[:, 0:TC], AF.Silu))
                        else:
                            B.lastev[pk] = c.inc(
                                c.e.activation(hu[:, (og - 4) * TC:(og - 3) * TC],
                                               pmm[:, 0:TC], AF.Identity))
                e_guev = B.ev("S")

                @B.sec("vector")
                def _(c):
                    c.wait(e_guev)
                    if not shared:
                        c.wait(e_cb)
                    ev = c.chain(c.e.tensor_tensor(hg[:], hg[:], hu[:], ALU.mult))
                    if not shared:
                        cb3 = cbb[:].unsqueeze(1).broadcast_to([P, 4, TC])
                        ev = c.chain(c.e.tensor_tensor(
                            hg[:].rearrange("p (k t) -> p k t", k=4),
                            hg[:].rearrange("p (k t) -> p k t", k=4),
                            cb3, ALU.mult))
                    B.lastev["4cbu"] = ev
                hh_ev[tc] = B.ev("V")

            def emit_back(tc):
                lo = tc * TC
                hg = hgd[tc % 2]

                for oc in range(NCH):
                    pmm = (ps_t0, ps_t1, ps_t2)[oc % 3]
                    pk = "4t%d" % (oc % 3)

                    @B.sec("tensor")
                    def _(c):
                        c.wait(hh_ev[tc], B.lastev.get(pk), e_w2)
                        for k in range(INTER // P):
                            ins = c.e.matmul(
                                pmm[:, 0:TC],
                                w2b[:, k * D + oc * P:k * D + oc * P + P],
                                hg[:, k * TC:(k + 1) * TC],
                                start=(k == 0), stop=(k == INTER // P - 1))
                        c.inc(ins)
                    e_s2 = B.ev("T")

                    if e == 0:
                        @B.sec("scalar")
                        def _(c):
                            c.wait(e_s2)
                            B.lastev[pk] = c.inc(c.e.activation(
                                ymoe[:, oc * T + lo:oc * T + lo + TC],
                                pmm[:, 0:TC], AF.Identity))
                    else:
                        @B.sec("scalar")
                        def _(c):
                            c.wait(e_s2, B.lastev.get("4tmp"))
                            B.lastev[pk] = c.inc(
                                c.e.activation(tmp[:], pmm[:, 0:TC], AF.Identity))
                        e_t = B.ev("S")

                        @B.sec("vector")
                        def _(c):
                            c.wait(e_t)
                            dst = ymoe[:, oc * T + lo:oc * T + lo + TC]
                            B.lastev["4tmp"] = c.chain(
                                c.e.tensor_tensor(dst, dst, tmp[:], ALU.add))

            for tc in range(NT + 1):
                if tc < NT:
                    emit_front(tc)
                if tc >= 1:
                    emit_back(tc - 1)
            B.lastev["4wT%d" % par_e] = B.ev("T")

        for tc in range(NT):
            lo = tc * TC

            @B.sec("vector")
            def _(c):
                c.wait(B.ev("S"))
                for oc in range(NCH):
                    dst = R[:, oc * T + lo:oc * T + lo + TC]
                    c.chain(c.e.tensor_tensor(
                        dst, dst, ymoe[:, oc * T + lo:oc * T + lo + TC], ALU.add))
    B.barrier()


def phase5_l1(B, nc, l1, dram, env):
    R = env["R"]
    onescol_b = env["onescol_b"]; onesrow_f = env["onesrow_f"]
    ident_b = env["ident_b"]; tri_f = env["tri_f"]

    TC = 512
    NT = T // TC
    st = contextlib.ExitStack()
    with st:
        def sb(name, shape, dtype, stack=None):
            return (stack or st).enter_context(nc.sbuf_tensor(name, shape, dtype))

        def psn(name, shape, dtype=F32, stack=None):
            return (stack or st).enter_context(nc.psum_tensor(name, shape, dtype))

        sta = contextlib.ExitStack()
        with sta:
            lg = sb("lg5", [P, NCH], F32, sta)
            lb = sb("lb5", [P, NCH], F32, sta)
            xb5 = sb("xb5", [P, NCH * TC], BF16, sta)
            xq5 = sb("xq5", [P, NCH * TC], BF16, sta)
            n1c = sb("n1c", [P, NCH * TC], BF16, sta)
            xnc = sb("xnc", [P, NCH * TC], BF16, sta)
            rows5 = sb("rows5", [1, 3 * TC], F32, sta)
            bc5 = sb("bc5", [P, 3 * TC], F32, sta)
            tf5 = sb("tf5", [P, TC], F32, sta)
            ps_r0 = psn("ps5_r0", [1, 512], F32, sta)
            ps_r1 = psn("ps5_r1", [1, 512], F32, sta)
            ps_b0 = psn("ps5_b0", [P, 512], F32, sta)
            ps_b1 = psn("ps5_b1", [P, 512], F32, sta)

            @B.sec("sync")
            def _(c):
                c.dma(lg[:], dram["l1_g"][:, :], "yM")
                c.dma(lb[:], dram["l1_b"][:, :], "yM")
            sload = B.ev("yM")

            for tc in range(NT):
                lo = tc * TC

                @B.sec("scalar")
                def _(c):
                    c.wait(B.ev("T"))
                    for k in range(NCH):
                        c.e.activation(xb5[:, k * TC:(k + 1) * TC],
                                       R[:, k * T + lo:k * T + lo + TC],
                                       AF.Identity)
                        ins = c.e.activation(xq5[:, k * TC:(k + 1) * TC],
                                             R[:, k * T + lo:k * T + lo + TC],
                                             AF.Square)
                    c.inc(ins)
                e_c = B.ev("S")

                @B.sec("tensor")
                def _(c):
                    c.wait(e_c, B.ev("S"))
                    for k in range(NCH):
                        c.e.matmul(ps_r0[0:1, 0:TC], onescol_b[:],
                                   xb5[:, k * TC:(k + 1) * TC],
                                   start=(k == 0), stop=(k == NCH - 1))
                    for k in range(NCH):
                        ins = c.e.matmul(ps_r1[0:1, 0:TC], onescol_b[:],
                                         xq5[:, k * TC:(k + 1) * TC],
                                         start=(k == 0), stop=(k == NCH - 1))
                    c.inc(ins)
                e_mm = B.ev("T")

                @B.sec("scalar")
                def _(c):
                    c.wait(e_mm)
                    c.e.activation(rows5[:, 0:TC], ps_r0[0:1, 0:TC], AF.Identity)
                    c.chain(c.e.activation(rows5[:, TC:2 * TC], ps_r0[0:1, 0:TC],
                                           AF.Square, scale=1.0 / D))
                e_r = B.ev("S")

                @B.sec("vector")
                def _(c):
                    c.wait(e_r)
                    c.chain(c.e.scalar_tensor_tensor(
                        rows5[:, TC:2 * TC], ps_r1[0:1, 0:TC], 1.0 / D,
                        rows5[:, TC:2 * TC], ALU.mult, ALU.subtract))
                e_v1 = B.ev("V")

                @B.sec("scalar")
                def _(c):
                    c.wait(e_v1)
                    c.chain(c.e.activation(rows5[:, 2 * TC:3 * TC],
                                           rows5[:, TC:2 * TC], AF.Ln, bias=EPS))
                    c.chain(c.e.activation(rows5[:, TC:2 * TC],
                                           rows5[:, 2 * TC:3 * TC], AF.Exp,
                                           scale=-0.5))
                e_s1 = B.ev("S")

                @B.sec("vector")
                def _(c):
                    c.wait(e_s1)
                    c.chain(c.e.tensor_tensor(rows5[:, 2 * TC:3 * TC],
                                              rows5[:, TC:2 * TC],
                                              rows5[:, TC:2 * TC], ALU.mult))
                    c.chain(c.e.tensor_scalar(rows5[:, 2 * TC:3 * TC],
                                              rows5[:, 2 * TC:3 * TC], -EPS, 1.0,
                                              ALU.mult, ALU.add))
                e_v2 = B.ev("V")

                @B.sec("scalar")
                def _(c):
                    c.wait(e_v2)
                    c.chain(c.e.activation(rows5[:, 2 * TC:3 * TC],
                                           rows5[:, 2 * TC:3 * TC], AF.Ln,
                                           bias=EPS))
                    c.chain(c.e.activation(rows5[:, 2 * TC:3 * TC],
                                           rows5[:, 2 * TC:3 * TC], AF.Exp,
                                           scale=-0.5))
                e_s2 = B.ev("S")

                @B.sec("vector")
                def _(c):
                    c.wait(e_s2)
                    c.chain(c.e.tensor_tensor(rows5[:, 2 * TC:3 * TC],
                                              rows5[:, 2 * TC:3 * TC],
                                              rows5[:, TC:2 * TC], ALU.mult))
                e_rows = B.ev("V")

                @B.sec("tensor")
                def _(c):
                    c.wait(e_rows, B.ev("S"))
                    c.e.matmul(ps_b0[:, 0:TC], onesrow_f[:], rows5[:, 0:TC],
                               start=True, stop=True)
                    ins = c.e.matmul(ps_b1[:, 0:TC], onesrow_f[:],
                                     rows5[:, TC:2 * TC], start=True, stop=True)
                    c.inc(ins)
                e_b1 = B.ev("T")

                @B.sec("scalar")
                def _(c):
                    c.wait(e_b1)
                    c.e.activation(bc5[:, 0:TC], ps_b0[:, 0:TC], AF.Identity,
                                   scale=1.0 / D)
                    c.inc(c.e.activation(bc5[:, TC:2 * TC], ps_b1[:, 0:TC],
                                         AF.Identity))
                e_b1e = B.ev("S")

                @B.sec("tensor")
                def _(c):
                    c.wait(e_b1e)
                    c.inc(c.e.matmul(ps_b0[:, 0:TC], onesrow_f[:],
                                     rows5[:, 2 * TC:3 * TC], start=True,
                                     stop=True))
                e_b2 = B.ev("T")

                @B.sec("scalar")
                def _(c):
                    c.wait(e_b2)
                    c.inc(c.e.activation(bc5[:, 2 * TC:3 * TC], ps_b0[:, 0:TC],
                                         AF.Identity))
                e_bc = B.ev("S")

                @B.sec("vector")
                def _(c):
                    c.wait(e_bc, B.ev("T"), B.ev("yS"))
                    for k in range(NCH):
                        c.chain(c.e.tensor_tensor(tf5[:],
                                                  R[:, k * T + lo:k * T + lo + TC],
                                                  bc5[:, 0:TC], ALU.subtract))
                        c.e.tensor_tensor(n1c[:, k * TC:(k + 1) * TC], tf5[:],
                                          bc5[:, TC:2 * TC], ALU.mult)
                        c.chain(c.e.tensor_tensor(xnc[:, k * TC:(k + 1) * TC],
                                                  tf5[:], bc5[:, 2 * TC:3 * TC],
                                                  ALU.mult))
                e_n = B.ev("V")

                @B.sec("scalar")
                def _(c):
                    c.wait(e_n, sload)
                    for k in range(NCH):
                        ins = c.e.activation(xnc[:, k * TC:(k + 1) * TC],
                                             xnc[:, k * TC:(k + 1) * TC],
                                             AF.Identity, scale=lg[:, k:k + 1],
                                             bias=lb[:, k:k + 1])
                    c.inc(ins)
                e_x = B.ev("S")

                @B.sec("sync")
                def _(c):
                    c.wait(e_x, e_n)
                    c.dma(dram["n1_sp"][:, :, lo:lo + TC],
                          n1c[:].rearrange("p (k t) -> p k t", k=NCH), "yS")
                    c.dma(dram["xn_sp"][:, :, lo:lo + TC],
                          xnc[:].rearrange("p (k t) -> p k t", k=NCH), "yS")
            B.barrier()

        stb = contextlib.ExitStack()
        with stb:
            wq = sb("wq5", [P, NCH * D], BF16, stb)
            wk = sb("wk5", [P, NCH * D], BF16, stb)
            wv = sb("wv5", [P, NCH * D], BF16, stb)
            rot_b = sb("rot_b", [P, P], BF16, stb)
            cosq = sb("cosq", [P, T], BF16, stb)
            sinq = sb("sinq", [P, T], BF16, stb)
            cosk = sb("cosk", [P, T], BF16, stb)
            sink = sb("sink", [P, T], BF16, stb)
            xnb = sb("xnb", [P, NCH * TC], BF16, stb)
            qbuf = sb("qbuf", [P, NH * TC], BF16, stb)
            vtb = sb("vtb", [P, NH * TC], BF16, stb)
            qt5 = sb("qt5", [P, TC], BF16, stb)
            rt5 = sb("rt5", [P, TC], BF16, stb)
            m15 = sb("m15", [P, TC], BF16, stb)
            ps_p0 = psn("ps5b_0", [P, 512], F32, stb)
            ps_p1 = psn("ps5b_1", [P, 512], F32, stb)
            ps_p2 = psn("ps5b_2", [P, 512], F32, stb)
            ps_rt = psn("ps5b_rt", [P, 512], F32, stb)
            ps_vt = psn("ps5b_vt", [P, 1024], BF16, stb)

            @B.sec("gpsimd")
            def _(c):
                c.wait(B.ev("T"), B.ev("V"), B.ev("S"))
                c.dma(wq[:].rearrange("p (k m) -> p k m", k=NCH),
                      dram["wqT"].rearrange("(k p) m -> p k m", p=P), "gA")
                c.dma(wk[:].rearrange("p (k m) -> p k m", k=NCH),
                      dram["wkT"].rearrange("(k p) m -> p k m", p=P), "gA")
                c.dma(wv[:].rearrange("p (k m) -> p k m", k=NCH),
                      dram["wvT"].rearrange("(k p) m -> p k m", p=P), "gA")
                c.dma(rot_b[:], dram["rot_sT"][:, :], "gA")
                c.dma(cosq[:], dram["cos_q"][:, :], "gA")
                c.dma(sinq[:], dram["sin_q"][:, :], "gA")
                c.dma(cosk[:], dram["cos_k"][:, :], "gA")
                c.dma(sink[:], dram["sin_k"][:, :], "gA")
            wload = B.ev("gA")

            for tc in range(NT):
                lo = tc * TC

                @B.sec("sync")
                def _(c):
                    c.wait(B.ev("T"), B.ev("yS"))
                    c.dma(xnb[:].rearrange("p (k t) -> p k t", k=NCH),
                          dram["xn_sp"][:, :, lo:lo + TC], "yA")
                e_ld = B.ev("yA")

                for which, wmat, ctab, stab, spname in (
                        ("q", wq, cosq, sinq, "q_sp"),
                        ("k", wk, cosk, sink, "k_sp")):
                    for h in range(NH):
                        pmm = (ps_p0, ps_p1, ps_p2)[h % 3]
                        pk = "5p%d" % (h % 3)

                        @B.sec("tensor")
                        def _(c):
                            c.wait(e_ld, wload, B.lastev.get(pk))
                            for k in range(NCH):
                                ins = c.e.matmul(
                                    pmm[:, 0:TC],
                                    wmat[:, k * D + h * P:k * D + h * P + P],
                                    xnb[:, k * TC:(k + 1) * TC],
                                    start=(k == 0), stop=(k == NCH - 1))
                            c.inc(ins)
                        e_p = B.ev("T")

                        @B.sec("vector")
                        def _(c):
                            c.wait(e_p, B.lastev.get("5qt"))
                            B.lastev[pk] = c.inc(
                                c.e.tensor_copy(qt5[:], pmm[:, 0:TC]))
                        e_qt = B.ev("V")

                        @B.sec("tensor")
                        def _(c):
                            c.wait(e_qt, B.lastev.get("5rt"))
                            c.inc(c.e.matmul(ps_rt[:, 0:TC], rot_b[:], qt5[:],
                                             start=True, stop=True))
                        e_rt = B.ev("T")

                        @B.sec("vector")
                        def _(c):
                            c.wait(e_rt, B.lastev.get("5rte"))
                            B.lastev["5rt"] = c.inc(
                                c.e.tensor_copy(rt5[:], ps_rt[:, 0:TC]))
                        e_rte = B.ev("V")

                        @B.sec("vector")
                        def _(c):
                            c.wait(e_rte, B.ev("yS"))
                            ev = c.chain(c.e.tensor_tensor(m15[:], qt5[:],
                                                           ctab[:, lo:lo + TC],
                                                           ALU.mult))
                            B.lastev["5qt"] = ev
                            c.chain(c.e.tensor_tensor(rt5[:], rt5[:],
                                                      stab[:, lo:lo + TC],
                                                      ALU.mult))
                            B.lastev["5rte"] = c.chain(c.e.tensor_tensor(
                                qbuf[:, h * TC:(h + 1) * TC], m15[:], rt5[:],
                                ALU.add))
                    e_rope = B.ev("V")

                    @B.sec("sync")
                    def _(c):
                        c.wait(e_rope)
                        c.dma(dram[spname][:, :, lo:lo + TC],
                              qbuf[:].rearrange("p (h t) -> p h t", h=NH), "yS")

                for h in range(NH):
                    pmm = (ps_p0, ps_p1, ps_p2)[h % 3]
                    pk = "5p%d" % (h % 3)

                    @B.sec("tensor")
                    def _(c):
                        c.wait(e_ld, wload, B.lastev.get(pk))
                        for k in range(NCH):
                            ins = c.e.matmul(
                                pmm[:, 0:TC],
                                wv[:, k * D + h * P:k * D + h * P + P],
                                xnb[:, k * TC:(k + 1) * TC],
                                start=(k == 0), stop=(k == NCH - 1))
                        c.inc(ins)
                    e_p = B.ev("T")

                    @B.sec("vector")
                    def _(c):
                        c.wait(e_p, B.ev("T"), B.lastev.get("5qt"))
                        B.lastev[pk] = c.inc(
                            c.e.tensor_copy(qt5[:], pmm[:, 0:TC]))
                    e_v = B.ev("V")

                    for j in range(TC // P):
                        @B.sec("tensor")
                        def _(c):
                            c.wait(e_v, B.lastev.get("5vt"))
                            c.inc(c.e.transpose(ps_vt[:, 0:P],
                                                qt5[:, j * P:(j + 1) * P],
                                                ident_b[:]))
                        e_t = B.ev("T")

                        @B.sec("vector")
                        def _(c):
                            c.wait(e_t, B.ev("yS"))
                            B.lastev["5vt"] = c.inc(c.e.tensor_copy(
                                vtb[:, h * TC + j * P:h * TC + (j + 1) * P],
                                ps_vt[:, 0:P]))
                e_vt = B.ev("V")

                @B.sec("sync")
                def _(c):
                    c.wait(e_vt, B.ev("S"))
                    c.dma(dram["vt_sp"][:, :, tc * (TC // P):(tc + 1) * (TC // P), :],
                          vtb[:].rearrange("p (h j q) -> p h j q", h=NH, j=TC // P),
                          "yS")
            B.barrier()

        stc = contextlib.ExitStack()
        with stc:
            dirs_b = sb("dirs_b", [P, NH], BF16, stc)
            hs_all = sb("hs_all", [P, NH], F32, stc)
            om_all = sb("om_all", [P, NH], F32, stc)
            hsr = sb("hsr", [1, NH], F32, stc)
            aout = sb("aout", [P, NCH * T], BF16, stc)
            stc2 = stc.enter_context(contextlib.ExitStack())
            qh = sb("qh", [P, T], BF16, stc2)
            kh = sb("kh", [P, T], BF16, stc2)
            vth = sb("vth", [P, (T // P) * P], BF16, stc2)
            qp_b = sb("qp_b", [1, T], BF16, stc2)
            kp_b = sb("kp_b", [1, T], BF16, stc2)
            attn = sb("attn", [P, T], BF16, stc2)
            scd = [sb("sc0", [P, T], BF16, stc2), sb("sc1", [P, T], BF16, stc2)]
            gsd = [sb("gs0", [P, T], BF16, stc2), sb("gs1", [P, T], BF16, stc2)]
            at0 = sb("at0", [P, 4 * P], BF16, stc2)
            at1 = sb("at1", [P, 4 * P], BF16, stc2)
            nmax = sb("nmax", [P, 1], F32, stc2)
            sums = sb("sums", [P, 1], F32, stc2)
            rsum = sb("rsum", [P, 1], F32, stc2)
            cmb5 = sb("cmb5", [P, P], F32, stc2)
            ps_sc2 = [psn("ps5c_sca", [P, 512], F32, stc2),
                      psn("ps5c_scb", [P, 512], F32, stc2)]
            ps_t0 = psn("ps5c_t0", [P, 1024], BF16, stc2)
            ps_t1 = psn("ps5c_t1", [P, 1024], BF16, stc2)
            ps_av = psn("ps5c_av", [P, 512], F32, stc2)
            ps_ag = psn("ps5c_ag", [P, 512], F32, stc2)

            @B.sec("gpsimd")
            def _(c):
                c.wait(B.ev("T"), B.ev("V"), B.ev("S"))
                c.dma(dirs_b[:], dram["dirs"][:, :], "gA")
                c.dma(hsr[:], dram["hscale"][:, :], "gA")
            e_dl = B.ev("gA")

            @B.sec("tensor")
            def _(c):
                c.wait(e_dl, B.ev("S"), B.ev("V"))
                c.inc(c.e.matmul(ps_av[0:P, 0:NH], onesrow_f[:], hsr[:, :],
                                 start=True, stop=True))
            e_hsm = B.ev("T")

            @B.sec("scalar")
            def _(c):
                c.wait(e_hsm)
                c.inc(c.e.activation(hs_all[:], ps_av[0:P, 0:NH], AF.Identity))
            e_hs = B.ev("S")

            @B.sec("vector")
            def _(c):
                c.wait(e_hs)
                c.chain(c.e.tensor_scalar(om_all[:], hs_all[:], -1.0, 1.0,
                                          ALU.mult, ALU.add))
            e_om = B.ev("V")

            for h in range(NH):
                @B.sec("sync")
                def _(c):
                    c.wait(B.ev("T"), B.ev("V"), B.ev("S"), B.ev("yS"))
                    c.dma(qh[:], dram["q_sp"][:, h, :], "yA")
                    c.dma(kh[:], dram["k_sp"][:, h, :], "yA")
                    c.dma(vth[:].rearrange("p (j q) -> p j q", j=T // P),
                          dram["vt_sp"][:, h, :, :], "yA")
                e_hl = B.ev("yA")

                for j in range(T // 512):
                    @B.sec("tensor")
                    def _(c):
                        c.wait(e_hl, e_om, B.ev("S"))
                        c.e.matmul(ps_av[0:1, 0:512], dirs_b[:, h:h + 1],
                                   qh[:, j * 512:(j + 1) * 512],
                                   start=True, stop=True)
                        c.inc(c.e.matmul(ps_ag[0:1, 0:512], dirs_b[:, h:h + 1],
                                         kh[:, j * 512:(j + 1) * 512],
                                         start=True, stop=True))
                    e_qpm = B.ev("T")

                    @B.sec("scalar")
                    def _(c):
                        c.wait(e_qpm)
                        c.e.activation(qp_b[:, j * 512:(j + 1) * 512],
                                       ps_av[0:1, 0:512], AF.Identity,
                                       scale=float(np.sqrt(HD)))
                        c.inc(c.e.activation(kp_b[:, j * 512:(j + 1) * 512],
                                             ps_ag[0:1, 0:512], AF.Identity))
                e_qp = B.ev("S")

                def scores_stage(qt):
                    par = qt % 2
                    kmax = (qt + 1) * P
                    nch_k = (kmax + 511) // 512
                    for which, lhs, dstb in (("m", qh, scd[par]),
                                             ("g", qp_b, gsd[par])):
                        for j in range(nch_k):
                            ke = min(kmax, (j + 1) * 512)
                            pb = ps_sc2[j % 2]
                            pkey = "5sb%d" % (j % 2)

                            @B.sec("tensor")
                            def _(c):
                                c.wait(e_hl, e_qp, B.lastev.get(pkey),
                                       B.lastev.get("5msk%s%d" % (which, par)))
                                if which == "m":
                                    ins = c.e.matmul(pb[:, 0:ke - j * 512],
                                                     qh[:, qt * P:(qt + 1) * P],
                                                     kh[:, j * 512:ke],
                                                     start=True, stop=True)
                                else:
                                    ins = c.e.matmul(pb[:, 0:ke - j * 512],
                                                     qp_b[:, qt * P:(qt + 1) * P],
                                                     kp_b[:, j * 512:ke],
                                                     start=True, stop=True)
                                c.inc(ins)
                            e_mm = B.ev("T")

                            @B.sec("scalar")
                            def _(c):
                                c.wait(e_mm)
                                B.lastev[pkey] = c.inc(c.e.activation(
                                    dstb[:, j * 512:ke], pb[:, 0:ke - j * 512],
                                    AF.Identity))
                    return B.ev("S")

                def softmax_av_stage(qt, e_sc):
                    par = qt % 2
                    kmax = (qt + 1) * P
                    e_avm = None
                    e_agm = None
                    for which, srcb, ps_out in (("m", scd[par], ps_av),
                                                ("g", gsd[par], ps_ag)):
                        @B.sec("vector")
                        def _(c):
                            c.wait(e_sc, B.lastev.get("5at"), B.ev("S"))
                            c.chain(c.e.tensor_tensor(
                                srcb[:, qt * P:(qt + 1) * P],
                                srcb[:, qt * P:(qt + 1) * P], tri_f[:],
                                ALU.add))
                            ev = c.chain(c.e.reduce_max(nmax[:], srcb[:, 0:kmax],
                                                        axis=AX.X, negate=True))
                            B.lastev["5msk%s%d" % (which, par)] = ev
                        e_mx = B.ev("V")

                        @B.sec("scalar")
                        def _(c):
                            c.wait(e_mx, B.ev("T"))
                            c.chain(c.e.activation(attn[:, 0:kmax],
                                                   srcb[:, 0:kmax], AF.Exp,
                                                   bias=nmax[:, 0:1],
                                                   accum_out=sums[:]))
                        e_ex = B.ev("S")

                        @B.sec("vector")
                        def _(c):
                            c.wait(e_ex)
                            c.chain(c.e.reciprocal(rsum[:], sums[:]))
                            c.chain(c.e.tensor_scalar(attn[:, 0:kmax],
                                                      attn[:, 0:kmax],
                                                      rsum[:, 0:1], None,
                                                      ALU.mult))
                        e_at = B.ev("V")
                        B.lastev["5at"] = e_at

                        groups = [(g, min(4, qt + 1 - g))
                                  for g in range(0, qt + 1, 4)]

                        def emit_tr(gi):
                            g, cnt = groups[gi]
                            pst = ps_t0 if (gi % 2 == 0) else ps_t1
                            tk = "5t%d" % (gi % 2)

                            @B.sec("tensor")
                            def _(c):
                                c.wait(e_at, B.lastev.get(tk))
                                for i in range(cnt):
                                    c.inc(c.e.transpose(
                                        pst[:, i * P:(i + 1) * P],
                                        attn[:, (g + i) * P:(g + i + 1) * P],
                                        ident_b[:]))
                            e_tr = B.ev("T")

                            @B.sec("vector")
                            def _(c):
                                c.wait(e_tr)
                                B.lastev[tk] = c.inc(c.e.tensor_copy(
                                    (at0 if gi % 2 == 0 else at1)[:, 0:cnt * P],
                                    pst[:, 0:cnt * P]))
                            return B.ev("V")

                        def emit_av(gi, e_te):
                            g, cnt = groups[gi]
                            atb = at0 if (gi % 2 == 0) else at1

                            @B.sec("tensor")
                            def _(c):
                                c.wait(e_te)
                                for i in range(cnt):
                                    c.inc(c.e.matmul(
                                        ps_out[:, 0:P],
                                        vth[:, (g + i) * P:(g + i + 1) * P],
                                        atb[:, i * P:(i + 1) * P],
                                        start=(g + i == 0),
                                        stop=(g + i == qt)))

                        pg = None
                        for gi in range(len(groups)):
                            e_te = emit_tr(gi)
                            if pg is not None:
                                emit_av(pg[0], pg[1])
                            pg = (gi, e_te)
                        emit_av(pg[0], pg[1])
                        if which == "m":
                            e_avm = B.ev("T")
                        else:
                            e_agm = B.ev("T")

                    @B.sec("scalar")
                    def _(c):
                        c.wait(e_agm, B.lastev.get("5cmb"))
                        c.inc(c.e.activation(cmb5[:], ps_ag[:, 0:P], AF.Identity,
                                             scale=hs_all[:, h:h + 1]))
                    e_cg = B.ev("S")

                    @B.sec("vector")
                    def _(c):
                        c.wait(e_cg, e_avm)
                        B.lastev["5cmb"] = c.chain(c.e.scalar_tensor_tensor(
                            aout[:, h * T + qt * P:h * T + (qt + 1) * P],
                            ps_av[:, 0:P], om_all[:, h:h + 1], cmb5[:],
                            ALU.mult, ALU.add))

                prev = None
                for qt in range(T // P):
                    e_sc = scores_stage(qt)
                    if prev is not None:
                        softmax_av_stage(prev[0], prev[1])
                    prev = (qt, e_sc)
                softmax_av_stage(prev[0], prev[1])
            B.barrier()
            stc2.close()

            std = contextlib.ExitStack()
            with std:
                wo = sb("wo5", [P, NCH * D], BF16, std)
                n1b5 = sb("n1b5", [P, NCH * TC], BF16, std)
                tfo = sb("tfo", [P, TC], F32, std)
                ps_d0 = psn("ps5d_0", [P, 512], F32, std)
                ps_d1 = psn("ps5d_1", [P, 512], F32, std)

                @B.sec("gpsimd")
                def _(c):
                    c.dma(wo[:].rearrange("p (k m) -> p k m", k=NCH),
                          dram["woT"].rearrange("(k p) m -> p k m", p=P), "gA")
                wlo = B.ev("gA")

                for tc in range(NT):
                    lo = tc * TC

                    @B.sec("sync")
                    def _(c):
                        c.wait(B.ev("V"))
                        c.dma(n1b5[:].rearrange("p (k t) -> p k t", k=NCH),
                              dram["n1_sp"][:, :, lo:lo + TC], "yA")
                    e_ld = B.ev("yA")

                    for oc in range(NCH):
                        pmm = ps_d0 if (oc % 2 == 0) else ps_d1

                        @B.sec("tensor")
                        def _(c):
                            c.wait(wlo, B.ev("V"))
                            for k in range(NCH):
                                ins = c.e.matmul(
                                    pmm[:, 0:TC],
                                    wo[:, k * D + oc * P:k * D + oc * P + P],
                                    aout[:, k * T + lo:k * T + lo + TC],
                                    start=(k == 0), stop=(k == NCH - 1))
                            c.inc(ins)
                        e_mm = B.ev("T")

                        @B.sec("vector")
                        def _(c):
                            c.wait(e_mm, e_ld)
                            c.chain(c.e.tensor_tensor(
                                tfo[:], pmm[:, 0:TC],
                                n1b5[:, oc * TC:(oc + 1) * TC], ALU.add))
                            c.chain(c.e.tensor_tensor(
                                R[:, oc * T + lo:oc * T + lo + TC],
                                R[:, oc * T + lo:oc * T + lo + TC],
                                tfo[:], ALU.add))
            B.barrier()


        ste = contextlib.ExitStack()
        with ste:
            wr = sb("wr5", [P, NCH * D], BF16, ste)
            wk2 = sb("wk25", [P, NCH * D], BF16, ste)
            wv2 = sb("wv25", [P, NCH * D], BF16, ste)
            mur = sb("mur5", [P, NCH], F32, ste)
            muk = sb("muk5", [P, NCH], F32, ste)
            n1x = sb("n1x", [P, NCH * (TC + 1)], BF16, ste)
            xr = sb("xr5", [P, NCH * TC], BF16, ste)
            xk = sb("xk5", [P, NCH * TC], BF16, ste)
            rb5 = sb("rb5", [P, NCH * TC], BF16, ste)
            kb5 = sb("kb5", [P, NCH * TC], BF16, ste)
            td5 = sb("td5", [P, TC], BF16, ste)
            tf6 = sb("tf6", [P, TC], F32, ste)
            ps_e0 = psn("ps5e_0", [P, 512], F32, ste)
            ps_e1 = psn("ps5e_1", [P, 512], F32, ste)
            ps_e2 = psn("ps5e_2", [P, 512], F32, ste)

            @B.sec("gpsimd")
            def _(c):
                c.wait(B.ev("T"), B.ev("V"), B.ev("S"))
                c.dma(wr[:].rearrange("p (k m) -> p k m", k=NCH),
                      dram["wrT"].rearrange("(k p) m -> p k m", p=P), "gA")
                c.dma(wk2[:].rearrange("p (k m) -> p k m", k=NCH),
                      dram["wkT2"].rearrange("(k p) m -> p k m", p=P), "gA")
                c.dma(wv2[:].rearrange("p (k m) -> p k m", k=NCH),
                      dram["wvT2"].rearrange("(k p) m -> p k m", p=P), "gA")
            wle = B.ev("gA")

            @B.sec("sync")
            def _(c):
                c.dma(mur[:], dram["mu_r"][:, :], "yM")
                c.dma(muk[:], dram["mu_k"][:, :], "yM")
            sle = B.ev("yM")

            for tc in range(NT):
                lo = tc * TC

                @B.sec("vector")
                def _(c):
                    c.wait(B.ev("S"), B.ev("T"))
                    if tc == 0:
                        for k in range(NCH):
                            c.inc(c.e.memset(n1x[:, k * (TC + 1):k * (TC + 1) + 1],
                                             0.0))
                e_z = B.ev("V")

                @B.sec("sync")
                def _(c):
                    c.wait(e_z, B.ev("T"))
                    if tc == 0:
                        c.dma(n1x[:].rearrange("p (k t) -> p k t", k=NCH)
                              [:, :, 1:TC + 1],
                              dram["n1_sp"][:, :, 0:TC], "yA")
                    else:
                        c.dma(n1x[:].rearrange("p (k t) -> p k t", k=NCH),
                              dram["n1_sp"][:, :, lo - 1:lo + TC], "yA")
                e_ld = B.ev("yA")

                @B.sec("vector")
                def _(c):
                    c.wait(e_ld, sle, B.ev("T"))
                    for k in range(NCH):
                        cur = n1x[:, k * (TC + 1) + 1:(k + 1) * (TC + 1)]
                        prv = n1x[:, k * (TC + 1):(k + 1) * (TC + 1) - 1]
                        c.chain(c.e.tensor_tensor(td5[:], cur, prv, ALU.subtract))
                        c.chain(c.e.scalar_tensor_tensor(
                            xr[:, k * TC:(k + 1) * TC], td5[:], mur[:, k:k + 1],
                            prv, ALU.mult, ALU.add))
                        c.chain(c.e.scalar_tensor_tensor(
                            xk[:, k * TC:(k + 1) * TC], td5[:], muk[:, k:k + 1],
                            prv, ALU.mult, ALU.add))
                e_mix = B.ev("V")

                for which, wmat, dst, act in (("r", wr, rb5, AF.Sigmoid),
                                              ("k", wk2, kb5, AF.Relu)):
                    for oc in range(NCH):
                        pmm = (ps_e0, ps_e1, ps_e2)[oc % 3]
                        pk = "5e%d" % (oc % 3)
                        src = xr if which == "r" else xk

                        @B.sec("tensor")
                        def _(c):
                            c.wait(e_mix, wle, B.lastev.get(pk))
                            for k in range(NCH):
                                ins = c.e.matmul(
                                    pmm[:, 0:TC],
                                    wmat[:, k * D + oc * P:k * D + oc * P + P],
                                    src[:, k * TC:(k + 1) * TC],
                                    start=(k == 0), stop=(k == NCH - 1))
                            c.inc(ins)
                        e_mm = B.ev("T")

                        @B.sec("scalar")
                        def _(c):
                            c.wait(e_mm)
                            ev = c.chain(c.e.activation(
                                dst[:, oc * TC:(oc + 1) * TC], pmm[:, 0:TC], act))
                            if which == "k":
                                ev = c.chain(c.e.activation(
                                    dst[:, oc * TC:(oc + 1) * TC],
                                    dst[:, oc * TC:(oc + 1) * TC], AF.Square))
                            B.lastev[pk] = ev
                e_rk = B.ev("S")

                for oc in range(NCH):
                    pmm = (ps_e0, ps_e1, ps_e2)[oc % 3]
                    pk = "5e%d" % (oc % 3)

                    @B.sec("tensor")
                    def _(c):
                        c.wait(e_rk, B.lastev.get(pk))
                        for k in range(NCH):
                            ins = c.e.matmul(
                                pmm[:, 0:TC],
                                wv2[:, k * D + oc * P:k * D + oc * P + P],
                                kb5[:, k * TC:(k + 1) * TC],
                                start=(k == 0), stop=(k == NCH - 1))
                        c.inc(ins)
                    e_mm = B.ev("T")

                    @B.sec("vector")
                    def _(c):
                        c.wait(e_mm)
                        B.lastev[pk] = c.chain(c.e.tensor_tensor(
                            tf6[:], pmm[:, 0:TC], rb5[:, oc * TC:(oc + 1) * TC],
                            ALU.mult))
                        c.chain(c.e.tensor_tensor(
                            R[:, oc * T + lo:oc * T + lo + TC],
                            R[:, oc * T + lo:oc * T + lo + TC], tf6[:], ALU.add))
    B.barrier()



def _fm(v, nch):
    return np.ascontiguousarray(v.reshape(nch, P).T.astype(np.float32))


def stage_inputs(inputs):
    i = {k: np.asarray(v) for k, v in inputs.items()}
    sh = {}
    sh["w_inproj"] = np.ascontiguousarray(i["L0_attn_in_proj_w"], np.float32)
    cw = i["L0_attn_conv_w"].reshape(DIN, 4)
    sh["conv_w"] = np.ascontiguousarray(
        cw.reshape(NIN, P, 4).transpose(1, 0, 2).reshape(P, NIN * 4), np.float32)
    sh["conv_b"] = _fm(i["L0_attn_conv_b"], NIN)
    xp = np.zeros((DIN, 112), np.float32)
    xp[:, 0:80] = i["L0_attn_x_proj_w"][:, 0:80]
    xp[:, 96:112] = i["L0_attn_x_proj_w"][:, 80:96]
    sh["w_xproj"] = xp
    sh["w_dtproj"] = np.ascontiguousarray(i["L0_attn_dt_proj_w"], np.float32)
    sh["dt_bias"] = _fm(i["L0_attn_dt_proj_b"], NIN)
    al = i["L0_attn_A_log"]
    sh["a_log"] = np.ascontiguousarray(
        al.reshape(NIN, P, DST).transpose(1, 0, 2).reshape(P, NIN * DST), np.float32)
    sh["d_vec"] = _fm(i["L0_attn_D"], NIN)
    sh["w_oproj"] = np.ascontiguousarray(i["L0_attn_out_proj_w"], np.float32)
    sh["ln1_g"] = _fm(i["L0_attn_ln1_g"], NCH)
    sh["ln1_b"] = _fm(i["L0_attn_ln1_b"], NCH)
    sh["ln2_g"] = _fm(i["L0_attn_ln2_g"], NCH)
    sh["ln2_b"] = _fm(i["L0_attn_ln2_b"], NCH)
    sh["w_ffn1"] = np.ascontiguousarray(i["L0_attn_ffn_w1"], np.float32)
    sh["ffn_b1"] = _fm(i["L0_attn_ffn_b1"], NFF)
    sh["w_ffn2"] = np.ascontiguousarray(i["L0_attn_ffn_w2"], np.float32)
    sh["ffn_b2"] = _fm(i["L0_attn_ffn_b2"], NCH)
    sh["gate_wT"] = np.ascontiguousarray(i["L0_ffn_gate_w"].T, np.float32)
    sh["gate_b"] = np.ascontiguousarray(i["L0_ffn_gate_b"].reshape(NEXP, 1),
                                        np.float32)
    sh["w13T"] = np.ascontiguousarray(i["L0_ffn_w13"].transpose(0, 2, 1), np.float32)
    sh["w2T"] = np.ascontiguousarray(i["L0_ffn_w2"].transpose(0, 2, 1), np.float32)
    sh["w13sT"] = np.ascontiguousarray(i["L0_ffn_w13_s"].T, np.float32)
    sh["w2sT"] = np.ascontiguousarray(i["L0_ffn_w2_s"].T, np.float32)
    sh["wqT"] = np.ascontiguousarray(i["L1_attn_Wq"].T, np.float32)
    sh["wkT"] = np.ascontiguousarray(i["L1_attn_Wk"].T, np.float32)
    sh["wvT"] = np.ascontiguousarray(i["L1_attn_Wv"].T, np.float32)
    sh["woT"] = np.ascontiguousarray(i["L1_attn_Wo"].T, np.float32)
    sh["l1_g"] = _fm(i["L1_attn_ln_g"], NCH)
    sh["l1_b"] = _fm(i["L1_attn_ln_b"], NCH)
    sh["hscale"] = np.ascontiguousarray(i["L1_attn_head_scales"].reshape(1, NH),
                                        np.float32)
    dirs = np.zeros((P, NH), np.float32)
    dirs[0:3, :] = i["L1_attn_head_dirs"].T
    sh["dirs"] = dirs
    sh["wrT"] = np.ascontiguousarray(i["L1_ffn_W_r"].T, np.float32)
    sh["wkT2"] = np.ascontiguousarray(i["L1_ffn_W_k"].T, np.float32)
    sh["wvT2"] = np.ascontiguousarray(i["L1_ffn_W_v"].T, np.float32)
    sh["mu_r"] = _fm(i["L1_ffn_mu_r"], NCH)
    sh["mu_k"] = _fm(i["L1_ffn_mu_k"], NCH)
    sh["identf"] = np.eye(P, dtype=np.float32)
    sh["ones_col"] = np.ones((P, 1), np.float32)
    sh["ones_row"] = np.ones((1, P), np.float32)
    tri = np.zeros((P, P), np.float32)
    tri[np.triu_indices(P, 1)] = NEG
    sh["trimask"] = tri
    rot = np.zeros((P, P), np.float32)
    for ii in range(64):
        rot[ii + 64, ii] = -1.0
        rot[ii, ii + 64] = 1.0
    sh["rot_sT"] = rot
    inv = 1.0 / (10000.0 ** (np.arange(0, HD, 2, dtype=np.float64) / HD))
    freqs = np.outer(np.arange(T, dtype=np.float64), inv)
    cos = np.concatenate([np.cos(freqs)] * 2, 1).T
    sin = np.concatenate([np.sin(freqs)] * 2, 1).T
    sc = 1.0 / np.sqrt(HD)
    sh["cos_q"] = np.ascontiguousarray(cos * sc, np.float32)
    sh["sin_q"] = np.ascontiguousarray(sin * sc, np.float32)
    sh["cos_k"] = np.ascontiguousarray(cos, np.float32)
    sh["sin_k"] = np.ascontiguousarray(sin, np.float32)
    sh["cpool"] = np.ascontiguousarray(
        np.broadcast_to(np.asarray(CONSTS, np.float32), (P, len(CONSTS))))
    s16 = np.zeros((NEXP, NEXP * P), np.float32)
    for n in range(NEXP):
        s16[n, n * P:(n + 1) * P] = 1.0
    sh["sel16"] = s16

    x = np.asarray(inputs["L0_attn_x"])
    per_core = [np.ascontiguousarray(x[c].T, np.float32) for c in range(8)]
    return sh, per_core


def kernel(**inputs):
    sh, per_core = stage_inputs(inputs)
    nc = build_program()
    in_maps = [dict(sh, x0=per_core[c]) for c in range(8)]
    res = run_bass_kernel_spmd(nc, in_maps, core_ids=list(range(8)))
    out = np.stack([np.asarray(res.results[c]["out"]).T for c in range(8)])
    return np.ascontiguousarray(out, np.float32)
